# version 05
# baseline (speedup 1.0000x reference)
import numpy as np
import ml_dtypes
from contextlib import ExitStack

import concourse.bass as bass
import concourse.mybir as mybir
from concourse.bass_utils import run_bass_kernel_spmd

F32 = mybir.dt.float32
BF16 = mybir.dt.bfloat16
AF = mybir.ActivationFunctionType
ALU = mybir.AluOpType
AX = mybir.AxisListType

MAGIC = float(np.float32(1.5 * 2 ** 23))
QMAX = 127.0
EPS = 1e-5


ENGINES = ("tensor", "vector", "scalar", "gpsimd", "sync")


class Tok:
    __slots__ = ("eng", "pos", "sem", "val", "clock", "force_wait")

    def __init__(self, eng, pos, sem, val, clock, force_wait=False):
        self.eng, self.pos, self.sem, self.val, self.clock = eng, pos, sem, val, clock
        self.force_wait = force_wait


class Buf:
    __slots__ = ("name", "writer", "readers")

    def __init__(self, name):
        self.name, self.writer, self.readers = name, None, []


class Sched:
    def __init__(self, nc, ctx):
        self.nc = nc
        self.ctx = ctx
        self.items = {e: [] for e in ENGINES}
        self.pos = {e: 0 for e in ENGINES}
        self.sem = {e: ctx.enter_context(nc.semaphore(f"sem_{e}")) for e in ENGINES}
        self.semval = {e: 0 for e in ENGINES}
        self.synced = {e: {} for e in ENGINES}
        self.drainmark = {e: -1 for e in ENGINES}
        self.dmaq = {}
        self.bufs = {}

    def buf(self, name):
        if name not in self.bufs:
            self.bufs[name] = Buf(name)
        return self.bufs[name]

    DMA_ROT = 8

    def add_dma_queue(self, name, eng):
        sems = [self.ctx.enter_context(self.nc.semaphore(f"dma_{name}{i}"))
                for i in range(self.DMA_ROT)]
        self.dmaq[name] = [eng, sems, [0] * self.DMA_ROT, 0]
        for i in range(self.DMA_ROT):
            self.pos[f"{name}#{i}"] = 0

    def _wait(self, eng, tok):
        syn = self.synced[eng]
        if tok.eng == eng and not tok.force_wait:
            if tok.pos <= self.drainmark[eng] or tok.pos <= syn.get(eng, -1):
                return False
            return True
        if tok.eng == eng and tok.force_wait:
            if tok.pos <= syn.get(eng, -1):
                return False
            self.items[eng].append(("wait", (tok.sem, tok.val)))
            syn[eng] = max(syn.get(eng, -1), tok.pos)
            return False
        if tok.pos <= syn.get(tok.eng, -1):
            return False
        assert tok.sem is not None, f"{eng} depends on un-inc'd op on {tok.eng}"
        self.items[eng].append(("wait", (tok.sem, tok.val)))
        for k, v in tok.clock.items():
            if syn.get(k, -1) < v:
                syn[k] = v
        return False

    def _deps(self, reads, writes):
        deps = []
        for b in reads:
            if b.writer is not None:
                deps.append(b.writer)
        for b in writes:
            if b.writer is not None:
                deps.append(b.writer)
            deps.extend(b.readers)
        return deps

    def op(self, eng, fn, reads=(), writes=(), inc=True, tag="", async_done=False):
        drain = False
        for tok in self._deps(reads, writes):
            drain |= self._wait(eng, tok)
        if drain:
            self.items[eng].append(("drain", None))
            self.drainmark[eng] = self.pos[eng]
        self.pos[eng] += 1
        mypos = self.pos[eng]
        clock = dict(self.synced[eng])
        clock[eng] = mypos
        if inc:
            self.semval[eng] += 1
            tok = Tok(eng, mypos, self.sem[eng], self.semval[eng], clock,
                      force_wait=async_done)
        else:
            assert not async_done
            tok = Tok(eng, mypos, None, None, clock)
        self.items[eng].append(("op", (fn, self.sem[eng] if inc else None,
                                       self.semval[eng], tag)))
        for b in reads:
            b.readers.append(tok)
        for b in writes:
            b.writer = tok
            b.readers = []
        return tok

    def dma(self, q, fn, reads=(), writes=(), tag=""):
        eng, sems, counts, total = self.dmaq[q]
        idx = total % self.DMA_ROT
        self.dmaq[q][3] = total + 1
        qn = f"{q}#{idx}"
        drain = False
        for tok in self._deps(reads, writes):
            drain |= self._wait(eng, tok)
        if counts[idx] > 0 and self.synced[eng].get(qn, -1) < counts[idx] * 16:
            self.items[eng].append(("wait", (sems[idx], counts[idx] * 16)))
            self.synced[eng][qn] = counts[idx] * 16
        if drain:
            self.items[eng].append(("drain", None))
            self.drainmark[eng] = self.pos[eng]
        self.pos[eng] += 1
        counts[idx] += 1
        cnt = counts[idx] * 16
        self.pos[qn] = cnt
        clock = dict(self.synced[eng])
        clock[eng] = self.pos[eng]
        clock[qn] = cnt
        tok = Tok(qn, cnt, sems[idx], cnt, clock)
        self.items[eng].append(("dma", (fn, sems[idx], cnt, tag)))
        for b in reads:
            b.readers.append(tok)
        for b in writes:
            b.writer = tok
            b.readers = []
        return tok

    def barrier(self):
        toks = []
        for e in ENGINES:
            self.pos[e] += 1
            self.semval[e] += 1
            clock = dict(self.synced[e])
            clock[e] = self.pos[e]
            toks.append(Tok(e, self.pos[e], self.sem[e], self.semval[e], clock))
            self.items[e].append(("drain", None))
            self.items[e].append(("rawop", (lambda eng, s_=self.sem[e]: eng.sem_inc(s_, 1),
                                            "bar")))
        dmatoks = []
        for q, (eng, sems, counts, total) in self.dmaq.items():
            for i in range(self.DMA_ROT):
                if counts[i] > 0:
                    dmatoks.append((f"{q}#{i}", sems[i], counts[i] * 16))
        for e in ENGINES:
            syn = self.synced[e]
            for t in toks:
                if t.eng == e or t.pos <= syn.get(t.eng, -1):
                    continue
                self.items[e].append(("wait", (t.sem, t.val)))
                for k, v in t.clock.items():
                    if syn.get(k, -1) < v:
                        syn[k] = v
            for q, sem, val in dmatoks:
                if val > syn.get(q, -1):
                    self.items[e].append(("wait", (sem, val)))
                    syn[q] = val
            self.items[e].append(("drain", None))
            self.drainmark[e] = self.pos[e]
        for b in self.bufs.values():
            b.readers = []
            if b.writer is not None:
                b.writer = toks[ENGINES.index(b.writer.eng)] if b.writer.eng in ENGINES \
                    else None

    def finalize(self):
        nc = self.nc
        with nc.Block() as blk:
            def make(name):
                def body(eng):
                    for kind, payload in self.items[name]:
                        if kind == "wait":
                            eng.wait_ge(*payload)
                        elif kind == "drain":
                            eng.drain()
                        elif kind == "rawop":
                            payload[0](eng)
                        else:
                            fn, sem, val, tag = payload[0], payload[1], payload[2], payload[3]
                            inst = fn(eng)
                            if sem is not None:
                                assert inst is not None, f"op '{tag}' returned None"
                                inst.then_inc(sem, 16 if kind == "dma" else 1)
                return body
            blk.tensor(make("tensor"))
            blk.vector(make("vector"))
            blk.scalar(make("scalar"))
            blk.gpsimd(make("gpsimd"))
            blk.sync(make("sync"))



def host_constants():
    c = {}
    c["ident_f32"] = np.eye(128, dtype=np.float32)
    c["ident_bf"] = np.eye(128, dtype=ml_dtypes.bfloat16)
    c["mask_sl"] = np.tril(np.ones((64, 64), np.float32), -1)
    su = np.triu(np.ones((64, 64), np.float32), 1)
    iu = np.triu(np.ones((64, 64), np.float32), 0)
    c["mask_quadT"] = np.concatenate([su, iu], axis=1)
    sel = np.zeros((128, 128), np.float32)
    for cc in range(8):
        for p in range(128):
            sel[p, 16 * cc + 2 * cc + (p // 64)] = 1.0
    c["sel_gn"] = sel.astype(ml_dtypes.bfloat16)
    selb = np.zeros((16, 1024), np.float32)
    for cc in range(8):
        for p in range(128):
            selb[2 * cc + (p // 64), 128 * cc + p] = 1.0
    c["sel_gnb"] = selb
    on = np.ones((128, 1), np.float32)
    c["ones_col"] = on.astype(ml_dtypes.bfloat16)
    c["ones_row"] = np.ones((1, 128), np.float32)
    c["ones_colf"] = np.ones((128, 1), np.float32)
    bs = np.zeros((64, 256), np.float32)
    for S in range(16):
        bs[:, 16 * S + S] = 1.0
    c["bsel_hm"] = bs.astype(ml_dtypes.bfloat16)
    hs = np.zeros((16, 1024), np.float32)
    for S in range(16):
        hs[S, 64 * S:64 * (S + 1)] = 1.0
    c["sel_hm"] = hs
    return c


CONST_SPECS = [("ident_f32", (128, 128), F32), ("ident_bf", (128, 128), BF16),
               ("mask_sl", (64, 64), F32), ("mask_quadT", (64, 128), F32),
               ("sel_gn", (128, 128), BF16), ("sel_gnb", (16, 1024), F32),
               ("ones_col", (128, 1), BF16), ("ones_row", (1, 128), F32),
               ("ones_colf", (128, 1), F32),
               ("bsel_hm", (64, 256), BF16), ("sel_hm", (16, 1024), F32)]

VEC_NAMES = ["x_r", "x_w", "x_k", "x_v", "x_a", "k_k", "k_a", "w_lbias", "a_lbias",
             "g_lbias", "gn_w", "gn_b"]


class Cfg:
    def __init__(self, T=8192, TA=256, TB=512, n_cores=8, stop_after=None,
                 ydt="f32", hdt="f16"):
        self.T, self.TA, self.TB, self.n_cores = T, TA, TB, n_cores
        self.stop_after = stop_after
        self.ydt = ydt
        self.hdt = hdt
        self.LC = 64
        self.NCH = 8
        self.I = 2816
        self.NI = 22
        self.NCC = TA // 64
        assert T % TA == 0 and TA % 64 == 0 and T % TB == 0



def _finish_early(nc, sch, ctx):
    sch.barrier()
    global _LAST_SCHED
    _LAST_SCHED = sch
    sch.finalize()
    ctx.close()
    return nc


def build_program(cfg):
    nc = bass.Bass("TRN2", target_bir_lowering=False)
    F16 = mybir.dt.float16
    YDT = {"f32": F32, "bf16": BF16, "f16": F16}[cfg.ydt]
    HDT = {"f32": F32, "bf16": BF16, "f16": F16}[cfg.hdt]
    T, TA, TB, LC, NCH, NI, NCC = cfg.T, cfg.TA, cfg.TB, cfg.LC, cfg.NCH, cfg.NI, cfg.NCC
    NT_A, NT_B = T // TA, T // TB

    dI = {}

    def din(name, shape, dt=F32):
        dI[name] = nc.dram_tensor(name, list(shape), dt, kind="ExternalInput").ap()

    din("x", (T, 1024))
    z_out = nc.dram_tensor("z", [T, 1024], F32, kind="ExternalOutput").ap()
    for n, s in [("W_r", (1024, 1024)), ("W_k", (1024, 1024)), ("W_v", (1024, 1024)),
                 ("W_o", (1024, 1024)), ("w_la", (1024, 64)), ("w_lb", (64, 1024)),
                 ("a_la", (1024, 64)), ("a_lb", (64, 1024)), ("g_la", (1024, 128)),
                 ("g_lb", (128, 1024)), ("wg", (2816, 1024)), ("wu", (2816, 1024)),
                 ("wd", (1024, 2816)), ("r_k", (16, 64)), ("clip_x", (1,)),
                 ("clip_down", (1,))]:
        din(n, s)
    for n in VEC_NAMES:
        din(n, (1024,))
    for n, s, dt in CONST_SPECS:
        din(n, s, dt)

    y_dram = nc.dram_tensor("y_spill", [NCH, 128, T], YDT).ap()
    h_dram = nc.dram_tensor("h_spill", [NI, 128, T], HDT).ap()
    wgT_dram = nc.dram_tensor("wgT", [8, 128, 2816], BF16).ap()
    wuT_dram = nc.dram_tensor("wuT", [8, 128, 2816], BF16).ap()
    wdT_dram = nc.dram_tensor("wdT", [22, 128, 1024], BF16).ap()
    cc_io = [nc.dram_tensor(f"cc_{i}", [1, 8], F32).ap() for i in range(4)]

    ctx = ExitStack()
    sch = Sched(nc, ctx)
    sch.add_dma_queue("ld", "sync")
    sch.add_dma_queue("st", "sync")
    sch.add_dma_queue("ldw", "scalar")
    sch.add_dma_queue("gld", "gpsimd")
    bf = sch.buf

    sb = {}

    _names = {}

    def sba(name, shape, dt, scope_ctx=None):
        _names[name] = _names.get(name, 0) + 1
        sb[name] = (scope_ctx or ctx).enter_context(
            nc.sbuf_tensor(f"sb_{name}_{_names[name]}", list(shape), dt))
        return sb[name]

    for n, s, dt in CONST_SPECS:
        sba(n, s, dt)
    for n in VEC_NAMES + ["onemka", "negwlb", "r_kv"]:
        sba(n, (128, 8), F32)
    for n in ["kkh", "kah", "onemkah", "albh", "negwlbh", "rkh"]:
        sba(n, (64, 16), F32)
    sba("scal", (128, 4), F32)
    sba("St", (64, 16 * 64), F32)
    sba("sg", (128, 22), F32)
    sba("su", (128, 22), F32)
    sba("sd", (128, 8), F32)
    sba("sgx", (128, 22), F32)
    sba("sux", (128, 22), F32)
    sba("sdh", (128, 8), F32)
    sba("ymax", (128, 1), F32)
    sba("hmax", (128, 1), F32)
    sba("sm", (128, 16), F32)

    PS = [ctx.enter_context(nc.psum_tensor(f"ps{i}", [128, 512], F32)) for i in range(8)]
    PSB = [p.ap().bitcast(BF16) for p in PS]
    WKbank = {0: PS[4], 1: PS[2]}
    QTbank = {0: PS[6], 1: PS[7]}

    def WKb(par, reg):
        return bf(f"WK{par}{reg}")

    PS2A = [bf("WK1r1"), bf("WK1r2"), bf("WK1r3")]

    for n, s, dt in CONST_SPECS:
        t, src = sb[n], dI[n]
        sch.dma("ld", lambda e, t=t, src=src: e.dma_start(out=t[:, :], in_=src),
                writes=[bf(n)], tag=f"ld_{n}")
    for n in VEC_NAMES:
        t, src = sb[n], dI[n]
        sch.dma("ld", lambda e, t=t, src=src: e.dma_start(
            out=t[:, :], in_=src.rearrange("(c p) -> p c", p=128),
            allow_slow_non_contiguous=True), writes=[bf(n)], tag=f"ld_{n}")
    sch.dma("ld", lambda e: e.dma_start(
        out=sb["r_kv"][:, :],
        in_=dI["r_k"].rearrange("h d -> (h d)").rearrange("(c p) -> p c", p=128),
        allow_slow_non_contiguous=True), writes=[bf("r_kv")], tag="ld_rk")
    for n, src in [("kkh", "k_k"), ("kah", "k_a"), ("albh", "a_lbias")]:
        t, sr = sb[n], dI[src]
        sch.dma("ld", lambda e, t=t, sr=sr: e.dma_start(
            out=t[:, :], in_=sr.rearrange("(s j) -> j s", j=64),
            allow_slow_non_contiguous=True), writes=[bf(n)], tag=f"ld_{n}")
    sch.dma("ld", lambda e: e.dma_start(
        out=sb["negwlbh"][:, :], in_=dI["w_lbias"].rearrange("(s j) -> j s", j=64),
        allow_slow_non_contiguous=True), writes=[bf("negwlbh")], tag="ld_nwh")
    sch.dma("ld", lambda e: e.dma_start(
        out=sb["rkh"][:, :],
        in_=dI["r_k"].rearrange("h d -> (h d)").rearrange("(s j) -> j s", j=64),
        allow_slow_non_contiguous=True), writes=[bf("rkh")], tag="ld_rkh")
    sch.op("vector", lambda e: e.tensor_scalar_mul(
        sb["negwlbh"][:, :], sb["negwlbh"][:, :], -1.0),
        reads=[bf("negwlbh")], writes=[bf("negwlbh")], tag="nwh")
    sch.op("vector", lambda e: e.tensor_scalar(
        sb["onemkah"][:, :], sb["kah"][:, :], -1.0, 1.0, op0=ALU.mult, op1=ALU.add),
        reads=[bf("kah")], writes=[bf("onemkah")], tag="omkh")
    sch.dma("ld", lambda e: e.dma_start(out=sb["sm"][0:1, 1:2], in_=dI["clip_x"].unsqueeze(0)),
            writes=[bf("clipx")], tag="ld_clipx")
    sch.dma("ld", lambda e: e.dma_start(out=sb["sm"][0:1, 2:3],
                                        in_=dI["clip_down"].unsqueeze(0)),
            writes=[bf("clipd")], tag="ld_clipd")

    sch.op("vector", lambda e: e.tensor_scalar(sb["onemka"][:, :], sb["k_a"][:, :],
           -1.0, 1.0, op0=ALU.mult, op1=ALU.add), reads=[bf("k_a")], writes=[bf("onemka")])
    sch.op("vector", lambda e: e.tensor_scalar_mul(sb["negwlb"][:, :], sb["w_lbias"][:, :],
           -1.0), reads=[bf("w_lbias")], writes=[bf("negwlb")])
    sch.op("vector", lambda e: e.memset(sb["St"][:, :], 0.0),
           writes=[bf(f"St_{c}") for c in range(8)])
    sch.op("vector", lambda e: e.memset(sb["ymax"][:, :], 0.0), writes=[bf("ymax")])
    sch.op("vector", lambda e: e.memset(sb["hmax"][:, :], 0.0), writes=[bf("hmax")])
    sch.op("vector", lambda e: e.memset(sb["sm"][:, 0:1], 0.0), writes=[bf("zero1")])
    psbufs = [bf("PS0"), bf("PS1"), WKb(1, "r1"), bf("PS3"),
              WKb(0, "r1"), bf("PS5"), bf("QT0"), bf("QT1")]
    for i in range(8):
        eng = "vector"
        extra = []
        if i == 4:
            extra = [WKb(0, "r2"), WKb(0, "r3")]
        if i == 2:
            extra = [WKb(1, "r2"), WKb(1, "r3")]
        sch.op(eng, lambda e, i=i: e.memset(PS[i][:, :], 0.0),
               writes=[psbufs[i]] + extra, tag="psz")

    with ExitStack() as qctx:
        wtl = [qctx.enter_context(nc.sbuf_tensor(f"wq_in{i}", [128, 2816], F32))
               for i in range(2)]
        qtl = [qctx.enter_context(nc.sbuf_tensor(f"wq_q{i}", [128, 2816], BF16))
               for i in range(2)]
        ttl = qctx.enter_context(nc.sbuf_tensor("wq_t", [128, 512], BF16))
        rmx = qctx.enter_context(nc.sbuf_tensor("wq_m", [128, 2], F32))

        def quant_weight(wsrc, n_chunks, row_len, scale_sb, qdst, tag):
            nblk = row_len // 128
            for i in range(n_chunks):
                p_ = i % 2
                wt, qt = wtl[p_], qtl[p_]
                bw, bq_ = bf(f"qw_w{p_}"), bf(f"qw_q{p_}")
                sch.dma("ldw", lambda e, wt=wt, i=i: e.dma_start(
                    out=wt[:, 0:row_len], in_=wsrc[i * 128:(i + 1) * 128, :]),
                    writes=[bw], tag=f"{tag}ld")
                sch.op("vector", lambda e, wt=wt: e.reduce_max(
                    rmx[:, 0:1], wt[:, 0:row_len], axis=AX.X, apply_absolute_value=True),
                    reads=[bw], writes=[bf("qwm")], tag=f"{tag}mx")
                sch.op("vector", lambda e, i=i, scale_sb=scale_sb: e.tensor_scalar_mul(
                    scale_sb[:, i:i + 1], rmx[:, 0:1], 1.0 / QMAX),
                    reads=[bf("qwm")], writes=[bf("qws")], tag=f"{tag}s")
                sch.op("vector", lambda e: e.reciprocal(rmx[:, 1:2], rmx[:, 0:1]),
                       reads=[bf("qwm")], writes=[bf("qwr")], tag=f"{tag}r")
                sch.op("vector", lambda e: e.tensor_scalar_mul(rmx[:, 1:2], rmx[:, 1:2],
                                                               QMAX),
                       reads=[bf("qwr")], writes=[bf("qwr")], tag=f"{tag}r2")
                sch.op("vector", lambda e, wt=wt: e.tensor_scalar(
                    wt[:, 0:row_len], wt[:, 0:row_len], rmx[:, 1:2], MAGIC,
                    op0=ALU.mult, op1=ALU.add),
                    reads=[bw, bf("qwr")], writes=[bw], tag=f"{tag}q1")
                sch.op("vector", lambda e, wt=wt: e.tensor_scalar(
                    wt[:, 0:row_len], wt[:, 0:row_len], -MAGIC, -(QMAX + 1.0),
                    op0=ALU.add, op1=ALU.max), reads=[bw], writes=[bw], tag=f"{tag}q2")
                sch.op("vector", lambda e, wt=wt, qt=qt: e.tensor_single_scalar(
                    qt[:, 0:row_len], wt[:, 0:row_len], QMAX, op=ALU.min),
                    reads=[bw], writes=[bq_], tag=f"{tag}q3")
                for g0 in range(0, nblk, 4):
                    ng = min(4, nblk - g0)
                    sch.op("tensor", lambda e, qt=qt, g0=g0, ng=ng: [
                        e.transpose(PSB[6][:, j * 128:(j + 1) * 128],
                                    qt[:, (g0 + j) * 128:(g0 + j + 1) * 128],
                                    sb["ident_bf"][:, :]) for j in range(ng)][-1],
                        reads=[bq_, bf("ident_bf"), bf("QT0")], writes=[bf("QT0")],
                        tag=f"{tag}tp")
                    sch.op("vector", lambda e, ng=ng: e.tensor_copy(
                        ttl[:, 0:ng * 128], PSB[6][:, 0:ng * 128]),
                        reads=[bf("QT0")], writes=[bf("qwt"), bf("QT0")], tag=f"{tag}te")
                    sch.dma("st", lambda e, qdst=qdst, i=i, g0=g0, ng=ng: e.dma_start(
                        out=qdst[g0:g0 + ng, :, i * 128:(i + 1) * 128]
                            .rearrange("k p c -> p k c"),
                        in_=ttl[:, 0:ng * 128].rearrange("p (k c) -> p k c", k=ng)),
                        reads=[bf("qwt")], tag=f"{tag}st")

        quant_weight(dI["wg"], NI, 1024, sb["sg"], wgT_dram, "qg")
        quant_weight(dI["wu"], NI, 1024, sb["su"], wuT_dram, "qu")
        quant_weight(dI["wd"], 8, 2816, sb["sd"], wdT_dram, "qd")
        sch.barrier()

    if cfg.stop_after == "wq":
        return _finish_early(nc, sch, ctx)

    actx = ExitStack()
    sba("Wb0", (128, 1024), BF16, actx)
    sba("Wb1", (128, 1024), BF16, actx)
    sba("wla", (128, 8 * 64), BF16, actx)
    sba("wlb", (64, 1024), BF16, actx)
    sba("ala", (128, 8 * 64), BF16, actx)
    sba("alb", (64, 1024), BF16, actx)
    sba("gla", (64, 16 * 128), BF16, actx)
    sba("glb", (128, 1024), BF16, actx)
    for n, src, ck in [("wla", "w_la", 8), ("ala", "a_la", 8)]:
        t, s = sb[n], dI[src]
        sch.dma("gld", lambda e, t=t, s=s, ck=ck: e.dma_start(
            out=t.ap().rearrange("p (k c) -> p k c", k=ck),
            in_=s.rearrange("(k p) c -> p k c", p=128)),
            writes=[bf(n)], tag=f"cast_{n}")
    for n, src in [("wlb", "w_lb"), ("alb", "a_lb"), ("glb", "g_lb")]:
        t, s = sb[n], dI[src]
        sch.dma("gld", lambda e, t=t, s=s: e.dma_start(out=t[:, :], in_=s),
                writes=[bf(n)], tag=f"cast_{n}")
    sch.dma("gld", lambda e: e.dma_start(
        out=sb["gla"].ap().rearrange("j (s m) -> j s m", s=16),
        in_=dI["g_la"].rearrange("(s j) m -> j s m", j=64)),
        writes=[bf("gla")], tag="cast_gla")

    sba("xtm", (128, 1024), F32, actx)
    sba("xfm", (128, NCH * TA), F32, actx)
    sba("delta", (128, NCH * TA), BF16, actx)
    sba("xprev", (128, 8), F32, actx)
    sba("wst", (64, 16 * TA), BF16, actx)
    sba("rst", (64, 16 * TA), BF16, actx)
    sba("kst", (64, 16 * TA), BF16, actx)
    sba("vst", (128, NCH * TA), BF16, actx)
    sba("ahst", (64, 16 * TA), BF16, actx)
    sba("kkst", (64, 16 * TA), BF16, actx)
    sba("gst", (128, NCH * TA), BF16, actx)
    sba("ost", (128, NCH * TA), BF16, actx)
    sba("mixA", (128, NCH * TA), BF16, actx)
    sba("mixB", (128, NCH * TA), BF16, actx)
    sba("lr", (64, 4 * NCH * TA), BF16, actx)
    sba("bk", (64, 4 * NCH * TA), BF16, actx)
    sba("z1", (64, TA), BF16, actx)
    sba("g1", (128, TA), BF16, actx)
    sba("ytile", (128, NCH * TA), YDT, actx)
    sba("prep", (64, 2 * TA), F32, actx)
    sba("prepE", (64, 4 * TA), BF16, actx)
    sba("prep2", (64, 2 * TA), F32, actx)
    sba("eb", (64, 16 * 3 * NCC + 6 * NCC), F32, actx)
    sba("kkr", (64, 2 * TA), BF16, actx)
    sba("sqh", (64, 2 * TA), BF16, actx)
    sba("sq", (128, TA), BF16, actx)
    sba("sq2", (128, TA), BF16, actx)
    sba("gnstat", (16, 3 * TA), F32, actx)
    sba("lnrow", (1, 3 * TA), F32, actx)
    for par in range(2):
        sba(f"qts{par}", (64, 512), BF16, actx)
        sba(f"m1{par}", (64, 128), BF16, actx)
        sba(f"tm{par}", (64, 512), BF16, actx)
        sba(f"x0{par}", (64, 256), BF16, actx)
        sba(f"x1{par}", (64, 256), BF16, actx)
        sba(f"xf{par}", (64, 256), F32, actx)
        sba(f"sqp{par}", (64, 5 * 256), BF16, actx)
        sba(f"wfm{par}", (64, 128), BF16, actx)
        sba(f"sgm{par}", (64, 128), F32, actx)
        sba(f"sgmb{par}", (64, 128), BF16, actx)
        sba(f"str{par}", (64, 128), BF16, actx)
        sba(f"stt{par}", (64, 128), F32, actx)
    sch.op("vector", lambda e: e.memset(sb["xprev"][:, :], 0.0), writes=[bf("xprev")])

    def mm_group(psum_ap_fn, lhs, lhs_cols, rhs, rhs_cols, nk):
        def fn(e):
            inst = None
            for k in range(nk):
                l0, l1 = lhs_cols(k)
                r0, r1 = rhs_cols(k)
                inst = e.matmul(psum_ap_fn(), lhs[:, l0:l1], rhs[:, r0:r1],
                                start=(k == 0), stop=(k == nk - 1))
            return inst
        return fn

    _sa = cfg.stop_after if isinstance(cfg.stop_after, str) else ""
    _alim = int(_sa[1:]) if _sa.startswith("A") and len(_sa) > 1 else 99
    for t in range(NT_A):
        t0 = t * TA
        xfm, delta = sb["xfm"], sb["delta"]
        bxfm, bdelta = bf("xfm"), bf("delta")

        for blk_i in range(TA // 128):
            bxt = bf("xtm0")
            sch.dma("ld", lambda e, blk_i=blk_i, t0=t0: e.dma_start(
                out=sb["xtm"][:, :],
                in_=dI["x"][t0 + blk_i * 128: t0 + (blk_i + 1) * 128, :]),
                writes=[bxt], tag="xld")
            for half in range(2):
                sch.op("tensor", lambda e, half=half: [
                    e.transpose(PS[0][:, i * 128:(i + 1) * 128],
                                sb["xtm"][:, (half * 4 + i) * 128:
                                          (half * 4 + i + 1) * 128],
                                sb["ident_f32"][:, :]) for i in range(4)][-1],
                    reads=[bxt, bf("ident_f32"), bf("PS0")], writes=[bf("PS0")], tag="xtp")
                sch.op("vector", lambda e, half=half, blk_i=blk_i: e.tensor_copy(
                    xfm.ap().rearrange("p (c u) -> p c u", c=NCH)[
                        :, half * 4:(half + 1) * 4, blk_i * 128:(blk_i + 1) * 128],
                    PS[0][:, 0:512].rearrange("p (c u) -> p c u", c=4)),
                    reads=[bf("PS0")], writes=[bxfm, bf("PS0")], tag="xev")

        if _alim < 1:
            continue
        for c in range(NCH):
            co = c * TA
            sch.op("vector", lambda e, co=co: e.tensor_sub(
                delta[:, co + 1:co + TA], xfm[:, co:co + TA - 1], xfm[:, co + 1:co + TA]),
                reads=[bxfm], writes=[bdelta], tag="dlt")
            sch.op("vector", lambda e, co=co, c=c: e.tensor_sub(
                delta[:, co:co + 1], sb["xprev"][:, c:c + 1], xfm[:, co:co + 1]),
                reads=[bxfm, bf("xprev")], writes=[bdelta], tag="dlt0")
        sch.op("vector", lambda e: e.tensor_copy(
            sb["xprev"][:, :],
            xfm.ap().rearrange("p (c u) -> p c u", c=NCH)[:, :, TA - 1]),
            reads=[bxfm], writes=[bf("xprev")], tag="xpv")

        def emit_mix(mixn, vecn, tag):
            mx, bm = sb[mixn], bf(mixn)
            for c in range(NCH):
                co = c * TA
                sch.op("vector", lambda e, co=co, c=c, mx=mx: e.scalar_tensor_tensor(
                    mx[:, co:co + TA], delta[:, co:co + TA], sb[vecn][:, c:c + 1],
                    xfm[:, co:co + TA], op0=ALU.mult, op1=ALU.add),
                    reads=[bdelta, bxfm, bf(vecn), bm], writes=[bm], tag=tag)

        def emit_proj(Wsrc, mixn, outn, tag):
            for f in range(NCH):
                bank, bb = (PS[0], bf("PS0")) if f % 2 == 0 else (PS[1], bf("PS1"))
                wn, bwo = ld_wblk(Wsrc, f, tag + "w")
                sch.op("tensor", mm_group(
                    lambda bank=bank: bank[:, 0:TA], sb[wn],
                    lambda k: (k * 128, (k + 1) * 128),
                    sb[mixn], lambda k: (k * TA, (k + 1) * TA), 8),
                    reads=[bwo, bf(mixn), bb], writes=[bb], tag=tag)
                sch.op("vector", lambda e, bank=bank, f=f, outn=outn: e.tensor_copy(
                    sb[outn][:, f * TA:(f + 1) * TA], bank[:, 0:TA]),
                    reads=[bb], writes=[bf(outn), bb], tag=tag + "e")

        def ld_wblk(Wsrc, f, tag):
            wn = f"Wb{f % 2}"
            bwo = bf(wn)
            sch.dma("gld", lambda e, f=f, wn=wn, Wsrc=Wsrc: e.dma_start(
                out=sb[wn].ap().rearrange("p (k c) -> p k c", k=8),
                in_=dI[Wsrc][:, f * 128:(f + 1) * 128].rearrange("(k p) c -> p k c",
                                                                 p=128)),
                writes=[bwo], tag=tag)
            return wn, bwo

        def emit_proj_hm(Wsrc, mixn, outn, tag):
            for c in range(NCH):
                bank, bb = (PS[0], bf("PS0")) if c % 2 == 0 else (PS[1], bf("PS1"))
                wn, bwo = ld_wblk(Wsrc, c, tag + "w")
                def fn(e, c=c, bank=bank, wn=wn, mixn=mixn):
                    inst = None
                    for h in range(2):
                        for k in range(8):
                            inst = e.matmul(
                                bank[0:64, h * TA:(h + 1) * TA],
                                sb[wn][:, k * 128 + h * 64:k * 128 + (h + 1) * 64],
                                sb[mixn][:, k * TA:(k + 1) * TA],
                                start=(k == 0), stop=(k == 7))
                    return inst
                sch.op("tensor", fn, reads=[bwo, bf(mixn), bb], writes=[bb], tag=tag)
                sch.op("vector", lambda e, bank=bank, c=c, outn=outn: e.tensor_copy(
                    sb[outn][:, 2 * c * TA:(2 * c + 2) * TA], bank[0:64, 0:2 * TA]),
                    reads=[bb], writes=[bf(outn), bb], tag=tag + "e")

        emit_mix("mixA", "x_r", "mxr")
        emit_proj_hm("W_r", "mixA", "rst", "pjr")
        emit_mix("mixB", "x_k", "mxk")
        emit_proj_hm("W_k", "mixB", "kst", "pjk")
        emit_mix("mixA", "x_v", "mxv")
        emit_proj("W_v", "mixA", "vst", "pjv")

        if _alim < 2:
            continue
        emit_mix("mixB", "x_w", "mxw")
        sch.op("tensor", mm_group(lambda: PS[2][0:64, 0:TA], sb["wla"],
                                  lambda k: (k * 64, (k + 1) * 64),
                                  sb["mixB"], lambda k: (k * TA, (k + 1) * TA), 8),
               reads=[bf("wla"), bf("mixB")] + PS2A, writes=PS2A, tag="wl1")
        sch.op("scalar", lambda e: e.activation(sb["z1"][:, 0:TA], PS[2][0:64, 0:TA],
                                                AF.Tanh),
               reads=PS2A, writes=[bf("z1")] + PS2A, tag="wlt")
        for f in range(NCH):
            def wl2(e, f=f):
                e.matmul(PS[3][0:64, 0:TA], sb["wlb"][:, f * 128:f * 128 + 64],
                         sb["z1"][:, 0:TA], start=True, stop=True)
                return e.matmul(PS[3][0:64, TA:2 * TA],
                                sb["wlb"][:, f * 128 + 64:(f + 1) * 128],
                                sb["z1"][:, 0:TA], start=True, stop=True)
            sch.op("tensor", wl2, reads=[bf("wlb"), bf("z1"), bf("PS3")],
                   writes=[bf("PS3")], tag="wl2")
            for h in range(2):
                S = 2 * f + h
                sch.op("scalar", lambda e, h=h, S=S: e.activation(
                    sb["prep2"][:, h * TA:(h + 1) * TA], PS[3][0:64, h * TA:(h + 1) * TA],
                    AF.Exp, bias=sb["negwlbh"][:, S:S + 1], scale=-1.0),
                    reads=[bf("PS3"), bf("negwlbh"), bf("prep2")],
                    writes=[bf("prep2")] + ([bf("PS3")] if h == 1 else []), tag="wls")
            sch.op("vector", lambda e: e.tensor_scalar_add(
                sb["prep2"][:, 0:2 * TA], sb["prep2"][:, 0:2 * TA], 1.0),
                reads=[bf("prep2")], writes=[bf("prep2")], tag="wls1")
            sch.op("scalar", lambda e: e.activation(
                sb["prep2"][:, 0:2 * TA], sb["prep2"][:, 0:2 * TA], AF.Ln),
                reads=[bf("prep2")], writes=[bf("prep2")], tag="wls2")
            sch.op("vector", lambda e, f=f: e.tensor_scalar(
                sb["wst"][:, 2 * f * TA:(2 * f + 2) * TA], sb["prep2"][:, 0:2 * TA],
                -1.0, -0.5, op0=ALU.mult, op1=ALU.add),
                reads=[bf("prep2")], writes=[bf("wst")], tag="wlf")

        emit_mix("mixB", "x_a", "mxa")
        sch.op("tensor", mm_group(lambda: PS[2][0:64, 0:TA], sb["ala"],
                                  lambda k: (k * 64, (k + 1) * 64),
                                  sb["mixB"], lambda k: (k * TA, (k + 1) * TA), 8),
               reads=[bf("ala"), bf("mixB")] + PS2A, writes=PS2A, tag="al1")
        sch.op("vector", lambda e: e.tensor_copy(sb["z1"][:, 0:TA], PS[2][0:64, 0:TA]),
               reads=PS2A, writes=[bf("z1")] + PS2A, tag="alc")
        for f in range(NCH):
            def al2(e, f=f):
                e.matmul(PS[3][0:64, 0:TA], sb["alb"][:, f * 128:f * 128 + 64],
                         sb["z1"][:, 0:TA], start=True, stop=True)
                return e.matmul(PS[3][0:64, TA:2 * TA],
                                sb["alb"][:, f * 128 + 64:(f + 1) * 128],
                                sb["z1"][:, 0:TA], start=True, stop=True)
            sch.op("tensor", al2, reads=[bf("alb"), bf("z1"), bf("PS3")],
                   writes=[bf("PS3")], tag="al2")
            for h in range(2):
                S = 2 * f + h
                sch.op("scalar", lambda e, h=h, S=S: e.activation(
                    sb["ahst"][:, S * TA:(S + 1) * TA], PS[3][0:64, h * TA:(h + 1) * TA],
                    AF.Sigmoid, bias=sb["albh"][:, S:S + 1]),
                    reads=[bf("PS3"), bf("albh")],
                    writes=[bf("ahst")] + ([bf("PS3")] if h == 1 else []), tag="als")

        sch.op("tensor", mm_group(lambda: PS[2][:, 0:TA], sb["gla"],
                                  lambda k: (k * 128, (k + 1) * 128),
                                  sb["rst"], lambda k: (k * TA, (k + 1) * TA), 16),
               reads=[bf("gla"), bf("rst")] + PS2A, writes=PS2A, tag="gl1")
        sch.op("scalar", lambda e: e.activation(sb["g1"][:, 0:TA], PS[2][:, 0:TA],
                                                AF.Sigmoid),
               reads=PS2A, writes=[bf("g1")] + PS2A, tag="gls")
        for f in range(NCH):
            sch.op("tensor", lambda e, f=f: e.matmul(
                PS[3][:, 0:TA], sb["glb"][:, f * 128:(f + 1) * 128], sb["g1"][:, 0:TA],
                start=True, stop=True),
                reads=[bf("glb"), bf("g1"), bf("PS3")], writes=[bf("PS3")], tag="gl2")
            sch.op("vector", lambda e, f=f: e.tensor_scalar(
                sb["gst"][:, f * TA:(f + 1) * TA], PS[3][:, 0:TA],
                sb["g_lbias"][:, f:f + 1], None, op0=ALU.add),
                reads=[bf("PS3"), bf("g_lbias")], writes=[bf("gst"), bf("PS3")], tag="glf")

        if _alim < 3:
            continue
        for c in range(NCH):
            ho = 2 * c * TA
            for h in range(2):
                S = 2 * c + h
                sch.op("vector", lambda e, ho=ho, h=h, S=S: e.tensor_scalar(
                    sb["kkr"][:, h * TA:(h + 1) * TA],
                    sb["kst"][:, ho + h * TA:ho + (h + 1) * TA],
                    sb["kkh"][:, S:S + 1], None, op0=ALU.mult),
                    reads=[bf("kst"), bf("kkh"), bf("kkr")], writes=[bf("kkr")], tag="kk1")
            sch.op("scalar", lambda e: e.activation(sb["sqh"][:, 0:2 * TA],
                                                    sb["kkr"][:, 0:2 * TA], AF.Square),
                   reads=[bf("kkr"), bf("sqh")], writes=[bf("sqh")], tag="kk2")
            def kk3(e, c=c):
                inst = None
                for h in range(2):
                    S = 2 * c + h
                    inst = e.matmul(PS[5][0:16, 0:TA], sb["bsel_hm"][:, 16 * S:16 * (S + 1)],
                                    sb["sqh"][:, h * TA:(h + 1) * TA],
                                    start=(S == 0), stop=(S == 15))
                return inst
            sch.op("tensor", kk3, reads=[bf("sqh"), bf("bsel_hm"), bf("PS5")],
                   writes=[bf("PS5")], tag="kk3")
            sch.op("scalar", lambda e, ho=ho: e.copy(
                sb["kkst"][:, ho:ho + 2 * TA], sb["kkr"][:, 0:2 * TA]),
                reads=[bf("kkr")], writes=[bf("kkst"), bf("kkr")], tag="kk4")
        g_ = sb["gnstat"]
        sch.op("scalar", lambda e: e.activation(g_[0:16, 0:TA], PS[5][0:16, 0:TA], AF.Sqrt),
               reads=[bf("PS5"), bf("gnstat")], writes=[bf("gnstat"), bf("PS5")], tag="kk5")
        sch.op("vector", lambda e: e.tensor_single_scalar(
            g_[0:16, 0:TA], g_[0:16, 0:TA], 1e-12, op=ALU.max),
            reads=[bf("gnstat")], writes=[bf("gnstat")], tag="kk6")
        sch.op("vector", lambda e: e.reciprocal(g_[0:16, TA:2 * TA], g_[0:16, 0:TA]),
               reads=[bf("gnstat")], writes=[bf("gnstat")], tag="kk7")
        for c in range(NCH):
            ho = 2 * c * TA
            def kk8(e, c=c):
                inst = None
                for h in range(2):
                    S = 2 * c + h
                    inst = e.matmul(PS[3][0:64, h * TA:(h + 1) * TA],
                                    sb["sel_hm"][0:16, 64 * S:64 * (S + 1)],
                                    g_[0:16, TA:2 * TA], start=True, stop=True)
                return inst
            sch.op("tensor", kk8, reads=[bf("gnstat"), bf("sel_hm"), bf("PS3")],
                   writes=[bf("PS3")], tag="kk8")
            sch.op("vector", lambda e, ho=ho: e.tensor_mul(
                sb["kkst"][:, ho:ho + 2 * TA], sb["kkst"][:, ho:ho + 2 * TA],
                PS[3][0:64, 0:2 * TA]),
                reads=[bf("PS3"), bf("kkst")], writes=[bf("kkst"), bf("PS3")], tag="kk9")
            for h in range(2):
                S = 2 * c + h
                sch.op("vector", lambda e, ho=ho, h=h, S=S: e.tensor_scalar(
                    sb["prep2"][:, h * TA:(h + 1) * TA],
                    sb["ahst"][:, ho + h * TA:ho + (h + 1) * TA],
                    sb["kah"][:, S:S + 1], sb["onemkah"][:, S:S + 1],
                    op0=ALU.mult, op1=ALU.add),
                    reads=[bf("ahst"), bf("kah"), bf("onemkah"), bf("prep2")],
                    writes=[bf("prep2")], tag="kh1")
            sch.op("vector", lambda e, ho=ho: e.tensor_mul(
                sb["kst"][:, ho:ho + 2 * TA], sb["kst"][:, ho:ho + 2 * TA],
                sb["prep2"][:, 0:2 * TA]),
                reads=[bf("prep2"), bf("kst")], writes=[bf("kst")], tag="kh2")

        if _alim < 4:
            continue
        for c in range(NCH):
            ho = 2 * c * TA
            P_, P2 = sb["prep"], sb["prep2"]
            bP, bP2 = bf("prep"), bf("prep2")
            W2 = 2 * TA
            for h in range(2):
                sch.op("vector", lambda e, ho=ho, h=h: e.tensor_tensor_scan(
                    P_[:, h * TA:(h + 1) * TA], sb["wst"][:, ho + h * TA:ho + (h + 1) * TA],
                    sb["sm"][0:64, 0:1].broadcast_to([64, TA]), 0.0,
                    op0=ALU.add, op1=ALU.add),
                    reads=[bf("wst"), bf("zero1"), bP], writes=[bP], tag="pg")
            ebc = 2 * c * 3 * NCC
            ebt = 16 * 3 * NCC
            def gsel(col):
                return P_.ap()[:, 0:W2].rearrange(
                    "p (s cc u) -> p s cc u", s=2, cc=NCC)[:, :, :, col]
            def ebsl(ofs, ebc=ebc):
                return sb["eb"].ap()[:, ebc:ebc + 6 * NCC].rearrange(
                    "p (s k cc) -> p s k cc", s=2, k=3)[:, :, ofs, :]
            def ebtmp(i):
                return sb["eb"].ap()[:, ebt + i * 2 * NCC:ebt + (i + 1) * 2 * NCC
                                     ].rearrange("p (s cc) -> p s cc", s=2)
            sch.op("vector", lambda e, ebtmp=ebtmp: e.memset(ebtmp(0)[:, :, 0:1], 0.0),
                   reads=[bf("eb")], writes=[bf("eb")], tag="pgp0")
            if NCC > 1:
                sch.op("vector", lambda e, gsel=gsel, ebtmp=ebtmp: e.tensor_copy(
                    ebtmp(0)[:, :, 1:NCC], gsel(63)[:, :, 0:NCC - 1]),
                    reads=[bP, bf("eb")], writes=[bf("eb")], tag="pgp1")
            sch.op("vector", lambda e, gsel=gsel, ebtmp=ebtmp: e.tensor_sub(
                ebtmp(1), gsel(31), ebtmp(0)),
                reads=[bP, bf("eb")], writes=[bf("eb")], tag="pgp2")
            sch.op("scalar", lambda e, ebsl=ebsl, ebtmp=ebtmp: e.activation(
                ebsl(0), ebtmp(1), AF.Exp),
                reads=[bf("eb")], writes=[bf("eb")], tag="per")
            sch.op("vector", lambda e, gsel=gsel, ebtmp=ebtmp: e.tensor_sub(
                ebtmp(1), gsel(63), ebtmp(0)),
                reads=[bP, bf("eb")], writes=[bf("eb")], tag="pgp3")
            sch.op("scalar", lambda e, ebsl=ebsl, ebtmp=ebtmp: e.activation(
                ebsl(2), ebtmp(1), AF.Exp),
                reads=[bf("eb")], writes=[bf("eb")], tag="peg")
            sch.op("vector", lambda e, gsel=gsel, ebtmp=ebtmp: e.tensor_sub(
                ebtmp(1), gsel(63), gsel(31)),
                reads=[bP, bf("eb")], writes=[bf("eb")], tag="ped0")
            sch.op("scalar", lambda e, ebsl=ebsl, ebtmp=ebtmp: e.activation(
                ebsl(1), ebtmp(1), AF.Exp),
                reads=[bf("eb")], writes=[bf("eb")], tag="ped")
            sch.op("vector", lambda e: e.tensor_sub(
                P_[:, 0:W2], P_[:, 0:W2],
                P_.ap()[:, 0:W2].rearrange("p (s cc u) -> p s cc u", s=2, cc=NCC)
                [:, :, :, 31:32].broadcast_to([64, 2, NCC, LC])),
                reads=[bP], writes=[bP], tag="pgr")
            sch.op("vector", lambda e, ho=ho: e.tensor_sub(
                P2[:, 0:W2], P_[:, 0:W2], sb["wst"][:, ho:ho + W2]),
                reads=[bP, bf("wst"), bP2], writes=[bP2], tag="pe2s")
            sch.op("scalar", lambda e: e.activation(P2[:, 0:W2], P2[:, 0:W2], AF.Exp),
                   reads=[bP2], writes=[bP2], tag="pe2")
            PE_ = sb["prepE"]
            bPE = bf("prepE")
            sch.op("scalar", lambda e, PE_=PE_: e.activation(
                PE_[:, 0:W2], P_[:, 0:W2], AF.Exp),
                reads=[bP, bPE], writes=[bPE], tag="pe1")
            sch.op("scalar", lambda e, PE_=PE_: e.activation(
                PE_[:, W2:2 * W2], P_[:, 0:W2], AF.Exp, scale=-1.0),
                reads=[bP], writes=[bPE], tag="pen")
            lrc0 = 4 * TA * c

            def ilv(tsr, off, lrc0=lrc0):
                return tsr.ap()[:, lrc0:lrc0 + 2 * W2].rearrange(
                    "p (s cc two u) -> p s cc two u", s=2, cc=NCC, two=2)[:, :, :, off, :]

            def ch(tsr, c0=None, ho=ho):
                base = ho if c0 is None else c0
                return tsr.ap()[:, base:base + W2].rearrange(
                    "p (s cc u) -> p s cc u", s=2, cc=NCC)

            sch.op("vector", lambda e, ilv=ilv, ch=ch, P2=P2: e.tensor_tensor(
                ilv(sb["lr"], 0), ch(sb["kkst"]), ch(P2, 0), op=ALU.mult),
                reads=[bf("kkst"), bP2, bf("lr")], writes=[bf("lr")], tag="psa")
            sch.op("vector", lambda e, ilv=ilv, ch=ch: e.tensor_tensor(
                ilv(sb["lr"], 1), ch(sb["rst"]), ch(sb["prepE"], 0), op=ALU.mult),
                reads=[bf("rst"), bf("prepE"), bf("lr")], writes=[bf("lr")], tag="psr")
            sch.op("vector", lambda e, ho=ho, P2=P2, W2=W2: e.scalar_tensor_tensor(
                P2[:, 0:W2], sb["kkst"][:, ho:ho + W2], -1.0, sb["ahst"][:, ho:ho + W2],
                op0=ALU.mult, op1=ALU.mult),
                reads=[bf("kkst"), bf("ahst"), bP2], writes=[bP2], tag="pbp")
            sch.op("vector", lambda e, ilv=ilv, ch=ch, P2=P2, W2=W2: e.tensor_tensor(
                ilv(sb["bk"], 0), ch(P2, 0), ch(sb["prepE"], W2), op=ALU.mult),
                reads=[bP2, bf("prepE"), bf("bk")], writes=[bf("bk")], tag="psb")
            sch.op("vector", lambda e, ilv=ilv, ch=ch, W2=W2: e.tensor_tensor(
                ilv(sb["bk"], 1), ch(sb["kst"]), ch(sb["prepE"], W2), op=ALU.mult),
                reads=[bf("kst"), bf("prepE"), bf("bk")], writes=[bf("bk")], tag="psk")

        if _alim < 5:
            continue
        for cc in range(NCC):
            for c in range(NCH):
                par = (cc * NCH + c) % 2
                unit_scan(sch, sb, PS, PSB, WKbank, QTbank, bf, cfg, cc, c, par)

        if _alim < 6:
            continue
        post_scan(sch, sb, PS, bf, dI, cfg, t, y_dram)

    if isinstance(cfg.stop_after, str) and cfg.stop_after.startswith("A"):
        actx.close()
        return _finish_early(nc, sch, ctx)
    finalize_scale(sch, sb, PS, bf, dI, cfg, "ymax", cc_io[0], cc_io[1], "clipx", 0)
    sch.barrier()
    actx.close()

    b1 = ExitStack()
    sba("WgT", (128, 8 * 2816), BF16, b1)
    sba("WuT", (128, 8 * 2816), BF16, b1)
    sba("yt", (128, NCH * TB), YDT, b1)
    sba("xq", (128, NCH * TB), BF16, b1)
    sba("qf", (128, TB), F32, b1)
    sba("hb", (128, TB), HDT, b1)
    sba("silu", (128, TB), BF16, b1)
    for k in range(8):
        sch.dma("ldw", lambda e, k=k: e.dma_start(
            out=sb["WgT"][:, k * 2816:(k + 1) * 2816], in_=wgT_dram[k]),
            writes=[bf("WgT")], tag="lWg")
        sch.dma("ldw", lambda e, k=k: e.dma_start(
            out=sb["WuT"][:, k * 2816:(k + 1) * 2816], in_=wuT_dram[k]),
            writes=[bf("WuT")], tag="lWu")
    sch.op("vector", lambda e: e.tensor_scalar(
        sb["sgx"][:, :], sb["sg"][:, :], sb["scal"][:, 0:1], None, op0=ALU.mult),
        reads=[bf("sg"), bf("scal")], writes=[bf("sgx")], tag="sgx")
    sch.op("vector", lambda e: e.tensor_scalar(
        sb["sux"][:, :], sb["su"][:, :], sb["scal"][:, 0:1], None, op0=ALU.mult),
        reads=[bf("su"), bf("scal")], writes=[bf("sux")], tag="sux")

    for t in range(NT_B):
        t0 = t * TB
        for c in range(NCH):
            sch.dma("ld", lambda e, c=c, t0=t0: e.dma_start(
                out=sb["yt"][:, c * TB:(c + 1) * TB], in_=y_dram[c, :, t0:t0 + TB]),
                writes=[bf("yt")], tag="lys")
        for c in range(NCH):
            co = c * TB
            sch.op("vector", lambda e, co=co: e.tensor_scalar(
                sb["qf"][:, 0:TB], sb["yt"][:, co:co + TB], sb["scal"][:, 1:2], MAGIC,
                op0=ALU.mult, op1=ALU.add),
                reads=[bf("yt"), bf("scal"), bf("qf")], writes=[bf("qf")], tag="xq1")
            sch.op("vector", lambda e: e.tensor_scalar(
                sb["qf"][:, 0:TB], sb["qf"][:, 0:TB], -MAGIC, -(QMAX + 1.0),
                op0=ALU.add, op1=ALU.max), reads=[bf("qf")], writes=[bf("qf")], tag="xq2")
            sch.op("vector", lambda e, co=co: e.tensor_single_scalar(
                sb["xq"][:, co:co + TB], sb["qf"][:, 0:TB], QMAX, op=ALU.min),
                reads=[bf("qf"), bf("xq")], writes=[bf("xq")], tag="xq3")
        for i in range(NI):
            sch.op("tensor", mm_group(
                lambda: PS[0][:, 0:TB], sb["WgT"],
                lambda k, i=i: (k * 2816 + i * 128, k * 2816 + (i + 1) * 128),
                sb["xq"], lambda k: (k * TB, (k + 1) * TB), 8),
                reads=[bf("WgT"), bf("xq"), bf("PS0")], writes=[bf("PS0")], tag="gmm")
            sch.op("tensor", mm_group(
                lambda: PS[1][:, 0:TB], sb["WuT"],
                lambda k, i=i: (k * 2816 + i * 128, k * 2816 + (i + 1) * 128),
                sb["xq"], lambda k: (k * TB, (k + 1) * TB), 8),
                reads=[bf("WuT"), bf("xq"), bf("PS1")], writes=[bf("PS1")], tag="umm")
            sch.op("scalar", lambda e, i=i: e.activation(
                sb["silu"][:, 0:TB], PS[0][:, 0:TB], AF.Silu, scale=sb["sgx"][:, i:i + 1]),
                reads=[bf("PS0"), bf("sgx"), bf("silu")], writes=[bf("silu"), bf("PS0")],
                tag="sil")
            sch.op("vector", lambda e, i=i: e.scalar_tensor_tensor(
                sb["hb"][:, 0:TB], PS[1][:, 0:TB], sb["sux"][:, i:i + 1],
                sb["silu"][:, 0:TB], op0=ALU.mult, op1=ALU.mult),
                reads=[bf("PS1"), bf("sux"), bf("silu"), bf("hb")],
                writes=[bf("hb"), bf("PS1")], tag="hm")
            sch.op("vector", lambda e: e.reduce_max(
                sb["sm"][:, 5:6], sb["hb"][:, 0:TB], axis=AX.X,
                apply_absolute_value=True),
                reads=[bf("hb")], writes=[bf("hmx")], tag="hx1")
            sch.op("vector", lambda e: e.tensor_max(
                sb["hmax"][:, 0:1], sb["hmax"][:, 0:1], sb["sm"][:, 5:6]),
                reads=[bf("hmx"), bf("hmax")], writes=[bf("hmax")], tag="hx2")
            sch.dma("st", lambda e, i=i, t0=t0: e.dma_start(
                out=h_dram[i, :, t0:t0 + TB], in_=sb["hb"][:, 0:TB]),
                reads=[bf("hb")], tag="sth")

    if cfg.stop_after == "B1":
        b1.close()
        return _finish_early(nc, sch, ctx)
    finalize_scale(sch, sb, PS, bf, dI, cfg, "hmax", cc_io[2], cc_io[3], "clipd", 2)
    sch.barrier()
    b1.close()

    b2 = ExitStack()
    sba("WdT", (128, 22 * 1024), BF16, b2)
    sba("yt2", (128, NCH * TB), YDT, b2)
    sba("ht", (128, NI * TB), HDT, b2)
    sba("hq", (128, NI * TB), BF16, b2)
    sba("qf2", (128, TB), F32, b2)
    sba("zpre", (128, NCH * TB), F32, b2)
    sba("sqb", (128, TB), BF16, b2)
    sba("zfin", (128, NCH * TB), F32, b2)
    sba("ztm", (128, 1024), F32, b2)
    sba("lnrow", (1, 3 * TB), F32, b2)
    for k in range(22):
        sch.dma("ldw", lambda e, k=k: e.dma_start(
            out=sb["WdT"][:, k * 1024:(k + 1) * 1024], in_=wdT_dram[k]),
            writes=[bf("WdT")], tag="lWd")
    sch.op("vector", lambda e: e.tensor_scalar(
        sb["sdh"][:, :], sb["sd"][:, :], sb["scal"][:, 2:3], None, op0=ALU.mult),
        reads=[bf("sd"), bf("scal")], writes=[bf("sdh")], tag="sdh")

    for t in range(NT_B):
        t0 = t * TB
        for c in range(NCH):
            sch.dma("ld", lambda e, c=c, t0=t0: e.dma_start(
                out=sb["yt2"][:, c * TB:(c + 1) * TB], in_=y_dram[c, :, t0:t0 + TB]),
                writes=[bf("yt2")], tag="ly2")
        for i in range(NI):
            sch.dma("ld", lambda e, i=i, t0=t0: e.dma_start(
                out=sb["ht"][:, i * TB:(i + 1) * TB], in_=h_dram[i, :, t0:t0 + TB]),
                writes=[bf("ht")], tag="lh")
        for i in range(NI):
            co = i * TB
            sch.op("vector", lambda e, co=co: e.tensor_scalar(
                sb["qf2"][:, 0:TB], sb["ht"][:, co:co + TB], sb["scal"][:, 3:4], MAGIC,
                op0=ALU.mult, op1=ALU.add),
                reads=[bf("ht"), bf("scal"), bf("qf2")], writes=[bf("qf2")], tag="hq1")
            sch.op("vector", lambda e: e.tensor_scalar(
                sb["qf2"][:, 0:TB], sb["qf2"][:, 0:TB], -MAGIC, -(QMAX + 1.0),
                op0=ALU.add, op1=ALU.max), reads=[bf("qf2")], writes=[bf("qf2")], tag="hq2")
            sch.op("vector", lambda e, co=co: e.tensor_single_scalar(
                sb["hq"][:, co:co + TB], sb["qf2"][:, 0:TB], QMAX, op=ALU.min),
                reads=[bf("qf2"), bf("hq")], writes=[bf("hq")], tag="hq3")
        for f in range(NCH):
            sch.op("tensor", mm_group(
                lambda: PS[0][:, 0:TB], sb["WdT"],
                lambda k, f=f: (k * 1024 + f * 128, k * 1024 + (f + 1) * 128),
                sb["hq"], lambda k: (k * TB, (k + 1) * TB), NI),
                reads=[bf("WdT"), bf("hq"), bf("PS0")], writes=[bf("PS0")], tag="dmm")
            sch.op("vector", lambda e, f=f: e.scalar_tensor_tensor(
                sb["zpre"][:, f * TB:(f + 1) * TB], PS[0][:, 0:TB], sb["sdh"][:, f:f + 1],
                sb["yt2"][:, f * TB:(f + 1) * TB], op0=ALU.mult, op1=ALU.add),
                reads=[bf("PS0"), bf("sdh"), bf("yt2"), bf("zpre")],
                writes=[bf("zpre"), bf("PS0")], tag="dre")
        emit_ln(sch, sb, PS, bf, cfg, "zpre", "zfin", TB, "sqb")
        for blk_i in range(TB // 128):
            for half in range(2):
                sch.op("tensor", lambda e, blk_i=blk_i, half=half: [
                    e.transpose(PS[1][:, i * 128:(i + 1) * 128],
                                sb["zfin"][:, (half * 4 + i) * TB + blk_i * 128:
                                           (half * 4 + i) * TB + (blk_i + 1) * 128],
                                sb["ident_f32"][:, :]) for i in range(4)][-1],
                    reads=[bf("zfin"), bf("ident_f32"), bf("PS1")], writes=[bf("PS1")],
                    tag="ztp")
                sch.op("vector", lambda e, half=half: e.tensor_copy(
                    sb["ztm"][:, half * 512:(half + 1) * 512], PS[1][:, 0:512]),
                    reads=[bf("PS1")], writes=[bf("ztm"), bf("PS1")], tag="zev")
            sch.dma("st", lambda e, t0=t0, blk_i=blk_i: e.dma_start(
                out=z_out[t0 + blk_i * 128:t0 + (blk_i + 1) * 128, :], in_=sb["ztm"][:, :]),
                reads=[bf("ztm")], tag="stz")
    b2.close()

    for i in range(sch.DMA_ROT):
        if sch.dmaq["st"][2][i] > 0:
            sch.items["sync"].append(("wait", (sch.dmaq["st"][1][i],
                                               sch.dmaq["st"][2][i] * 16)))
    global _LAST_SCHED
    _LAST_SCHED = sch
    sch.finalize()
    ctx.close()
    return nc



def unit_scan(sch, sb, PS, PSB, WKbank, QTbank, bf, cfg, cc, c, par):
    ulim = getattr(cfg, "ulim", 99)
    _ucnt = [0]
    _sch_op = sch.op

    def _op(*a, **k):
        if _ucnt[0] >= ulim:
            return None
        _ucnt[0] += 1
        return _sch_op(*a, **k)

    class _S:
        op = staticmethod(_op)
        buf = sch.buf
    sch = _S()

    TA, LC, NCC, NCH = cfg.TA, cfg.LC, cfg.NCC, cfg.NCH

    def lrc(h):
        return (2 * c + h) * 2 * TA + cc * 128
    vco = c * TA + cc * LC
    ebb = [(2 * c + h) * 3 * NCC for h in range(2)]
    qts, m1, tm = sb[f"qts{par}"], sb[f"m1{par}"], sb[f"tm{par}"]
    x0, x1 = sb[f"x0{par}"], sb[f"x1{par}"]
    sqp, wfm = sb[f"sqp{par}"], sb[f"wfm{par}"]
    sgm, stR, stt = sb[f"sgm{par}"], sb[f"str{par}"], sb[f"stt{par}"]
    bq, bm1, btm = bf(f"qts{par}"), bf(f"m1{par}"), bf(f"tm{par}")
    bx = [bf(f"x0{par}"), bf(f"x1{par}")]
    bsq, bwfm = bf(f"sqp{par}"), bf(f"wfm{par}")
    bsgm, bstR, bstt = bf(f"sgm{par}"), bf(f"str{par}"), bf(f"stt{par}")
    blr, bbk, bSt = bf("lr"), bf("bk"), bf(f"St_{c}")
    beb = bf("eb")
    St = sb["St"]
    stc = 2 * c * 64
    QT = QTbank[par]
    QTB = PSB[6 if par == 0 else 7]
    WK = WKbank[par]
    WKB = PSB[4 if par == 0 else 2]
    bQT = bf(f"QT{par}")
    bR1, bR2, bR3 = bf(f"WK{par}r1"), bf(f"WK{par}r2"), bf(f"WK{par}r3")
    O_ = PS[5]
    bO = bf("PS5")

    def quad(e):
        inst = None
        for h in range(2):
            lo = lrc(h)
            for part in range(2):
                colo = h * 256 + part * 128
                inst = e.matmul(QT[0:64, colo:colo + 128],
                                sb["bk"][:, lo + part * 64:lo + (part + 1) * 64],
                                sb["lr"][:, lo:lo + 128], start=True, stop=True)
        return inst
    sch.op("tensor", quad, reads=[blr, bbk, bQT], writes=[bQT], tag="uq")
    sch.op("vector", lambda e: e.tensor_tensor(
        qts.ap().rearrange("p (a b) -> p a b", a=4),
        QT[0:64, 0:512].rearrange("p (a b) -> p a b", a=4),
        sb["mask_quadT"].ap().unsqueeze(1).broadcast_to([64, 4, 128]),
        op=ALU.mult),
        reads=[bQT, bf("mask_quadT")], writes=[bq, bQT], tag="uqe")

    def fwd(e):
        inst = None
        for h in range(2):
            lo = lrc(h)
            inst = e.matmul(WK[0:64, h * 64:(h + 1) * 64],
                            sb["lr"][:, lo:lo + 64], sb["bk"][:, lo:lo + 64],
                            start=True, stop=True)
        return inst
    sch.op("tensor", fwd, reads=[blr, bbk, bR1], writes=[bR1], tag="uf")
    sch.op("vector", lambda e: e.tensor_tensor(
        m1.ap().rearrange("p (a b) -> p a b", a=2),
        WK[0:64, 0:128].rearrange("p (a b) -> p a b", a=2),
        sb["mask_sl"].ap().unsqueeze(1).broadcast_to([64, 2, 64]), op=ALU.mult),
        reads=[bR1, bf("mask_sl")], writes=[bm1, bR1], tag="ufe")

    def tps(e):
        e.transpose(QTB[0:64, 0:64], sb["lr"][:, lrc(0):lrc(0) + 64],
                    sb["ident_bf"][0:64, 0:64])
        e.transpose(QTB[0:64, 64:128], sb["lr"][:, lrc(1):lrc(1) + 64],
                    sb["ident_bf"][0:64, 0:64])
        e.transpose(QTB[0:64, 128:256], sb["vst"][:, vco:vco + 64],
                    sb["ident_bf"][:, :])
        e.transpose(QTB[0:64, 256:320], sb["bk"][:, lrc(0):lrc(0) + 64],
                    sb["ident_bf"][0:64, 0:64])
        e.transpose(QTB[0:64, 320:384], sb["bk"][:, lrc(1):lrc(1) + 64],
                    sb["ident_bf"][0:64, 0:64])
        e.transpose(QTB[0:64, 384:448], sb["bk"][:, lrc(0) + 64:lrc(0) + 128],
                    sb["ident_bf"][0:64, 0:64])
        return e.transpose(QTB[0:64, 448:512], sb["bk"][:, lrc(1) + 64:lrc(1) + 128],
                           sb["ident_bf"][0:64, 0:64])
    sch.op("tensor", tps, reads=[blr, bbk, bf("vst"), bf("ident_bf"), bQT],
           writes=[bQT], tag="utp")
    sch.op("scalar", lambda e: e.copy(tm[:, :], QTB[0:64, 0:512]),
           reads=[bQT], writes=[btm, bQT], tag="utme")

    def nv(e):
        e.matmul(WK[0:64, 0:64], qts[:, 128:192], tm[:, 128:192], start=True, stop=True)
        return e.matmul(WK[0:64, 64:128], qts[:, 384:448], tm[:, 192:256],
                        start=True, stop=True)
    sch.op("tensor", nv, reads=[bq, btm, bR1], writes=[bR1], tag="unv")

    def x0a(e):
        e.tensor_copy(x0.ap().rearrange("p (h q) -> p h q", h=2)[:, :, 0:64],
                      tm.ap()[:, 0:128].rearrange("p (h q) -> p h q", h=2))
        return e.tensor_copy(x0.ap().rearrange("p (h q) -> p h q", h=2)[:, :, 64:128],
                             WK[0:64, 0:128].rearrange("p (h q) -> p h q", h=2))
    sch.op("vector", x0a, reads=[btm, bR1, bx[0]], writes=[bx[0], bR1], tag="ux0")

    def p1(h):
        return qts[:, h * 256:h * 256 + 64]
    for lvl in range(5):
        def sq_(e, lvl=lvl):
            inst = None
            for h in range(2):
                if lvl == 0:
                    mp, pp = m1[:, h * 64:(h + 1) * 64], p1(h)
                else:
                    b0 = (lvl - 1) * 256
                    mp = sqp[:, b0 + h * 64:b0 + (h + 1) * 64]
                    pp = sqp[:, b0 + 128 + h * 64:b0 + 128 + (h + 1) * 64]
                e.matmul(WK[0:64, h * 64:(h + 1) * 64], pp, mp, start=True, stop=True)
                inst = e.matmul(WK[0:64, 128 + h * 64:128 + (h + 1) * 64], mp, pp,
                                start=True, stop=True)
            return inst
        sch.op("tensor", sq_, reads=[bm1, bq, bsq, bR1], writes=[bR1], tag=f"us{lvl}")
        sch.op("scalar", lambda e, lvl=lvl: e.copy(
            sqp[:, lvl * 256:(lvl + 1) * 256], WK[0:64, 0:256]),
            reads=[bR1, bsq], writes=[bsq, bR1], tag=f"use{lvl}")

    xf = sb[f"xf{par}"]
    bxf = bf(f"xf{par}")
    a_, b_ = 0, 1
    for lvl in range(6):
        def ap_(e, lvl=lvl, a_=a_):
            inst = None
            xa = sb[f"x{a_}{par}"]
            for h in range(2):
                if lvl == 0:
                    pp = p1(h)
                else:
                    b0 = (lvl - 1) * 256
                    pp = sqp[:, b0 + 128 + h * 64:b0 + 128 + (h + 1) * 64]
                inst = e.matmul(WK[0:64, h * 128:(h + 1) * 128], pp,
                                xa[:, h * 128:(h + 1) * 128], start=True, stop=True)
            return inst
        sch.op("tensor", ap_, reads=[bq, bsq, bx[a_], bR1], writes=[bR1], tag=f"ua{lvl}")
        if lvl == 0:
            sch.op("vector", lambda e: e.tensor_add(
                xf[:, :], WK[0:64, 0:256], sb[f"x0{par}"][:, :]),
                reads=[bR1, bx[0], bxf], writes=[bxf, bR1], tag=f"uae{lvl}")
        else:
            sch.op("vector", lambda e: e.tensor_add(
                xf[:, :], WK[0:64, 0:256], xf[:, :]),
                reads=[bR1, bxf], writes=[bxf, bR1], tag=f"uae{lvl}")
        if lvl < 5:
            sch.op("scalar", lambda e, b_=b_: e.copy(sb[f"x{b_}{par}"][:, :], xf[:, :]),
                   reads=[bxf, bx[b_]], writes=[bx[b_]], tag=f"uac{lvl}")
        a_, b_ = b_, a_
    sch.op("scalar", lambda e: e.copy(sb[f"x{a_}{par}"][:, :], xf[:, :]),
           reads=[bxf, bx[a_]], writes=[bx[a_]], tag="uacf")
    xa = sb[f"x{a_}{par}"]
    bxa = bx[a_]

    def strop(e):
        e.tensor_scalar(stR[:, 0:64], St[:, stc:stc + 64],
                        sb["eb"][:, ebb[0] + cc:ebb[0] + cc + 1], None, op0=ALU.mult)
        return e.tensor_scalar(stR[:, 64:128], St[:, stc + 64:stc + 128],
                               sb["eb"][:, ebb[1] + cc:ebb[1] + cc + 1], None,
                               op0=ALU.mult)
    sch.op("vector", strop, reads=[bSt, beb, bstR], writes=[bstR], tag="ustr")

    def wtp(e):
        e.transpose(WKB[0:64, 768:832], xa[:, 0:64], sb["ident_bf"][0:64, 0:64])
        return e.transpose(WKB[0:64, 832:896], xa[:, 128:192], sb["ident_bf"][0:64, 0:64])
    sch.op("tensor", wtp, reads=[bxa, bf("ident_bf"), bR3], writes=[bR3], tag="uwt")
    sch.op("scalar", lambda e: e.copy(wfm[:, :], WKB[0:64, 768:896]),
           reads=[bR3, bwfm], writes=[bwfm, bR3], tag="uwe")

    def smm(e):
        e.matmul(WK[0:64, 256:320], wfm[:, 0:64], stR[:, 0:64], start=True, stop=True)
        return e.matmul(WK[0:64, 320:384], wfm[:, 64:128], stR[:, 64:128],
                        start=True, stop=True)
    sch.op("tensor", smm, reads=[bwfm, bstR, bR2], writes=[bR2], tag="usm")
    sch.op("vector", lambda e: e.tensor_add(
        sgm.ap().rearrange("p (h q) -> p h q", h=2),
        WK[0:64, 256:384].rearrange("p (h q) -> p h q", h=2),
        xf.ap().rearrange("p (h q) -> p h q", h=2)[:, :, 64:128]),
        reads=[bR2, bxf, bsgm], writes=[bsgm, bR2], tag="usge")
    sgmb = sb[f"sgmb{par}"]
    bsgmb = bf(f"sgmb{par}")
    sch.op("scalar", lambda e: e.copy(sgmb[:, :], sgm[:, :]),
           reads=[bsgm, bsgmb], writes=[bsgmb], tag="usgc")

    def stm(e):
        btm_, vtm_, ktm_ = tm[:, 256:384], tm[:, 128:256], tm[:, 384:512]
        e.matmul(WK[0:64, 384:448], btm_[:, 0:64], sgmb[:, 0:64], start=True, stop=False)
        e.matmul(WK[0:64, 384:448], ktm_[:, 0:64], vtm_[:, 0:64], start=False, stop=True)
        e.matmul(WK[0:64, 448:512], btm_[:, 64:128], sgmb[:, 64:128],
                 start=True, stop=False)
        return e.matmul(WK[0:64, 448:512], ktm_[:, 64:128], vtm_[:, 64:128],
                        start=False, stop=True)
    sch.op("tensor", stm, reads=[btm, bsgmb, bR3], writes=[bR3], tag="ust")

    def stup(e):
        e.tensor_scalar(stt[:, 0:64], WK[0:64, 384:448],
                        sb["eb"][:, ebb[0] + NCC + cc:ebb[0] + NCC + cc + 1], None,
                        op0=ALU.mult)
        return e.tensor_scalar(stt[:, 64:128], WK[0:64, 448:512],
                               sb["eb"][:, ebb[1] + NCC + cc:ebb[1] + NCC + cc + 1],
                               None, op0=ALU.mult)
    sch.op("vector", stup, reads=[bR3, beb, bstt], writes=[bstt, bR3], tag="usu1")

    def stup2(e):
        e.scalar_tensor_tensor(
            St[:, stc:stc + 64], St[:, stc:stc + 64],
            sb["eb"][:, ebb[0] + 2 * NCC + cc:ebb[0] + 2 * NCC + cc + 1],
            stt[:, 0:64], op0=ALU.mult, op1=ALU.add)
        return e.scalar_tensor_tensor(
            St[:, stc + 64:stc + 128], St[:, stc + 64:stc + 128],
            sb["eb"][:, ebb[1] + 2 * NCC + cc:ebb[1] + 2 * NCC + cc + 1],
            stt[:, 64:128], op0=ALU.mult, op1=ALU.add)
    sch.op("vector", stup2, reads=[bstt, beb, bSt], writes=[bSt], tag="usu2")

    def omm(e):
        e.matmul(O_[0:64, 448:512], stR[:, 0:64], sb["lr"][:, lrc(0) + 64:lrc(0) + 128],
                 start=True, stop=False)
        e.matmul(O_[0:64, 448:512], sgmb[:, 0:64], qts[:, 64:128],
                 start=False, stop=False)
        e.matmul(O_[0:64, 448:512], tm[:, 128:192], qts[:, 192:256],
                 start=False, stop=True)
        e.matmul(O_[64:128, 448:512], stR[:, 64:128],
                 sb["lr"][:, lrc(1) + 64:lrc(1) + 128],
                 start=True, stop=False, tile_position=(0, 64))
        e.matmul(O_[64:128, 448:512], sgmb[:, 64:128], qts[:, 320:384],
                 start=False, stop=False, tile_position=(0, 64))
        return e.matmul(O_[64:128, 448:512], tm[:, 192:256], qts[:, 448:512],
                        start=False, stop=True, tile_position=(0, 64))
    sch.op("tensor", omm, reads=[bstR, bsgmb, btm, bq, blr, bO], writes=[bO], tag="uo")
    sch.op("scalar", lambda e: e.copy(sb["ost"][:, vco:vco + 64], O_[:, 448:512]),
           reads=[bO], writes=[bf("ost"), bO], tag="uoe")



def post_scan(sch, sb, PS, bf, dI, cfg, t, y_dram):
    TA, NCH = cfg.TA, cfg.NCH
    t0 = t * TA
    g_ = sb["gnstat"]
    bg = bf("gnstat")
    for c in range(NCH):
        co = c * TA
        sch.op("scalar", lambda e, co=co: e.activation(
            sb["sq"][:, 0:TA], sb["ost"][:, co:co + TA], AF.Square),
            reads=[bf("ost"), bf("sq")], writes=[bf("sq")], tag="g0")
        sch.op("tensor", lambda e, c=c, co=co: [
            e.matmul(PS[5][0:16, 0:TA], sb["sel_gn"][:, 16 * c:16 * (c + 1)],
                     sb["ost"][:, co:co + TA], start=(c == 0), stop=(c == NCH - 1)),
            e.matmul(PS[6][0:16, 0:TA], sb["sel_gn"][:, 16 * c:16 * (c + 1)],
                     sb["sq"][:, 0:TA], start=(c == 0), stop=(c == NCH - 1))][-1],
            reads=[bf("ost"), bf("sq"), bf("sel_gn"), bf("PS5"), bf("QT0")],
            writes=[bf("PS5"), bf("QT0")], tag="g1")
        ho = 2 * c * TA
        for h in range(2):
            S = 2 * c + h
            sch.op("vector", lambda e, ho=ho, h=h, S=S: e.scalar_tensor_tensor(
                sb["kkr"][:, h * TA:(h + 1) * TA],
                sb["rst"][:, ho + h * TA:ho + (h + 1) * TA], sb["rkh"][:, S:S + 1],
                sb["kst"][:, ho + h * TA:ho + (h + 1) * TA], op0=ALU.mult, op1=ALU.mult),
                reads=[bf("rst"), bf("kst"), bf("rkh"), bf("kkr")], writes=[bf("kkr")],
                tag="g2")
        def g3(e, c=c):
            inst = None
            for h in range(2):
                S = 2 * c + h
                inst = e.matmul(PS[3][0:16, 0:TA], sb["bsel_hm"][:, 16 * S:16 * (S + 1)],
                                sb["kkr"][:, h * TA:(h + 1) * TA],
                                start=(S == 0), stop=(S == 15))
            return inst
        sch.op("tensor", g3, reads=[bf("kkr"), bf("bsel_hm"), bf("PS3")],
               writes=[bf("PS3")], tag="g3")
    sch.op("vector", lambda e: e.tensor_scalar_mul(g_[0:16, 0:TA], PS[5][0:16, 0:TA],
                                                   1.0 / 64.0),
           reads=[bf("PS5"), bg], writes=[bg], tag="g4")
    sch.op("vector", lambda e: e.tensor_scalar_mul(g_[0:16, TA:2 * TA],
                                                   PS[6][0:16, 0:TA], 1.0 / 64.0),
           reads=[bf("QT0")], writes=[bg, bf("QT0")], tag="g5")
    sch.op("vector", lambda e: e.scalar_tensor_tensor(
        g_[0:16, 2 * TA:3 * TA], g_[0:16, 0:TA], -1.0, g_[0:16, 0:TA],
        op0=ALU.mult, op1=ALU.mult), reads=[bg], writes=[bg], tag="g6")
    sch.op("vector", lambda e: e.tensor_tensor(
        g_[0:16, TA:2 * TA], g_[0:16, TA:2 * TA], g_[0:16, 2 * TA:3 * TA], op=ALU.add),
        reads=[bg], writes=[bg], tag="g7")
    sch.op("vector", lambda e: e.tensor_scalar_add(g_[0:16, TA:2 * TA],
                                                   g_[0:16, TA:2 * TA], EPS),
           reads=[bg], writes=[bg], tag="g8")
    sch.op("vector", lambda e: e.reciprocal(g_[0:16, 2 * TA:3 * TA], g_[0:16, TA:2 * TA]),
           reads=[bg], writes=[bg], tag="g9")
    sch.op("scalar", lambda e: e.activation(g_[0:16, TA:2 * TA], g_[0:16, 2 * TA:3 * TA],
                                            AF.Sqrt),
           reads=[bg], writes=[bg], tag="gA")
    sch.op("vector", lambda e: e.scalar_tensor_tensor(
        g_[0:16, 2 * TA:3 * TA], g_[0:16, 0:TA], -1.0, g_[0:16, TA:2 * TA],
        op0=ALU.mult, op1=ALU.mult), reads=[bg], writes=[bg], tag="gB")
    sch.op("vector", lambda e: e.tensor_copy(g_[0:16, 0:TA], PS[3][0:16, 0:TA]),
           reads=[bf("PS3")], writes=[bg, bf("PS3")], tag="gC")
    for c in range(NCH):
        co = c * TA
        sch.op("tensor", lambda e, c=c: [
            e.matmul(PS[3][:, 0:TA], sb["sel_gnb"][0:16, 128 * c:128 * (c + 1)],
                     g_[0:16, TA:2 * TA], start=True, stop=True),
            e.matmul(PS[3][:, 256:256 + TA], sb["sel_gnb"][0:16, 128 * c:128 * (c + 1)],
                     g_[0:16, 2 * TA:3 * TA], start=True, stop=True)][-1],
            reads=[bg, bf("sel_gnb"), bf("PS3")], writes=[bf("PS3")], tag="gD")
        sch.op("tensor", lambda e, c=c: e.matmul(
            PS[5][:, 0:TA], sb["sel_gnb"][0:16, 128 * c:128 * (c + 1)], g_[0:16, 0:TA],
            start=True, stop=True),
            reads=[bg, bf("sel_gnb"), bf("PS5")], writes=[bf("PS5")], tag="gE")
        sch.op("vector", lambda e, co=co, c=c: e.scalar_tensor_tensor(
            sb["sq2"][:, 0:TA], sb["ost"][:, co:co + TA], sb["gn_w"][:, c:c + 1],
            PS[3][:, 0:TA], op0=ALU.mult, op1=ALU.mult),
            reads=[bf("ost"), bf("gn_w"), bf("PS3"), bf("sq2")], writes=[bf("sq2")],
            tag="gF")
        sch.op("vector", lambda e, c=c: e.tensor_scalar(
            sb["sq"][:, 0:TA], PS[3][:, 256:256 + TA], sb["gn_w"][:, c:c + 1],
            sb["gn_b"][:, c:c + 1], op0=ALU.mult, op1=ALU.add),
            reads=[bf("PS3"), bf("gn_w"), bf("gn_b"), bf("sq")],
            writes=[bf("sq"), bf("PS3")], tag="gG")
        sch.op("vector", lambda e: e.tensor_add(sb["sq2"][:, 0:TA], sb["sq2"][:, 0:TA],
                                                sb["sq"][:, 0:TA]),
               reads=[bf("sq"), bf("sq2")], writes=[bf("sq2"), bf("sq")], tag="gH")
        sch.op("vector", lambda e, co=co: e.tensor_mul(
            sb["sq"][:, 0:TA], sb["vst"][:, co:co + TA], PS[5][:, 0:TA]),
            reads=[bf("vst"), bf("PS5"), bf("sq")], writes=[bf("sq"), bf("PS5")], tag="gI")
        sch.op("vector", lambda e: e.tensor_add(sb["sq2"][:, 0:TA], sb["sq2"][:, 0:TA],
                                                sb["sq"][:, 0:TA]),
               reads=[bf("sq"), bf("sq2")], writes=[bf("sq2"), bf("sq")], tag="gJ")
        sch.op("vector", lambda e, co=co: e.tensor_mul(
            sb["mixA"][:, co:co + TA], sb["sq2"][:, 0:TA], sb["gst"][:, co:co + TA]),
            reads=[bf("sq2"), bf("gst"), bf("mixA")], writes=[bf("mixA"), bf("sq2")],
            tag="gK")
    for f in range(NCH):
        bank, bb = (PS[0], bf("PS0")) if f % 2 == 0 else (PS[1], bf("PS1"))
        won = f"Wb{f % 2}"
        bwo = bf(won)
        sch.dma("gld", lambda e, f=f, won=won: e.dma_start(
            out=sb[won].ap().rearrange("p (k c) -> p k c", k=8),
            in_=dI["W_o"][:, f * 128:(f + 1) * 128].rearrange("(k p) c -> p k c", p=128)),
            writes=[bwo], tag="ldwo")
        sch.op("tensor", lambda e, f=f, bank=bank, won=won: [
            e.matmul(bank[:, 0:TA], sb[won][:, k * 128:(k + 1) * 128],
                     sb["mixA"][:, k * TA:(k + 1) * TA], start=(k == 0), stop=(k == 7))
            for k in range(8)][-1],
            reads=[bwo, bf("mixA"), bb], writes=[bb], tag="wo")
        sch.op("vector", lambda e, f=f, bank=bank: e.tensor_add(
            sb["ytile"][:, f * TA:(f + 1) * TA], bank[:, 0:TA],
            sb["xfm"][:, f * TA:(f + 1) * TA]),
            reads=[bb, bf("xfm"), bf("ytile")], writes=[bf("ytile"), bb], tag="yp")
    emit_ln(sch, sb, PS, bf, cfg, "ytile", "ytile", TA, "sq")
    sch.op("vector", lambda e: e.reduce_max(
        sb["sm"][:, 6:7], sb["ytile"][:, 0:NCH * TA], axis=AX.X,
        apply_absolute_value=True),
        reads=[bf("ytile")], writes=[bf("ymx")], tag="ym1")
    sch.op("vector", lambda e: e.tensor_max(
        sb["ymax"][:, 0:1], sb["ymax"][:, 0:1], sb["sm"][:, 6:7]),
        reads=[bf("ymx"), bf("ymax")], writes=[bf("ymax")], tag="ym2")
    for c in range(NCH):
        sch.dma("st", lambda e, c=c, t0=t0: e.dma_start(
            out=y_dram[c, :, t0:t0 + TA], in_=sb["ytile"][:, c * TA:(c + 1) * TA]),
            reads=[bf("ytile")], tag="sty")



def emit_ln(sch, sb, PS, bf, cfg, src_n, dst_n, TT, scratch_n):
    NCH = cfg.NCH
    src, dst, scr = sb[src_n], sb[dst_n], sb[scratch_n]
    bs, bd, bscr = bf(src_n), bf(dst_n), bf(scratch_n)
    ln = sb["lnrow"]
    bln = bf("lnrow")
    src_f32 = src.dtype == F32
    ones1 = sb["ones_colf"] if src_f32 else sb["ones_col"]
    bones = bf("ones_colf") if src_f32 else bf("ones_col")
    for c in range(NCH):
        co = c * TT
        sch.op("scalar", lambda e, co=co: e.activation(
            scr[:, 0:TT], src[:, co:co + TT], AF.Square),
            reads=[bs, bscr], writes=[bscr], tag="l0")
        sch.op("tensor", lambda e, c=c, co=co, ones1=ones1: [
            e.matmul(PS[6][0:1, 0:TT], ones1[:, :], src[:, co:co + TT],
                     start=(c == 0), stop=(c == NCH - 1)),
            e.matmul(PS[7][0:1, 0:TT], sb["ones_col"][:, :], scr[:, 0:TT],
                     start=(c == 0), stop=(c == NCH - 1))][-1],
            reads=[bs, bscr, bones, bf("ones_col"), bf("QT0"), bf("QT1")],
            writes=[bf("QT0"), bf("QT1")], tag="l1")
    sch.op("vector", lambda e: e.tensor_scalar_mul(ln[0:1, 0:TT], PS[6][0:1, 0:TT],
                                                   1.0 / 1024.0),
           reads=[bf("QT0"), bln], writes=[bln, bf("QT0")], tag="l2")
    sch.op("vector", lambda e: e.tensor_scalar_mul(ln[0:1, TT:2 * TT], PS[7][0:1, 0:TT],
                                                   1.0 / 1024.0),
           reads=[bf("QT1")], writes=[bln, bf("QT1")], tag="l3")
    sch.op("vector", lambda e: e.scalar_tensor_tensor(
        ln[0:1, 2 * TT:3 * TT], ln[0:1, 0:TT], -1.0, ln[0:1, 0:TT],
        op0=ALU.mult, op1=ALU.mult), reads=[bln], writes=[bln], tag="l4")
    sch.op("vector", lambda e: e.tensor_tensor(
        ln[0:1, TT:2 * TT], ln[0:1, TT:2 * TT], ln[0:1, 2 * TT:3 * TT], op=ALU.add),
        reads=[bln], writes=[bln], tag="l5")
    sch.op("vector", lambda e: e.tensor_scalar_add(ln[0:1, TT:2 * TT], ln[0:1, TT:2 * TT],
                                                   EPS),
           reads=[bln], writes=[bln], tag="l6")
    sch.op("vector", lambda e: e.reciprocal(ln[0:1, 2 * TT:3 * TT], ln[0:1, TT:2 * TT]),
           reads=[bln], writes=[bln], tag="l7")
    sch.op("scalar", lambda e: e.activation(ln[0:1, TT:2 * TT], ln[0:1, 2 * TT:3 * TT],
                                            AF.Sqrt),
           reads=[bln], writes=[bln], tag="l8")
    sch.op("vector", lambda e: e.scalar_tensor_tensor(
        ln[0:1, 2 * TT:3 * TT], ln[0:1, 0:TT], -1.0, ln[0:1, TT:2 * TT],
        op0=ALU.mult, op1=ALU.mult), reads=[bln], writes=[bln], tag="l9")
    sch.op("tensor", lambda e: e.matmul(
        PS[6][:, 0:TT], sb["ones_row"][0:1, :], ln[0:1, TT:2 * TT],
        start=True, stop=True),
        reads=[bln, bf("ones_row"), bf("QT0")], writes=[bf("QT0")], tag="lA")
    sch.op("tensor", lambda e: e.matmul(
        PS[7][:, 0:TT], sb["ones_row"][0:1, :], ln[0:1, 2 * TT:3 * TT],
        start=True, stop=True),
        reads=[bln, bf("ones_row"), bf("QT1")], writes=[bf("QT1")], tag="lB")
    for c in range(NCH):
        co = c * TT
        sch.op("vector", lambda e, co=co: e.tensor_mul(
            scr[:, 0:TT], src[:, co:co + TT], PS[6][:, 0:TT]),
            reads=[bs, bf("QT0"), bscr], writes=[bscr], tag="lC")
        sch.op("vector", lambda e, co=co: e.tensor_add(
            dst[:, co:co + TT], scr[:, 0:TT], PS[7][:, 0:TT]),
            reads=[bscr, bf("QT1"), bd], writes=[bd], tag="lD")
    sch.op("vector", lambda e: e.nop(), reads=[bf("QT0"), bf("QT1")],
           writes=[bf("QT0"), bf("QT1")], tag="lE")



def finalize_scale(sch, sb, PS, bf, dI, cfg, maxn, ccin, ccout, clipb, col):
    sch.op("tensor", lambda e: e.transpose(PS[3][0:1, 0:128], sb[maxn][:, 0:1],
                                           sb["ident_f32"][:, :]),
           reads=[bf(maxn), bf("ident_f32"), bf("PS3")], writes=[bf("PS3")], tag="f0")
    sch.op("vector", lambda e: e.reduce_max(sb["sm"][0:1, 6:7], PS[3][0:1, 0:128],
                                            axis=AX.X),
           reads=[bf("PS3")], writes=[bf("gm1"), bf("PS3")], tag="f1")
    sch.op("vector", lambda e: e.tensor_copy(
        sb["sm"][0:1, 8:16], sb["sm"][0:1, 6:7].broadcast_to([1, 8])),
        reads=[bf("gm1")], writes=[bf("cc8")], tag="f2")
    sch.dma("st", lambda e: e.dma_start(out=ccin[0:1, 0:8], in_=sb["sm"][0:1, 8:16]),
            reads=[bf("cc8")], writes=[bf("ccd_" + maxn)], tag="f3")
    sch.op("gpsimd", lambda e: e.collective_compute(
        "AllReduce", ALU.max, replica_groups=[list(range(cfg.n_cores))],
        ins=[ccin.opt()], outs=[ccout.opt()]),
        reads=[bf("ccd_" + maxn)], writes=[bf("cco_" + maxn)], tag="f4",
        async_done=True)
    sch.dma("gld", lambda e: e.dma_start(out=sb["sm"][0:1, 8:16], in_=ccout[0:1, 0:8]),
            reads=[bf("cco_" + maxn)], writes=[bf("cc8")], tag="f5")
    sch.op("scalar", lambda e: e.activation(sb["sm"][0:1, 4:5], sb["sm"][0:1,
           (1 if clipb == "clipx" else 2):(2 if clipb == "clipx" else 3)], AF.Sigmoid),
           reads=[bf(clipb)], writes=[bf("sig_" + maxn)], tag="f6")
    sch.op("vector", lambda e: e.tensor_mul(sb["sm"][0:1, 3:4], sb["sm"][0:1, 8:9],
                                            sb["sm"][0:1, 4:5]),
           reads=[bf("cc8"), bf("sig_" + maxn), bf("gmx")], writes=[bf("gmx")], tag="f7")
    sch.op("vector", lambda e: e.tensor_scalar_mul(sb["sm"][0:1, 3:4], sb["sm"][0:1, 3:4],
                                                   1.0 / QMAX),
           reads=[bf("gmx")], writes=[bf("gmx")], tag="f8")
    sch.op("vector", lambda e: e.reciprocal(sb["sm"][0:1, 4:5], sb["sm"][0:1, 3:4]),
           reads=[bf("gmx"), bf("sig_" + maxn)], writes=[bf("sig_" + maxn)], tag="f9")
    sch.op("tensor", lambda e: e.matmul(
        PS[3][:, 0:2], sb["ones_row"][0:1, :], sb["sm"][0:1, 3:5],
        start=True, stop=True),
        reads=[bf("gmx"), bf("sig_" + maxn), bf("ones_row"), bf("PS3")],
        writes=[bf("PS3")], tag="fA")
    sch.op("vector", lambda e, col=col: e.tensor_copy(
        sb["scal"][:, col:col + 2], PS[3][:, 0:2]),
        reads=[bf("PS3"), bf("scal")], writes=[bf("scal"), bf("PS3")], tag="fB")



_PROGRAM_CACHE = {}
_LAST_SCHED = None


def get_program(cfg):
    key = (cfg.T, cfg.TA, cfg.TB, cfg.n_cores)
    if key not in _PROGRAM_CACHE:
        _PROGRAM_CACHE[key] = build_program(cfg)
    return _PROGRAM_CACHE[key]


def make_in_maps(cfg, inputs, consts):
    x = np.ascontiguousarray(inputs["L0_attn_hidden_states"])
    n = cfg.n_cores
    maps = []
    common = {
        "W_r": inputs["L0_attn_W_r"], "W_k": inputs["L0_attn_W_k"],
        "W_v": inputs["L0_attn_W_v"], "W_o": inputs["L0_attn_W_o"],
        "w_la": inputs["L0_attn_w_lora_a"], "w_lb": inputs["L0_attn_w_lora_b"],
        "a_la": inputs["L0_attn_a_lora_a"], "a_lb": inputs["L0_attn_a_lora_b"],
        "g_la": inputs["L0_attn_g_lora_a"], "g_lb": inputs["L0_attn_g_lora_b"],
        "wg": inputs["L0_ffn_w_gate"], "wu": inputs["L0_ffn_w_up"],
        "wd": inputs["L0_ffn_w_down"], "r_k": inputs["L0_attn_r_k"],
        "x_r": inputs["L0_attn_x_r"], "x_w": inputs["L0_attn_x_w"],
        "x_k": inputs["L0_attn_x_k"], "x_v": inputs["L0_attn_x_v"],
        "x_a": inputs["L0_attn_x_a"], "k_k": inputs["L0_attn_k_k"],
        "k_a": inputs["L0_attn_k_a"], "w_lbias": inputs["L0_attn_w_lora_bias"],
        "a_lbias": inputs["L0_attn_a_lora_bias"], "g_lbias": inputs["L0_attn_g_lora_bias"],
        "gn_w": inputs["L0_attn_gn_weight"], "gn_b": inputs["L0_attn_gn_bias"],
        "clip_x": inputs["L0_ffn_clip_x"], "clip_down": inputs["L0_ffn_clip_down"],
    }
    common = {k: np.ascontiguousarray(np.asarray(v, dtype=np.float32))
              for k, v in common.items()}
    for i in range(n):
        m = dict(common)
        m["x"] = np.ascontiguousarray(x[i].astype(np.float32))
        for name, _, _ in CONST_SPECS:
            m[name] = consts[name]
        maps.append(m)
    return maps


def kernel(**inputs):
    cfg = Cfg()
    consts = host_constants()
    nc = get_program(cfg)
    in_maps = make_in_maps(cfg, inputs, consts)
    res = run_bass_kernel_spmd(nc, in_maps, core_ids=list(range(cfg.n_cores)))
    out = np.stack([r["z"] for r in res.results], axis=0)
    return out.astype(np.float32)


# version 06
# speedup vs baseline: 1.1055x; 1.1055x over previous
import numpy as np
import ml_dtypes
from contextlib import ExitStack

import concourse.bass as bass
import concourse.mybir as mybir
from concourse.bass_utils import run_bass_kernel_spmd

F32 = mybir.dt.float32
BF16 = mybir.dt.bfloat16
AF = mybir.ActivationFunctionType
ALU = mybir.AluOpType
AX = mybir.AxisListType

MAGIC = float(np.float32(1.5 * 2 ** 23))
QMAX = 127.0
EPS = 1e-5


ENGINES = ("tensor", "vector", "scalar", "gpsimd", "sync")


class Tok:
    __slots__ = ("eng", "pos", "sem", "val", "clock", "force_wait")

    def __init__(self, eng, pos, sem, val, clock, force_wait=False):
        self.eng, self.pos, self.sem, self.val, self.clock = eng, pos, sem, val, clock
        self.force_wait = force_wait


class Buf:
    __slots__ = ("name", "writer", "readers")

    def __init__(self, name):
        self.name, self.writer, self.readers = name, None, []


class Sched:
    def __init__(self, nc, ctx):
        self.nc = nc
        self.ctx = ctx
        self.items = {e: [] for e in ENGINES}
        self.pos = {e: 0 for e in ENGINES}
        self.sem = {e: ctx.enter_context(nc.semaphore(f"sem_{e}")) for e in ENGINES}
        self.semval = {e: 0 for e in ENGINES}
        self.synced = {e: {} for e in ENGINES}
        self.drainmark = {e: -1 for e in ENGINES}
        self.dmaq = {}
        self.bufs = {}
        self.banklast = {}

    @staticmethod
    def bank_of(name):
        if name.startswith("PS3"):
            return 3
        if name.startswith("PS5"):
            return 5
        if name.startswith("WK0"):
            return 4
        if name.startswith("WK1"):
            return 2
        if name in ("PS0", "PS1"):
            return int(name[2])
        if name == "QT0":
            return 6
        if name == "QT1":
            return 7
        return None

    def buf(self, name):
        if name not in self.bufs:
            self.bufs[name] = Buf(name)
        return self.bufs[name]

    DMA_ROT = 8

    def add_dma_queue(self, name, eng):
        sems = [self.ctx.enter_context(self.nc.semaphore(f"dma_{name}{i}"))
                for i in range(self.DMA_ROT)]
        self.dmaq[name] = [eng, sems, [0] * self.DMA_ROT, 0]
        for i in range(self.DMA_ROT):
            self.pos[f"{name}#{i}"] = 0

    def _wait(self, eng, tok):
        syn = self.synced[eng]
        if tok.eng == eng and not tok.force_wait:
            if eng == "tensor":
                return False
            if tok.pos <= self.drainmark[eng] or tok.pos <= syn.get(eng, -1):
                return False
            return True
        if tok.eng == eng and tok.force_wait:
            if tok.pos <= syn.get(eng, -1):
                return False
            self.items[eng].append(("wait", (tok.sem, tok.val)))
            syn[eng] = max(syn.get(eng, -1), tok.pos)
            return False
        if tok.pos <= syn.get(tok.eng, -1):
            return False
        assert tok.sem is not None, f"{eng} depends on un-inc'd op on {tok.eng}"
        self.items[eng].append(("wait", (tok.sem, tok.val)))
        for k, v in tok.clock.items():
            if syn.get(k, -1) < v:
                syn[k] = v
        return False

    def _deps(self, reads, writes):
        deps = []
        for b in reads:
            if b.writer is not None:
                deps.append(b.writer)
        for b in writes:
            if b.writer is not None:
                deps.append(b.writer)
            deps.extend(b.readers)
        return deps

    def op(self, eng, fn, reads=(), writes=(), inc=True, tag="", async_done=False):
        banks = {self.bank_of(b.name) for b in list(reads) + list(writes)}
        banks.discard(None)
        drain = False
        for bk in banks:
            assert inc, "bank-touching ops must inc their sem"
            for e2, tok in self.banklast.get(bk, {}).items():
                if e2 != eng:
                    drain |= self._wait(eng, tok)
        for tok in self._deps(reads, writes):
            drain |= self._wait(eng, tok)
        if drain:
            self.items[eng].append(("drain", None))
            self.drainmark[eng] = self.pos[eng]
        self.pos[eng] += 1
        mypos = self.pos[eng]
        clock = dict(self.synced[eng])
        clock[eng] = mypos
        if inc:
            self.semval[eng] += 1
            tok = Tok(eng, mypos, self.sem[eng], self.semval[eng], clock,
                      force_wait=async_done)
        else:
            assert not async_done
            tok = Tok(eng, mypos, None, None, clock)
        self.items[eng].append(("op", (fn, self.sem[eng] if inc else None,
                                       self.semval[eng], tag)))
        for b in reads:
            b.readers.append(tok)
        for b in writes:
            b.writer = tok
            b.readers = []
        for bk in banks:
            self.banklast.setdefault(bk, {})[eng] = tok
        return tok

    def dma(self, q, fn, reads=(), writes=(), tag=""):
        assert not any(self.bank_of(b.name) is not None
                       for b in list(reads) + list(writes))
        eng, sems, counts, total = self.dmaq[q]
        idx = total % self.DMA_ROT
        self.dmaq[q][3] = total + 1
        qn = f"{q}#{idx}"
        drain = False
        for tok in self._deps(reads, writes):
            drain |= self._wait(eng, tok)
        if counts[idx] > 0 and self.synced[eng].get(qn, -1) < counts[idx] * 16:
            self.items[eng].append(("wait", (sems[idx], counts[idx] * 16)))
            self.synced[eng][qn] = counts[idx] * 16
        if drain:
            self.items[eng].append(("drain", None))
            self.drainmark[eng] = self.pos[eng]
        self.pos[eng] += 1
        counts[idx] += 1
        cnt = counts[idx] * 16
        self.pos[qn] = cnt
        clock = dict(self.synced[eng])
        clock[eng] = self.pos[eng]
        clock[qn] = cnt
        tok = Tok(qn, cnt, sems[idx], cnt, clock)
        self.items[eng].append(("dma", (fn, sems[idx], cnt, tag)))
        for b in reads:
            b.readers.append(tok)
        for b in writes:
            b.writer = tok
            b.readers = []
        return tok

    def barrier(self):
        toks = []
        for e in ENGINES:
            self.pos[e] += 1
            self.semval[e] += 1
            clock = dict(self.synced[e])
            clock[e] = self.pos[e]
            toks.append(Tok(e, self.pos[e], self.sem[e], self.semval[e], clock))
            self.items[e].append(("drain", None))
            self.items[e].append(("rawop", (lambda eng, s_=self.sem[e]: eng.sem_inc(s_, 1),
                                            "bar")))
        dmatoks = []
        for q, (eng, sems, counts, total) in self.dmaq.items():
            for i in range(self.DMA_ROT):
                if counts[i] > 0:
                    dmatoks.append((f"{q}#{i}", sems[i], counts[i] * 16))
        for e in ENGINES:
            syn = self.synced[e]
            for t in toks:
                if t.eng == e or t.pos <= syn.get(t.eng, -1):
                    continue
                self.items[e].append(("wait", (t.sem, t.val)))
                for k, v in t.clock.items():
                    if syn.get(k, -1) < v:
                        syn[k] = v
            for q, sem, val in dmatoks:
                if val > syn.get(q, -1):
                    self.items[e].append(("wait", (sem, val)))
                    syn[q] = val
            self.items[e].append(("drain", None))
            self.drainmark[e] = self.pos[e]
        for b in self.bufs.values():
            b.readers = []
            if b.writer is not None:
                b.writer = toks[ENGINES.index(b.writer.eng)] if b.writer.eng in ENGINES \
                    else None

    def finalize(self):
        nc = self.nc
        with nc.Block() as blk:
            def make(name):
                def body(eng):
                    for kind, payload in self.items[name]:
                        if kind == "wait":
                            eng.wait_ge(*payload)
                        elif kind == "drain":
                            eng.drain()
                        elif kind == "rawop":
                            payload[0](eng)
                        else:
                            fn, sem, val, tag = payload[0], payload[1], payload[2], payload[3]
                            inst = fn(eng)
                            if sem is not None:
                                assert inst is not None, f"op '{tag}' returned None"
                                inst.then_inc(sem, 16 if kind == "dma" else 1)
                return body
            blk.tensor(make("tensor"))
            blk.vector(make("vector"))
            blk.scalar(make("scalar"))
            blk.gpsimd(make("gpsimd"))
            blk.sync(make("sync"))



def host_constants():
    c = {}
    c["ident_f32"] = np.eye(128, dtype=np.float32)
    c["ident_bf"] = np.eye(128, dtype=ml_dtypes.bfloat16)
    c["mask_sl"] = np.tril(np.ones((64, 64), np.float32), -1)
    su = np.triu(np.ones((64, 64), np.float32), 1)
    iu = np.triu(np.ones((64, 64), np.float32), 0)
    c["mask_quadT"] = np.concatenate([su, iu], axis=1)
    sel = np.zeros((128, 128), np.float32)
    for cc in range(8):
        for p in range(128):
            sel[p, 16 * cc + 2 * cc + (p // 64)] = 1.0
    c["sel_gn"] = sel.astype(ml_dtypes.bfloat16)
    selb = np.zeros((16, 1024), np.float32)
    for cc in range(8):
        for p in range(128):
            selb[2 * cc + (p // 64), 128 * cc + p] = 1.0
    c["sel_gnb"] = selb
    on = np.ones((128, 1), np.float32)
    c["ones_col"] = on.astype(ml_dtypes.bfloat16)
    c["ones_row"] = np.ones((1, 128), np.float32)
    c["ones_colf"] = np.ones((128, 1), np.float32)
    bs = np.zeros((64, 256), np.float32)
    for S in range(16):
        bs[:, 16 * S + S] = 1.0
    c["bsel_hm"] = bs.astype(ml_dtypes.bfloat16)
    hs = np.zeros((16, 1024), np.float32)
    for S in range(16):
        hs[S, 64 * S:64 * (S + 1)] = 1.0
    c["sel_hm"] = hs
    return c


CONST_SPECS = [("ident_f32", (128, 128), F32), ("ident_bf", (128, 128), BF16),
               ("mask_sl", (64, 64), F32), ("mask_quadT", (64, 128), F32),
               ("sel_gn", (128, 128), BF16), ("sel_gnb", (16, 1024), F32),
               ("ones_col", (128, 1), BF16), ("ones_row", (1, 128), F32),
               ("ones_colf", (128, 1), F32),
               ("bsel_hm", (64, 256), BF16), ("sel_hm", (16, 1024), F32)]

VEC_NAMES = ["x_r", "x_w", "x_k", "x_v", "x_a", "k_k", "k_a", "w_lbias", "a_lbias",
             "g_lbias", "gn_w", "gn_b"]


class Cfg:
    def __init__(self, T=8192, TA=256, TB=512, n_cores=8, stop_after=None,
                 ydt="f32", hdt="f16"):
        self.T, self.TA, self.TB, self.n_cores = T, TA, TB, n_cores
        self.stop_after = stop_after
        self.ydt = ydt
        self.hdt = hdt
        self.LC = 64
        self.NCH = 8
        self.I = 2816
        self.NI = 22
        self.NCC = TA // 64
        assert T % TA == 0 and TA % 64 == 0 and T % TB == 0



def _finish_early(nc, sch, ctx):
    sch.barrier()
    global _LAST_SCHED
    _LAST_SCHED = sch
    sch.finalize()
    ctx.close()
    return nc


def build_program(cfg):
    FLATG = getattr(cfg, "flatg", False)
    nc = bass.Bass("TRN2", target_bir_lowering=False)
    F16 = mybir.dt.float16
    YDT = {"f32": F32, "bf16": BF16, "f16": F16}[cfg.ydt]
    HDT = {"f32": F32, "bf16": BF16, "f16": F16}[cfg.hdt]
    T, TA, TB, LC, NCH, NI, NCC = cfg.T, cfg.TA, cfg.TB, cfg.LC, cfg.NCH, cfg.NI, cfg.NCC
    NT_A, NT_B = T // TA, T // TB

    dI = {}

    def din(name, shape, dt=F32):
        dI[name] = nc.dram_tensor(name, list(shape), dt, kind="ExternalInput").ap()

    din("x", (T, 1024))
    z_out = nc.dram_tensor("z", [T, 1024], F32, kind="ExternalOutput").ap()
    for n, s in [("W_r", (1024, 1024)), ("W_k", (1024, 1024)), ("W_v", (1024, 1024)),
                 ("W_o", (1024, 1024)), ("w_la", (1024, 64)), ("w_lb", (64, 1024)),
                 ("a_la", (1024, 64)), ("a_lb", (64, 1024)), ("g_la", (1024, 128)),
                 ("g_lb", (128, 1024)), ("wg", (2816, 1024)), ("wu", (2816, 1024)),
                 ("wd", (1024, 2816)), ("r_k", (16, 64)), ("clip_x", (1,)),
                 ("clip_down", (1,))]:
        din(n, s)
    for n in VEC_NAMES:
        din(n, (1024,))
    for n, s, dt in CONST_SPECS:
        din(n, s, dt)

    y_dram = nc.dram_tensor("y_spill", [NCH, 128, T], YDT).ap()
    h_dram = nc.dram_tensor("h_spill", [NI, 128, T], HDT).ap()
    wgT_dram = nc.dram_tensor("wgT", [8, 128, 2816], BF16).ap()
    wuT_dram = nc.dram_tensor("wuT", [8, 128, 2816], BF16).ap()
    wdT_dram = nc.dram_tensor("wdT", [22, 128, 1024], BF16).ap()
    cc_io = [nc.dram_tensor(f"cc_{i}", [1, 8], F32).ap() for i in range(4)]

    ctx = ExitStack()
    sch = Sched(nc, ctx)
    sch.add_dma_queue("ld", "sync")
    sch.add_dma_queue("st", "sync")
    sch.add_dma_queue("ldw", "scalar")
    sch.add_dma_queue("gld", "gpsimd")
    bf = sch.buf

    sb = {}

    _names = {}

    def sba(name, shape, dt, scope_ctx=None):
        _names[name] = _names.get(name, 0) + 1
        sb[name] = (scope_ctx or ctx).enter_context(
            nc.sbuf_tensor(f"sb_{name}_{_names[name]}", list(shape), dt))
        return sb[name]

    for n, s, dt in CONST_SPECS:
        sba(n, s, dt)
    for n in VEC_NAMES + ["onemka", "negwlb", "r_kv"]:
        sba(n, (128, 8), F32)
    for n in ["kkh", "kah", "onemkah", "albh", "negwlbh", "rkh"]:
        sba(n, (64, 16), F32)
    sba("scal", (128, 4), F32)
    sba("St", (64, 16 * 64), F32)
    sba("sg", (128, 22), F32)
    sba("su", (128, 22), F32)
    sba("sd", (128, 8), F32)
    sba("sgx", (128, 22), F32)
    sba("sux", (128, 22), F32)
    sba("sdh", (128, 8), F32)
    sba("ymax", (128, 1), F32)
    sba("hmax", (128, 1), F32)
    sba("sm", (128, 16), F32)

    PS = [ctx.enter_context(nc.psum_tensor(f"ps{i}", [128, 512], F32)) for i in range(8)]
    PSB = [p.ap().bitcast(BF16) for p in PS]
    WKbank = {0: PS[4], 1: PS[2]}
    QTbank = {0: PS[6], 1: PS[7]}

    def WKb(par, reg):
        return bf(f"WK{par}{reg}")

    PS2A = [bf("WK1r1"), bf("WK1r2"), bf("WK1r3")]

    for n, s, dt in CONST_SPECS:
        t, src = sb[n], dI[n]
        sch.dma("ld", lambda e, t=t, src=src: e.dma_start(out=t[:, :], in_=src),
                writes=[bf(n)], tag=f"ld_{n}")
    for n in VEC_NAMES:
        t, src = sb[n], dI[n]
        sch.dma("ld", lambda e, t=t, src=src: e.dma_start(
            out=t[:, :], in_=src.rearrange("(c p) -> p c", p=128),
            allow_slow_non_contiguous=True), writes=[bf(n)], tag=f"ld_{n}")
    sch.dma("ld", lambda e: e.dma_start(
        out=sb["r_kv"][:, :],
        in_=dI["r_k"].rearrange("h d -> (h d)").rearrange("(c p) -> p c", p=128),
        allow_slow_non_contiguous=True), writes=[bf("r_kv")], tag="ld_rk")
    for n, src in [("kkh", "k_k"), ("kah", "k_a"), ("albh", "a_lbias")]:
        t, sr = sb[n], dI[src]
        sch.dma("ld", lambda e, t=t, sr=sr: e.dma_start(
            out=t[:, :], in_=sr.rearrange("(s j) -> j s", j=64),
            allow_slow_non_contiguous=True), writes=[bf(n)], tag=f"ld_{n}")
    sch.dma("ld", lambda e: e.dma_start(
        out=sb["negwlbh"][:, :], in_=dI["w_lbias"].rearrange("(s j) -> j s", j=64),
        allow_slow_non_contiguous=True), writes=[bf("negwlbh")], tag="ld_nwh")
    sch.dma("ld", lambda e: e.dma_start(
        out=sb["rkh"][:, :],
        in_=dI["r_k"].rearrange("h d -> (h d)").rearrange("(s j) -> j s", j=64),
        allow_slow_non_contiguous=True), writes=[bf("rkh")], tag="ld_rkh")
    sch.op("vector", lambda e: e.tensor_scalar_mul(
        sb["negwlbh"][:, :], sb["negwlbh"][:, :], -1.0),
        reads=[bf("negwlbh")], writes=[bf("negwlbh")], tag="nwh")
    sch.op("vector", lambda e: e.tensor_scalar(
        sb["onemkah"][:, :], sb["kah"][:, :], -1.0, 1.0, op0=ALU.mult, op1=ALU.add),
        reads=[bf("kah")], writes=[bf("onemkah")], tag="omkh")
    sch.dma("ld", lambda e: e.dma_start(out=sb["sm"][0:1, 1:2], in_=dI["clip_x"].unsqueeze(0)),
            writes=[bf("clipx")], tag="ld_clipx")
    sch.dma("ld", lambda e: e.dma_start(out=sb["sm"][0:1, 2:3],
                                        in_=dI["clip_down"].unsqueeze(0)),
            writes=[bf("clipd")], tag="ld_clipd")

    sch.op("vector", lambda e: e.tensor_scalar(sb["onemka"][:, :], sb["k_a"][:, :],
           -1.0, 1.0, op0=ALU.mult, op1=ALU.add), reads=[bf("k_a")], writes=[bf("onemka")])
    sch.op("vector", lambda e: e.tensor_scalar_mul(sb["negwlb"][:, :], sb["w_lbias"][:, :],
           -1.0), reads=[bf("w_lbias")], writes=[bf("negwlb")])
    sch.op("vector", lambda e: e.memset(sb["St"][:, :], 0.0),
           writes=[bf(f"St_{c}") for c in range(8)])
    sch.op("vector", lambda e: e.memset(sb["ymax"][:, :], 0.0), writes=[bf("ymax")])
    sch.op("vector", lambda e: e.memset(sb["hmax"][:, :], 0.0), writes=[bf("hmax")])
    sch.op("vector", lambda e: e.memset(sb["sm"][:, 0:1], 0.0), writes=[bf("zero1")])
    P3ALL = [bf("PS3q00"), bf("PS3q01"), bf("PS3q10"), bf("PS3q11")]
    psbufs = [[bf("PS0")], [bf("PS1")], [WKb(1, "r1")], P3ALL,
              [WKb(0, "r1")],
              [bf("PS5s"), bf("PS5o0"), bf("PS5o1"), bf("PS5o2"), bf("PS5o3")],
              [bf("QT0")], [bf("QT1")]]
    for i in range(8):
        eng = "vector"
        extra = []
        if i == 4:
            extra = [WKb(0, "r2"), WKb(0, "r3")]
        if i == 2:
            extra = [WKb(1, "r2"), WKb(1, "r3")]
        sch.op(eng, lambda e, i=i: e.memset(PS[i][:, :], 0.0),
               writes=psbufs[i] + extra, tag="psz")

    with ExitStack() as qctx:
        wtl = [qctx.enter_context(nc.sbuf_tensor(f"wq_in{i}", [128, 2816], F32))
               for i in range(2)]
        qtl = [qctx.enter_context(nc.sbuf_tensor(f"wq_q{i}", [128, 2816], BF16))
               for i in range(2)]
        ttl = qctx.enter_context(nc.sbuf_tensor("wq_t", [128, 1024], BF16))
        rmx = qctx.enter_context(nc.sbuf_tensor("wq_m", [128, 4], F32))

        def quant_weight(wsrc, n_chunks, row_len, scale_sb, qdst, tag):
            nblk = row_len // 128
            for i in range(n_chunks):
                p_ = i % 2
                wt, qt = wtl[p_], qtl[p_]
                bw, bq_ = bf(f"qw_w{p_}"), bf(f"qw_q{p_}")
                sch.dma("ldw", lambda e, wt=wt, i=i: e.dma_start(
                    out=wt[:, 0:row_len], in_=wsrc[i * 128:(i + 1) * 128, :]),
                    writes=[bw], tag=f"{tag}ld")
                mo = (i % 2) * 2
                sch.op("vector", lambda e, wt=wt, mo=mo: e.reduce_max(
                    rmx[:, mo:mo + 1], wt[:, 0:row_len], axis=AX.X,
                    apply_absolute_value=True),
                    reads=[bw], writes=[bf(f"qwm{p_}")], tag=f"{tag}mx")
                sch.op("vector", lambda e, i=i, scale_sb=scale_sb, mo=mo:
                    e.tensor_scalar_mul(
                        scale_sb[:, i:i + 1], rmx[:, mo:mo + 1], 1.0 / QMAX),
                    reads=[bf(f"qwm{p_}")], writes=[bf("qws")], tag=f"{tag}s")
                sch.op("vector", lambda e, mo=mo: e.reciprocal(
                    rmx[:, mo + 1:mo + 2], rmx[:, mo:mo + 1]),
                    reads=[bf(f"qwm{p_}")], writes=[bf(f"qwr{p_}")], tag=f"{tag}r")
                sch.op("vector", lambda e, mo=mo: e.tensor_scalar_mul(
                    rmx[:, mo + 1:mo + 2], rmx[:, mo + 1:mo + 2], QMAX),
                    reads=[bf(f"qwr{p_}")], writes=[bf(f"qwr{p_}")], tag=f"{tag}r2")
                sch.op("vector", lambda e, wt=wt, mo=mo: e.tensor_scalar(
                    wt[:, 0:row_len], wt[:, 0:row_len], rmx[:, mo + 1:mo + 2], MAGIC,
                    op0=ALU.mult, op1=ALU.add),
                    reads=[bw, bf(f"qwr{p_}")], writes=[bw], tag=f"{tag}q1")
                sch.op("vector", lambda e, wt=wt: e.tensor_scalar(
                    wt[:, 0:row_len], wt[:, 0:row_len], -MAGIC, -(QMAX + 1.0),
                    op0=ALU.add, op1=ALU.max), reads=[bw], writes=[bw], tag=f"{tag}q2")
                sch.op("vector", lambda e, wt=wt, qt=qt: e.tensor_single_scalar(
                    qt[:, 0:row_len], wt[:, 0:row_len], QMAX, op=ALU.min),
                    reads=[bw], writes=[bq_], tag=f"{tag}q3")
                for g0 in range(0, nblk, 4):
                    ng = min(4, nblk - g0)
                    tp_ = (g0 // 4) % 2
                    to = tp_ * 512
                    sch.op("tensor", lambda e, qt=qt, g0=g0, ng=ng: [
                        e.transpose(PSB[6][:, j * 128:(j + 1) * 128],
                                    qt[:, (g0 + j) * 128:(g0 + j + 1) * 128],
                                    sb["ident_bf"][:, :]) for j in range(ng)][-1],
                        reads=[bq_, bf("ident_bf"), bf("QT0")], writes=[bf("QT0")],
                        tag=f"{tag}tp")
                    sch.op("vector", lambda e, ng=ng, to=to: e.tensor_copy(
                        ttl[:, to:to + ng * 128], PSB[6][:, 0:ng * 128]),
                        reads=[bf("QT0")], writes=[bf(f"qwt{tp_}"), bf("QT0")],
                        tag=f"{tag}te")
                    sch.dma("st", lambda e, qdst=qdst, i=i, g0=g0, ng=ng, to=to:
                        e.dma_start(
                            out=qdst[g0:g0 + ng, :, i * 128:(i + 1) * 128]
                                .rearrange("k p c -> p k c"),
                            in_=ttl[:, to:to + ng * 128].rearrange(
                                "p (k c) -> p k c", k=ng)),
                        reads=[bf(f"qwt{tp_}")], tag=f"{tag}st")

        quant_weight(dI["wg"], NI, 1024, sb["sg"], wgT_dram, "qg")
        quant_weight(dI["wu"], NI, 1024, sb["su"], wuT_dram, "qu")
        quant_weight(dI["wd"], 8, 2816, sb["sd"], wdT_dram, "qd")
        sch.barrier()

    if cfg.stop_after == "wq":
        return _finish_early(nc, sch, ctx)

    actx = ExitStack()
    sba("Wb0", (128, 1024), BF16, actx)
    sba("Wb1", (128, 1024), BF16, actx)
    sba("wla", (128, 8 * 64), BF16, actx)
    sba("wlb", (64, 1024), BF16, actx)
    sba("ala", (128, 8 * 64), BF16, actx)
    sba("alb", (64, 1024), BF16, actx)
    sba("gla", (64, 16 * 128), BF16, actx)
    sba("glb", (128, 1024), BF16, actx)
    for n, src, ck in [("wla", "w_la", 8), ("ala", "a_la", 8)]:
        t, s = sb[n], dI[src]
        sch.dma("gld", lambda e, t=t, s=s, ck=ck: e.dma_start(
            out=t.ap().rearrange("p (k c) -> p k c", k=ck),
            in_=s.rearrange("(k p) c -> p k c", p=128)),
            writes=[bf(n)], tag=f"cast_{n}")
    for n, src in [("wlb", "w_lb"), ("alb", "a_lb"), ("glb", "g_lb")]:
        t, s = sb[n], dI[src]
        sch.dma("gld", lambda e, t=t, s=s: e.dma_start(out=t[:, :], in_=s),
                writes=[bf(n)], tag=f"cast_{n}")
    sch.dma("gld", lambda e: e.dma_start(
        out=sb["gla"].ap().rearrange("j (s m) -> j s m", s=16),
        in_=dI["g_la"].rearrange("(s j) m -> j s m", j=64)),
        writes=[bf("gla")], tag="cast_gla")

    sba("xtm", (128, 2048), F32, actx)
    sba("xfm", (128, NCH * TA), F32, actx)
    sba("delta", (128, NCH * TA), BF16, actx)
    sba("xprev", (128, 8), F32, actx)
    sba("wst", (64, 16 * TA), BF16, actx)
    sba("rst", (64, 16 * TA), BF16, actx)
    sba("kst", (64, 16 * TA), BF16, actx)
    sba("vst", (128, NCH * TA), BF16, actx)
    sba("ahst", (64, 16 * TA), BF16, actx)
    sba("kkst", (64, 16 * TA), BF16, actx)
    sba("gst", (128, NCH * TA), BF16, actx)
    sba("ost", (128, NCH * TA), BF16, actx)
    sba("mixA", (128, NCH * TA), BF16, actx)
    sba("mixB", (128, NCH * TA), BF16, actx)
    sba("lr", (64, 4 * NCH * TA), BF16, actx)
    sba("bk", (64, 4 * NCH * TA), BF16, actx)
    sba("z1", (64, TA), BF16, actx)
    sba("g1", (128, TA), BF16, actx)
    sba("ytile", (128, NCH * TA), YDT, actx)
    sba("prep", (64, 2 * 2 * TA), F32, actx)
    sba("prepE", (64, 2 * 4 * TA), BF16, actx)
    sba("prep2", (64, 2 * 2 * TA), F32, actx)
    sba("eb", (64, 16 * 3 * NCC + 8 * NCC), F32, actx)
    sba("kkr", (64, 2 * 2 * TA), BF16, actx)
    sba("sqh", (64, 2 * 2 * TA), BF16, actx)
    sba("sq", (128, 2 * TA), BF16, actx)
    sba("sq2", (128, 2 * TA), BF16, actx)
    sba("gnstat", (16, 3 * TA), F32, actx)
    sba("lnrow", (1, 3 * TA), F32, actx)
    for par in range(2):
        sba(f"qts{par}", (64, 512), BF16, actx)
        sba(f"m1{par}", (64, 128), BF16, actx)
        sba(f"tm{par}", (64, 512), BF16, actx)
        sba(f"x0{par}", (64, 256), BF16, actx)
        sba(f"x1{par}", (64, 256), BF16, actx)
        sba(f"xf{par}", (64, 256), F32, actx)
        sba(f"sqp{par}", (64, 5 * 256), BF16, actx)
        sba(f"wfm{par}", (64, 128), BF16, actx)
        sba(f"sgm{par}", (64, 128), F32, actx)
        sba(f"sgmb{par}", (64, 128), BF16, actx)
        sba(f"str{par}", (64, 128), BF16, actx)
        sba(f"stt{par}", (64, 128), F32, actx)
    sch.op("vector", lambda e: e.memset(sb["xprev"][:, :], 0.0), writes=[bf("xprev")])

    def mm_group(psum_ap_fn, lhs, lhs_cols, rhs, rhs_cols, nk):
        def fn(e):
            inst = None
            for k in range(nk):
                l0, l1 = lhs_cols(k)
                r0, r1 = rhs_cols(k)
                inst = e.matmul(psum_ap_fn(), lhs[:, l0:l1], rhs[:, r0:r1],
                                start=(k == 0), stop=(k == nk - 1))
            return inst
        return fn

    _sa = cfg.stop_after if isinstance(cfg.stop_after, str) else ""
    _alim = float(_sa[1:]) if _sa.startswith("A") and len(_sa) > 1 else 99
    for t in range(NT_A):
        t0 = t * TA
        xfm, delta = sb["xfm"], sb["delta"]

        for blk_i in range(TA // 128):
            bxt = bf(f"xtm{blk_i % 2}")
            xo = (blk_i % 2) * 1024
            sch.dma("ld", lambda e, blk_i=blk_i, t0=t0, xo=xo: e.dma_start(
                out=sb["xtm"][:, xo:xo + 1024],
                in_=dI["x"][t0 + blk_i * 128: t0 + (blk_i + 1) * 128, :]),
                writes=[bxt], tag="xld")
            for half in range(2):
                sch.op("tensor", lambda e, half=half, xo=xo: [
                    e.transpose(PS[0][:, i * 128:(i + 1) * 128],
                                sb["xtm"][:, xo + (half * 4 + i) * 128:
                                          xo + (half * 4 + i + 1) * 128],
                                sb["ident_f32"][:, :]) for i in range(4)][-1],
                    reads=[bxt, bf("ident_f32"), bf("PS0")], writes=[bf("PS0")], tag="xtp")
                sch.op("vector", lambda e, half=half, blk_i=blk_i: e.tensor_copy(
                    xfm.ap().rearrange("p (c u) -> p c u", c=NCH)[
                        :, half * 4:(half + 1) * 4, blk_i * 128:(blk_i + 1) * 128],
                    PS[0][:, 0:512].rearrange("p (c u) -> p c u", c=4)),
                    reads=[bf("PS0")],
                    writes=[bf(f"xfm{q}") for q in range(half * 4, half * 4 + 4)] +
                           [bf("PS0")], tag="xev")

        if _alim < 1:
            continue
        for c in range(NCH):
            co = c * TA
            sch.op("vector", lambda e, co=co, c=c: e.tensor_sub(
                delta[:, co + 1:co + TA], xfm[:, co:co + TA - 1], xfm[:, co + 1:co + TA]),
                reads=[bf(f"xfm{c}")], writes=[bf(f"delta{c}")], tag="dlt")
            sch.op("vector", lambda e, co=co, c=c: e.tensor_sub(
                delta[:, co:co + 1], sb["xprev"][:, c:c + 1], xfm[:, co:co + 1]),
                reads=[bf(f"xfm{c}"), bf("xprev")], writes=[bf(f"delta{c}")], tag="dlt0")
        sch.op("vector", lambda e: e.tensor_copy(
            sb["xprev"][:, :],
            xfm.ap().rearrange("p (c u) -> p c u", c=NCH)[:, :, TA - 1]),
            reads=[bf(f"xfm{c}") for c in range(NCH)], writes=[bf("xprev")], tag="xpv")

        def emit_mix(mixn, vecn, tag):
            mx = sb[mixn]
            for c in range(NCH):
                co = c * TA
                sch.op("vector", lambda e, co=co, c=c, mx=mx: e.scalar_tensor_tensor(
                    mx[:, co:co + TA], delta[:, co:co + TA], sb[vecn][:, c:c + 1],
                    xfm[:, co:co + TA], op0=ALU.mult, op1=ALU.add),
                    reads=[bf(f"delta{c}"), bf(f"xfm{c}"), bf(vecn), bf(f"{mixn}{c}")],
                    writes=[bf(f"{mixn}{c}")], tag=tag)

        def emit_proj(Wsrc, mixn, outn, tag):
            for f in range(NCH):
                bank, bb = (PS[0], bf("PS0")) if f % 2 == 0 else (PS[1], bf("PS1"))
                wn, bwo = ld_wblk(Wsrc, f, tag + "w")
                sch.op("tensor", mm_group(
                    lambda bank=bank: bank[:, 0:TA], sb[wn],
                    lambda k: (k * 128, (k + 1) * 128),
                    sb[mixn], lambda k: (k * TA, (k + 1) * TA), 8),
                    reads=[bwo, bb] + [bf(f"{mixn}{k}") for k in range(NCH)],
                    writes=[bb], tag=tag)
                sch.op("vector", lambda e, bank=bank, f=f, outn=outn: e.tensor_copy(
                    sb[outn][:, f * TA:(f + 1) * TA], bank[:, 0:TA]),
                    reads=[bb], writes=[bf(f"{outn}{f}"), bb], tag=tag + "e")

        def ld_wblk(Wsrc, f, tag):
            wn = f"Wb{f % 2}"
            bwo = bf(wn)
            sch.dma("gld", lambda e, f=f, wn=wn, Wsrc=Wsrc: e.dma_start(
                out=sb[wn].ap().rearrange("p (k c) -> p k c", k=8),
                in_=dI[Wsrc][:, f * 128:(f + 1) * 128].rearrange("(k p) c -> p k c",
                                                                 p=128)),
                writes=[bwo], tag=tag)
            return wn, bwo

        def emit_proj_hm(Wsrc, mixn, outn, tag):
            for c in range(NCH):
                bank, bb = (PS[0], bf("PS0")) if c % 2 == 0 else (PS[1], bf("PS1"))
                wn, bwo = ld_wblk(Wsrc, c, tag + "w")
                def fn(e, c=c, bank=bank, wn=wn, mixn=mixn):
                    inst = None
                    for h in range(2):
                        for k in range(8):
                            inst = e.matmul(
                                bank[0:64, h * TA:(h + 1) * TA],
                                sb[wn][:, k * 128 + h * 64:k * 128 + (h + 1) * 64],
                                sb[mixn][:, k * TA:(k + 1) * TA],
                                start=(k == 0), stop=(k == 7))
                    return inst
                sch.op("tensor", fn,
                       reads=[bwo, bb] + [bf(f"{mixn}{k}") for k in range(NCH)],
                       writes=[bb], tag=tag)
                sch.op("vector", lambda e, bank=bank, c=c, outn=outn: e.tensor_copy(
                    sb[outn][:, 2 * c * TA:(2 * c + 2) * TA], bank[0:64, 0:2 * TA]),
                    reads=[bb], writes=[bf(f"{outn}{c}"), bb], tag=tag + "e")

        emit_mix("mixA", "x_r", "mxr")
        emit_proj_hm("W_r", "mixA", "rst", "pjr")
        emit_mix("mixB", "x_k", "mxk")
        emit_proj_hm("W_k", "mixB", "kst", "pjk")
        emit_mix("mixA", "x_v", "mxv")
        emit_proj("W_v", "mixA", "vst", "pjv")

        if _alim < 2:
            continue
        emit_mix("mixB", "x_w", "mxw")
        sch.op("tensor", mm_group(lambda: PS[2][0:64, 0:TA], sb["wla"],
                                  lambda k: (k * 64, (k + 1) * 64),
                                  sb["mixB"], lambda k: (k * TA, (k + 1) * TA), 8),
               reads=[bf("wla")] + [bf(f"mixB{k}") for k in range(NCH)] + PS2A,
               writes=PS2A, tag="wl1")
        sch.op("scalar", lambda e: e.activation(sb["z1"][:, 0:TA], PS[2][0:64, 0:TA],
                                                AF.Tanh),
               reads=PS2A, writes=[bf("z1")] + PS2A, tag="wlt")
        for f in range(NCH):
            pb = 64 * (f % 2)
            P3Q = [bf(f"PS3q{f % 2}0"), bf(f"PS3q{f % 2}1")]
            po2 = (f % 2) * 2 * TA
            bp2 = bf(f"prep2{f % 2}")
            def wl2(e, f=f, pb=pb):
                e.matmul(PS[3][pb:pb + 64, 0:TA], sb["wlb"][:, f * 128:f * 128 + 64],
                         sb["z1"][:, 0:TA], start=True, stop=True,
                         tile_position=(0, pb))
                return e.matmul(PS[3][pb:pb + 64, TA:2 * TA],
                                sb["wlb"][:, f * 128 + 64:(f + 1) * 128],
                                sb["z1"][:, 0:TA], start=True, stop=True,
                                tile_position=(0, pb))
            sch.op("tensor", wl2, reads=[bf("wlb"), bf("z1")] + P3Q,
                   writes=P3Q, tag="wl2")
            for h in range(2):
                S = 2 * f + h
                sch.op("scalar", lambda e, h=h, S=S, pb=pb, po2=po2: e.activation(
                    sb["prep2"][:, po2 + h * TA:po2 + (h + 1) * TA],
                    PS[3][pb:pb + 64, h * TA:(h + 1) * TA],
                    AF.Exp, bias=sb["negwlbh"][:, S:S + 1], scale=-1.0),
                    reads=P3Q + [bf("negwlbh"), bp2],
                    writes=[bp2] + (P3Q if h == 1 else []), tag="wls")
            sch.op("vector", lambda e, po2=po2: e.tensor_scalar_add(
                sb["prep2"][:, po2:po2 + 2 * TA], sb["prep2"][:, po2:po2 + 2 * TA], 1.0),
                reads=[bp2], writes=[bp2], tag="wls1")
            sch.op("scalar", lambda e, po2=po2: e.activation(
                sb["prep2"][:, po2:po2 + 2 * TA], sb["prep2"][:, po2:po2 + 2 * TA],
                AF.Ln),
                reads=[bp2], writes=[bp2], tag="wls2")
            sch.op("vector", lambda e, f=f, po2=po2: e.tensor_scalar(
                sb["wst"][:, 2 * f * TA:(2 * f + 2) * TA],
                sb["prep2"][:, po2:po2 + 2 * TA],
                -1.0, -0.5, op0=ALU.mult, op1=ALU.add),
                reads=[bp2], writes=[bf(f"wst{f}")], tag="wlf")

        if _alim < 2.2:
            continue
        emit_mix("mixB", "x_a", "mxa")
        sch.op("tensor", mm_group(lambda: PS[2][0:64, 0:TA], sb["ala"],
                                  lambda k: (k * 64, (k + 1) * 64),
                                  sb["mixB"], lambda k: (k * TA, (k + 1) * TA), 8),
               reads=[bf("ala")] + [bf(f"mixB{k}") for k in range(NCH)] + PS2A,
               writes=PS2A, tag="al1")
        sch.op("vector", lambda e: e.tensor_copy(sb["z1"][:, 0:TA], PS[2][0:64, 0:TA]),
               reads=PS2A, writes=[bf("z1")] + PS2A, tag="alc")
        for f in range(NCH):
            pb = 64 * (f % 2)
            P3Q = [bf(f"PS3q{f % 2}0"), bf(f"PS3q{f % 2}1")]
            def al2(e, f=f, pb=pb):
                e.matmul(PS[3][pb:pb + 64, 0:TA], sb["alb"][:, f * 128:f * 128 + 64],
                         sb["z1"][:, 0:TA], start=True, stop=True,
                         tile_position=(0, pb))
                return e.matmul(PS[3][pb:pb + 64, TA:2 * TA],
                                sb["alb"][:, f * 128 + 64:(f + 1) * 128],
                                sb["z1"][:, 0:TA], start=True, stop=True,
                                tile_position=(0, pb))
            sch.op("tensor", al2, reads=[bf("alb"), bf("z1")] + P3Q,
                   writes=P3Q, tag="al2")
            for h in range(2):
                S = 2 * f + h
                sch.op("scalar", lambda e, h=h, S=S, pb=pb: e.activation(
                    sb["ahst"][:, S * TA:(S + 1) * TA],
                    PS[3][pb:pb + 64, h * TA:(h + 1) * TA],
                    AF.Sigmoid, bias=sb["albh"][:, S:S + 1]),
                    reads=P3Q + [bf("albh")],
                    writes=[bf(f"ahst{f}")] + (P3Q if h == 1 else []), tag="als")

        if _alim < 2.5:
            continue
        sch.op("tensor", mm_group(lambda: PS[2][:, 0:TA], sb["gla"],
                                  lambda k: (k * 128, (k + 1) * 128),
                                  sb["rst"], lambda k: (k * TA, (k + 1) * TA), 16),
               reads=[bf("gla")] + [bf(f"rst{k}") for k in range(NCH)] + PS2A,
               writes=PS2A, tag="gl1")
        sch.op("scalar", lambda e: e.activation(sb["g1"][:, 0:TA], PS[2][:, 0:TA],
                                                AF.Sigmoid),
               reads=PS2A, writes=[bf("g1")] + PS2A, tag="gls")
        if _alim < 2.7:
            continue
        for f in range(NCH):
            vo = 0 if FLATG else (f % 2) * TA
            P3Q = ([bf("PS3q00"), bf("PS3q10")] if FLATG else
                   [bf(f"PS3q0{f % 2}"), bf(f"PS3q1{f % 2}")])
            sch.op("tensor", lambda e, f=f, vo=vo: e.matmul(
                PS[3][:, vo:vo + TA], sb["glb"][:, f * 128:(f + 1) * 128],
                sb["g1"][:, 0:TA], start=True, stop=True),
                reads=[bf("glb"), bf("g1")] + P3Q, writes=P3Q, tag="gl2")
            sch.op("vector", lambda e, f=f, vo=vo: e.tensor_scalar(
                sb["gst"][:, f * TA:(f + 1) * TA], PS[3][:, vo:vo + TA],
                sb["g_lbias"][:, f:f + 1], None, op0=ALU.add),
                reads=P3Q + [bf("g_lbias")],
                writes=[bf(f"gst{f}")] + P3Q, tag="glf")

        if _alim < 3:
            continue
        for c in range(NCH):
            ho = 2 * c * TA
            pk = (c % 2) * 2 * TA
            bkr, bsh = bf(f"kkr{c % 2}"), bf(f"sqh{c % 2}")
            for h in range(2):
                S = 2 * c + h
                sch.op("vector", lambda e, ho=ho, h=h, S=S, pk=pk, c=c: e.tensor_scalar(
                    sb["kkr"][:, pk + h * TA:pk + (h + 1) * TA],
                    sb["kst"][:, ho + h * TA:ho + (h + 1) * TA],
                    sb["kkh"][:, S:S + 1], None, op0=ALU.mult),
                    reads=[bf(f"kst{c}"), bf("kkh"), bkr], writes=[bkr], tag="kk1")
            if _alim >= 2.81:
                sch.op("scalar", lambda e, pk=pk: e.activation(
                    sb["sqh"][:, pk:pk + 2 * TA], sb["kkr"][:, pk:pk + 2 * TA],
                    AF.Square),
                    reads=[bkr, bsh], writes=[bsh], tag="kk2")
            if _alim >= 2.83:
                def kk3(e, c=c, pk=pk):
                    inst = None
                    for h in range(2):
                        S = 2 * c + h
                        inst = e.matmul(PS[5][0:16, 0:TA],
                                        sb["bsel_hm"][:, 16 * S:16 * (S + 1)],
                                        sb["sqh"][:, pk + h * TA:pk + (h + 1) * TA],
                                        start=(S == 0), stop=(S == 15))
                    return inst
                sch.op("tensor", kk3, reads=[bsh, bf("bsel_hm"), bf("PS5s")],
                       writes=[bf("PS5s")], tag="kk3")
            if _alim >= 2.86:
                sch.op("scalar", lambda e, ho=ho, pk=pk: e.copy(
                    sb["kkst"][:, ho:ho + 2 * TA], sb["kkr"][:, pk:pk + 2 * TA]),
                    reads=[bkr], writes=[bf(f"kkst{c}"), bkr], tag="kk4")
        g_ = sb["gnstat"]
        if _alim < 2.8:
            continue
        sch.op("scalar", lambda e: e.activation(g_[0:16, 0:TA], PS[5][0:16, 0:TA], AF.Sqrt),
               reads=[bf("PS5s"), bf("gnstat")], writes=[bf("gnstat"), bf("PS5s")],
               tag="kk5")
        sch.op("vector", lambda e: e.tensor_single_scalar(
            g_[0:16, 0:TA], g_[0:16, 0:TA], 1e-12, op=ALU.max),
            reads=[bf("gnstat")], writes=[bf("gnstat")], tag="kk6")
        sch.op("vector", lambda e: e.reciprocal(g_[0:16, TA:2 * TA], g_[0:16, 0:TA]),
               reads=[bf("gnstat")], writes=[bf("gnstat")], tag="kk7")
        for c in range(NCH):
            ho = 2 * c * TA
            pb = 0 if FLATG else 64 * (c % 2)
            pk = (c % 2) * 2 * TA
            P3Q = ([bf("PS3q00"), bf("PS3q01")] if FLATG else
                   [bf(f"PS3q{c % 2}0"), bf(f"PS3q{c % 2}1")])
            bp2 = bf(f"prep2{c % 2}")
            def kk8(e, c=c, pb=pb):
                inst = None
                for h in range(2):
                    S = 2 * c + h
                    inst = e.matmul(PS[3][pb:pb + 64, h * TA:(h + 1) * TA],
                                    sb["sel_hm"][0:16, 64 * S:64 * (S + 1)],
                                    g_[0:16, TA:2 * TA], start=True, stop=True,
                                    tile_position=(0, pb))
                return inst
            sch.op("tensor", kk8, reads=[bf("gnstat"), bf("sel_hm")] + P3Q,
                   writes=P3Q, tag="kk8")
            sch.op("vector", lambda e, ho=ho, c=c, pb=pb: e.tensor_mul(
                sb["kkst"][:, ho:ho + 2 * TA], sb["kkst"][:, ho:ho + 2 * TA],
                PS[3][pb:pb + 64, 0:2 * TA]),
                reads=P3Q + [bf(f"kkst{c}")],
                writes=[bf(f"kkst{c}")] + P3Q, tag="kk9")
            if _alim < 2.9:
                continue
            for h in range(2):
                S = 2 * c + h
                sch.op("vector", lambda e, ho=ho, h=h, S=S, pk=pk, c=c: e.tensor_scalar(
                    sb["prep2"][:, pk + h * TA:pk + (h + 1) * TA],
                    sb["ahst"][:, ho + h * TA:ho + (h + 1) * TA],
                    sb["kah"][:, S:S + 1], sb["onemkah"][:, S:S + 1],
                    op0=ALU.mult, op1=ALU.add),
                    reads=[bf(f"ahst{c}"), bf("kah"), bf("onemkah"), bp2],
                    writes=[bp2], tag="kh1")
            sch.op("vector", lambda e, ho=ho, pk=pk, c=c: e.tensor_mul(
                sb["kst"][:, ho:ho + 2 * TA], sb["kst"][:, ho:ho + 2 * TA],
                sb["prep2"][:, pk:pk + 2 * TA]),
                reads=[bp2, bf(f"kst{c}")], writes=[bf(f"kst{c}")], tag="kh2")

        if _alim < 4:
            continue
        for c in range(NCH):
            ho = 2 * c * TA
            P_, P2 = sb["prep"], sb["prep2"]
            W2 = 2 * TA
            po = (c % 2) * W2
            poE = (c % 2) * 2 * W2
            bP, bP2 = bf(f"prep{c % 2}"), bf(f"prep2{c % 2}")
            beb_, bebt = bf(f"eb{c}"), bf(f"ebt{c % 2}")
            for h in range(2):
                sch.op("vector", lambda e, ho=ho, h=h, po=po: e.tensor_tensor_scan(
                    P_[:, po + h * TA:po + (h + 1) * TA],
                    sb["wst"][:, ho + h * TA:ho + (h + 1) * TA],
                    sb["sm"][0:64, 0:1].broadcast_to([64, TA]), 0.0,
                    op0=ALU.add, op1=ALU.add),
                    reads=[bf(f"wst{c}"), bf("zero1"), bP], writes=[bP], tag="pg")
            ebc = 2 * c * 3 * NCC
            ebt = 16 * 3 * NCC + (c % 2) * 4 * NCC
            def gsel(col, po=po):
                return P_.ap()[:, po:po + W2].rearrange(
                    "p (s cc u) -> p s cc u", s=2, cc=NCC)[:, :, :, col]
            def ebsl(ofs, ebc=ebc):
                return sb["eb"].ap()[:, ebc:ebc + 6 * NCC].rearrange(
                    "p (s k cc) -> p s k cc", s=2, k=3)[:, :, ofs, :]
            def ebtmp(i, ebt=ebt):
                return sb["eb"].ap()[:, ebt + i * 2 * NCC:ebt + (i + 1) * 2 * NCC
                                     ].rearrange("p (s cc) -> p s cc", s=2)
            sch.op("vector", lambda e, ebtmp=ebtmp: e.memset(ebtmp(0)[:, :, 0:1], 0.0),
                   reads=[bebt], writes=[bebt], tag="pgp0")
            if NCC > 1:
                sch.op("vector", lambda e, gsel=gsel, ebtmp=ebtmp: e.tensor_copy(
                    ebtmp(0)[:, :, 1:NCC], gsel(63)[:, :, 0:NCC - 1]),
                    reads=[bP, bebt], writes=[bebt], tag="pgp1")
            sch.op("vector", lambda e, gsel=gsel, ebtmp=ebtmp: e.tensor_sub(
                ebtmp(1), gsel(31), ebtmp(0)),
                reads=[bP, bebt], writes=[bebt], tag="pgp2")
            sch.op("scalar", lambda e, ebsl=ebsl, ebtmp=ebtmp: e.activation(
                ebsl(0), ebtmp(1), AF.Exp),
                reads=[bebt], writes=[beb_], tag="per")
            sch.op("vector", lambda e, gsel=gsel, ebtmp=ebtmp: e.tensor_sub(
                ebtmp(1), gsel(63), ebtmp(0)),
                reads=[bP, bebt, beb_], writes=[bebt], tag="pgp3")
            sch.op("scalar", lambda e, ebsl=ebsl, ebtmp=ebtmp: e.activation(
                ebsl(2), ebtmp(1), AF.Exp),
                reads=[bebt], writes=[beb_], tag="peg")
            sch.op("vector", lambda e, gsel=gsel, ebtmp=ebtmp: e.tensor_sub(
                ebtmp(1), gsel(63), gsel(31)),
                reads=[bP, beb_, bebt], writes=[bebt], tag="ped0")
            sch.op("scalar", lambda e, ebsl=ebsl, ebtmp=ebtmp: e.activation(
                ebsl(1), ebtmp(1), AF.Exp),
                reads=[bebt], writes=[beb_], tag="ped")
            sch.op("vector", lambda e, po=po: e.tensor_sub(
                P_[:, po:po + W2], P_[:, po:po + W2],
                P_.ap()[:, po:po + W2].rearrange("p (s cc u) -> p s cc u", s=2, cc=NCC)
                [:, :, :, 31:32].broadcast_to([64, 2, NCC, LC])),
                reads=[bP], writes=[bP], tag="pgr")
            sch.op("vector", lambda e, ho=ho, P2=P2, W2=W2, po=po: e.tensor_sub(
                P2[:, po:po + W2], P_[:, po:po + W2], sb["wst"][:, ho:ho + W2]),
                reads=[bP, bf(f"wst{c}"), bP2], writes=[bP2], tag="pe2s")
            sch.op("scalar", lambda e, po=po: e.activation(
                P2[:, po:po + W2], P2[:, po:po + W2], AF.Exp),
                reads=[bP2], writes=[bP2], tag="pe2")
            PE_ = sb["prepE"]
            bPE = bf(f"prepE{c % 2}")
            sch.op("scalar", lambda e, PE_=PE_, po=po, poE=poE: e.activation(
                PE_[:, poE:poE + W2], P_[:, po:po + W2], AF.Exp),
                reads=[bP, bPE], writes=[bPE], tag="pe1")
            sch.op("scalar", lambda e, PE_=PE_, po=po, poE=poE: e.activation(
                PE_[:, poE + W2:poE + 2 * W2], P_[:, po:po + W2], AF.Exp, scale=-1.0),
                reads=[bP], writes=[bPE], tag="pen")
            lrc0 = 4 * TA * c

            def ilv(tsr, off, lrc0=lrc0):
                return tsr.ap()[:, lrc0:lrc0 + 2 * W2].rearrange(
                    "p (s cc two u) -> p s cc two u", s=2, cc=NCC, two=2)[:, :, :, off, :]

            def ch(tsr, c0=None, ho=ho):
                base = ho if c0 is None else c0
                return tsr.ap()[:, base:base + W2].rearrange(
                    "p (s cc u) -> p s cc u", s=2, cc=NCC)

            blr_, bbk_ = bf(f"lr{c}"), bf(f"bk{c}")
            sch.op("vector", lambda e, ilv=ilv, ch=ch, P2=P2, po=po: e.tensor_tensor(
                ilv(sb["lr"], 0), ch(sb["kkst"]), ch(P2, po), op=ALU.mult),
                reads=[bf(f"kkst{c}"), bP2, blr_], writes=[blr_], tag="psa")
            sch.op("vector", lambda e, ilv=ilv, ch=ch, poE=poE: e.tensor_tensor(
                ilv(sb["lr"], 1), ch(sb["rst"]), ch(sb["prepE"], poE), op=ALU.mult),
                reads=[bf(f"rst{c}"), bPE, blr_], writes=[blr_], tag="psr")
            sch.op("vector", lambda e, ho=ho, P2=P2, W2=W2, po=po: e.scalar_tensor_tensor(
                P2[:, po:po + W2], sb["kkst"][:, ho:ho + W2], -1.0,
                sb["ahst"][:, ho:ho + W2],
                op0=ALU.mult, op1=ALU.mult),
                reads=[bf(f"kkst{c}"), bf(f"ahst{c}"), bP2], writes=[bP2], tag="pbp")
            sch.op("vector", lambda e, ilv=ilv, ch=ch, P2=P2, W2=W2, po=po, poE=poE:
                e.tensor_tensor(
                    ilv(sb["bk"], 0), ch(P2, po), ch(sb["prepE"], poE + W2),
                    op=ALU.mult),
                reads=[bP2, bPE, bbk_], writes=[bbk_], tag="psb")
            sch.op("vector", lambda e, ilv=ilv, ch=ch, W2=W2, poE=poE: e.tensor_tensor(
                ilv(sb["bk"], 1), ch(sb["kst"]), ch(sb["prepE"], poE + W2),
                op=ALU.mult),
                reads=[bf(f"kst{c}"), bPE, bbk_], writes=[bbk_], tag="psk")

        if _alim < 5:
            continue
        for cc in range(NCC):
            for c in range(NCH):
                par = (cc * NCH + c) % 2
                unit_scan(sch, sb, PS, PSB, WKbank, QTbank, bf, cfg, cc, c, par)

        if _alim < 6:
            continue
        post_scan(sch, sb, PS, bf, dI, cfg, t, y_dram)

    if isinstance(cfg.stop_after, str) and cfg.stop_after.startswith("A"):
        actx.close()
        return _finish_early(nc, sch, ctx)
    finalize_scale(sch, sb, PS, bf, dI, cfg, "ymax", cc_io[0], cc_io[1], "clipx", 0)
    sch.barrier()
    actx.close()

    b1 = ExitStack()
    sba("WgT", (128, 8 * 2816), BF16, b1)
    sba("WuT", (128, 8 * 2816), BF16, b1)
    sba("yt", (128, NCH * TB), YDT, b1)
    sba("xq", (128, NCH * TB), BF16, b1)
    sba("qf", (128, 2 * TB), F32, b1)
    sba("hb", (128, 2 * TB), HDT, b1)
    sba("silu", (128, 2 * TB), BF16, b1)
    for k in range(8):
        sch.dma("ldw", lambda e, k=k: e.dma_start(
            out=sb["WgT"][:, k * 2816:(k + 1) * 2816], in_=wgT_dram[k]),
            writes=[bf("WgT")], tag="lWg")
        sch.dma("ldw", lambda e, k=k: e.dma_start(
            out=sb["WuT"][:, k * 2816:(k + 1) * 2816], in_=wuT_dram[k]),
            writes=[bf("WuT")], tag="lWu")
    sch.op("vector", lambda e: e.tensor_scalar(
        sb["sgx"][:, :], sb["sg"][:, :], sb["scal"][:, 0:1], None, op0=ALU.mult),
        reads=[bf("sg"), bf("scal")], writes=[bf("sgx")], tag="sgx")
    sch.op("vector", lambda e: e.tensor_scalar(
        sb["sux"][:, :], sb["su"][:, :], sb["scal"][:, 0:1], None, op0=ALU.mult),
        reads=[bf("su"), bf("scal")], writes=[bf("sux")], tag="sux")

    for t in range(NT_B):
        t0 = t * TB
        for c in range(NCH):
            sch.dma("ld", lambda e, c=c, t0=t0: e.dma_start(
                out=sb["yt"][:, c * TB:(c + 1) * TB], in_=y_dram[c, :, t0:t0 + TB]),
                writes=[bf(f"yt{c}")], tag="lys")
        for c in range(NCH):
            co = c * TB
            po = (c % 2) * TB
            bq_ = bf(f"qf{c % 2}")
            sch.op("vector", lambda e, co=co, po=po, c=c: e.tensor_scalar(
                sb["qf"][:, po:po + TB], sb["yt"][:, co:co + TB], sb["scal"][:, 1:2],
                MAGIC, op0=ALU.mult, op1=ALU.add),
                reads=[bf(f"yt{c}"), bf("scal"), bq_], writes=[bq_], tag="xq1")
            sch.op("vector", lambda e, po=po: e.tensor_scalar(
                sb["qf"][:, po:po + TB], sb["qf"][:, po:po + TB], -MAGIC, -(QMAX + 1.0),
                op0=ALU.add, op1=ALU.max), reads=[bq_], writes=[bq_], tag="xq2")
            sch.op("vector", lambda e, co=co, po=po: e.tensor_single_scalar(
                sb["xq"][:, co:co + TB], sb["qf"][:, po:po + TB], QMAX, op=ALU.min),
                reads=[bq_, bf(f"xq{c}")], writes=[bf(f"xq{c}")], tag="xq3")
        BXQ = [bf(f"xq{k}") for k in range(NCH)]
        for i in range(NI):
            pi = i % 2
            bhb, bsil = bf(f"hb{pi}"), bf(f"silu{pi}")
            sch.op("tensor", mm_group(
                lambda: PS[0][:, 0:TB], sb["WgT"],
                lambda k, i=i: (k * 2816 + i * 128, k * 2816 + (i + 1) * 128),
                sb["xq"], lambda k: (k * TB, (k + 1) * TB), 8),
                reads=[bf("WgT"), bf("PS0")] + BXQ, writes=[bf("PS0")], tag="gmm")
            sch.op("tensor", mm_group(
                lambda: PS[1][:, 0:TB], sb["WuT"],
                lambda k, i=i: (k * 2816 + i * 128, k * 2816 + (i + 1) * 128),
                sb["xq"], lambda k: (k * TB, (k + 1) * TB), 8),
                reads=[bf("WuT"), bf("PS1")] + BXQ, writes=[bf("PS1")], tag="umm")
            sch.op("scalar", lambda e, i=i, pi=pi: e.activation(
                sb["silu"][:, pi * TB:pi * TB + TB], PS[0][:, 0:TB], AF.Silu,
                scale=sb["sgx"][:, i:i + 1]),
                reads=[bf("PS0"), bf("sgx"), bsil], writes=[bsil, bf("PS0")],
                tag="sil")
            sch.op("vector", lambda e, i=i, pi=pi: e.scalar_tensor_tensor(
                sb["hb"][:, pi * TB:pi * TB + TB], PS[1][:, 0:TB], sb["sux"][:, i:i + 1],
                sb["silu"][:, pi * TB:pi * TB + TB], op0=ALU.mult, op1=ALU.mult),
                reads=[bf("PS1"), bf("sux"), bsil, bhb],
                writes=[bhb, bf("PS1")], tag="hm")
            sch.op("vector", lambda e, pi=pi: e.reduce_max(
                sb["sm"][:, 5:6], sb["hb"][:, pi * TB:pi * TB + TB], axis=AX.X,
                apply_absolute_value=True),
                reads=[bhb], writes=[bf("hmx")], tag="hx1")
            sch.op("vector", lambda e: e.tensor_max(
                sb["hmax"][:, 0:1], sb["hmax"][:, 0:1], sb["sm"][:, 5:6]),
                reads=[bf("hmx"), bf("hmax")], writes=[bf("hmax")], tag="hx2")
            sch.dma("st", lambda e, i=i, t0=t0, pi=pi: e.dma_start(
                out=h_dram[i, :, t0:t0 + TB], in_=sb["hb"][:, pi * TB:pi * TB + TB]),
                reads=[bhb], tag="sth")

    if cfg.stop_after == "B1":
        b1.close()
        return _finish_early(nc, sch, ctx)
    finalize_scale(sch, sb, PS, bf, dI, cfg, "hmax", cc_io[2], cc_io[3], "clipd", 2)
    sch.barrier()
    b1.close()

    b2 = ExitStack()
    sba("WdT", (128, 22 * 1024), BF16, b2)
    sba("yt2", (128, NCH * TB), YDT, b2)
    sba("ht", (128, NI * TB), HDT, b2)
    sba("hq", (128, NI * TB), BF16, b2)
    sba("qf2", (128, 2 * TB), F32, b2)
    sba("zpre", (128, NCH * TB), F32, b2)
    sba("sqb", (128, 2 * TB), BF16, b2)
    sba("zfin", (128, NCH * TB), F32, b2)
    sba("ztm", (128, 2048), F32, b2)
    sba("lnrow", (1, 3 * TB), F32, b2)
    for k in range(22):
        sch.dma("ldw", lambda e, k=k: e.dma_start(
            out=sb["WdT"][:, k * 1024:(k + 1) * 1024], in_=wdT_dram[k]),
            writes=[bf("WdT")], tag="lWd")
    sch.op("vector", lambda e: e.tensor_scalar(
        sb["sdh"][:, :], sb["sd"][:, :], sb["scal"][:, 2:3], None, op0=ALU.mult),
        reads=[bf("sd"), bf("scal")], writes=[bf("sdh")], tag="sdh")

    for t in range(NT_B):
        t0 = t * TB
        for c in range(NCH):
            sch.dma("ld", lambda e, c=c, t0=t0: e.dma_start(
                out=sb["yt2"][:, c * TB:(c + 1) * TB], in_=y_dram[c, :, t0:t0 + TB]),
                writes=[bf(f"yt2{c}")], tag="ly2")
        for i in range(NI):
            sch.dma("ld", lambda e, i=i, t0=t0: e.dma_start(
                out=sb["ht"][:, i * TB:(i + 1) * TB], in_=h_dram[i, :, t0:t0 + TB]),
                writes=[bf(f"ht{i}")], tag="lh")
        for i in range(NI):
            co = i * TB
            po = (i % 2) * TB
            bq2 = bf(f"qf2{i % 2}")
            sch.op("vector", lambda e, co=co, po=po, i=i: e.tensor_scalar(
                sb["qf2"][:, po:po + TB], sb["ht"][:, co:co + TB], sb["scal"][:, 3:4],
                MAGIC, op0=ALU.mult, op1=ALU.add),
                reads=[bf(f"ht{i}"), bf("scal"), bq2], writes=[bq2], tag="hq1")
            sch.op("vector", lambda e, po=po: e.tensor_scalar(
                sb["qf2"][:, po:po + TB], sb["qf2"][:, po:po + TB], -MAGIC,
                -(QMAX + 1.0),
                op0=ALU.add, op1=ALU.max), reads=[bq2], writes=[bq2], tag="hq2")
            sch.op("vector", lambda e, co=co, po=po, i=i: e.tensor_single_scalar(
                sb["hq"][:, co:co + TB], sb["qf2"][:, po:po + TB], QMAX, op=ALU.min),
                reads=[bq2, bf(f"hq{i}")], writes=[bf(f"hq{i}")], tag="hq3")
        BHQ = [bf(f"hq{k}") for k in range(NI)]
        for f in range(NCH):
            bank, bb = (PS[0], bf("PS0")) if f % 2 == 0 else (PS[1], bf("PS1"))
            sch.op("tensor", mm_group(
                lambda bank=bank: bank[:, 0:TB], sb["WdT"],
                lambda k, f=f: (k * 1024 + f * 128, k * 1024 + (f + 1) * 128),
                sb["hq"], lambda k: (k * TB, (k + 1) * TB), NI),
                reads=[bf("WdT"), bb] + BHQ, writes=[bb], tag="dmm")
            sch.op("vector", lambda e, f=f, bank=bank: e.scalar_tensor_tensor(
                sb["zpre"][:, f * TB:(f + 1) * TB], bank[:, 0:TB], sb["sdh"][:, f:f + 1],
                sb["yt2"][:, f * TB:(f + 1) * TB], op0=ALU.mult, op1=ALU.add),
                reads=[bb, bf("sdh"), bf(f"yt2{f}"), bf(f"zpre{f}")],
                writes=[bf(f"zpre{f}"), bb], tag="dre")
        emit_ln(sch, sb, PS, bf, cfg, "zpre", "zfin", TB, "sqb")
        for blk_i in range(TB // 128):
            zo = (blk_i % 2) * 1024
            bzt = bf(f"ztm{blk_i % 2}")
            for half in range(2):
                sch.op("tensor", lambda e, blk_i=blk_i, half=half: [
                    e.transpose(PS[1][:, i * 128:(i + 1) * 128],
                                sb["zfin"][:, (half * 4 + i) * TB + blk_i * 128:
                                           (half * 4 + i) * TB + (blk_i + 1) * 128],
                                sb["ident_f32"][:, :]) for i in range(4)][-1],
                    reads=[bf(f"zfin{q}") for q in range(half * 4, half * 4 + 4)] +
                          [bf("ident_f32"), bf("PS1")], writes=[bf("PS1")],
                    tag="ztp")
                sch.op("vector", lambda e, half=half, zo=zo: e.tensor_copy(
                    sb["ztm"][:, zo + half * 512:zo + (half + 1) * 512], PS[1][:, 0:512]),
                    reads=[bf("PS1"), bzt], writes=[bzt, bf("PS1")], tag="zev")
            sch.dma("st", lambda e, t0=t0, blk_i=blk_i, zo=zo: e.dma_start(
                out=z_out[t0 + blk_i * 128:t0 + (blk_i + 1) * 128, :],
                in_=sb["ztm"][:, zo:zo + 1024]),
                reads=[bzt], tag="stz")
    b2.close()

    for i in range(sch.DMA_ROT):
        if sch.dmaq["st"][2][i] > 0:
            sch.items["sync"].append(("wait", (sch.dmaq["st"][1][i],
                                               sch.dmaq["st"][2][i] * 16)))
    global _LAST_SCHED
    _LAST_SCHED = sch
    sch.finalize()
    ctx.close()
    return nc



def unit_scan(sch, sb, PS, PSB, WKbank, QTbank, bf, cfg, cc, c, par):
    ulim = getattr(cfg, "ulim", 99)
    _ucnt = [0]
    _sch_op = sch.op

    def _op(*a, **k):
        if _ucnt[0] >= ulim:
            return None
        _ucnt[0] += 1
        return _sch_op(*a, **k)

    class _S:
        op = staticmethod(_op)
        buf = sch.buf
    sch = _S()

    TA, LC, NCC, NCH = cfg.TA, cfg.LC, cfg.NCC, cfg.NCH

    def lrc(h):
        return (2 * c + h) * 2 * TA + cc * 128
    vco = c * TA + cc * LC
    ebb = [(2 * c + h) * 3 * NCC for h in range(2)]
    qts, m1, tm = sb[f"qts{par}"], sb[f"m1{par}"], sb[f"tm{par}"]
    x0, x1 = sb[f"x0{par}"], sb[f"x1{par}"]
    sqp, wfm = sb[f"sqp{par}"], sb[f"wfm{par}"]
    sgm, stR, stt = sb[f"sgm{par}"], sb[f"str{par}"], sb[f"stt{par}"]
    bq, bm1, btm = bf(f"qts{par}"), bf(f"m1{par}"), bf(f"tm{par}")
    bx = [bf(f"x0{par}"), bf(f"x1{par}")]
    bsq, bwfm = bf(f"sqp{par}"), bf(f"wfm{par}")
    bsgm, bstR, bstt = bf(f"sgm{par}"), bf(f"str{par}"), bf(f"stt{par}")
    blr, bbk, bSt = bf(f"lr{c}"), bf(f"bk{c}"), bf(f"St_{c}")
    beb = bf(f"eb{c}")
    St = sb["St"]
    stc = 2 * c * 64
    QT = QTbank[par]
    QTB = PSB[6 if par == 0 else 7]
    WK = WKbank[par]
    WKB = PSB[4 if par == 0 else 2]
    bQT = bf(f"QT{par}")
    bR1, bR2, bR3 = bf(f"WK{par}r1"), bf(f"WK{par}r2"), bf(f"WK{par}r3")
    O_ = PS[5]
    oq = (cc * NCH + c) % 4
    oc0 = 256 + oq * 64
    bO = bf(f"PS5o{oq}")

    def quad(e):
        inst = None
        for h in range(2):
            lo = lrc(h)
            for part in range(2):
                colo = h * 256 + part * 128
                inst = e.matmul(QT[0:64, colo:colo + 128],
                                sb["bk"][:, lo + part * 64:lo + (part + 1) * 64],
                                sb["lr"][:, lo:lo + 128], start=True, stop=True)
        return inst
    sch.op("tensor", quad, reads=[blr, bbk, bQT], writes=[bQT], tag="uq")
    sch.op("vector", lambda e: e.tensor_tensor(
        qts.ap().rearrange("p (a b) -> p a b", a=4),
        QT[0:64, 0:512].rearrange("p (a b) -> p a b", a=4),
        sb["mask_quadT"].ap().unsqueeze(1).broadcast_to([64, 4, 128]),
        op=ALU.mult),
        reads=[bQT, bf("mask_quadT")], writes=[bq, bQT], tag="uqe")

    def fwd(e):
        inst = None
        for h in range(2):
            lo = lrc(h)
            inst = e.matmul(WK[0:64, h * 64:(h + 1) * 64],
                            sb["lr"][:, lo:lo + 64], sb["bk"][:, lo:lo + 64],
                            start=True, stop=True)
        return inst
    sch.op("tensor", fwd, reads=[blr, bbk, bR1], writes=[bR1], tag="uf")
    sch.op("vector", lambda e: e.tensor_tensor(
        m1.ap().rearrange("p (a b) -> p a b", a=2),
        WK[0:64, 0:128].rearrange("p (a b) -> p a b", a=2),
        sb["mask_sl"].ap().unsqueeze(1).broadcast_to([64, 2, 64]), op=ALU.mult),
        reads=[bR1, bf("mask_sl")], writes=[bm1, bR1], tag="ufe")

    def tps(e):
        e.transpose(QTB[0:64, 0:64], sb["lr"][:, lrc(0):lrc(0) + 64],
                    sb["ident_bf"][0:64, 0:64])
        e.transpose(QTB[0:64, 64:128], sb["lr"][:, lrc(1):lrc(1) + 64],
                    sb["ident_bf"][0:64, 0:64])
        e.transpose(QTB[0:64, 128:256], sb["vst"][:, vco:vco + 64],
                    sb["ident_bf"][:, :])
        e.transpose(QTB[0:64, 256:320], sb["bk"][:, lrc(0):lrc(0) + 64],
                    sb["ident_bf"][0:64, 0:64])
        e.transpose(QTB[0:64, 320:384], sb["bk"][:, lrc(1):lrc(1) + 64],
                    sb["ident_bf"][0:64, 0:64])
        e.transpose(QTB[0:64, 384:448], sb["bk"][:, lrc(0) + 64:lrc(0) + 128],
                    sb["ident_bf"][0:64, 0:64])
        return e.transpose(QTB[0:64, 448:512], sb["bk"][:, lrc(1) + 64:lrc(1) + 128],
                           sb["ident_bf"][0:64, 0:64])
    sch.op("tensor", tps, reads=[blr, bbk, bf(f"vst{c}"), bf("ident_bf"), bQT],
           writes=[bQT], tag="utp")
    sch.op("scalar", lambda e: e.copy(tm[:, :], QTB[0:64, 0:512]),
           reads=[bQT], writes=[btm, bQT], tag="utme")

    def nv(e):
        e.matmul(WK[0:64, 0:64], qts[:, 128:192], tm[:, 128:192], start=True, stop=True)
        return e.matmul(WK[0:64, 64:128], qts[:, 384:448], tm[:, 192:256],
                        start=True, stop=True)
    sch.op("tensor", nv, reads=[bq, btm, bR1], writes=[bR1], tag="unv")

    def x0a(e):
        e.tensor_copy(x0.ap().rearrange("p (h q) -> p h q", h=2)[:, :, 0:64],
                      tm.ap()[:, 0:128].rearrange("p (h q) -> p h q", h=2))
        return e.tensor_copy(x0.ap().rearrange("p (h q) -> p h q", h=2)[:, :, 64:128],
                             WK[0:64, 0:128].rearrange("p (h q) -> p h q", h=2))
    sch.op("vector", x0a, reads=[btm, bR1, bx[0]], writes=[bx[0], bR1], tag="ux0")

    def p1(h):
        return qts[:, h * 256:h * 256 + 64]
    for lvl in range(5):
        def sq_(e, lvl=lvl):
            inst = None
            for h in range(2):
                if lvl == 0:
                    mp, pp = m1[:, h * 64:(h + 1) * 64], p1(h)
                else:
                    b0 = (lvl - 1) * 256
                    mp = sqp[:, b0 + h * 64:b0 + (h + 1) * 64]
                    pp = sqp[:, b0 + 128 + h * 64:b0 + 128 + (h + 1) * 64]
                e.matmul(WK[0:64, h * 64:(h + 1) * 64], pp, mp, start=True, stop=True)
                inst = e.matmul(WK[0:64, 128 + h * 64:128 + (h + 1) * 64], mp, pp,
                                start=True, stop=True)
            return inst
        sch.op("tensor", sq_, reads=[bm1, bq, bsq, bR1], writes=[bR1], tag=f"us{lvl}")
        sch.op("scalar", lambda e, lvl=lvl: e.copy(
            sqp[:, lvl * 256:(lvl + 1) * 256], WK[0:64, 0:256]),
            reads=[bR1, bsq], writes=[bsq, bR1], tag=f"use{lvl}")

    xf = sb[f"xf{par}"]
    bxf = bf(f"xf{par}")
    a_, b_ = 0, 1
    for lvl in range(6):
        def ap_(e, lvl=lvl, a_=a_):
            inst = None
            xa = sb[f"x{a_}{par}"]
            for h in range(2):
                if lvl == 0:
                    pp = p1(h)
                else:
                    b0 = (lvl - 1) * 256
                    pp = sqp[:, b0 + 128 + h * 64:b0 + 128 + (h + 1) * 64]
                inst = e.matmul(WK[0:64, h * 128:(h + 1) * 128], pp,
                                xa[:, h * 128:(h + 1) * 128], start=True, stop=True)
            return inst
        sch.op("tensor", ap_, reads=[bq, bsq, bx[a_], bR1], writes=[bR1], tag=f"ua{lvl}")
        if lvl == 0:
            sch.op("vector", lambda e: e.tensor_add(
                xf[:, :], WK[0:64, 0:256], sb[f"x0{par}"][:, :]),
                reads=[bR1, bx[0], bxf], writes=[bxf, bR1], tag=f"uae{lvl}")
        else:
            sch.op("vector", lambda e: e.tensor_add(
                xf[:, :], WK[0:64, 0:256], xf[:, :]),
                reads=[bR1, bxf], writes=[bxf, bR1], tag=f"uae{lvl}")
        if lvl < 5:
            sch.op("scalar", lambda e, b_=b_: e.copy(sb[f"x{b_}{par}"][:, :], xf[:, :]),
                   reads=[bxf, bx[b_]], writes=[bx[b_]], tag=f"uac{lvl}")
        a_, b_ = b_, a_
    sch.op("scalar", lambda e: e.copy(sb[f"x{a_}{par}"][:, :], xf[:, :]),
           reads=[bxf, bx[a_]], writes=[bx[a_]], tag="uacf")
    xa = sb[f"x{a_}{par}"]
    bxa = bx[a_]

    def strop(e):
        e.tensor_scalar(stR[:, 0:64], St[:, stc:stc + 64],
                        sb["eb"][:, ebb[0] + cc:ebb[0] + cc + 1], None, op0=ALU.mult)
        return e.tensor_scalar(stR[:, 64:128], St[:, stc + 64:stc + 128],
                               sb["eb"][:, ebb[1] + cc:ebb[1] + cc + 1], None,
                               op0=ALU.mult)
    sch.op("vector", strop, reads=[bSt, beb, bstR], writes=[bstR], tag="ustr")

    def wtp(e):
        e.transpose(WKB[0:64, 768:832], xa[:, 0:64], sb["ident_bf"][0:64, 0:64])
        return e.transpose(WKB[0:64, 832:896], xa[:, 128:192], sb["ident_bf"][0:64, 0:64])
    sch.op("tensor", wtp, reads=[bxa, bf("ident_bf"), bR3], writes=[bR3], tag="uwt")
    sch.op("scalar", lambda e: e.copy(wfm[:, :], WKB[0:64, 768:896]),
           reads=[bR3, bwfm], writes=[bwfm, bR3], tag="uwe")

    def smm(e):
        e.matmul(WK[0:64, 256:320], wfm[:, 0:64], stR[:, 0:64], start=True, stop=True)
        return e.matmul(WK[0:64, 320:384], wfm[:, 64:128], stR[:, 64:128],
                        start=True, stop=True)
    sch.op("tensor", smm, reads=[bwfm, bstR, bR2], writes=[bR2], tag="usm")
    sch.op("vector", lambda e: e.tensor_add(
        sgm.ap().rearrange("p (h q) -> p h q", h=2),
        WK[0:64, 256:384].rearrange("p (h q) -> p h q", h=2),
        xf.ap().rearrange("p (h q) -> p h q", h=2)[:, :, 64:128]),
        reads=[bR2, bxf, bsgm], writes=[bsgm, bR2], tag="usge")
    sgmb = sb[f"sgmb{par}"]
    bsgmb = bf(f"sgmb{par}")
    sch.op("scalar", lambda e: e.copy(sgmb[:, :], sgm[:, :]),
           reads=[bsgm, bsgmb], writes=[bsgmb], tag="usgc")

    def stm(e):
        btm_, vtm_, ktm_ = tm[:, 256:384], tm[:, 128:256], tm[:, 384:512]
        e.matmul(WK[0:64, 384:448], btm_[:, 0:64], sgmb[:, 0:64], start=True, stop=False)
        e.matmul(WK[0:64, 384:448], ktm_[:, 0:64], vtm_[:, 0:64], start=False, stop=True)
        e.matmul(WK[0:64, 448:512], btm_[:, 64:128], sgmb[:, 64:128],
                 start=True, stop=False)
        return e.matmul(WK[0:64, 448:512], ktm_[:, 64:128], vtm_[:, 64:128],
                        start=False, stop=True)
    sch.op("tensor", stm, reads=[btm, bsgmb, bR3], writes=[bR3], tag="ust")

    def stup(e):
        e.tensor_scalar(stt[:, 0:64], WK[0:64, 384:448],
                        sb["eb"][:, ebb[0] + NCC + cc:ebb[0] + NCC + cc + 1], None,
                        op0=ALU.mult)
        return e.tensor_scalar(stt[:, 64:128], WK[0:64, 448:512],
                               sb["eb"][:, ebb[1] + NCC + cc:ebb[1] + NCC + cc + 1],
                               None, op0=ALU.mult)
    sch.op("vector", stup, reads=[bR3, beb, bstt], writes=[bstt, bR3], tag="usu1")

    def stup2(e):
        e.scalar_tensor_tensor(
            St[:, stc:stc + 64], St[:, stc:stc + 64],
            sb["eb"][:, ebb[0] + 2 * NCC + cc:ebb[0] + 2 * NCC + cc + 1],
            stt[:, 0:64], op0=ALU.mult, op1=ALU.add)
        return e.scalar_tensor_tensor(
            St[:, stc + 64:stc + 128], St[:, stc + 64:stc + 128],
            sb["eb"][:, ebb[1] + 2 * NCC + cc:ebb[1] + 2 * NCC + cc + 1],
            stt[:, 64:128], op0=ALU.mult, op1=ALU.add)
    sch.op("vector", stup2, reads=[bstt, beb, bSt], writes=[bSt], tag="usu2")

    def omm(e):
        e.matmul(O_[0:64, oc0:oc0 + 64], stR[:, 0:64],
                 sb["lr"][:, lrc(0) + 64:lrc(0) + 128],
                 start=True, stop=False)
        e.matmul(O_[0:64, oc0:oc0 + 64], sgmb[:, 0:64], qts[:, 64:128],
                 start=False, stop=False)
        e.matmul(O_[0:64, oc0:oc0 + 64], tm[:, 128:192], qts[:, 192:256],
                 start=False, stop=True)
        e.matmul(O_[64:128, oc0:oc0 + 64], stR[:, 64:128],
                 sb["lr"][:, lrc(1) + 64:lrc(1) + 128],
                 start=True, stop=False, tile_position=(0, 64))
        e.matmul(O_[64:128, oc0:oc0 + 64], sgmb[:, 64:128], qts[:, 320:384],
                 start=False, stop=False, tile_position=(0, 64))
        return e.matmul(O_[64:128, oc0:oc0 + 64], tm[:, 192:256], qts[:, 448:512],
                        start=False, stop=True, tile_position=(0, 64))
    sch.op("tensor", omm, reads=[bstR, bsgmb, btm, bq, blr, bO], writes=[bO], tag="uo")
    if oq == 3:
        OB = [bf(f"PS5o{j}") for j in range(4)]
        sch.op("scalar", lambda e: e.copy(
            sb["ost"].ap().rearrange("p (k u) -> p k u", k=NCH)
            [:, c - 3:c + 1, cc * LC:(cc + 1) * LC],
            O_[:, 256:512].rearrange("p (k u) -> p k u", k=4)),
            reads=OB, writes=[bf(f"ost{cj}") for cj in range(c - 3, c + 1)] + OB,
            tag="uoe")



def post_scan(sch, sb, PS, bf, dI, cfg, t, y_dram):
    TA, NCH = cfg.TA, cfg.NCH
    t0 = t * TA
    g_ = sb["gnstat"]
    bg = bf("gnstat")
    for c in range(NCH):
        co = c * TA
        pq = (c % 2) * TA
        bsq_, bkr = bf(f"sq{c % 2}"), bf(f"kkr{c % 2}")
        pk = (c % 2) * 2 * TA
        sch.op("scalar", lambda e, co=co, pq=pq: e.activation(
            sb["sq"][:, pq:pq + TA], sb["ost"][:, co:co + TA], AF.Square),
            reads=[bf(f"ost{c}"), bsq_], writes=[bsq_], tag="g0")
        sch.op("tensor", lambda e, c=c, co=co, pq=pq: [
            e.matmul(PS[5][0:16, 0:TA], sb["sel_gn"][:, 16 * c:16 * (c + 1)],
                     sb["ost"][:, co:co + TA], start=(c == 0), stop=(c == NCH - 1)),
            e.matmul(PS[6][0:16, 0:TA], sb["sel_gn"][:, 16 * c:16 * (c + 1)],
                     sb["sq"][:, pq:pq + TA], start=(c == 0), stop=(c == NCH - 1))][-1],
            reads=[bf(f"ost{c}"), bsq_, bf("sel_gn"), bf("PS5s"), bf("QT0")],
            writes=[bf("PS5s"), bf("QT0")], tag="g1")
        ho = 2 * c * TA
        for h in range(2):
            S = 2 * c + h
            sch.op("vector", lambda e, ho=ho, h=h, S=S, pk=pk, c=c:
                e.scalar_tensor_tensor(
                    sb["kkr"][:, pk + h * TA:pk + (h + 1) * TA],
                    sb["rst"][:, ho + h * TA:ho + (h + 1) * TA], sb["rkh"][:, S:S + 1],
                    sb["kst"][:, ho + h * TA:ho + (h + 1) * TA],
                    op0=ALU.mult, op1=ALU.mult),
                reads=[bf(f"rst{c}"), bf(f"kst{c}"), bf("rkh"), bkr], writes=[bkr],
                tag="g2")
        def g3(e, c=c, pk=pk):
            inst = None
            for h in range(2):
                S = 2 * c + h
                inst = e.matmul(PS[3][0:16, 0:TA], sb["bsel_hm"][:, 16 * S:16 * (S + 1)],
                                sb["kkr"][:, pk + h * TA:pk + (h + 1) * TA],
                                start=(S == 0), stop=(S == 15))
            return inst
        sch.op("tensor", g3, reads=[bkr, bf("bsel_hm"), bf("PS3q00")],
               writes=[bf("PS3q00")], tag="g3")
    sch.op("vector", lambda e: e.tensor_scalar_mul(g_[0:16, 0:TA], PS[5][0:16, 0:TA],
                                                   1.0 / 64.0),
           reads=[bf("PS5s"), bg], writes=[bg], tag="g4")
    sch.op("vector", lambda e: e.tensor_scalar_mul(g_[0:16, TA:2 * TA],
                                                   PS[6][0:16, 0:TA], 1.0 / 64.0),
           reads=[bf("QT0")], writes=[bg, bf("QT0")], tag="g5")
    sch.op("vector", lambda e: e.scalar_tensor_tensor(
        g_[0:16, 2 * TA:3 * TA], g_[0:16, 0:TA], -1.0, g_[0:16, 0:TA],
        op0=ALU.mult, op1=ALU.mult), reads=[bg], writes=[bg], tag="g6")
    sch.op("vector", lambda e: e.tensor_tensor(
        g_[0:16, TA:2 * TA], g_[0:16, TA:2 * TA], g_[0:16, 2 * TA:3 * TA], op=ALU.add),
        reads=[bg], writes=[bg], tag="g7")
    sch.op("vector", lambda e: e.tensor_scalar_add(g_[0:16, TA:2 * TA],
                                                   g_[0:16, TA:2 * TA], EPS),
           reads=[bg], writes=[bg], tag="g8")
    sch.op("vector", lambda e: e.reciprocal(g_[0:16, 2 * TA:3 * TA], g_[0:16, TA:2 * TA]),
           reads=[bg], writes=[bg], tag="g9")
    sch.op("scalar", lambda e: e.activation(g_[0:16, TA:2 * TA], g_[0:16, 2 * TA:3 * TA],
                                            AF.Sqrt),
           reads=[bg], writes=[bg], tag="gA")
    sch.op("vector", lambda e: e.scalar_tensor_tensor(
        g_[0:16, 2 * TA:3 * TA], g_[0:16, 0:TA], -1.0, g_[0:16, TA:2 * TA],
        op0=ALU.mult, op1=ALU.mult), reads=[bg], writes=[bg], tag="gB")
    sch.op("vector", lambda e: e.tensor_copy(g_[0:16, 0:TA], PS[3][0:16, 0:TA]),
           reads=[bf("PS3q00")], writes=[bg, bf("PS3q00")], tag="gC")
    P3ALL = [bf("PS3q00"), bf("PS3q01"), bf("PS3q10"), bf("PS3q11")]
    for c in range(NCH):
        co = c * TA
        pq = (c % 2) * TA
        bsq_, bsq2 = bf(f"sq{c % 2}"), bf(f"sq2{c % 2}")
        sch.op("tensor", lambda e, c=c: [
            e.matmul(PS[3][:, 0:TA], sb["sel_gnb"][0:16, 128 * c:128 * (c + 1)],
                     g_[0:16, TA:2 * TA], start=True, stop=True),
            e.matmul(PS[3][:, 256:256 + TA], sb["sel_gnb"][0:16, 128 * c:128 * (c + 1)],
                     g_[0:16, 2 * TA:3 * TA], start=True, stop=True)][-1],
            reads=[bg, bf("sel_gnb")] + P3ALL, writes=P3ALL, tag="gD")
        sch.op("tensor", lambda e, c=c: e.matmul(
            PS[5][:, 0:TA], sb["sel_gnb"][0:16, 128 * c:128 * (c + 1)], g_[0:16, 0:TA],
            start=True, stop=True),
            reads=[bg, bf("sel_gnb"), bf("PS5s")], writes=[bf("PS5s")], tag="gE")
        sch.op("vector", lambda e, co=co, c=c, pq=pq: e.scalar_tensor_tensor(
            sb["sq2"][:, pq:pq + TA], sb["ost"][:, co:co + TA], sb["gn_w"][:, c:c + 1],
            PS[3][:, 0:TA], op0=ALU.mult, op1=ALU.mult),
            reads=[bf(f"ost{c}"), bf("gn_w"), bf("PS3q00"), bf("PS3q10"), bsq2],
            writes=[bsq2], tag="gF")
        sch.op("vector", lambda e, c=c, pq=pq: e.tensor_scalar(
            sb["sq"][:, pq:pq + TA], PS[3][:, 256:256 + TA], sb["gn_w"][:, c:c + 1],
            sb["gn_b"][:, c:c + 1], op0=ALU.mult, op1=ALU.add),
            reads=[bf("PS3q01"), bf("PS3q11"), bf("gn_w"), bf("gn_b"), bsq_],
            writes=[bsq_], tag="gG")
        sch.op("vector", lambda e, pq=pq: e.tensor_add(
            sb["sq2"][:, pq:pq + TA], sb["sq2"][:, pq:pq + TA], sb["sq"][:, pq:pq + TA]),
            reads=[bsq_, bsq2], writes=[bsq2, bsq_], tag="gH")
        sch.op("vector", lambda e, co=co, pq=pq: e.tensor_mul(
            sb["sq"][:, pq:pq + TA], sb["vst"][:, co:co + TA], PS[5][:, 0:TA]),
            reads=[bf(f"vst{c}"), bf("PS5s"), bsq_], writes=[bsq_, bf("PS5s")], tag="gI")
        sch.op("vector", lambda e, pq=pq: e.tensor_add(
            sb["sq2"][:, pq:pq + TA], sb["sq2"][:, pq:pq + TA], sb["sq"][:, pq:pq + TA]),
            reads=[bsq_, bsq2], writes=[bsq2, bsq_], tag="gJ")
        sch.op("vector", lambda e, co=co, c=c, pq=pq: e.tensor_mul(
            sb["mixA"][:, co:co + TA], sb["sq2"][:, pq:pq + TA],
            sb["gst"][:, co:co + TA]),
            reads=[bsq2, bf(f"gst{c}"), bf(f"mixA{c}")],
            writes=[bf(f"mixA{c}"), bsq2], tag="gK")
    for f in range(NCH):
        bank, bb = (PS[0], bf("PS0")) if f % 2 == 0 else (PS[1], bf("PS1"))
        won = f"Wb{f % 2}"
        bwo = bf(won)
        sch.dma("gld", lambda e, f=f, won=won: e.dma_start(
            out=sb[won].ap().rearrange("p (k c) -> p k c", k=8),
            in_=dI["W_o"][:, f * 128:(f + 1) * 128].rearrange("(k p) c -> p k c", p=128)),
            writes=[bwo], tag="ldwo")
        sch.op("tensor", lambda e, f=f, bank=bank, won=won: [
            e.matmul(bank[:, 0:TA], sb[won][:, k * 128:(k + 1) * 128],
                     sb["mixA"][:, k * TA:(k + 1) * TA], start=(k == 0), stop=(k == 7))
            for k in range(8)][-1],
            reads=[bwo, bb] + [bf(f"mixA{k}") for k in range(NCH)],
            writes=[bb], tag="wo")
        sch.op("vector", lambda e, f=f, bank=bank: e.tensor_add(
            sb["ytile"][:, f * TA:(f + 1) * TA], bank[:, 0:TA],
            sb["xfm"][:, f * TA:(f + 1) * TA]),
            reads=[bb, bf(f"xfm{f}"), bf(f"ytile{f}")],
            writes=[bf(f"ytile{f}"), bb], tag="yp")
    emit_ln(sch, sb, PS, bf, cfg, "ytile", "ytile", TA, "sq")
    sch.op("vector", lambda e: e.reduce_max(
        sb["sm"][:, 6:7], sb["ytile"][:, 0:NCH * TA], axis=AX.X,
        apply_absolute_value=True),
        reads=[bf(f"ytile{k}") for k in range(NCH)], writes=[bf("ymx")], tag="ym1")
    sch.op("vector", lambda e: e.tensor_max(
        sb["ymax"][:, 0:1], sb["ymax"][:, 0:1], sb["sm"][:, 6:7]),
        reads=[bf("ymx"), bf("ymax")], writes=[bf("ymax")], tag="ym2")
    for c in range(NCH):
        sch.dma("st", lambda e, c=c, t0=t0: e.dma_start(
            out=y_dram[c, :, t0:t0 + TA], in_=sb["ytile"][:, c * TA:(c + 1) * TA]),
            reads=[bf(f"ytile{c}")], tag="sty")



def emit_ln(sch, sb, PS, bf, cfg, src_n, dst_n, TT, scratch_n):
    NCH = cfg.NCH
    src, dst, scr = sb[src_n], sb[dst_n], sb[scratch_n]
    ln = sb["lnrow"]
    bln = bf("lnrow")
    src_f32 = src.dtype == F32
    ones1 = sb["ones_colf"] if src_f32 else sb["ones_col"]
    bones = bf("ones_colf") if src_f32 else bf("ones_col")
    for c in range(NCH):
        co = c * TT
        po = (c % 2) * TT
        bs, bscr = bf(f"{src_n}{c}"), bf(f"{scratch_n}{c % 2}")
        sch.op("scalar", lambda e, co=co, po=po: e.activation(
            scr[:, po:po + TT], src[:, co:co + TT], AF.Square),
            reads=[bs, bscr], writes=[bscr], tag="l0")
        sch.op("tensor", lambda e, c=c, co=co, po=po, ones1=ones1: [
            e.matmul(PS[6][0:1, 0:TT], ones1[:, :], src[:, co:co + TT],
                     start=(c == 0), stop=(c == NCH - 1)),
            e.matmul(PS[7][0:1, 0:TT], sb["ones_col"][:, :], scr[:, po:po + TT],
                     start=(c == 0), stop=(c == NCH - 1))][-1],
            reads=[bs, bscr, bones, bf("ones_col"), bf("QT0"), bf("QT1")],
            writes=[bf("QT0"), bf("QT1")], tag="l1")
    sch.op("vector", lambda e: e.tensor_scalar_mul(ln[0:1, 0:TT], PS[6][0:1, 0:TT],
                                                   1.0 / 1024.0),
           reads=[bf("QT0"), bln], writes=[bln, bf("QT0")], tag="l2")
    sch.op("vector", lambda e: e.tensor_scalar_mul(ln[0:1, TT:2 * TT], PS[7][0:1, 0:TT],
                                                   1.0 / 1024.0),
           reads=[bf("QT1")], writes=[bln, bf("QT1")], tag="l3")
    sch.op("vector", lambda e: e.scalar_tensor_tensor(
        ln[0:1, 2 * TT:3 * TT], ln[0:1, 0:TT], -1.0, ln[0:1, 0:TT],
        op0=ALU.mult, op1=ALU.mult), reads=[bln], writes=[bln], tag="l4")
    sch.op("vector", lambda e: e.tensor_tensor(
        ln[0:1, TT:2 * TT], ln[0:1, TT:2 * TT], ln[0:1, 2 * TT:3 * TT], op=ALU.add),
        reads=[bln], writes=[bln], tag="l5")
    sch.op("vector", lambda e: e.tensor_scalar_add(ln[0:1, TT:2 * TT], ln[0:1, TT:2 * TT],
                                                   EPS),
           reads=[bln], writes=[bln], tag="l6")
    sch.op("vector", lambda e: e.reciprocal(ln[0:1, 2 * TT:3 * TT], ln[0:1, TT:2 * TT]),
           reads=[bln], writes=[bln], tag="l7")
    sch.op("scalar", lambda e: e.activation(ln[0:1, TT:2 * TT], ln[0:1, 2 * TT:3 * TT],
                                            AF.Sqrt),
           reads=[bln], writes=[bln], tag="l8")
    sch.op("vector", lambda e: e.scalar_tensor_tensor(
        ln[0:1, 2 * TT:3 * TT], ln[0:1, 0:TT], -1.0, ln[0:1, TT:2 * TT],
        op0=ALU.mult, op1=ALU.mult), reads=[bln], writes=[bln], tag="l9")
    sch.op("tensor", lambda e: e.matmul(
        PS[6][:, 0:TT], sb["ones_row"][0:1, :], ln[0:1, TT:2 * TT],
        start=True, stop=True),
        reads=[bln, bf("ones_row"), bf("QT0")], writes=[bf("QT0")], tag="lA")
    sch.op("tensor", lambda e: e.matmul(
        PS[7][:, 0:TT], sb["ones_row"][0:1, :], ln[0:1, 2 * TT:3 * TT],
        start=True, stop=True),
        reads=[bln, bf("ones_row"), bf("QT1")], writes=[bf("QT1")], tag="lB")
    for c in range(NCH):
        co = c * TT
        po = (c % 2) * TT
        bs, bscr = bf(f"{src_n}{c}"), bf(f"{scratch_n}{c % 2}")
        bd = bf(f"{dst_n}{c}")
        sch.op("vector", lambda e, co=co, po=po: e.tensor_mul(
            scr[:, po:po + TT], src[:, co:co + TT], PS[6][:, 0:TT]),
            reads=[bs, bf("QT0"), bscr], writes=[bscr], tag="lC")
        sch.op("vector", lambda e, co=co, po=po: e.tensor_add(
            dst[:, co:co + TT], scr[:, po:po + TT], PS[7][:, 0:TT]),
            reads=[bscr, bf("QT1"), bd], writes=[bd], tag="lD")
    sch.op("vector", lambda e: e.nop(), reads=[bf("QT0"), bf("QT1")],
           writes=[bf("QT0"), bf("QT1")], tag="lE")



def finalize_scale(sch, sb, PS, bf, dI, cfg, maxn, ccin, ccout, clipb, col):
    P3ALL = [bf("PS3q00"), bf("PS3q01"), bf("PS3q10"), bf("PS3q11")]
    sch.op("tensor", lambda e: e.transpose(PS[3][0:1, 0:128], sb[maxn][:, 0:1],
                                           sb["ident_f32"][:, :]),
           reads=[bf(maxn), bf("ident_f32")] + P3ALL, writes=P3ALL, tag="f0")
    sch.op("vector", lambda e: e.reduce_max(sb["sm"][0:1, 6:7], PS[3][0:1, 0:128],
                                            axis=AX.X),
           reads=P3ALL, writes=[bf("gm1")] + P3ALL, tag="f1")
    sch.op("vector", lambda e: e.tensor_copy(
        sb["sm"][0:1, 8:16], sb["sm"][0:1, 6:7].broadcast_to([1, 8])),
        reads=[bf("gm1")], writes=[bf("cc8")], tag="f2")
    sch.dma("st", lambda e: e.dma_start(out=ccin[0:1, 0:8], in_=sb["sm"][0:1, 8:16]),
            reads=[bf("cc8")], writes=[bf("ccd_" + maxn)], tag="f3")
    sch.op("gpsimd", lambda e: e.collective_compute(
        "AllReduce", ALU.max, replica_groups=[list(range(cfg.n_cores))],
        ins=[ccin.opt()], outs=[ccout.opt()]),
        reads=[bf("ccd_" + maxn)], writes=[bf("cco_" + maxn)], tag="f4",
        async_done=True)
    sch.dma("gld", lambda e: e.dma_start(out=sb["sm"][0:1, 8:16], in_=ccout[0:1, 0:8]),
            reads=[bf("cco_" + maxn)], writes=[bf("cc8")], tag="f5")
    sch.op("scalar", lambda e: e.activation(sb["sm"][0:1, 4:5], sb["sm"][0:1,
           (1 if clipb == "clipx" else 2):(2 if clipb == "clipx" else 3)], AF.Sigmoid),
           reads=[bf(clipb)], writes=[bf("sig_" + maxn)], tag="f6")
    sch.op("vector", lambda e: e.tensor_mul(sb["sm"][0:1, 3:4], sb["sm"][0:1, 8:9],
                                            sb["sm"][0:1, 4:5]),
           reads=[bf("cc8"), bf("sig_" + maxn), bf("gmx")], writes=[bf("gmx")], tag="f7")
    sch.op("vector", lambda e: e.tensor_scalar_mul(sb["sm"][0:1, 3:4], sb["sm"][0:1, 3:4],
                                                   1.0 / QMAX),
           reads=[bf("gmx")], writes=[bf("gmx")], tag="f8")
    sch.op("vector", lambda e: e.reciprocal(sb["sm"][0:1, 4:5], sb["sm"][0:1, 3:4]),
           reads=[bf("gmx"), bf("sig_" + maxn)], writes=[bf("sig_" + maxn)], tag="f9")
    sch.op("tensor", lambda e: e.matmul(
        PS[3][:, 0:2], sb["ones_row"][0:1, :], sb["sm"][0:1, 3:5],
        start=True, stop=True),
        reads=[bf("gmx"), bf("sig_" + maxn), bf("ones_row")] + P3ALL,
        writes=P3ALL, tag="fA")
    sch.op("vector", lambda e, col=col: e.tensor_copy(
        sb["scal"][:, col:col + 2], PS[3][:, 0:2]),
        reads=P3ALL + [bf("scal")], writes=[bf("scal")] + P3ALL, tag="fB")



_PROGRAM_CACHE = {}
_LAST_SCHED = None


def get_program(cfg):
    key = (cfg.T, cfg.TA, cfg.TB, cfg.n_cores)
    if key not in _PROGRAM_CACHE:
        _PROGRAM_CACHE[key] = build_program(cfg)
    return _PROGRAM_CACHE[key]


def make_in_maps(cfg, inputs, consts):
    x = np.ascontiguousarray(inputs["L0_attn_hidden_states"])
    n = cfg.n_cores
    maps = []
    common = {
        "W_r": inputs["L0_attn_W_r"], "W_k": inputs["L0_attn_W_k"],
        "W_v": inputs["L0_attn_W_v"], "W_o": inputs["L0_attn_W_o"],
        "w_la": inputs["L0_attn_w_lora_a"], "w_lb": inputs["L0_attn_w_lora_b"],
        "a_la": inputs["L0_attn_a_lora_a"], "a_lb": inputs["L0_attn_a_lora_b"],
        "g_la": inputs["L0_attn_g_lora_a"], "g_lb": inputs["L0_attn_g_lora_b"],
        "wg": inputs["L0_ffn_w_gate"], "wu": inputs["L0_ffn_w_up"],
        "wd": inputs["L0_ffn_w_down"], "r_k": inputs["L0_attn_r_k"],
        "x_r": inputs["L0_attn_x_r"], "x_w": inputs["L0_attn_x_w"],
        "x_k": inputs["L0_attn_x_k"], "x_v": inputs["L0_attn_x_v"],
        "x_a": inputs["L0_attn_x_a"], "k_k": inputs["L0_attn_k_k"],
        "k_a": inputs["L0_attn_k_a"], "w_lbias": inputs["L0_attn_w_lora_bias"],
        "a_lbias": inputs["L0_attn_a_lora_bias"], "g_lbias": inputs["L0_attn_g_lora_bias"],
        "gn_w": inputs["L0_attn_gn_weight"], "gn_b": inputs["L0_attn_gn_bias"],
        "clip_x": inputs["L0_ffn_clip_x"], "clip_down": inputs["L0_ffn_clip_down"],
    }
    common = {k: np.ascontiguousarray(np.asarray(v, dtype=np.float32))
              for k, v in common.items()}
    for i in range(n):
        m = dict(common)
        m["x"] = np.ascontiguousarray(x[i].astype(np.float32))
        for name, _, _ in CONST_SPECS:
            m[name] = consts[name]
        maps.append(m)
    return maps


def kernel(**inputs):
    cfg = Cfg()
    consts = host_constants()
    nc = get_program(cfg)
    in_maps = make_in_maps(cfg, inputs, consts)
    res = run_bass_kernel_spmd(nc, in_maps, core_ids=list(range(cfg.n_cores)))
    out = np.stack([r["z"] for r in res.results], axis=0)
    return out.astype(np.float32)


# version 07
# speedup vs baseline: 1.5404x; 1.3934x over previous
import numpy as np
import ml_dtypes
from contextlib import ExitStack

import concourse.bass as bass
import concourse.mybir as mybir
from concourse.bass_utils import run_bass_kernel_spmd

F32 = mybir.dt.float32
BF16 = mybir.dt.bfloat16
AF = mybir.ActivationFunctionType
ALU = mybir.AluOpType
AX = mybir.AxisListType

MAGIC = float(np.float32(1.5 * 2 ** 23))
QMAX = 127.0
EPS = 1e-5


ENGINES = ("tensor", "vector", "scalar", "gpsimd", "sync")


class Tok:
    __slots__ = ("eng", "pos", "sem", "val", "clock", "force_wait")

    def __init__(self, eng, pos, sem, val, clock, force_wait=False):
        self.eng, self.pos, self.sem, self.val, self.clock = eng, pos, sem, val, clock
        self.force_wait = force_wait


class Buf:
    __slots__ = ("name", "writer", "readers")

    def __init__(self, name):
        self.name, self.writer, self.readers = name, None, []


class Sched:
    def __init__(self, nc, ctx):
        self.nc = nc
        self.ctx = ctx
        self.items = {e: [] for e in ENGINES}
        self.pos = {e: 0 for e in ENGINES}
        self.sem = {e: ctx.enter_context(nc.semaphore(f"sem_{e}")) for e in ENGINES}
        self.semval = {e: 0 for e in ENGINES}
        self.synced = {e: {} for e in ENGINES}
        self.drainmark = {e: -1 for e in ENGINES}
        self.dmaq = {}
        self.bufs = {}
        self.banklast = {}

    @staticmethod
    def bank_of(name):
        if name.startswith("PS3"):
            return 3
        if name.startswith("PS5"):
            return 5
        if name.startswith("WK0"):
            return 4
        if name.startswith("WK1"):
            return 2
        if name in ("PS0", "PS1"):
            return int(name[2])
        if name == "QT0":
            return 6
        if name == "QT1":
            return 7
        return None

    def buf(self, name):
        if name not in self.bufs:
            self.bufs[name] = Buf(name)
        return self.bufs[name]

    DMA_ROT = 8

    def add_dma_queue(self, name, eng):
        sems = [self.ctx.enter_context(self.nc.semaphore(f"dma_{name}{i}"))
                for i in range(self.DMA_ROT)]
        self.dmaq[name] = [eng, sems, [0] * self.DMA_ROT, 0]
        for i in range(self.DMA_ROT):
            self.pos[f"{name}#{i}"] = 0

    def _wait(self, eng, tok):
        syn = self.synced[eng]
        if tok.eng == eng and not tok.force_wait:
            if eng == "tensor":
                return False
            if tok.pos <= self.drainmark[eng] or tok.pos <= syn.get(eng, -1):
                return False
            return True
        if tok.eng == eng and tok.force_wait:
            if tok.pos <= syn.get(eng, -1):
                return False
            self.items[eng].append(("wait", (tok.sem, tok.val)))
            syn[eng] = max(syn.get(eng, -1), tok.pos)
            return False
        if tok.pos <= syn.get(tok.eng, -1):
            return False
        assert tok.sem is not None, f"{eng} depends on un-inc'd op on {tok.eng}"
        self.items[eng].append(("wait", (tok.sem, tok.val)))
        for k, v in tok.clock.items():
            if syn.get(k, -1) < v:
                syn[k] = v
        return False

    def _deps(self, reads, writes):
        deps = []
        for b in reads:
            if b.writer is not None:
                deps.append(b.writer)
        for b in writes:
            if b.writer is not None:
                deps.append(b.writer)
            deps.extend(b.readers)
        return deps

    def op(self, eng, fn, reads=(), writes=(), inc=True, tag="", async_done=False):
        banks = {self.bank_of(b.name) for b in list(reads) + list(writes)}
        banks.discard(None)
        drain = False
        for bk in banks:
            assert inc, "bank-touching ops must inc their sem"
            for e2, tok in self.banklast.get(bk, {}).items():
                if e2 != eng:
                    drain |= self._wait(eng, tok)
        for tok in self._deps(reads, writes):
            drain |= self._wait(eng, tok)
        if drain:
            self.items[eng].append(("drain", None))
            self.drainmark[eng] = self.pos[eng]
        self.pos[eng] += 1
        mypos = self.pos[eng]
        clock = dict(self.synced[eng])
        clock[eng] = mypos
        if inc:
            self.semval[eng] += 1
            tok = Tok(eng, mypos, self.sem[eng], self.semval[eng], clock,
                      force_wait=async_done)
        else:
            assert not async_done
            tok = Tok(eng, mypos, None, None, clock)
        self.items[eng].append(("op", (fn, self.sem[eng] if inc else None,
                                       self.semval[eng], tag)))
        for b in reads:
            b.readers.append(tok)
        for b in writes:
            b.writer = tok
            b.readers = []
        for bk in banks:
            self.banklast.setdefault(bk, {})[eng] = tok
        return tok

    def dma(self, q, fn, reads=(), writes=(), tag=""):
        assert not any(self.bank_of(b.name) is not None
                       for b in list(reads) + list(writes))
        eng, sems, counts, total = self.dmaq[q]
        idx = total % self.DMA_ROT
        self.dmaq[q][3] = total + 1
        qn = f"{q}#{idx}"
        drain = False
        for tok in self._deps(reads, writes):
            drain |= self._wait(eng, tok)
        if counts[idx] > 0 and self.synced[eng].get(qn, -1) < counts[idx] * 16:
            self.items[eng].append(("wait", (sems[idx], counts[idx] * 16)))
            self.synced[eng][qn] = counts[idx] * 16
        if drain:
            self.items[eng].append(("drain", None))
            self.drainmark[eng] = self.pos[eng]
        self.pos[eng] += 1
        counts[idx] += 1
        cnt = counts[idx] * 16
        self.pos[qn] = cnt
        clock = dict(self.synced[eng])
        clock[eng] = self.pos[eng]
        clock[qn] = cnt
        tok = Tok(qn, cnt, sems[idx], cnt, clock)
        self.items[eng].append(("dma", (fn, sems[idx], cnt, tag)))
        for b in reads:
            b.readers.append(tok)
        for b in writes:
            b.writer = tok
            b.readers = []
        return tok

    def barrier(self):
        toks = []
        for e in ENGINES:
            self.pos[e] += 1
            self.semval[e] += 1
            clock = dict(self.synced[e])
            clock[e] = self.pos[e]
            toks.append(Tok(e, self.pos[e], self.sem[e], self.semval[e], clock))
            self.items[e].append(("drain", None))
            self.items[e].append(("rawop", (lambda eng, s_=self.sem[e]: eng.sem_inc(s_, 1),
                                            "bar")))
        dmatoks = []
        for q, (eng, sems, counts, total) in self.dmaq.items():
            for i in range(self.DMA_ROT):
                if counts[i] > 0:
                    dmatoks.append((f"{q}#{i}", sems[i], counts[i] * 16))
        for e in ENGINES:
            syn = self.synced[e]
            for t in toks:
                if t.eng == e or t.pos <= syn.get(t.eng, -1):
                    continue
                self.items[e].append(("wait", (t.sem, t.val)))
                for k, v in t.clock.items():
                    if syn.get(k, -1) < v:
                        syn[k] = v
            for q, sem, val in dmatoks:
                if val > syn.get(q, -1):
                    self.items[e].append(("wait", (sem, val)))
                    syn[q] = val
            self.items[e].append(("drain", None))
            self.drainmark[e] = self.pos[e]
        for b in self.bufs.values():
            b.readers = []
            if b.writer is not None:
                b.writer = toks[ENGINES.index(b.writer.eng)] if b.writer.eng in ENGINES \
                    else None

    def finalize(self):
        nc = self.nc
        with nc.Block() as blk:
            def make(name):
                def body(eng):
                    for kind, payload in self.items[name]:
                        if kind == "wait":
                            eng.wait_ge(*payload)
                        elif kind == "drain":
                            eng.drain()
                        elif kind == "rawop":
                            payload[0](eng)
                        else:
                            fn, sem, val, tag = payload[0], payload[1], payload[2], payload[3]
                            inst = fn(eng)
                            if sem is not None:
                                assert inst is not None, f"op '{tag}' returned None"
                                inst.then_inc(sem, 16 if kind == "dma" else 1)
                return body
            blk.tensor(make("tensor"))
            blk.vector(make("vector"))
            blk.scalar(make("scalar"))
            blk.gpsimd(make("gpsimd"))
            blk.sync(make("sync"))



class OpCollector:

    def __init__(self, sch):
        self.sch = sch
        self.ops = []

    def buf(self, name):
        return self.sch.buf(name)

    def op(self, eng, fn, reads=(), writes=(), tag=""):
        self.ops.append((eng, fn, reads, writes, tag))


def zip_emit(sch, cols):
    n = max(len(c.ops) for c in cols)
    for i in range(n):
        for c in cols:
            if i < len(c.ops):
                eng, fn, reads, writes, tag = c.ops[i]
                sch.op(eng, fn, reads=reads, writes=writes, tag=tag)


def host_constants():
    c = {}
    c["ident_f32"] = np.eye(128, dtype=np.float32)
    c["ident_bf"] = np.eye(128, dtype=ml_dtypes.bfloat16)
    c["mask_sl"] = np.tril(np.ones((64, 64), np.float32), -1)
    su = np.triu(np.ones((64, 64), np.float32), 1)
    iu = np.triu(np.ones((64, 64), np.float32), 0)
    c["mask_quadT"] = np.concatenate([su, iu], axis=1)
    sel = np.zeros((128, 128), np.float32)
    for cc in range(8):
        for p in range(128):
            sel[p, 16 * cc + 2 * cc + (p // 64)] = 1.0
    c["sel_gn"] = sel.astype(ml_dtypes.bfloat16)
    selb = np.zeros((16, 1024), np.float32)
    for cc in range(8):
        for p in range(128):
            selb[2 * cc + (p // 64), 128 * cc + p] = 1.0
    c["sel_gnb"] = selb
    on = np.ones((128, 1), np.float32)
    c["ones_col"] = on.astype(ml_dtypes.bfloat16)
    c["ones_row"] = np.ones((1, 128), np.float32)
    c["ones_colf"] = np.ones((128, 1), np.float32)
    bs = np.zeros((64, 256), np.float32)
    for S in range(16):
        bs[:, 16 * S + S] = 1.0
    c["bsel_hm"] = bs.astype(ml_dtypes.bfloat16)
    hs = np.zeros((16, 1024), np.float32)
    for S in range(16):
        hs[S, 64 * S:64 * (S + 1)] = 1.0
    c["sel_hm"] = hs
    return c


CONST_SPECS = [("ident_f32", (128, 128), F32), ("ident_bf", (128, 128), BF16),
               ("mask_sl", (64, 64), F32), ("mask_quadT", (64, 128), F32),
               ("sel_gn", (128, 128), BF16), ("sel_gnb", (16, 1024), F32),
               ("ones_col", (128, 1), BF16), ("ones_row", (1, 128), F32),
               ("ones_colf", (128, 1), F32),
               ("bsel_hm", (64, 256), BF16), ("sel_hm", (16, 1024), F32)]

VEC_NAMES = ["x_r", "x_w", "x_k", "x_v", "x_a", "k_k", "k_a", "w_lbias", "a_lbias",
             "g_lbias", "gn_w", "gn_b"]


class Cfg:
    def __init__(self, T=8192, TA=256, TB=512, n_cores=8, stop_after=None,
                 ydt="f32", hdt="f16"):
        self.T, self.TA, self.TB, self.n_cores = T, TA, TB, n_cores
        self.stop_after = stop_after
        self.ydt = ydt
        self.hdt = hdt
        self.LC = 64
        self.NCH = 8
        self.I = 2816
        self.NI = 22
        self.NCC = TA // 64
        assert T % TA == 0 and TA % 64 == 0 and T % TB == 0



def _finish_early(nc, sch, ctx):
    sch.barrier()
    global _LAST_SCHED
    _LAST_SCHED = sch
    sch.finalize()
    ctx.close()
    return nc


def build_program(cfg):
    FLATG = getattr(cfg, "flatg", False)
    nc = bass.Bass("TRN2", target_bir_lowering=False)
    F16 = mybir.dt.float16
    YDT = {"f32": F32, "bf16": BF16, "f16": F16}[cfg.ydt]
    HDT = {"f32": F32, "bf16": BF16, "f16": F16}[cfg.hdt]
    T, TA, TB, LC, NCH, NI, NCC = cfg.T, cfg.TA, cfg.TB, cfg.LC, cfg.NCH, cfg.NI, cfg.NCC
    NT_A, NT_B = T // TA, T // TB

    dI = {}

    def din(name, shape, dt=F32):
        dI[name] = nc.dram_tensor(name, list(shape), dt, kind="ExternalInput").ap()

    din("x", (T, 1024))
    z_out = nc.dram_tensor("z", [T, 1024], F32, kind="ExternalOutput").ap()
    for n, s in [("W_r", (1024, 1024)), ("W_k", (1024, 1024)), ("W_v", (1024, 1024)),
                 ("W_o", (1024, 1024)), ("w_la", (1024, 64)), ("w_lb", (64, 1024)),
                 ("a_la", (1024, 64)), ("a_lb", (64, 1024)), ("g_la", (1024, 128)),
                 ("g_lb", (128, 1024)), ("wg", (2816, 1024)), ("wu", (2816, 1024)),
                 ("wd", (1024, 2816)), ("r_k", (16, 64)), ("clip_x", (1,)),
                 ("clip_down", (1,))]:
        din(n, s)
    for n in VEC_NAMES:
        din(n, (1024,))
    for n, s, dt in CONST_SPECS:
        din(n, s, dt)

    y_dram = nc.dram_tensor("y_spill", [NCH, 128, T], YDT).ap()
    h_dram = nc.dram_tensor("h_spill", [NI, 128, T], HDT).ap()
    wgT_dram = nc.dram_tensor("wgT", [8, 128, 2816], BF16).ap()
    wuT_dram = nc.dram_tensor("wuT", [8, 128, 2816], BF16).ap()
    wdT_dram = nc.dram_tensor("wdT", [22, 128, 1024], BF16).ap()
    cc_io = [nc.dram_tensor(f"cc_{i}", [1, 8], F32).ap() for i in range(4)]

    ctx = ExitStack()
    sch = Sched(nc, ctx)
    sch.add_dma_queue("ld", "sync")
    sch.add_dma_queue("st", "sync")
    sch.add_dma_queue("ldw", "scalar")
    sch.add_dma_queue("gld", "gpsimd")
    bf = sch.buf

    sb = {}

    _names = {}

    def sba(name, shape, dt, scope_ctx=None):
        _names[name] = _names.get(name, 0) + 1
        sb[name] = (scope_ctx or ctx).enter_context(
            nc.sbuf_tensor(f"sb_{name}_{_names[name]}", list(shape), dt))
        return sb[name]

    for n, s, dt in CONST_SPECS:
        sba(n, s, dt)
    for n in VEC_NAMES + ["onemka", "negwlb", "r_kv"]:
        sba(n, (128, 8), F32)
    for n in ["kkh", "kah", "onemkah", "albh", "negwlbh", "rkh"]:
        sba(n, (64, 16), F32)
    sba("scal", (128, 4), F32)
    sba("St", (64, 16 * 64), F32)
    sba("sg", (128, 22), F32)
    sba("su", (128, 22), F32)
    sba("sd", (128, 8), F32)
    sba("sgx", (128, 22), F32)
    sba("sux", (128, 22), F32)
    sba("sdh", (128, 8), F32)
    sba("ymax", (128, 1), F32)
    sba("hmax", (128, 1), F32)
    sba("sm", (128, 16), F32)

    PS = [ctx.enter_context(nc.psum_tensor(f"ps{i}", [128, 512], F32)) for i in range(8)]
    PSB = [p.ap().bitcast(BF16) for p in PS]
    WKbank = {0: PS[4], 1: PS[2]}
    QTbank = {0: PS[6], 1: PS[7]}

    def WKb(par, reg):
        return bf(f"WK{par}{reg}")

    PS2A = [bf("WK1r1"), bf("WK1r2"), bf("WK1r3")]

    for n, s, dt in CONST_SPECS:
        t, src = sb[n], dI[n]
        sch.dma("ld", lambda e, t=t, src=src: e.dma_start(out=t[:, :], in_=src),
                writes=[bf(n)], tag=f"ld_{n}")
    for n in VEC_NAMES:
        t, src = sb[n], dI[n]
        sch.dma("ld", lambda e, t=t, src=src: e.dma_start(
            out=t[:, :], in_=src.rearrange("(c p) -> p c", p=128),
            allow_slow_non_contiguous=True), writes=[bf(n)], tag=f"ld_{n}")
    sch.dma("ld", lambda e: e.dma_start(
        out=sb["r_kv"][:, :],
        in_=dI["r_k"].rearrange("h d -> (h d)").rearrange("(c p) -> p c", p=128),
        allow_slow_non_contiguous=True), writes=[bf("r_kv")], tag="ld_rk")
    for n, src in [("kkh", "k_k"), ("kah", "k_a"), ("albh", "a_lbias")]:
        t, sr = sb[n], dI[src]
        sch.dma("ld", lambda e, t=t, sr=sr: e.dma_start(
            out=t[:, :], in_=sr.rearrange("(s j) -> j s", j=64),
            allow_slow_non_contiguous=True), writes=[bf(n)], tag=f"ld_{n}")
    sch.dma("ld", lambda e: e.dma_start(
        out=sb["negwlbh"][:, :], in_=dI["w_lbias"].rearrange("(s j) -> j s", j=64),
        allow_slow_non_contiguous=True), writes=[bf("negwlbh")], tag="ld_nwh")
    sch.dma("ld", lambda e: e.dma_start(
        out=sb["rkh"][:, :],
        in_=dI["r_k"].rearrange("h d -> (h d)").rearrange("(s j) -> j s", j=64),
        allow_slow_non_contiguous=True), writes=[bf("rkh")], tag="ld_rkh")
    sch.op("vector", lambda e: e.tensor_scalar_mul(
        sb["negwlbh"][:, :], sb["negwlbh"][:, :], -1.0),
        reads=[bf("negwlbh")], writes=[bf("negwlbh")], tag="nwh")
    sch.op("vector", lambda e: e.tensor_scalar(
        sb["onemkah"][:, :], sb["kah"][:, :], -1.0, 1.0, op0=ALU.mult, op1=ALU.add),
        reads=[bf("kah")], writes=[bf("onemkah")], tag="omkh")
    sch.dma("ld", lambda e: e.dma_start(out=sb["sm"][0:1, 1:2], in_=dI["clip_x"].unsqueeze(0)),
            writes=[bf("clipx")], tag="ld_clipx")
    sch.dma("ld", lambda e: e.dma_start(out=sb["sm"][0:1, 2:3],
                                        in_=dI["clip_down"].unsqueeze(0)),
            writes=[bf("clipd")], tag="ld_clipd")

    sch.op("vector", lambda e: e.tensor_scalar(sb["onemka"][:, :], sb["k_a"][:, :],
           -1.0, 1.0, op0=ALU.mult, op1=ALU.add), reads=[bf("k_a")], writes=[bf("onemka")])
    sch.op("vector", lambda e: e.tensor_scalar_mul(sb["negwlb"][:, :], sb["w_lbias"][:, :],
           -1.0), reads=[bf("w_lbias")], writes=[bf("negwlb")])
    sch.op("vector", lambda e: e.memset(sb["St"][:, :], 0.0),
           writes=[bf(f"St_{c}") for c in range(8)])
    sch.op("vector", lambda e: e.memset(sb["ymax"][:, :], 0.0), writes=[bf("ymax")])
    sch.op("vector", lambda e: e.memset(sb["hmax"][:, :], 0.0), writes=[bf("hmax")])
    sch.op("vector", lambda e: e.memset(sb["sm"][:, 0:1], 0.0), writes=[bf("zero1")])
    P3ALL = [bf("PS3q00"), bf("PS3q01"), bf("PS3q10"), bf("PS3q11")]
    psbufs = [[bf("PS0")], [bf("PS1")], [WKb(1, "r1")], P3ALL,
              [WKb(0, "r1")],
              [bf("PS5s"), bf("PS5o0"), bf("PS5o1"), bf("PS5o2"), bf("PS5o3")],
              [bf("QT0")], [bf("QT1")]]
    for i in range(8):
        eng = "vector"
        extra = []
        if i == 4:
            extra = [WKb(0, "r2"), WKb(0, "r3")]
        if i == 2:
            extra = [WKb(1, "r2"), WKb(1, "r3")]
        sch.op(eng, lambda e, i=i: e.memset(PS[i][:, :], 0.0),
               writes=psbufs[i] + extra, tag="psz")

    with ExitStack() as qctx:
        wtl = [qctx.enter_context(nc.sbuf_tensor(f"wq_in{i}", [128, 2816], F32))
               for i in range(2)]
        qtl = [qctx.enter_context(nc.sbuf_tensor(f"wq_q{i}", [128, 2816], BF16))
               for i in range(2)]
        ttl = qctx.enter_context(nc.sbuf_tensor("wq_t", [128, 1024], BF16))
        rmx = qctx.enter_context(nc.sbuf_tensor("wq_m", [128, 4], F32))

        def quant_weight(wsrc, n_chunks, row_len, scale_sb, qdst, tag):
            nblk = row_len // 128
            for i in range(n_chunks):
                p_ = i % 2
                wt, qt = wtl[p_], qtl[p_]
                bw, bq_ = bf(f"qw_w{p_}"), bf(f"qw_q{p_}")
                sch.dma("ldw", lambda e, wt=wt, i=i: e.dma_start(
                    out=wt[:, 0:row_len], in_=wsrc[i * 128:(i + 1) * 128, :]),
                    writes=[bw], tag=f"{tag}ld")
                mo = (i % 2) * 2
                sch.op("vector", lambda e, wt=wt, mo=mo: e.reduce_max(
                    rmx[:, mo:mo + 1], wt[:, 0:row_len], axis=AX.X,
                    apply_absolute_value=True),
                    reads=[bw], writes=[bf(f"qwm{p_}")], tag=f"{tag}mx")
                sch.op("vector", lambda e, i=i, scale_sb=scale_sb, mo=mo:
                    e.tensor_scalar_mul(
                        scale_sb[:, i:i + 1], rmx[:, mo:mo + 1], 1.0 / QMAX),
                    reads=[bf(f"qwm{p_}")], writes=[bf("qws")], tag=f"{tag}s")
                sch.op("vector", lambda e, mo=mo: e.reciprocal(
                    rmx[:, mo + 1:mo + 2], rmx[:, mo:mo + 1]),
                    reads=[bf(f"qwm{p_}")], writes=[bf(f"qwr{p_}")], tag=f"{tag}r")
                sch.op("vector", lambda e, mo=mo: e.tensor_scalar_mul(
                    rmx[:, mo + 1:mo + 2], rmx[:, mo + 1:mo + 2], QMAX),
                    reads=[bf(f"qwr{p_}")], writes=[bf(f"qwr{p_}")], tag=f"{tag}r2")
                sch.op("vector", lambda e, wt=wt, mo=mo: e.tensor_scalar(
                    wt[:, 0:row_len], wt[:, 0:row_len], rmx[:, mo + 1:mo + 2], MAGIC,
                    op0=ALU.mult, op1=ALU.add),
                    reads=[bw, bf(f"qwr{p_}")], writes=[bw], tag=f"{tag}q1")
                sch.op("vector", lambda e, wt=wt: e.tensor_scalar(
                    wt[:, 0:row_len], wt[:, 0:row_len], -MAGIC, -(QMAX + 1.0),
                    op0=ALU.add, op1=ALU.max), reads=[bw], writes=[bw], tag=f"{tag}q2")
                sch.op("vector", lambda e, wt=wt, qt=qt: e.tensor_single_scalar(
                    qt[:, 0:row_len], wt[:, 0:row_len], QMAX, op=ALU.min),
                    reads=[bw], writes=[bq_], tag=f"{tag}q3")
                for g0 in range(0, nblk, 4):
                    ng = min(4, nblk - g0)
                    tp_ = (g0 // 4) % 2
                    to = tp_ * 512
                    sch.op("tensor", lambda e, qt=qt, g0=g0, ng=ng: [
                        e.transpose(PSB[6][:, j * 128:(j + 1) * 128],
                                    qt[:, (g0 + j) * 128:(g0 + j + 1) * 128],
                                    sb["ident_bf"][:, :]) for j in range(ng)][-1],
                        reads=[bq_, bf("ident_bf"), bf("QT0")], writes=[bf("QT0")],
                        tag=f"{tag}tp")
                    sch.op("vector", lambda e, ng=ng, to=to: e.tensor_copy(
                        ttl[:, to:to + ng * 128], PSB[6][:, 0:ng * 128]),
                        reads=[bf("QT0")], writes=[bf(f"qwt{tp_}"), bf("QT0")],
                        tag=f"{tag}te")
                    sch.dma("st", lambda e, qdst=qdst, i=i, g0=g0, ng=ng, to=to:
                        e.dma_start(
                            out=qdst[g0:g0 + ng, :, i * 128:(i + 1) * 128]
                                .rearrange("k p c -> p k c"),
                            in_=ttl[:, to:to + ng * 128].rearrange(
                                "p (k c) -> p k c", k=ng)),
                        reads=[bf(f"qwt{tp_}")], tag=f"{tag}st")

        quant_weight(dI["wg"], NI, 1024, sb["sg"], wgT_dram, "qg")
        quant_weight(dI["wu"], NI, 1024, sb["su"], wuT_dram, "qu")
        quant_weight(dI["wd"], 8, 2816, sb["sd"], wdT_dram, "qd")
        sch.barrier()

    if cfg.stop_after == "wq":
        return _finish_early(nc, sch, ctx)

    actx = ExitStack()
    sba("Wb0", (128, 1024), BF16, actx)
    sba("Wb1", (128, 1024), BF16, actx)
    sba("wla", (128, 8 * 64), BF16, actx)
    sba("wlb", (64, 1024), BF16, actx)
    sba("ala", (128, 8 * 64), BF16, actx)
    sba("alb", (64, 1024), BF16, actx)
    sba("gla", (64, 16 * 128), BF16, actx)
    sba("glb", (128, 1024), BF16, actx)
    for n, src, ck in [("wla", "w_la", 8), ("ala", "a_la", 8)]:
        t, s = sb[n], dI[src]
        sch.dma("gld", lambda e, t=t, s=s, ck=ck: e.dma_start(
            out=t.ap().rearrange("p (k c) -> p k c", k=ck),
            in_=s.rearrange("(k p) c -> p k c", p=128)),
            writes=[bf(n)], tag=f"cast_{n}")
    for n, src in [("wlb", "w_lb"), ("alb", "a_lb"), ("glb", "g_lb")]:
        t, s = sb[n], dI[src]
        sch.dma("gld", lambda e, t=t, s=s: e.dma_start(out=t[:, :], in_=s),
                writes=[bf(n)], tag=f"cast_{n}")
    sch.dma("gld", lambda e: e.dma_start(
        out=sb["gla"].ap().rearrange("j (s m) -> j s m", s=16),
        in_=dI["g_la"].rearrange("(s j) m -> j s m", j=64)),
        writes=[bf("gla")], tag="cast_gla")

    sba("xtm", (128, 2048), F32, actx)
    sba("xfm", (128, NCH * TA), F32, actx)
    sba("delta", (128, NCH * TA), BF16, actx)
    sba("xprev", (128, 8), F32, actx)
    sba("wst", (64, 16 * TA), BF16, actx)
    sba("rst", (64, 16 * TA), BF16, actx)
    sba("kst", (64, 16 * TA), BF16, actx)
    sba("vst", (128, NCH * TA), BF16, actx)
    sba("ahst", (64, 16 * TA), BF16, actx)
    sba("kkst", (64, 16 * TA), BF16, actx)
    sba("gst", (128, NCH * TA), BF16, actx)
    sba("ost", (128, NCH * TA), BF16, actx)
    sba("mixA", (128, NCH * TA), BF16, actx)
    sba("mixB", (128, NCH * TA), BF16, actx)
    sba("lr", (64, 4 * NCH * TA), BF16, actx)
    sba("bk", (64, 4 * NCH * TA), BF16, actx)
    sba("z1", (64, TA), BF16, actx)
    sba("g1", (128, TA), BF16, actx)
    sba("ytile", (128, NCH * TA), YDT, actx)
    sba("prep", (64, 2 * 2 * TA), F32, actx)
    sba("prepE", (64, 2 * 4 * TA), BF16, actx)
    sba("prep2", (64, 2 * 2 * TA), F32, actx)
    sba("eb", (64, 16 * 3 * NCC + 8 * NCC), F32, actx)
    sba("kkr", (64, 2 * 2 * TA), BF16, actx)
    sba("sqh", (64, 2 * 2 * TA), BF16, actx)
    sba("sq", (128, 2 * TA), BF16, actx)
    sba("sq2", (128, 2 * TA), BF16, actx)
    sba("gnstat", (16, 3 * TA), F32, actx)
    sba("lnrow", (1, 3 * TA), F32, actx)
    for par in range(2):
        sba(f"qts{par}", (64, 512), BF16, actx)
        sba(f"m1{par}", (64, 128), BF16, actx)
        sba(f"tm{par}", (64, 512), BF16, actx)
        sba(f"x0{par}", (64, 256), BF16, actx)
        sba(f"x1{par}", (64, 256), BF16, actx)
        sba(f"xf{par}", (64, 256), F32, actx)
        sba(f"sqp{par}", (64, 5 * 256), BF16, actx)
        sba(f"wfm{par}", (64, 128), BF16, actx)
        sba(f"sgm{par}", (64, 128), F32, actx)
        sba(f"sgmb{par}", (64, 128), BF16, actx)
        sba(f"str{par}", (64, 128), BF16, actx)
        sba(f"stt{par}", (64, 128), F32, actx)
    sch.op("vector", lambda e: e.memset(sb["xprev"][:, :], 0.0), writes=[bf("xprev")])

    def mm_group(psum_ap_fn, lhs, lhs_cols, rhs, rhs_cols, nk):
        def fn(e):
            inst = None
            for k in range(nk):
                l0, l1 = lhs_cols(k)
                r0, r1 = rhs_cols(k)
                inst = e.matmul(psum_ap_fn(), lhs[:, l0:l1], rhs[:, r0:r1],
                                start=(k == 0), stop=(k == nk - 1))
            return inst
        return fn

    _sa = cfg.stop_after if isinstance(cfg.stop_after, str) else ""
    _alim = float(_sa[1:]) if _sa.startswith("A") and len(_sa) > 1 else 99
    for t in range(NT_A):
        t0 = t * TA
        xfm, delta = sb["xfm"], sb["delta"]

        for blk_i in range(TA // 128):
            bxt = bf(f"xtm{blk_i % 2}")
            xo = (blk_i % 2) * 1024
            sch.dma("ld", lambda e, blk_i=blk_i, t0=t0, xo=xo: e.dma_start(
                out=sb["xtm"][:, xo:xo + 1024],
                in_=dI["x"][t0 + blk_i * 128: t0 + (blk_i + 1) * 128, :]),
                writes=[bxt], tag="xld")
            for half in range(2):
                sch.op("tensor", lambda e, half=half, xo=xo: [
                    e.transpose(PS[0][:, i * 128:(i + 1) * 128],
                                sb["xtm"][:, xo + (half * 4 + i) * 128:
                                          xo + (half * 4 + i + 1) * 128],
                                sb["ident_f32"][:, :]) for i in range(4)][-1],
                    reads=[bxt, bf("ident_f32"), bf("PS0")], writes=[bf("PS0")], tag="xtp")
                sch.op("vector", lambda e, half=half, blk_i=blk_i: e.tensor_copy(
                    xfm.ap().rearrange("p (c u) -> p c u", c=NCH)[
                        :, half * 4:(half + 1) * 4, blk_i * 128:(blk_i + 1) * 128],
                    PS[0][:, 0:512].rearrange("p (c u) -> p c u", c=4)),
                    reads=[bf("PS0")],
                    writes=[bf(f"xfm{q}") for q in range(half * 4, half * 4 + 4)] +
                           [bf("PS0")], tag="xev")

        if _alim < 1:
            continue
        for c in range(NCH):
            co = c * TA
            sch.op("vector", lambda e, co=co, c=c: e.tensor_sub(
                delta[:, co + 1:co + TA], xfm[:, co:co + TA - 1], xfm[:, co + 1:co + TA]),
                reads=[bf(f"xfm{c}")], writes=[bf(f"delta{c}")], tag="dlt")
            sch.op("vector", lambda e, co=co, c=c: e.tensor_sub(
                delta[:, co:co + 1], sb["xprev"][:, c:c + 1], xfm[:, co:co + 1]),
                reads=[bf(f"xfm{c}"), bf("xprev")], writes=[bf(f"delta{c}")], tag="dlt0")
        sch.op("vector", lambda e: e.tensor_copy(
            sb["xprev"][:, :],
            xfm.ap().rearrange("p (c u) -> p c u", c=NCH)[:, :, TA - 1]),
            reads=[bf(f"xfm{c}") for c in range(NCH)], writes=[bf("xprev")], tag="xpv")

        def emit_mix(mixn, vecn, tag):
            mx = sb[mixn]
            for c in range(NCH):
                co = c * TA
                sch.op("vector", lambda e, co=co, c=c, mx=mx: e.scalar_tensor_tensor(
                    mx[:, co:co + TA], delta[:, co:co + TA], sb[vecn][:, c:c + 1],
                    xfm[:, co:co + TA], op0=ALU.mult, op1=ALU.add),
                    reads=[bf(f"delta{c}"), bf(f"xfm{c}"), bf(vecn), bf(f"{mixn}{c}")],
                    writes=[bf(f"{mixn}{c}")], tag=tag)

        def emit_proj(Wsrc, mixn, outn, tag):
            for f in range(NCH):
                bank, bb = (PS[0], bf("PS0")) if f % 2 == 0 else (PS[1], bf("PS1"))
                wn, bwo = ld_wblk(Wsrc, f, tag + "w")
                sch.op("tensor", mm_group(
                    lambda bank=bank: bank[:, 0:TA], sb[wn],
                    lambda k: (k * 128, (k + 1) * 128),
                    sb[mixn], lambda k: (k * TA, (k + 1) * TA), 8),
                    reads=[bwo, bb] + [bf(f"{mixn}{k}") for k in range(NCH)],
                    writes=[bb], tag=tag)
                sch.op("vector", lambda e, bank=bank, f=f, outn=outn: e.tensor_copy(
                    sb[outn][:, f * TA:(f + 1) * TA], bank[:, 0:TA]),
                    reads=[bb], writes=[bf(f"{outn}{f}"), bb], tag=tag + "e")

        def ld_wblk(Wsrc, f, tag):
            wn = f"Wb{f % 2}"
            bwo = bf(wn)
            sch.dma("gld", lambda e, f=f, wn=wn, Wsrc=Wsrc: e.dma_start(
                out=sb[wn].ap().rearrange("p (k c) -> p k c", k=8),
                in_=dI[Wsrc][:, f * 128:(f + 1) * 128].rearrange("(k p) c -> p k c",
                                                                 p=128)),
                writes=[bwo], tag=tag)
            return wn, bwo

        def emit_proj_hm(Wsrc, mixn, outn, tag):
            for c in range(NCH):
                bank, bb = (PS[0], bf("PS0")) if c % 2 == 0 else (PS[1], bf("PS1"))
                wn, bwo = ld_wblk(Wsrc, c, tag + "w")
                def fn(e, c=c, bank=bank, wn=wn, mixn=mixn):
                    inst = None
                    for h in range(2):
                        for k in range(8):
                            inst = e.matmul(
                                bank[0:64, h * TA:(h + 1) * TA],
                                sb[wn][:, k * 128 + h * 64:k * 128 + (h + 1) * 64],
                                sb[mixn][:, k * TA:(k + 1) * TA],
                                start=(k == 0), stop=(k == 7))
                    return inst
                sch.op("tensor", fn,
                       reads=[bwo, bb] + [bf(f"{mixn}{k}") for k in range(NCH)],
                       writes=[bb], tag=tag)
                sch.op("vector", lambda e, bank=bank, c=c, outn=outn: e.tensor_copy(
                    sb[outn][:, 2 * c * TA:(2 * c + 2) * TA], bank[0:64, 0:2 * TA]),
                    reads=[bb], writes=[bf(f"{outn}{c}"), bb], tag=tag + "e")

        emit_mix("mixA", "x_r", "mxr")
        emit_proj_hm("W_r", "mixA", "rst", "pjr")
        emit_mix("mixB", "x_k", "mxk")
        emit_proj_hm("W_k", "mixB", "kst", "pjk")
        emit_mix("mixA", "x_v", "mxv")
        emit_proj("W_v", "mixA", "vst", "pjv")

        if _alim < 2:
            continue
        emit_mix("mixB", "x_w", "mxw")
        sch.op("tensor", mm_group(lambda: PS[2][0:64, 0:TA], sb["wla"],
                                  lambda k: (k * 64, (k + 1) * 64),
                                  sb["mixB"], lambda k: (k * TA, (k + 1) * TA), 8),
               reads=[bf("wla")] + [bf(f"mixB{k}") for k in range(NCH)] + PS2A,
               writes=PS2A, tag="wl1")
        sch.op("scalar", lambda e: e.activation(sb["z1"][:, 0:TA], PS[2][0:64, 0:TA],
                                                AF.Tanh),
               reads=PS2A, writes=[bf("z1")] + PS2A, tag="wlt")
        for f in range(NCH):
            pb = 64 * (f % 2)
            P3Q = [bf(f"PS3q{f % 2}0"), bf(f"PS3q{f % 2}1")]
            po2 = (f % 2) * 2 * TA
            bp2 = bf(f"prep2{f % 2}")
            def wl2(e, f=f, pb=pb):
                e.matmul(PS[3][pb:pb + 64, 0:TA], sb["wlb"][:, f * 128:f * 128 + 64],
                         sb["z1"][:, 0:TA], start=True, stop=True,
                         tile_position=(0, pb))
                return e.matmul(PS[3][pb:pb + 64, TA:2 * TA],
                                sb["wlb"][:, f * 128 + 64:(f + 1) * 128],
                                sb["z1"][:, 0:TA], start=True, stop=True,
                                tile_position=(0, pb))
            sch.op("tensor", wl2, reads=[bf("wlb"), bf("z1")] + P3Q,
                   writes=P3Q, tag="wl2")
            for h in range(2):
                S = 2 * f + h
                sch.op("scalar", lambda e, h=h, S=S, pb=pb, po2=po2: e.activation(
                    sb["prep2"][:, po2 + h * TA:po2 + (h + 1) * TA],
                    PS[3][pb:pb + 64, h * TA:(h + 1) * TA],
                    AF.Exp, bias=sb["negwlbh"][:, S:S + 1], scale=-1.0),
                    reads=P3Q + [bf("negwlbh"), bp2],
                    writes=[bp2] + (P3Q if h == 1 else []), tag="wls")
            sch.op("vector", lambda e, po2=po2: e.tensor_scalar_add(
                sb["prep2"][:, po2:po2 + 2 * TA], sb["prep2"][:, po2:po2 + 2 * TA], 1.0),
                reads=[bp2], writes=[bp2], tag="wls1")
            sch.op("scalar", lambda e, po2=po2: e.activation(
                sb["prep2"][:, po2:po2 + 2 * TA], sb["prep2"][:, po2:po2 + 2 * TA],
                AF.Ln),
                reads=[bp2], writes=[bp2], tag="wls2")
            sch.op("vector", lambda e, f=f, po2=po2: e.tensor_scalar(
                sb["wst"][:, 2 * f * TA:(2 * f + 2) * TA],
                sb["prep2"][:, po2:po2 + 2 * TA],
                -1.0, -0.5, op0=ALU.mult, op1=ALU.add),
                reads=[bp2], writes=[bf(f"wst{f}")], tag="wlf")

        if _alim < 2.2:
            continue
        emit_mix("mixB", "x_a", "mxa")
        sch.op("tensor", mm_group(lambda: PS[2][0:64, 0:TA], sb["ala"],
                                  lambda k: (k * 64, (k + 1) * 64),
                                  sb["mixB"], lambda k: (k * TA, (k + 1) * TA), 8),
               reads=[bf("ala")] + [bf(f"mixB{k}") for k in range(NCH)] + PS2A,
               writes=PS2A, tag="al1")
        sch.op("vector", lambda e: e.tensor_copy(sb["z1"][:, 0:TA], PS[2][0:64, 0:TA]),
               reads=PS2A, writes=[bf("z1")] + PS2A, tag="alc")
        for f in range(NCH):
            pb = 64 * (f % 2)
            P3Q = [bf(f"PS3q{f % 2}0"), bf(f"PS3q{f % 2}1")]
            def al2(e, f=f, pb=pb):
                e.matmul(PS[3][pb:pb + 64, 0:TA], sb["alb"][:, f * 128:f * 128 + 64],
                         sb["z1"][:, 0:TA], start=True, stop=True,
                         tile_position=(0, pb))
                return e.matmul(PS[3][pb:pb + 64, TA:2 * TA],
                                sb["alb"][:, f * 128 + 64:(f + 1) * 128],
                                sb["z1"][:, 0:TA], start=True, stop=True,
                                tile_position=(0, pb))
            sch.op("tensor", al2, reads=[bf("alb"), bf("z1")] + P3Q,
                   writes=P3Q, tag="al2")
            for h in range(2):
                S = 2 * f + h
                sch.op("scalar", lambda e, h=h, S=S, pb=pb: e.activation(
                    sb["ahst"][:, S * TA:(S + 1) * TA],
                    PS[3][pb:pb + 64, h * TA:(h + 1) * TA],
                    AF.Sigmoid, bias=sb["albh"][:, S:S + 1]),
                    reads=P3Q + [bf("albh")],
                    writes=[bf(f"ahst{f}")] + (P3Q if h == 1 else []), tag="als")

        if _alim < 2.5:
            continue
        sch.op("tensor", mm_group(lambda: PS[2][:, 0:TA], sb["gla"],
                                  lambda k: (k * 128, (k + 1) * 128),
                                  sb["rst"], lambda k: (k * TA, (k + 1) * TA), 16),
               reads=[bf("gla")] + [bf(f"rst{k}") for k in range(NCH)] + PS2A,
               writes=PS2A, tag="gl1")
        sch.op("scalar", lambda e: e.activation(sb["g1"][:, 0:TA], PS[2][:, 0:TA],
                                                AF.Sigmoid),
               reads=PS2A, writes=[bf("g1")] + PS2A, tag="gls")
        if _alim < 2.7:
            continue
        for f in range(NCH):
            vo = 0 if FLATG else (f % 2) * TA
            P3Q = ([bf("PS3q00"), bf("PS3q10")] if FLATG else
                   [bf(f"PS3q0{f % 2}"), bf(f"PS3q1{f % 2}")])
            sch.op("tensor", lambda e, f=f, vo=vo: e.matmul(
                PS[3][:, vo:vo + TA], sb["glb"][:, f * 128:(f + 1) * 128],
                sb["g1"][:, 0:TA], start=True, stop=True),
                reads=[bf("glb"), bf("g1")] + P3Q, writes=P3Q, tag="gl2")
            sch.op("vector", lambda e, f=f, vo=vo: e.tensor_scalar(
                sb["gst"][:, f * TA:(f + 1) * TA], PS[3][:, vo:vo + TA],
                sb["g_lbias"][:, f:f + 1], None, op0=ALU.add),
                reads=P3Q + [bf("g_lbias")],
                writes=[bf(f"gst{f}")] + P3Q, tag="glf")

        if _alim < 3:
            continue
        for c in range(NCH):
            ho = 2 * c * TA
            pk = (c % 2) * 2 * TA
            bkr, bsh = bf(f"kkr{c % 2}"), bf(f"sqh{c % 2}")
            for h in range(2):
                S = 2 * c + h
                sch.op("vector", lambda e, ho=ho, h=h, S=S, pk=pk, c=c: e.tensor_scalar(
                    sb["kkr"][:, pk + h * TA:pk + (h + 1) * TA],
                    sb["kst"][:, ho + h * TA:ho + (h + 1) * TA],
                    sb["kkh"][:, S:S + 1], None, op0=ALU.mult),
                    reads=[bf(f"kst{c}"), bf("kkh"), bkr], writes=[bkr], tag="kk1")
            if _alim >= 2.81:
                sch.op("scalar", lambda e, pk=pk: e.activation(
                    sb["sqh"][:, pk:pk + 2 * TA], sb["kkr"][:, pk:pk + 2 * TA],
                    AF.Square),
                    reads=[bkr, bsh], writes=[bsh], tag="kk2")
            if _alim >= 2.83:
                def kk3(e, c=c, pk=pk):
                    inst = None
                    for h in range(2):
                        S = 2 * c + h
                        inst = e.matmul(PS[5][0:16, 0:TA],
                                        sb["bsel_hm"][:, 16 * S:16 * (S + 1)],
                                        sb["sqh"][:, pk + h * TA:pk + (h + 1) * TA],
                                        start=(S == 0), stop=(S == 15))
                    return inst
                sch.op("tensor", kk3, reads=[bsh, bf("bsel_hm"), bf("PS5s")],
                       writes=[bf("PS5s")], tag="kk3")
            if _alim >= 2.86:
                sch.op("scalar", lambda e, ho=ho, pk=pk: e.copy(
                    sb["kkst"][:, ho:ho + 2 * TA], sb["kkr"][:, pk:pk + 2 * TA]),
                    reads=[bkr], writes=[bf(f"kkst{c}"), bkr], tag="kk4")
        g_ = sb["gnstat"]
        if _alim < 2.8:
            continue
        sch.op("scalar", lambda e: e.activation(g_[0:16, 0:TA], PS[5][0:16, 0:TA], AF.Sqrt),
               reads=[bf("PS5s"), bf("gnstat")], writes=[bf("gnstat"), bf("PS5s")],
               tag="kk5")
        sch.op("vector", lambda e: e.tensor_single_scalar(
            g_[0:16, 0:TA], g_[0:16, 0:TA], 1e-12, op=ALU.max),
            reads=[bf("gnstat")], writes=[bf("gnstat")], tag="kk6")
        sch.op("vector", lambda e: e.reciprocal(g_[0:16, TA:2 * TA], g_[0:16, 0:TA]),
               reads=[bf("gnstat")], writes=[bf("gnstat")], tag="kk7")
        for c in range(NCH):
            ho = 2 * c * TA
            pb = 0 if FLATG else 64 * (c % 2)
            pk = (c % 2) * 2 * TA
            P3Q = ([bf("PS3q00"), bf("PS3q01")] if FLATG else
                   [bf(f"PS3q{c % 2}0"), bf(f"PS3q{c % 2}1")])
            bp2 = bf(f"prep2{c % 2}")
            def kk8(e, c=c, pb=pb):
                inst = None
                for h in range(2):
                    S = 2 * c + h
                    inst = e.matmul(PS[3][pb:pb + 64, h * TA:(h + 1) * TA],
                                    sb["sel_hm"][0:16, 64 * S:64 * (S + 1)],
                                    g_[0:16, TA:2 * TA], start=True, stop=True,
                                    tile_position=(0, pb))
                return inst
            sch.op("tensor", kk8, reads=[bf("gnstat"), bf("sel_hm")] + P3Q,
                   writes=P3Q, tag="kk8")
            sch.op("vector", lambda e, ho=ho, c=c, pb=pb: e.tensor_mul(
                sb["kkst"][:, ho:ho + 2 * TA], sb["kkst"][:, ho:ho + 2 * TA],
                PS[3][pb:pb + 64, 0:2 * TA]),
                reads=P3Q + [bf(f"kkst{c}")],
                writes=[bf(f"kkst{c}")] + P3Q, tag="kk9")
            if _alim < 2.9:
                continue
            for h in range(2):
                S = 2 * c + h
                sch.op("vector", lambda e, ho=ho, h=h, S=S, pk=pk, c=c: e.tensor_scalar(
                    sb["prep2"][:, pk + h * TA:pk + (h + 1) * TA],
                    sb["ahst"][:, ho + h * TA:ho + (h + 1) * TA],
                    sb["kah"][:, S:S + 1], sb["onemkah"][:, S:S + 1],
                    op0=ALU.mult, op1=ALU.add),
                    reads=[bf(f"ahst{c}"), bf("kah"), bf("onemkah"), bp2],
                    writes=[bp2], tag="kh1")
            sch.op("vector", lambda e, ho=ho, pk=pk, c=c: e.tensor_mul(
                sb["kst"][:, ho:ho + 2 * TA], sb["kst"][:, ho:ho + 2 * TA],
                sb["prep2"][:, pk:pk + 2 * TA]),
                reads=[bp2, bf(f"kst{c}")], writes=[bf(f"kst{c}")], tag="kh2")

        if _alim < 4:
            continue
        def prep_ops(PSCH, c):
            ho = 2 * c * TA
            P_, P2 = sb["prep"], sb["prep2"]
            W2 = 2 * TA
            po = (c % 2) * W2
            poE = (c % 2) * 2 * W2
            bP, bP2 = bf(f"prep{c % 2}"), bf(f"prep2{c % 2}")
            beb_, bebt = bf(f"eb{c}"), bf(f"ebt{c % 2}")
            for h in range(2):
                PSCH.op("vector", lambda e, ho=ho, h=h, po=po: e.tensor_tensor_scan(
                    P_[:, po + h * TA:po + (h + 1) * TA],
                    sb["wst"][:, ho + h * TA:ho + (h + 1) * TA],
                    sb["sm"][0:64, 0:1].broadcast_to([64, TA]), 0.0,
                    op0=ALU.add, op1=ALU.add),
                    reads=[bf(f"wst{c}"), bf("zero1"), bP], writes=[bP], tag="pg")
            ebc = 2 * c * 3 * NCC
            ebt = 16 * 3 * NCC + (c % 2) * 4 * NCC
            def gsel(col, po=po):
                return P_.ap()[:, po:po + W2].rearrange(
                    "p (s cc u) -> p s cc u", s=2, cc=NCC)[:, :, :, col]
            def ebsl(ofs, ebc=ebc):
                return sb["eb"].ap()[:, ebc:ebc + 6 * NCC].rearrange(
                    "p (s k cc) -> p s k cc", s=2, k=3)[:, :, ofs, :]
            def ebtmp(i, ebt=ebt):
                return sb["eb"].ap()[:, ebt + i * 2 * NCC:ebt + (i + 1) * 2 * NCC
                                     ].rearrange("p (s cc) -> p s cc", s=2)
            PSCH.op("vector", lambda e, ebtmp=ebtmp: e.memset(ebtmp(0)[:, :, 0:1], 0.0),
                   reads=[bebt], writes=[bebt], tag="pgp0")
            if NCC > 1:
                PSCH.op("vector", lambda e, gsel=gsel, ebtmp=ebtmp: e.tensor_copy(
                    ebtmp(0)[:, :, 1:NCC], gsel(63)[:, :, 0:NCC - 1]),
                    reads=[bP, bebt], writes=[bebt], tag="pgp1")
            PSCH.op("vector", lambda e, gsel=gsel, ebtmp=ebtmp: e.tensor_sub(
                ebtmp(1), gsel(31), ebtmp(0)),
                reads=[bP, bebt], writes=[bebt], tag="pgp2")
            PSCH.op("scalar", lambda e, ebsl=ebsl, ebtmp=ebtmp: e.activation(
                ebsl(0), ebtmp(1), AF.Exp),
                reads=[bebt], writes=[beb_], tag="per")
            PSCH.op("vector", lambda e, gsel=gsel, ebtmp=ebtmp: e.tensor_sub(
                ebtmp(1), gsel(63), ebtmp(0)),
                reads=[bP, bebt, beb_], writes=[bebt], tag="pgp3")
            PSCH.op("scalar", lambda e, ebsl=ebsl, ebtmp=ebtmp: e.activation(
                ebsl(2), ebtmp(1), AF.Exp),
                reads=[bebt], writes=[beb_], tag="peg")
            PSCH.op("vector", lambda e, gsel=gsel, ebtmp=ebtmp: e.tensor_sub(
                ebtmp(1), gsel(63), gsel(31)),
                reads=[bP, beb_, bebt], writes=[bebt], tag="ped0")
            PSCH.op("scalar", lambda e, ebsl=ebsl, ebtmp=ebtmp: e.activation(
                ebsl(1), ebtmp(1), AF.Exp),
                reads=[bebt], writes=[beb_], tag="ped")
            PSCH.op("vector", lambda e, po=po: e.tensor_sub(
                P_[:, po:po + W2], P_[:, po:po + W2],
                P_.ap()[:, po:po + W2].rearrange("p (s cc u) -> p s cc u", s=2, cc=NCC)
                [:, :, :, 31:32].broadcast_to([64, 2, NCC, LC])),
                reads=[bP], writes=[bP], tag="pgr")
            PSCH.op("vector", lambda e, ho=ho, P2=P2, W2=W2, po=po: e.tensor_sub(
                P2[:, po:po + W2], P_[:, po:po + W2], sb["wst"][:, ho:ho + W2]),
                reads=[bP, bf(f"wst{c}"), bP2], writes=[bP2], tag="pe2s")
            PSCH.op("scalar", lambda e, po=po: e.activation(
                P2[:, po:po + W2], P2[:, po:po + W2], AF.Exp),
                reads=[bP2], writes=[bP2], tag="pe2")
            PE_ = sb["prepE"]
            bPE = bf(f"prepE{c % 2}")
            PSCH.op("scalar", lambda e, PE_=PE_, po=po, poE=poE: e.activation(
                PE_[:, poE:poE + W2], P_[:, po:po + W2], AF.Exp),
                reads=[bP, bPE], writes=[bPE], tag="pe1")
            PSCH.op("scalar", lambda e, PE_=PE_, po=po, poE=poE: e.activation(
                PE_[:, poE + W2:poE + 2 * W2], P_[:, po:po + W2], AF.Exp, scale=-1.0),
                reads=[bP], writes=[bPE], tag="pen")
            lrc0 = 4 * TA * c

            def ilv(tsr, off, lrc0=lrc0):
                return tsr.ap()[:, lrc0:lrc0 + 2 * W2].rearrange(
                    "p (s cc two u) -> p s cc two u", s=2, cc=NCC, two=2)[:, :, :, off, :]

            def ch(tsr, c0=None, ho=ho):
                base = ho if c0 is None else c0
                return tsr.ap()[:, base:base + W2].rearrange(
                    "p (s cc u) -> p s cc u", s=2, cc=NCC)

            blr_, bbk_ = bf(f"lr{c}"), bf(f"bk{c}")
            PSCH.op("vector", lambda e, ilv=ilv, ch=ch, P2=P2, po=po: e.tensor_tensor(
                ilv(sb["lr"], 0), ch(sb["kkst"]), ch(P2, po), op=ALU.mult),
                reads=[bf(f"kkst{c}"), bP2, blr_], writes=[blr_], tag="psa")
            PSCH.op("vector", lambda e, ilv=ilv, ch=ch, poE=poE: e.tensor_tensor(
                ilv(sb["lr"], 1), ch(sb["rst"]), ch(sb["prepE"], poE), op=ALU.mult),
                reads=[bf(f"rst{c}"), bPE, blr_], writes=[blr_], tag="psr")
            PSCH.op("vector", lambda e, ho=ho, P2=P2, W2=W2, po=po: e.scalar_tensor_tensor(
                P2[:, po:po + W2], sb["kkst"][:, ho:ho + W2], -1.0,
                sb["ahst"][:, ho:ho + W2],
                op0=ALU.mult, op1=ALU.mult),
                reads=[bf(f"kkst{c}"), bf(f"ahst{c}"), bP2], writes=[bP2], tag="pbp")
            PSCH.op("vector", lambda e, ilv=ilv, ch=ch, P2=P2, W2=W2, po=po, poE=poE:
                e.tensor_tensor(
                    ilv(sb["bk"], 0), ch(P2, po), ch(sb["prepE"], poE + W2),
                    op=ALU.mult),
                reads=[bP2, bPE, bbk_], writes=[bbk_], tag="psb")
            PSCH.op("vector", lambda e, ilv=ilv, ch=ch, W2=W2, poE=poE: e.tensor_tensor(
                ilv(sb["bk"], 1), ch(sb["kst"]), ch(sb["prepE"], poE + W2),
                op=ALU.mult),
                reads=[bf(f"kst{c}"), bPE, bbk_], writes=[bbk_], tag="psk")

        for c0 in range(0, NCH, 2):
            pcols = []
            for c in (c0, c0 + 1):
                pcol = OpCollector(sch)
                prep_ops(pcol, c)
                pcols.append(pcol)
            zip_emit(sch, pcols)

        if _alim < 5:
            continue
        for cc in range(NCC):
            for c0 in range(0, NCH, 2):
                cols = []
                for c in (c0, c0 + 1):
                    col = OpCollector(sch)
                    unit_scan(col, sb, PS, PSB, WKbank, QTbank, bf, cfg, cc, c, c % 2)
                    cols.append(col)
                zip_emit(sch, cols)

        if _alim < 6:
            continue
        post_scan(sch, sb, PS, bf, dI, cfg, t, y_dram)

    if isinstance(cfg.stop_after, str) and cfg.stop_after.startswith("A"):
        actx.close()
        return _finish_early(nc, sch, ctx)
    finalize_scale(sch, sb, PS, bf, dI, cfg, "ymax", cc_io[0], cc_io[1], "clipx", 0)
    sch.barrier()
    actx.close()

    b1 = ExitStack()
    sba("WgT", (128, 8 * 2816), BF16, b1)
    sba("WuT", (128, 8 * 2816), BF16, b1)
    sba("yt", (128, NCH * TB), YDT, b1)
    sba("xq", (128, NCH * TB), BF16, b1)
    sba("qf", (128, 2 * TB), F32, b1)
    sba("hb", (128, 2 * TB), HDT, b1)
    sba("silu", (128, 2 * TB), BF16, b1)
    for k in range(8):
        sch.dma("ldw", lambda e, k=k: e.dma_start(
            out=sb["WgT"][:, k * 2816:(k + 1) * 2816], in_=wgT_dram[k]),
            writes=[bf("WgT")], tag="lWg")
        sch.dma("ldw", lambda e, k=k: e.dma_start(
            out=sb["WuT"][:, k * 2816:(k + 1) * 2816], in_=wuT_dram[k]),
            writes=[bf("WuT")], tag="lWu")
    sch.op("vector", lambda e: e.tensor_scalar(
        sb["sgx"][:, :], sb["sg"][:, :], sb["scal"][:, 0:1], None, op0=ALU.mult),
        reads=[bf("sg"), bf("scal")], writes=[bf("sgx")], tag="sgx")
    sch.op("vector", lambda e: e.tensor_scalar(
        sb["sux"][:, :], sb["su"][:, :], sb["scal"][:, 0:1], None, op0=ALU.mult),
        reads=[bf("su"), bf("scal")], writes=[bf("sux")], tag="sux")

    for t in range(NT_B):
        t0 = t * TB
        for c in range(NCH):
            sch.dma("ld", lambda e, c=c, t0=t0: e.dma_start(
                out=sb["yt"][:, c * TB:(c + 1) * TB], in_=y_dram[c, :, t0:t0 + TB]),
                writes=[bf(f"yt{c}")], tag="lys")
        for c in range(NCH):
            co = c * TB
            po = (c % 2) * TB
            bq_ = bf(f"qf{c % 2}")
            sch.op("vector", lambda e, co=co, po=po, c=c: e.tensor_scalar(
                sb["qf"][:, po:po + TB], sb["yt"][:, co:co + TB], sb["scal"][:, 1:2],
                MAGIC, op0=ALU.mult, op1=ALU.add),
                reads=[bf(f"yt{c}"), bf("scal"), bq_], writes=[bq_], tag="xq1")
            sch.op("vector", lambda e, po=po: e.tensor_scalar(
                sb["qf"][:, po:po + TB], sb["qf"][:, po:po + TB], -MAGIC, -(QMAX + 1.0),
                op0=ALU.add, op1=ALU.max), reads=[bq_], writes=[bq_], tag="xq2")
            sch.op("vector", lambda e, co=co, po=po: e.tensor_single_scalar(
                sb["xq"][:, co:co + TB], sb["qf"][:, po:po + TB], QMAX, op=ALU.min),
                reads=[bq_, bf(f"xq{c}")], writes=[bf(f"xq{c}")], tag="xq3")
        BXQ = [bf(f"xq{k}") for k in range(NCH)]
        for i in range(NI):
            pi = i % 2
            bhb, bsil = bf(f"hb{pi}"), bf(f"silu{pi}")
            sch.op("tensor", mm_group(
                lambda: PS[0][:, 0:TB], sb["WgT"],
                lambda k, i=i: (k * 2816 + i * 128, k * 2816 + (i + 1) * 128),
                sb["xq"], lambda k: (k * TB, (k + 1) * TB), 8),
                reads=[bf("WgT"), bf("PS0")] + BXQ, writes=[bf("PS0")], tag="gmm")
            sch.op("tensor", mm_group(
                lambda: PS[1][:, 0:TB], sb["WuT"],
                lambda k, i=i: (k * 2816 + i * 128, k * 2816 + (i + 1) * 128),
                sb["xq"], lambda k: (k * TB, (k + 1) * TB), 8),
                reads=[bf("WuT"), bf("PS1")] + BXQ, writes=[bf("PS1")], tag="umm")
            sch.op("scalar", lambda e, i=i, pi=pi: e.activation(
                sb["silu"][:, pi * TB:pi * TB + TB], PS[0][:, 0:TB], AF.Silu,
                scale=sb["sgx"][:, i:i + 1]),
                reads=[bf("PS0"), bf("sgx"), bsil], writes=[bsil, bf("PS0")],
                tag="sil")
            sch.op("vector", lambda e, i=i, pi=pi: e.scalar_tensor_tensor(
                sb["hb"][:, pi * TB:pi * TB + TB], PS[1][:, 0:TB], sb["sux"][:, i:i + 1],
                sb["silu"][:, pi * TB:pi * TB + TB], op0=ALU.mult, op1=ALU.mult),
                reads=[bf("PS1"), bf("sux"), bsil, bhb],
                writes=[bhb, bf("PS1")], tag="hm")
            sch.op("vector", lambda e, pi=pi: e.reduce_max(
                sb["sm"][:, 5:6], sb["hb"][:, pi * TB:pi * TB + TB], axis=AX.X,
                apply_absolute_value=True),
                reads=[bhb], writes=[bf("hmx")], tag="hx1")
            sch.op("vector", lambda e: e.tensor_max(
                sb["hmax"][:, 0:1], sb["hmax"][:, 0:1], sb["sm"][:, 5:6]),
                reads=[bf("hmx"), bf("hmax")], writes=[bf("hmax")], tag="hx2")
            sch.dma("st", lambda e, i=i, t0=t0, pi=pi: e.dma_start(
                out=h_dram[i, :, t0:t0 + TB], in_=sb["hb"][:, pi * TB:pi * TB + TB]),
                reads=[bhb], tag="sth")

    if cfg.stop_after == "B1":
        b1.close()
        return _finish_early(nc, sch, ctx)
    finalize_scale(sch, sb, PS, bf, dI, cfg, "hmax", cc_io[2], cc_io[3], "clipd", 2)
    sch.barrier()
    b1.close()

    b2 = ExitStack()
    sba("WdT", (128, 22 * 1024), BF16, b2)
    sba("yt2", (128, NCH * TB), YDT, b2)
    sba("ht", (128, NI * TB), HDT, b2)
    sba("hq", (128, NI * TB), BF16, b2)
    sba("qf2", (128, 2 * TB), F32, b2)
    sba("zpre", (128, NCH * TB), F32, b2)
    sba("sqb", (128, 2 * TB), BF16, b2)
    sba("zfin", (128, NCH * TB), F32, b2)
    sba("ztm", (128, 2048), F32, b2)
    sba("lnrow", (1, 3 * TB), F32, b2)
    for k in range(22):
        sch.dma("ldw", lambda e, k=k: e.dma_start(
            out=sb["WdT"][:, k * 1024:(k + 1) * 1024], in_=wdT_dram[k]),
            writes=[bf("WdT")], tag="lWd")
    sch.op("vector", lambda e: e.tensor_scalar(
        sb["sdh"][:, :], sb["sd"][:, :], sb["scal"][:, 2:3], None, op0=ALU.mult),
        reads=[bf("sd"), bf("scal")], writes=[bf("sdh")], tag="sdh")

    for t in range(NT_B):
        t0 = t * TB
        for c in range(NCH):
            sch.dma("ld", lambda e, c=c, t0=t0: e.dma_start(
                out=sb["yt2"][:, c * TB:(c + 1) * TB], in_=y_dram[c, :, t0:t0 + TB]),
                writes=[bf(f"yt2{c}")], tag="ly2")
        for i in range(NI):
            sch.dma("ld", lambda e, i=i, t0=t0: e.dma_start(
                out=sb["ht"][:, i * TB:(i + 1) * TB], in_=h_dram[i, :, t0:t0 + TB]),
                writes=[bf(f"ht{i}")], tag="lh")
        for i in range(NI):
            co = i * TB
            po = (i % 2) * TB
            bq2 = bf(f"qf2{i % 2}")
            sch.op("vector", lambda e, co=co, po=po, i=i: e.tensor_scalar(
                sb["qf2"][:, po:po + TB], sb["ht"][:, co:co + TB], sb["scal"][:, 3:4],
                MAGIC, op0=ALU.mult, op1=ALU.add),
                reads=[bf(f"ht{i}"), bf("scal"), bq2], writes=[bq2], tag="hq1")
            sch.op("vector", lambda e, po=po: e.tensor_scalar(
                sb["qf2"][:, po:po + TB], sb["qf2"][:, po:po + TB], -MAGIC,
                -(QMAX + 1.0),
                op0=ALU.add, op1=ALU.max), reads=[bq2], writes=[bq2], tag="hq2")
            sch.op("vector", lambda e, co=co, po=po, i=i: e.tensor_single_scalar(
                sb["hq"][:, co:co + TB], sb["qf2"][:, po:po + TB], QMAX, op=ALU.min),
                reads=[bq2, bf(f"hq{i}")], writes=[bf(f"hq{i}")], tag="hq3")
        BHQ = [bf(f"hq{k}") for k in range(NI)]
        for f in range(NCH):
            bank, bb = (PS[0], bf("PS0")) if f % 2 == 0 else (PS[1], bf("PS1"))
            sch.op("tensor", mm_group(
                lambda bank=bank: bank[:, 0:TB], sb["WdT"],
                lambda k, f=f: (k * 1024 + f * 128, k * 1024 + (f + 1) * 128),
                sb["hq"], lambda k: (k * TB, (k + 1) * TB), NI),
                reads=[bf("WdT"), bb] + BHQ, writes=[bb], tag="dmm")
            sch.op("vector", lambda e, f=f, bank=bank: e.scalar_tensor_tensor(
                sb["zpre"][:, f * TB:(f + 1) * TB], bank[:, 0:TB], sb["sdh"][:, f:f + 1],
                sb["yt2"][:, f * TB:(f + 1) * TB], op0=ALU.mult, op1=ALU.add),
                reads=[bb, bf("sdh"), bf(f"yt2{f}"), bf(f"zpre{f}")],
                writes=[bf(f"zpre{f}"), bb], tag="dre")
        emit_ln(sch, sb, PS, bf, cfg, "zpre", "zfin", TB, "sqb")
        for blk_i in range(TB // 128):
            zo = (blk_i % 2) * 1024
            bzt = bf(f"ztm{blk_i % 2}")
            for half in range(2):
                sch.op("tensor", lambda e, blk_i=blk_i, half=half: [
                    e.transpose(PS[1][:, i * 128:(i + 1) * 128],
                                sb["zfin"][:, (half * 4 + i) * TB + blk_i * 128:
                                           (half * 4 + i) * TB + (blk_i + 1) * 128],
                                sb["ident_f32"][:, :]) for i in range(4)][-1],
                    reads=[bf(f"zfin{q}") for q in range(half * 4, half * 4 + 4)] +
                          [bf("ident_f32"), bf("PS1")], writes=[bf("PS1")],
                    tag="ztp")
                sch.op("vector", lambda e, half=half, zo=zo: e.tensor_copy(
                    sb["ztm"][:, zo + half * 512:zo + (half + 1) * 512], PS[1][:, 0:512]),
                    reads=[bf("PS1"), bzt], writes=[bzt, bf("PS1")], tag="zev")
            sch.dma("st", lambda e, t0=t0, blk_i=blk_i, zo=zo: e.dma_start(
                out=z_out[t0 + blk_i * 128:t0 + (blk_i + 1) * 128, :],
                in_=sb["ztm"][:, zo:zo + 1024]),
                reads=[bzt], tag="stz")
    b2.close()

    for i in range(sch.DMA_ROT):
        if sch.dmaq["st"][2][i] > 0:
            sch.items["sync"].append(("wait", (sch.dmaq["st"][1][i],
                                               sch.dmaq["st"][2][i] * 16)))
    global _LAST_SCHED
    _LAST_SCHED = sch
    sch.finalize()
    ctx.close()
    return nc



def unit_scan(sch, sb, PS, PSB, WKbank, QTbank, bf, cfg, cc, c, par):
    ulim = getattr(cfg, "ulim", 99)
    _ucnt = [0]
    _sch_op = sch.op

    def _op(*a, **k):
        if _ucnt[0] >= ulim:
            return None
        _ucnt[0] += 1
        return _sch_op(*a, **k)

    class _S:
        op = staticmethod(_op)
        buf = sch.buf
    sch = _S()

    TA, LC, NCC, NCH = cfg.TA, cfg.LC, cfg.NCC, cfg.NCH

    def lrc(h):
        return (2 * c + h) * 2 * TA + cc * 128
    vco = c * TA + cc * LC
    ebb = [(2 * c + h) * 3 * NCC for h in range(2)]
    qts, m1, tm = sb[f"qts{par}"], sb[f"m1{par}"], sb[f"tm{par}"]
    x0, x1 = sb[f"x0{par}"], sb[f"x1{par}"]
    sqp, wfm = sb[f"sqp{par}"], sb[f"wfm{par}"]
    sgm, stR, stt = sb[f"sgm{par}"], sb[f"str{par}"], sb[f"stt{par}"]
    bq, bm1, btm = bf(f"qts{par}"), bf(f"m1{par}"), bf(f"tm{par}")
    bx = [bf(f"x0{par}"), bf(f"x1{par}")]
    bsq, bwfm = bf(f"sqp{par}"), bf(f"wfm{par}")
    bsgm, bstR, bstt = bf(f"sgm{par}"), bf(f"str{par}"), bf(f"stt{par}")
    blr, bbk, bSt = bf(f"lr{c}"), bf(f"bk{c}"), bf(f"St_{c}")
    beb = bf(f"eb{c}")
    St = sb["St"]
    stc = 2 * c * 64
    QT = QTbank[par]
    QTB = PSB[6 if par == 0 else 7]
    WK = WKbank[par]
    WKB = PSB[4 if par == 0 else 2]
    bQT = bf(f"QT{par}")
    bR1, bR2, bR3 = bf(f"WK{par}r1"), bf(f"WK{par}r2"), bf(f"WK{par}r3")
    O_ = PS[5]
    oq = (cc * NCH + c) % 4
    oc0 = 256 + oq * 64
    bO = bf(f"PS5o{oq}")

    def quad(e):
        inst = None
        for h in range(2):
            lo = lrc(h)
            for part in range(2):
                colo = h * 256 + part * 128
                inst = e.matmul(QT[0:64, colo:colo + 128],
                                sb["bk"][:, lo + part * 64:lo + (part + 1) * 64],
                                sb["lr"][:, lo:lo + 128], start=True, stop=True)
        return inst
    sch.op("tensor", quad, reads=[blr, bbk, bQT], writes=[bQT], tag="uq")
    sch.op("vector", lambda e: e.tensor_tensor(
        qts.ap().rearrange("p (a b) -> p a b", a=4),
        QT[0:64, 0:512].rearrange("p (a b) -> p a b", a=4),
        sb["mask_quadT"].ap().unsqueeze(1).broadcast_to([64, 4, 128]),
        op=ALU.mult),
        reads=[bQT, bf("mask_quadT")], writes=[bq, bQT], tag="uqe")

    def fwd(e):
        inst = None
        for h in range(2):
            lo = lrc(h)
            inst = e.matmul(WK[0:64, h * 64:(h + 1) * 64],
                            sb["lr"][:, lo:lo + 64], sb["bk"][:, lo:lo + 64],
                            start=True, stop=True)
        return inst
    sch.op("tensor", fwd, reads=[blr, bbk, bR1], writes=[bR1], tag="uf")
    sch.op("vector", lambda e: e.tensor_tensor(
        m1.ap().rearrange("p (a b) -> p a b", a=2),
        WK[0:64, 0:128].rearrange("p (a b) -> p a b", a=2),
        sb["mask_sl"].ap().unsqueeze(1).broadcast_to([64, 2, 64]), op=ALU.mult),
        reads=[bR1, bf("mask_sl")], writes=[bm1, bR1], tag="ufe")

    def tps(e):
        e.transpose(QTB[0:64, 0:64], sb["lr"][:, lrc(0):lrc(0) + 64],
                    sb["ident_bf"][0:64, 0:64])
        e.transpose(QTB[0:64, 64:128], sb["lr"][:, lrc(1):lrc(1) + 64],
                    sb["ident_bf"][0:64, 0:64])
        e.transpose(QTB[0:64, 128:256], sb["vst"][:, vco:vco + 64],
                    sb["ident_bf"][:, :])
        e.transpose(QTB[0:64, 256:320], sb["bk"][:, lrc(0):lrc(0) + 64],
                    sb["ident_bf"][0:64, 0:64])
        e.transpose(QTB[0:64, 320:384], sb["bk"][:, lrc(1):lrc(1) + 64],
                    sb["ident_bf"][0:64, 0:64])
        e.transpose(QTB[0:64, 384:448], sb["bk"][:, lrc(0) + 64:lrc(0) + 128],
                    sb["ident_bf"][0:64, 0:64])
        return e.transpose(QTB[0:64, 448:512], sb["bk"][:, lrc(1) + 64:lrc(1) + 128],
                           sb["ident_bf"][0:64, 0:64])
    sch.op("tensor", tps, reads=[blr, bbk, bf(f"vst{c}"), bf("ident_bf"), bQT],
           writes=[bQT], tag="utp")
    sch.op("scalar", lambda e: e.copy(tm[:, :], QTB[0:64, 0:512]),
           reads=[bQT], writes=[btm, bQT], tag="utme")

    def nv(e):
        e.matmul(WK[0:64, 0:64], qts[:, 128:192], tm[:, 128:192], start=True, stop=True)
        return e.matmul(WK[0:64, 64:128], qts[:, 384:448], tm[:, 192:256],
                        start=True, stop=True)
    sch.op("tensor", nv, reads=[bq, btm, bR1], writes=[bR1], tag="unv")

    def x0a(e):
        e.tensor_copy(x0.ap().rearrange("p (h q) -> p h q", h=2)[:, :, 0:64],
                      tm.ap()[:, 0:128].rearrange("p (h q) -> p h q", h=2))
        return e.tensor_copy(x0.ap().rearrange("p (h q) -> p h q", h=2)[:, :, 64:128],
                             WK[0:64, 0:128].rearrange("p (h q) -> p h q", h=2))
    sch.op("vector", x0a, reads=[btm, bR1, bx[0]], writes=[bx[0], bR1], tag="ux0")

    def p1(h):
        return qts[:, h * 256:h * 256 + 64]
    for lvl in range(5):
        def sq_(e, lvl=lvl):
            inst = None
            for h in range(2):
                if lvl == 0:
                    mp, pp = m1[:, h * 64:(h + 1) * 64], p1(h)
                else:
                    b0 = (lvl - 1) * 256
                    mp = sqp[:, b0 + h * 64:b0 + (h + 1) * 64]
                    pp = sqp[:, b0 + 128 + h * 64:b0 + 128 + (h + 1) * 64]
                e.matmul(WK[0:64, h * 64:(h + 1) * 64], pp, mp, start=True, stop=True)
                inst = e.matmul(WK[0:64, 128 + h * 64:128 + (h + 1) * 64], mp, pp,
                                start=True, stop=True)
            return inst
        sch.op("tensor", sq_, reads=[bm1, bq, bsq, bR1], writes=[bR1], tag=f"us{lvl}")
        sch.op("scalar", lambda e, lvl=lvl: e.copy(
            sqp[:, lvl * 256:(lvl + 1) * 256], WK[0:64, 0:256]),
            reads=[bR1, bsq], writes=[bsq, bR1], tag=f"use{lvl}")

    xf = sb[f"xf{par}"]
    bxf = bf(f"xf{par}")
    a_, b_ = 0, 1
    for lvl in range(6):
        def ap_(e, lvl=lvl, a_=a_):
            inst = None
            xa = sb[f"x{a_}{par}"]
            for h in range(2):
                if lvl == 0:
                    pp = p1(h)
                else:
                    b0 = (lvl - 1) * 256
                    pp = sqp[:, b0 + 128 + h * 64:b0 + 128 + (h + 1) * 64]
                inst = e.matmul(WK[0:64, h * 128:(h + 1) * 128], pp,
                                xa[:, h * 128:(h + 1) * 128], start=True, stop=True)
            return inst
        sch.op("tensor", ap_, reads=[bq, bsq, bx[a_], bR1], writes=[bR1], tag=f"ua{lvl}")
        if lvl == 0:
            sch.op("vector", lambda e: e.tensor_add(
                xf[:, :], WK[0:64, 0:256], sb[f"x0{par}"][:, :]),
                reads=[bR1, bx[0], bxf], writes=[bxf, bR1], tag=f"uae{lvl}")
        else:
            sch.op("vector", lambda e: e.tensor_add(
                xf[:, :], WK[0:64, 0:256], xf[:, :]),
                reads=[bR1, bxf], writes=[bxf, bR1], tag=f"uae{lvl}")
        if lvl < 5:
            sch.op("scalar", lambda e, b_=b_: e.copy(sb[f"x{b_}{par}"][:, :], xf[:, :]),
                   reads=[bxf, bx[b_]], writes=[bx[b_]], tag=f"uac{lvl}")
        a_, b_ = b_, a_
    sch.op("scalar", lambda e: e.copy(sb[f"x{a_}{par}"][:, :], xf[:, :]),
           reads=[bxf, bx[a_]], writes=[bx[a_]], tag="uacf")
    xa = sb[f"x{a_}{par}"]
    bxa = bx[a_]

    def strop(e):
        e.tensor_scalar(stR[:, 0:64], St[:, stc:stc + 64],
                        sb["eb"][:, ebb[0] + cc:ebb[0] + cc + 1], None, op0=ALU.mult)
        return e.tensor_scalar(stR[:, 64:128], St[:, stc + 64:stc + 128],
                               sb["eb"][:, ebb[1] + cc:ebb[1] + cc + 1], None,
                               op0=ALU.mult)
    sch.op("vector", strop, reads=[bSt, beb, bstR], writes=[bstR], tag="ustr")

    def wtp(e):
        e.transpose(WKB[0:64, 768:832], xa[:, 0:64], sb["ident_bf"][0:64, 0:64])
        return e.transpose(WKB[0:64, 832:896], xa[:, 128:192], sb["ident_bf"][0:64, 0:64])
    sch.op("tensor", wtp, reads=[bxa, bf("ident_bf"), bR3], writes=[bR3], tag="uwt")
    sch.op("scalar", lambda e: e.copy(wfm[:, :], WKB[0:64, 768:896]),
           reads=[bR3, bwfm], writes=[bwfm, bR3], tag="uwe")

    def smm(e):
        e.matmul(WK[0:64, 256:320], wfm[:, 0:64], stR[:, 0:64], start=True, stop=True)
        return e.matmul(WK[0:64, 320:384], wfm[:, 64:128], stR[:, 64:128],
                        start=True, stop=True)
    sch.op("tensor", smm, reads=[bwfm, bstR, bR2], writes=[bR2], tag="usm")
    sch.op("vector", lambda e: e.tensor_add(
        sgm.ap().rearrange("p (h q) -> p h q", h=2),
        WK[0:64, 256:384].rearrange("p (h q) -> p h q", h=2),
        xf.ap().rearrange("p (h q) -> p h q", h=2)[:, :, 64:128]),
        reads=[bR2, bxf, bsgm], writes=[bsgm, bR2], tag="usge")
    sgmb = sb[f"sgmb{par}"]
    bsgmb = bf(f"sgmb{par}")
    sch.op("scalar", lambda e: e.copy(sgmb[:, :], sgm[:, :]),
           reads=[bsgm, bsgmb], writes=[bsgmb], tag="usgc")

    def stm(e):
        btm_, vtm_, ktm_ = tm[:, 256:384], tm[:, 128:256], tm[:, 384:512]
        e.matmul(WK[0:64, 384:448], btm_[:, 0:64], sgmb[:, 0:64], start=True, stop=False)
        e.matmul(WK[0:64, 384:448], ktm_[:, 0:64], vtm_[:, 0:64], start=False, stop=True)
        e.matmul(WK[0:64, 448:512], btm_[:, 64:128], sgmb[:, 64:128],
                 start=True, stop=False)
        return e.matmul(WK[0:64, 448:512], ktm_[:, 64:128], vtm_[:, 64:128],
                        start=False, stop=True)
    sch.op("tensor", stm, reads=[btm, bsgmb, bR3], writes=[bR3], tag="ust")

    def stup(e):
        e.tensor_scalar(stt[:, 0:64], WK[0:64, 384:448],
                        sb["eb"][:, ebb[0] + NCC + cc:ebb[0] + NCC + cc + 1], None,
                        op0=ALU.mult)
        return e.tensor_scalar(stt[:, 64:128], WK[0:64, 448:512],
                               sb["eb"][:, ebb[1] + NCC + cc:ebb[1] + NCC + cc + 1],
                               None, op0=ALU.mult)
    sch.op("vector", stup, reads=[bR3, beb, bstt], writes=[bstt, bR3], tag="usu1")

    def stup2(e):
        e.scalar_tensor_tensor(
            St[:, stc:stc + 64], St[:, stc:stc + 64],
            sb["eb"][:, ebb[0] + 2 * NCC + cc:ebb[0] + 2 * NCC + cc + 1],
            stt[:, 0:64], op0=ALU.mult, op1=ALU.add)
        return e.scalar_tensor_tensor(
            St[:, stc + 64:stc + 128], St[:, stc + 64:stc + 128],
            sb["eb"][:, ebb[1] + 2 * NCC + cc:ebb[1] + 2 * NCC + cc + 1],
            stt[:, 64:128], op0=ALU.mult, op1=ALU.add)
    sch.op("vector", stup2, reads=[bstt, beb, bSt], writes=[bSt], tag="usu2")

    def omm(e):
        e.matmul(O_[0:64, oc0:oc0 + 64], stR[:, 0:64],
                 sb["lr"][:, lrc(0) + 64:lrc(0) + 128],
                 start=True, stop=False)
        e.matmul(O_[0:64, oc0:oc0 + 64], sgmb[:, 0:64], qts[:, 64:128],
                 start=False, stop=False)
        e.matmul(O_[0:64, oc0:oc0 + 64], tm[:, 128:192], qts[:, 192:256],
                 start=False, stop=True)
        e.matmul(O_[64:128, oc0:oc0 + 64], stR[:, 64:128],
                 sb["lr"][:, lrc(1) + 64:lrc(1) + 128],
                 start=True, stop=False, tile_position=(0, 64))
        e.matmul(O_[64:128, oc0:oc0 + 64], sgmb[:, 64:128], qts[:, 320:384],
                 start=False, stop=False, tile_position=(0, 64))
        return e.matmul(O_[64:128, oc0:oc0 + 64], tm[:, 192:256], qts[:, 448:512],
                        start=False, stop=True, tile_position=(0, 64))
    sch.op("tensor", omm, reads=[bstR, bsgmb, btm, bq, blr, bO], writes=[bO], tag="uo")
    if oq == 3:
        OB = [bf(f"PS5o{j}") for j in range(4)]
        sch.op("scalar", lambda e: e.copy(
            sb["ost"].ap().rearrange("p (k u) -> p k u", k=NCH)
            [:, c - 3:c + 1, cc * LC:(cc + 1) * LC],
            O_[:, 256:512].rearrange("p (k u) -> p k u", k=4)),
            reads=OB, writes=[bf(f"ost{cj}") for cj in range(c - 3, c + 1)] + OB,
            tag="uoe")



def post_scan(sch, sb, PS, bf, dI, cfg, t, y_dram):
    TA, NCH = cfg.TA, cfg.NCH
    t0 = t * TA
    g_ = sb["gnstat"]
    bg = bf("gnstat")
    for c in range(NCH):
        co = c * TA
        pq = (c % 2) * TA
        bsq_, bkr = bf(f"sq{c % 2}"), bf(f"kkr{c % 2}")
        pk = (c % 2) * 2 * TA
        sch.op("scalar", lambda e, co=co, pq=pq: e.activation(
            sb["sq"][:, pq:pq + TA], sb["ost"][:, co:co + TA], AF.Square),
            reads=[bf(f"ost{c}"), bsq_], writes=[bsq_], tag="g0")
        sch.op("tensor", lambda e, c=c, co=co, pq=pq: [
            e.matmul(PS[5][0:16, 0:TA], sb["sel_gn"][:, 16 * c:16 * (c + 1)],
                     sb["ost"][:, co:co + TA], start=(c == 0), stop=(c == NCH - 1)),
            e.matmul(PS[6][0:16, 0:TA], sb["sel_gn"][:, 16 * c:16 * (c + 1)],
                     sb["sq"][:, pq:pq + TA], start=(c == 0), stop=(c == NCH - 1))][-1],
            reads=[bf(f"ost{c}"), bsq_, bf("sel_gn"), bf("PS5s"), bf("QT0")],
            writes=[bf("PS5s"), bf("QT0")], tag="g1")
        ho = 2 * c * TA
        for h in range(2):
            S = 2 * c + h
            sch.op("vector", lambda e, ho=ho, h=h, S=S, pk=pk, c=c:
                e.scalar_tensor_tensor(
                    sb["kkr"][:, pk + h * TA:pk + (h + 1) * TA],
                    sb["rst"][:, ho + h * TA:ho + (h + 1) * TA], sb["rkh"][:, S:S + 1],
                    sb["kst"][:, ho + h * TA:ho + (h + 1) * TA],
                    op0=ALU.mult, op1=ALU.mult),
                reads=[bf(f"rst{c}"), bf(f"kst{c}"), bf("rkh"), bkr], writes=[bkr],
                tag="g2")
        def g3(e, c=c, pk=pk):
            inst = None
            for h in range(2):
                S = 2 * c + h
                inst = e.matmul(PS[3][0:16, 0:TA], sb["bsel_hm"][:, 16 * S:16 * (S + 1)],
                                sb["kkr"][:, pk + h * TA:pk + (h + 1) * TA],
                                start=(S == 0), stop=(S == 15))
            return inst
        sch.op("tensor", g3, reads=[bkr, bf("bsel_hm"), bf("PS3q00")],
               writes=[bf("PS3q00")], tag="g3")
    sch.op("vector", lambda e: e.tensor_scalar_mul(g_[0:16, 0:TA], PS[5][0:16, 0:TA],
                                                   1.0 / 64.0),
           reads=[bf("PS5s"), bg], writes=[bg], tag="g4")
    sch.op("vector", lambda e: e.tensor_scalar_mul(g_[0:16, TA:2 * TA],
                                                   PS[6][0:16, 0:TA], 1.0 / 64.0),
           reads=[bf("QT0")], writes=[bg, bf("QT0")], tag="g5")
    sch.op("vector", lambda e: e.scalar_tensor_tensor(
        g_[0:16, 2 * TA:3 * TA], g_[0:16, 0:TA], -1.0, g_[0:16, 0:TA],
        op0=ALU.mult, op1=ALU.mult), reads=[bg], writes=[bg], tag="g6")
    sch.op("vector", lambda e: e.tensor_tensor(
        g_[0:16, TA:2 * TA], g_[0:16, TA:2 * TA], g_[0:16, 2 * TA:3 * TA], op=ALU.add),
        reads=[bg], writes=[bg], tag="g7")
    sch.op("vector", lambda e: e.tensor_scalar_add(g_[0:16, TA:2 * TA],
                                                   g_[0:16, TA:2 * TA], EPS),
           reads=[bg], writes=[bg], tag="g8")
    sch.op("vector", lambda e: e.reciprocal(g_[0:16, 2 * TA:3 * TA], g_[0:16, TA:2 * TA]),
           reads=[bg], writes=[bg], tag="g9")
    sch.op("scalar", lambda e: e.activation(g_[0:16, TA:2 * TA], g_[0:16, 2 * TA:3 * TA],
                                            AF.Sqrt),
           reads=[bg], writes=[bg], tag="gA")
    sch.op("vector", lambda e: e.scalar_tensor_tensor(
        g_[0:16, 2 * TA:3 * TA], g_[0:16, 0:TA], -1.0, g_[0:16, TA:2 * TA],
        op0=ALU.mult, op1=ALU.mult), reads=[bg], writes=[bg], tag="gB")
    sch.op("vector", lambda e: e.tensor_copy(g_[0:16, 0:TA], PS[3][0:16, 0:TA]),
           reads=[bf("PS3q00")], writes=[bg, bf("PS3q00")], tag="gC")
    P3ALL = [bf("PS3q00"), bf("PS3q01"), bf("PS3q10"), bf("PS3q11")]
    for c in range(NCH):
        co = c * TA
        pq = (c % 2) * TA
        bsq_, bsq2 = bf(f"sq{c % 2}"), bf(f"sq2{c % 2}")
        sch.op("tensor", lambda e, c=c: [
            e.matmul(PS[3][:, 0:TA], sb["sel_gnb"][0:16, 128 * c:128 * (c + 1)],
                     g_[0:16, TA:2 * TA], start=True, stop=True),
            e.matmul(PS[3][:, 256:256 + TA], sb["sel_gnb"][0:16, 128 * c:128 * (c + 1)],
                     g_[0:16, 2 * TA:3 * TA], start=True, stop=True)][-1],
            reads=[bg, bf("sel_gnb")] + P3ALL, writes=P3ALL, tag="gD")
        sch.op("tensor", lambda e, c=c: e.matmul(
            PS[5][:, 0:TA], sb["sel_gnb"][0:16, 128 * c:128 * (c + 1)], g_[0:16, 0:TA],
            start=True, stop=True),
            reads=[bg, bf("sel_gnb"), bf("PS5s")], writes=[bf("PS5s")], tag="gE")
        sch.op("vector", lambda e, co=co, c=c, pq=pq: e.scalar_tensor_tensor(
            sb["sq2"][:, pq:pq + TA], sb["ost"][:, co:co + TA], sb["gn_w"][:, c:c + 1],
            PS[3][:, 0:TA], op0=ALU.mult, op1=ALU.mult),
            reads=[bf(f"ost{c}"), bf("gn_w"), bf("PS3q00"), bf("PS3q10"), bsq2],
            writes=[bsq2], tag="gF")
        sch.op("vector", lambda e, c=c, pq=pq: e.tensor_scalar(
            sb["sq"][:, pq:pq + TA], PS[3][:, 256:256 + TA], sb["gn_w"][:, c:c + 1],
            sb["gn_b"][:, c:c + 1], op0=ALU.mult, op1=ALU.add),
            reads=[bf("PS3q01"), bf("PS3q11"), bf("gn_w"), bf("gn_b"), bsq_],
            writes=[bsq_], tag="gG")
        sch.op("vector", lambda e, pq=pq: e.tensor_add(
            sb["sq2"][:, pq:pq + TA], sb["sq2"][:, pq:pq + TA], sb["sq"][:, pq:pq + TA]),
            reads=[bsq_, bsq2], writes=[bsq2, bsq_], tag="gH")
        sch.op("vector", lambda e, co=co, pq=pq: e.tensor_mul(
            sb["sq"][:, pq:pq + TA], sb["vst"][:, co:co + TA], PS[5][:, 0:TA]),
            reads=[bf(f"vst{c}"), bf("PS5s"), bsq_], writes=[bsq_, bf("PS5s")], tag="gI")
        sch.op("vector", lambda e, pq=pq: e.tensor_add(
            sb["sq2"][:, pq:pq + TA], sb["sq2"][:, pq:pq + TA], sb["sq"][:, pq:pq + TA]),
            reads=[bsq_, bsq2], writes=[bsq2, bsq_], tag="gJ")
        sch.op("vector", lambda e, co=co, c=c, pq=pq: e.tensor_mul(
            sb["mixA"][:, co:co + TA], sb["sq2"][:, pq:pq + TA],
            sb["gst"][:, co:co + TA]),
            reads=[bsq2, bf(f"gst{c}"), bf(f"mixA{c}")],
            writes=[bf(f"mixA{c}"), bsq2], tag="gK")
    for f in range(NCH):
        bank, bb = (PS[0], bf("PS0")) if f % 2 == 0 else (PS[1], bf("PS1"))
        won = f"Wb{f % 2}"
        bwo = bf(won)
        sch.dma("gld", lambda e, f=f, won=won: e.dma_start(
            out=sb[won].ap().rearrange("p (k c) -> p k c", k=8),
            in_=dI["W_o"][:, f * 128:(f + 1) * 128].rearrange("(k p) c -> p k c", p=128)),
            writes=[bwo], tag="ldwo")
        sch.op("tensor", lambda e, f=f, bank=bank, won=won: [
            e.matmul(bank[:, 0:TA], sb[won][:, k * 128:(k + 1) * 128],
                     sb["mixA"][:, k * TA:(k + 1) * TA], start=(k == 0), stop=(k == 7))
            for k in range(8)][-1],
            reads=[bwo, bb] + [bf(f"mixA{k}") for k in range(NCH)],
            writes=[bb], tag="wo")
        sch.op("vector", lambda e, f=f, bank=bank: e.tensor_add(
            sb["ytile"][:, f * TA:(f + 1) * TA], bank[:, 0:TA],
            sb["xfm"][:, f * TA:(f + 1) * TA]),
            reads=[bb, bf(f"xfm{f}"), bf(f"ytile{f}")],
            writes=[bf(f"ytile{f}"), bb], tag="yp")
    emit_ln(sch, sb, PS, bf, cfg, "ytile", "ytile", TA, "sq")
    sch.op("vector", lambda e: e.reduce_max(
        sb["sm"][:, 6:7], sb["ytile"][:, 0:NCH * TA], axis=AX.X,
        apply_absolute_value=True),
        reads=[bf(f"ytile{k}") for k in range(NCH)], writes=[bf("ymx")], tag="ym1")
    sch.op("vector", lambda e: e.tensor_max(
        sb["ymax"][:, 0:1], sb["ymax"][:, 0:1], sb["sm"][:, 6:7]),
        reads=[bf("ymx"), bf("ymax")], writes=[bf("ymax")], tag="ym2")
    for c in range(NCH):
        sch.dma("st", lambda e, c=c, t0=t0: e.dma_start(
            out=y_dram[c, :, t0:t0 + TA], in_=sb["ytile"][:, c * TA:(c + 1) * TA]),
            reads=[bf(f"ytile{c}")], tag="sty")



def emit_ln(sch, sb, PS, bf, cfg, src_n, dst_n, TT, scratch_n):
    NCH = cfg.NCH
    src, dst, scr = sb[src_n], sb[dst_n], sb[scratch_n]
    ln = sb["lnrow"]
    bln = bf("lnrow")
    src_f32 = src.dtype == F32
    ones1 = sb["ones_colf"] if src_f32 else sb["ones_col"]
    bones = bf("ones_colf") if src_f32 else bf("ones_col")
    for c in range(NCH):
        co = c * TT
        po = (c % 2) * TT
        bs, bscr = bf(f"{src_n}{c}"), bf(f"{scratch_n}{c % 2}")
        sch.op("scalar", lambda e, co=co, po=po: e.activation(
            scr[:, po:po + TT], src[:, co:co + TT], AF.Square),
            reads=[bs, bscr], writes=[bscr], tag="l0")
        sch.op("tensor", lambda e, c=c, co=co, po=po, ones1=ones1: [
            e.matmul(PS[6][0:1, 0:TT], ones1[:, :], src[:, co:co + TT],
                     start=(c == 0), stop=(c == NCH - 1)),
            e.matmul(PS[7][0:1, 0:TT], sb["ones_col"][:, :], scr[:, po:po + TT],
                     start=(c == 0), stop=(c == NCH - 1))][-1],
            reads=[bs, bscr, bones, bf("ones_col"), bf("QT0"), bf("QT1")],
            writes=[bf("QT0"), bf("QT1")], tag="l1")
    sch.op("vector", lambda e: e.tensor_scalar_mul(ln[0:1, 0:TT], PS[6][0:1, 0:TT],
                                                   1.0 / 1024.0),
           reads=[bf("QT0"), bln], writes=[bln, bf("QT0")], tag="l2")
    sch.op("vector", lambda e: e.tensor_scalar_mul(ln[0:1, TT:2 * TT], PS[7][0:1, 0:TT],
                                                   1.0 / 1024.0),
           reads=[bf("QT1")], writes=[bln, bf("QT1")], tag="l3")
    sch.op("vector", lambda e: e.scalar_tensor_tensor(
        ln[0:1, 2 * TT:3 * TT], ln[0:1, 0:TT], -1.0, ln[0:1, 0:TT],
        op0=ALU.mult, op1=ALU.mult), reads=[bln], writes=[bln], tag="l4")
    sch.op("vector", lambda e: e.tensor_tensor(
        ln[0:1, TT:2 * TT], ln[0:1, TT:2 * TT], ln[0:1, 2 * TT:3 * TT], op=ALU.add),
        reads=[bln], writes=[bln], tag="l5")
    sch.op("vector", lambda e: e.tensor_scalar_add(ln[0:1, TT:2 * TT], ln[0:1, TT:2 * TT],
                                                   EPS),
           reads=[bln], writes=[bln], tag="l6")
    sch.op("vector", lambda e: e.reciprocal(ln[0:1, 2 * TT:3 * TT], ln[0:1, TT:2 * TT]),
           reads=[bln], writes=[bln], tag="l7")
    sch.op("scalar", lambda e: e.activation(ln[0:1, TT:2 * TT], ln[0:1, 2 * TT:3 * TT],
                                            AF.Sqrt),
           reads=[bln], writes=[bln], tag="l8")
    sch.op("vector", lambda e: e.scalar_tensor_tensor(
        ln[0:1, 2 * TT:3 * TT], ln[0:1, 0:TT], -1.0, ln[0:1, TT:2 * TT],
        op0=ALU.mult, op1=ALU.mult), reads=[bln], writes=[bln], tag="l9")
    sch.op("tensor", lambda e: e.matmul(
        PS[6][:, 0:TT], sb["ones_row"][0:1, :], ln[0:1, TT:2 * TT],
        start=True, stop=True),
        reads=[bln, bf("ones_row"), bf("QT0")], writes=[bf("QT0")], tag="lA")
    sch.op("tensor", lambda e: e.matmul(
        PS[7][:, 0:TT], sb["ones_row"][0:1, :], ln[0:1, 2 * TT:3 * TT],
        start=True, stop=True),
        reads=[bln, bf("ones_row"), bf("QT1")], writes=[bf("QT1")], tag="lB")
    for c in range(NCH):
        co = c * TT
        po = (c % 2) * TT
        bs, bscr = bf(f"{src_n}{c}"), bf(f"{scratch_n}{c % 2}")
        bd = bf(f"{dst_n}{c}")
        sch.op("vector", lambda e, co=co, po=po: e.tensor_mul(
            scr[:, po:po + TT], src[:, co:co + TT], PS[6][:, 0:TT]),
            reads=[bs, bf("QT0"), bscr], writes=[bscr], tag="lC")
        sch.op("vector", lambda e, co=co, po=po: e.tensor_add(
            dst[:, co:co + TT], scr[:, po:po + TT], PS[7][:, 0:TT]),
            reads=[bscr, bf("QT1"), bd], writes=[bd], tag="lD")
    sch.op("vector", lambda e: e.nop(), reads=[bf("QT0"), bf("QT1")],
           writes=[bf("QT0"), bf("QT1")], tag="lE")



def finalize_scale(sch, sb, PS, bf, dI, cfg, maxn, ccin, ccout, clipb, col):
    P3ALL = [bf("PS3q00"), bf("PS3q01"), bf("PS3q10"), bf("PS3q11")]
    sch.op("tensor", lambda e: e.transpose(PS[3][0:1, 0:128], sb[maxn][:, 0:1],
                                           sb["ident_f32"][:, :]),
           reads=[bf(maxn), bf("ident_f32")] + P3ALL, writes=P3ALL, tag="f0")
    sch.op("vector", lambda e: e.reduce_max(sb["sm"][0:1, 6:7], PS[3][0:1, 0:128],
                                            axis=AX.X),
           reads=P3ALL, writes=[bf("gm1")] + P3ALL, tag="f1")
    sch.op("vector", lambda e: e.tensor_copy(
        sb["sm"][0:1, 8:16], sb["sm"][0:1, 6:7].broadcast_to([1, 8])),
        reads=[bf("gm1")], writes=[bf("cc8")], tag="f2")
    sch.dma("st", lambda e: e.dma_start(out=ccin[0:1, 0:8], in_=sb["sm"][0:1, 8:16]),
            reads=[bf("cc8")], writes=[bf("ccd_" + maxn)], tag="f3")
    sch.op("gpsimd", lambda e: e.collective_compute(
        "AllReduce", ALU.max, replica_groups=[list(range(cfg.n_cores))],
        ins=[ccin.opt()], outs=[ccout.opt()]),
        reads=[bf("ccd_" + maxn)], writes=[bf("cco_" + maxn)], tag="f4",
        async_done=True)
    sch.dma("gld", lambda e: e.dma_start(out=sb["sm"][0:1, 8:16], in_=ccout[0:1, 0:8]),
            reads=[bf("cco_" + maxn)], writes=[bf("cc8")], tag="f5")
    sch.op("scalar", lambda e: e.activation(sb["sm"][0:1, 4:5], sb["sm"][0:1,
           (1 if clipb == "clipx" else 2):(2 if clipb == "clipx" else 3)], AF.Sigmoid),
           reads=[bf(clipb)], writes=[bf("sig_" + maxn)], tag="f6")
    sch.op("vector", lambda e: e.tensor_mul(sb["sm"][0:1, 3:4], sb["sm"][0:1, 8:9],
                                            sb["sm"][0:1, 4:5]),
           reads=[bf("cc8"), bf("sig_" + maxn), bf("gmx")], writes=[bf("gmx")], tag="f7")
    sch.op("vector", lambda e: e.tensor_scalar_mul(sb["sm"][0:1, 3:4], sb["sm"][0:1, 3:4],
                                                   1.0 / QMAX),
           reads=[bf("gmx")], writes=[bf("gmx")], tag="f8")
    sch.op("vector", lambda e: e.reciprocal(sb["sm"][0:1, 4:5], sb["sm"][0:1, 3:4]),
           reads=[bf("gmx"), bf("sig_" + maxn)], writes=[bf("sig_" + maxn)], tag="f9")
    sch.op("tensor", lambda e: e.matmul(
        PS[3][:, 0:2], sb["ones_row"][0:1, :], sb["sm"][0:1, 3:5],
        start=True, stop=True),
        reads=[bf("gmx"), bf("sig_" + maxn), bf("ones_row")] + P3ALL,
        writes=P3ALL, tag="fA")
    sch.op("vector", lambda e, col=col: e.tensor_copy(
        sb["scal"][:, col:col + 2], PS[3][:, 0:2]),
        reads=P3ALL + [bf("scal")], writes=[bf("scal")] + P3ALL, tag="fB")



_PROGRAM_CACHE = {}
_LAST_SCHED = None


def get_program(cfg):
    key = (cfg.T, cfg.TA, cfg.TB, cfg.n_cores)
    if key not in _PROGRAM_CACHE:
        _PROGRAM_CACHE[key] = build_program(cfg)
    return _PROGRAM_CACHE[key]


def make_in_maps(cfg, inputs, consts):
    x = np.ascontiguousarray(inputs["L0_attn_hidden_states"])
    n = cfg.n_cores
    maps = []
    common = {
        "W_r": inputs["L0_attn_W_r"], "W_k": inputs["L0_attn_W_k"],
        "W_v": inputs["L0_attn_W_v"], "W_o": inputs["L0_attn_W_o"],
        "w_la": inputs["L0_attn_w_lora_a"], "w_lb": inputs["L0_attn_w_lora_b"],
        "a_la": inputs["L0_attn_a_lora_a"], "a_lb": inputs["L0_attn_a_lora_b"],
        "g_la": inputs["L0_attn_g_lora_a"], "g_lb": inputs["L0_attn_g_lora_b"],
        "wg": inputs["L0_ffn_w_gate"], "wu": inputs["L0_ffn_w_up"],
        "wd": inputs["L0_ffn_w_down"], "r_k": inputs["L0_attn_r_k"],
        "x_r": inputs["L0_attn_x_r"], "x_w": inputs["L0_attn_x_w"],
        "x_k": inputs["L0_attn_x_k"], "x_v": inputs["L0_attn_x_v"],
        "x_a": inputs["L0_attn_x_a"], "k_k": inputs["L0_attn_k_k"],
        "k_a": inputs["L0_attn_k_a"], "w_lbias": inputs["L0_attn_w_lora_bias"],
        "a_lbias": inputs["L0_attn_a_lora_bias"], "g_lbias": inputs["L0_attn_g_lora_bias"],
        "gn_w": inputs["L0_attn_gn_weight"], "gn_b": inputs["L0_attn_gn_bias"],
        "clip_x": inputs["L0_ffn_clip_x"], "clip_down": inputs["L0_ffn_clip_down"],
    }
    common = {k: np.ascontiguousarray(np.asarray(v, dtype=np.float32))
              for k, v in common.items()}
    for i in range(n):
        m = dict(common)
        m["x"] = np.ascontiguousarray(x[i].astype(np.float32))
        for name, _, _ in CONST_SPECS:
            m[name] = consts[name]
        maps.append(m)
    return maps


def kernel(**inputs):
    cfg = Cfg()
    consts = host_constants()
    nc = get_program(cfg)
    in_maps = make_in_maps(cfg, inputs, consts)
    res = run_bass_kernel_spmd(nc, in_maps, core_ids=list(range(cfg.n_cores)))
    out = np.stack([r["z"] for r in res.results], axis=0)
    return out.astype(np.float32)


# version 08
# speedup vs baseline: 1.6934x; 1.0994x over previous
import numpy as np
import ml_dtypes
from contextlib import ExitStack

import concourse.bass as bass
import concourse.mybir as mybir
from concourse.bass_utils import run_bass_kernel_spmd

F32 = mybir.dt.float32
BF16 = mybir.dt.bfloat16
AF = mybir.ActivationFunctionType
ALU = mybir.AluOpType
AX = mybir.AxisListType

MAGIC = float(np.float32(1.5 * 2 ** 23))
QMAX = 127.0
EPS = 1e-5


ENGINES = ("tensor", "vector", "scalar", "gpsimd", "sync")


class Tok:
    __slots__ = ("eng", "pos", "sem", "val", "clock", "force_wait")

    def __init__(self, eng, pos, sem, val, clock, force_wait=False):
        self.eng, self.pos, self.sem, self.val, self.clock = eng, pos, sem, val, clock
        self.force_wait = force_wait


class Buf:
    __slots__ = ("name", "writer", "readers")

    def __init__(self, name):
        self.name, self.writer, self.readers = name, None, []


class Sched:
    def __init__(self, nc, ctx):
        self.nc = nc
        self.ctx = ctx
        self.items = {e: [] for e in ENGINES}
        self.pos = {e: 0 for e in ENGINES}
        self.sem = {e: ctx.enter_context(nc.semaphore(f"sem_{e}")) for e in ENGINES}
        self.semval = {e: 0 for e in ENGINES}
        self.synced = {e: {} for e in ENGINES}
        self.drainmark = {e: -1 for e in ENGINES}
        self.dmaq = {}
        self.bufs = {}
        self.banklast = {}

    @staticmethod
    def bank_of(name):
        if name.startswith("PS3"):
            return 3
        if name.startswith("PS5"):
            return 5
        if name.startswith("WK0"):
            return 4
        if name.startswith("WK1"):
            return 2
        if name.startswith("WK2"):
            return 1
        if name in ("PS0", "PS1"):
            return int(name[2])
        if name == "QT0":
            return 6
        if name == "QT1":
            return 7
        if name == "QT2":
            return 0
        return None

    def buf(self, name):
        if name not in self.bufs:
            self.bufs[name] = Buf(name)
        return self.bufs[name]

    DMA_ROT = 8

    def add_dma_queue(self, name, eng):
        sems = [self.ctx.enter_context(self.nc.semaphore(f"dma_{name}{i}"))
                for i in range(self.DMA_ROT)]
        self.dmaq[name] = [eng, sems, [0] * self.DMA_ROT, 0]
        for i in range(self.DMA_ROT):
            self.pos[f"{name}#{i}"] = 0

    def _wait(self, eng, tok):
        syn = self.synced[eng]
        if tok.eng == eng and not tok.force_wait:
            if eng == "tensor":
                return False
            if tok.pos <= self.drainmark[eng] or tok.pos <= syn.get(eng, -1):
                return False
            return True
        if tok.eng == eng and tok.force_wait:
            if tok.pos <= syn.get(eng, -1):
                return False
            self.items[eng].append(("wait", (tok.sem, tok.val)))
            syn[eng] = max(syn.get(eng, -1), tok.pos)
            return False
        if tok.pos <= syn.get(tok.eng, -1):
            return False
        assert tok.sem is not None, f"{eng} depends on un-inc'd op on {tok.eng}"
        self.items[eng].append(("wait", (tok.sem, tok.val)))
        for k, v in tok.clock.items():
            if syn.get(k, -1) < v:
                syn[k] = v
        return False

    def _deps(self, reads, writes):
        deps = []
        for b in reads:
            if b.writer is not None:
                deps.append(b.writer)
        for b in writes:
            if b.writer is not None:
                deps.append(b.writer)
            deps.extend(b.readers)
        return deps

    def op(self, eng, fn, reads=(), writes=(), inc=True, tag="", async_done=False):
        banks = {self.bank_of(b.name) for b in list(reads) + list(writes)}
        banks.discard(None)
        drain = False
        for bk in banks:
            assert inc, "bank-touching ops must inc their sem"
            for e2, tok in self.banklast.get(bk, {}).items():
                if e2 != eng:
                    drain |= self._wait(eng, tok)
        for tok in self._deps(reads, writes):
            drain |= self._wait(eng, tok)
        if drain:
            self.items[eng].append(("drain", None))
            self.drainmark[eng] = self.pos[eng]
        self.pos[eng] += 1
        mypos = self.pos[eng]
        clock = dict(self.synced[eng])
        clock[eng] = mypos
        if inc:
            self.semval[eng] += 1
            tok = Tok(eng, mypos, self.sem[eng], self.semval[eng], clock,
                      force_wait=async_done)
        else:
            assert not async_done
            tok = Tok(eng, mypos, None, None, clock)
        self.items[eng].append(("op", (fn, self.sem[eng] if inc else None,
                                       self.semval[eng], tag)))
        for b in reads:
            b.readers.append(tok)
        for b in writes:
            b.writer = tok
            b.readers = []
        for bk in banks:
            self.banklast.setdefault(bk, {})[eng] = tok
        return tok

    def dma(self, q, fn, reads=(), writes=(), tag=""):
        assert not any(self.bank_of(b.name) is not None
                       for b in list(reads) + list(writes))
        eng, sems, counts, total = self.dmaq[q]
        idx = total % self.DMA_ROT
        self.dmaq[q][3] = total + 1
        qn = f"{q}#{idx}"
        drain = False
        for tok in self._deps(reads, writes):
            drain |= self._wait(eng, tok)
        if counts[idx] > 0 and self.synced[eng].get(qn, -1) < counts[idx] * 16:
            self.items[eng].append(("wait", (sems[idx], counts[idx] * 16)))
            self.synced[eng][qn] = counts[idx] * 16
        if drain:
            self.items[eng].append(("drain", None))
            self.drainmark[eng] = self.pos[eng]
        self.pos[eng] += 1
        counts[idx] += 1
        cnt = counts[idx] * 16
        self.pos[qn] = cnt
        clock = dict(self.synced[eng])
        clock[eng] = self.pos[eng]
        clock[qn] = cnt
        tok = Tok(qn, cnt, sems[idx], cnt, clock)
        self.items[eng].append(("dma", (fn, sems[idx], cnt, tag)))
        for b in reads:
            b.readers.append(tok)
        for b in writes:
            b.writer = tok
            b.readers = []
        return tok

    def barrier(self):
        toks = []
        for e in ENGINES:
            self.pos[e] += 1
            self.semval[e] += 1
            clock = dict(self.synced[e])
            clock[e] = self.pos[e]
            toks.append(Tok(e, self.pos[e], self.sem[e], self.semval[e], clock))
            self.items[e].append(("drain", None))
            self.items[e].append(("rawop", (lambda eng, s_=self.sem[e]: eng.sem_inc(s_, 1),
                                            "bar")))
        dmatoks = []
        for q, (eng, sems, counts, total) in self.dmaq.items():
            for i in range(self.DMA_ROT):
                if counts[i] > 0:
                    dmatoks.append((f"{q}#{i}", sems[i], counts[i] * 16))
        for e in ENGINES:
            syn = self.synced[e]
            for t in toks:
                if t.eng == e or t.pos <= syn.get(t.eng, -1):
                    continue
                self.items[e].append(("wait", (t.sem, t.val)))
                for k, v in t.clock.items():
                    if syn.get(k, -1) < v:
                        syn[k] = v
            for q, sem, val in dmatoks:
                if val > syn.get(q, -1):
                    self.items[e].append(("wait", (sem, val)))
                    syn[q] = val
            self.items[e].append(("drain", None))
            self.drainmark[e] = self.pos[e]
        for b in self.bufs.values():
            b.readers = []
            if b.writer is not None:
                b.writer = toks[ENGINES.index(b.writer.eng)] if b.writer.eng in ENGINES \
                    else None

    def finalize(self):
        nc = self.nc
        with nc.Block() as blk:
            def make(name):
                def body(eng):
                    for kind, payload in self.items[name]:
                        if kind == "wait":
                            eng.wait_ge(*payload)
                        elif kind == "drain":
                            eng.drain()
                        elif kind == "rawop":
                            payload[0](eng)
                        else:
                            fn, sem, val, tag = payload[0], payload[1], payload[2], payload[3]
                            inst = fn(eng)
                            if sem is not None:
                                assert inst is not None, f"op '{tag}' returned None"
                                inst.then_inc(sem, 16 if kind == "dma" else 1)
                return body
            blk.tensor(make("tensor"))
            blk.vector(make("vector"))
            blk.scalar(make("scalar"))
            blk.gpsimd(make("gpsimd"))
            blk.sync(make("sync"))



class OpCollector:

    def __init__(self, sch):
        self.sch = sch
        self.ops = []

    def buf(self, name):
        return self.sch.buf(name)

    def op(self, eng, fn, reads=(), writes=(), tag=""):
        self.ops.append((eng, fn, reads, writes, tag))


def zip_emit(sch, cols):
    n = max(len(c.ops) for c in cols)
    for i in range(n):
        for c in cols:
            if i < len(c.ops):
                eng, fn, reads, writes, tag = c.ops[i]
                sch.op(eng, fn, reads=reads, writes=writes, tag=tag)


def host_constants():
    c = {}
    c["ident_f32"] = np.eye(128, dtype=np.float32)
    c["ident_bf"] = np.eye(128, dtype=ml_dtypes.bfloat16)
    c["mask_sl"] = np.tril(np.ones((64, 64), np.float32), -1)
    su = np.triu(np.ones((64, 64), np.float32), 1)
    iu = np.triu(np.ones((64, 64), np.float32), 0)
    c["mask_quadT"] = np.concatenate([su, iu], axis=1)
    sel = np.zeros((128, 128), np.float32)
    for cc in range(8):
        for p in range(128):
            sel[p, 16 * cc + 2 * cc + (p // 64)] = 1.0
    c["sel_gn"] = sel.astype(ml_dtypes.bfloat16)
    selb = np.zeros((16, 1024), np.float32)
    for cc in range(8):
        for p in range(128):
            selb[2 * cc + (p // 64), 128 * cc + p] = 1.0
    c["sel_gnb"] = selb
    on = np.ones((128, 1), np.float32)
    c["ones_col"] = on.astype(ml_dtypes.bfloat16)
    c["ones_row"] = np.ones((1, 128), np.float32)
    c["ones_colf"] = np.ones((128, 1), np.float32)
    bs = np.zeros((64, 256), np.float32)
    for S in range(16):
        bs[:, 16 * S + S] = 1.0
    c["bsel_hm"] = bs.astype(ml_dtypes.bfloat16)
    hs = np.zeros((16, 1024), np.float32)
    for S in range(16):
        hs[S, 64 * S:64 * (S + 1)] = 1.0
    c["sel_hm"] = hs
    return c


CONST_SPECS = [("ident_f32", (128, 128), F32), ("ident_bf", (128, 128), BF16),
               ("mask_sl", (64, 64), F32), ("mask_quadT", (64, 128), F32),
               ("sel_gn", (128, 128), BF16), ("sel_gnb", (16, 1024), F32),
               ("ones_col", (128, 1), BF16), ("ones_row", (1, 128), F32),
               ("ones_colf", (128, 1), F32),
               ("bsel_hm", (64, 256), BF16), ("sel_hm", (16, 1024), F32)]

VEC_NAMES = ["x_r", "x_w", "x_k", "x_v", "x_a", "k_k", "k_a", "w_lbias", "a_lbias",
             "g_lbias", "gn_w", "gn_b"]


class Cfg:
    def __init__(self, T=8192, TA=256, TB=512, n_cores=8, stop_after=None,
                 ydt="f32", hdt="f16"):
        self.T, self.TA, self.TB, self.n_cores = T, TA, TB, n_cores
        self.stop_after = stop_after
        self.ydt = ydt
        self.hdt = hdt
        self.LC = 64
        self.NCH = 8
        self.I = 2816
        self.NI = 22
        self.NCC = TA // 64
        assert T % TA == 0 and TA % 64 == 0 and T % TB == 0



def _finish_early(nc, sch, ctx):
    sch.barrier()
    global _LAST_SCHED
    _LAST_SCHED = sch
    sch.finalize()
    ctx.close()
    return nc


def build_program(cfg):
    FLATG = getattr(cfg, "flatg", False)
    nc = bass.Bass("TRN2", target_bir_lowering=False)
    F16 = mybir.dt.float16
    YDT = {"f32": F32, "bf16": BF16, "f16": F16}[cfg.ydt]
    HDT = {"f32": F32, "bf16": BF16, "f16": F16}[cfg.hdt]
    T, TA, TB, LC, NCH, NI, NCC = cfg.T, cfg.TA, cfg.TB, cfg.LC, cfg.NCH, cfg.NI, cfg.NCC
    NT_A, NT_B = T // TA, T // TB

    dI = {}

    def din(name, shape, dt=F32):
        dI[name] = nc.dram_tensor(name, list(shape), dt, kind="ExternalInput").ap()

    din("x", (T, 1024))
    z_out = nc.dram_tensor("z", [T, 1024], F32, kind="ExternalOutput").ap()
    for n, s in [("W_r", (1024, 1024)), ("W_k", (1024, 1024)), ("W_v", (1024, 1024)),
                 ("W_o", (1024, 1024)), ("w_la", (1024, 64)), ("w_lb", (64, 1024)),
                 ("a_la", (1024, 64)), ("a_lb", (64, 1024)), ("g_la", (1024, 128)),
                 ("g_lb", (128, 1024)), ("wg", (2816, 1024)), ("wu", (2816, 1024)),
                 ("wd", (1024, 2816)), ("r_k", (16, 64)), ("clip_x", (1,)),
                 ("clip_down", (1,))]:
        din(n, s)
    for n in VEC_NAMES:
        din(n, (1024,))
    for n, s, dt in CONST_SPECS:
        din(n, s, dt)

    y_dram = nc.dram_tensor("y_spill", [NCH, 128, T], YDT).ap()
    h_dram = nc.dram_tensor("h_spill", [NI, 128, T], HDT).ap()
    wgT_dram = nc.dram_tensor("wgT", [8, 128, 2816], BF16).ap()
    wuT_dram = nc.dram_tensor("wuT", [8, 128, 2816], BF16).ap()
    wdT_dram = nc.dram_tensor("wdT", [22, 128, 1024], BF16).ap()
    cc_io = [nc.dram_tensor(f"cc_{i}", [1, 8], F32).ap() for i in range(4)]

    ctx = ExitStack()
    sch = Sched(nc, ctx)
    sch.add_dma_queue("ld", "sync")
    sch.add_dma_queue("st", "sync")
    sch.add_dma_queue("ldw", "scalar")
    sch.add_dma_queue("gld", "gpsimd")
    bf = sch.buf

    sb = {}

    _names = {}

    def sba(name, shape, dt, scope_ctx=None):
        _names[name] = _names.get(name, 0) + 1
        sb[name] = (scope_ctx or ctx).enter_context(
            nc.sbuf_tensor(f"sb_{name}_{_names[name]}", list(shape), dt))
        return sb[name]

    for n, s, dt in CONST_SPECS:
        sba(n, s, dt)
    for n in VEC_NAMES + ["onemka", "negwlb", "r_kv"]:
        sba(n, (128, 8), F32)
    for n in ["kkh", "kah", "onemkah", "albh", "negwlbh", "rkh"]:
        sba(n, (64, 16), F32)
    sba("scal", (128, 4), F32)
    sba("St", (64, 16 * 64), F32)
    sba("sg", (128, 22), F32)
    sba("su", (128, 22), F32)
    sba("sd", (128, 8), F32)
    sba("sgx", (128, 22), F32)
    sba("sux", (128, 22), F32)
    sba("sdh", (128, 8), F32)
    sba("ymax", (128, 1), F32)
    sba("hmax", (128, 1), F32)
    sba("sm", (128, 16), F32)

    PS = [ctx.enter_context(nc.psum_tensor(f"ps{i}", [128, 512], F32)) for i in range(8)]
    PSB = [p.ap().bitcast(BF16) for p in PS]
    WKbank = {0: PS[4], 1: PS[2], 2: PS[1]}
    QTbank = {0: PS[6], 1: PS[7], 2: PS[0]}

    def WKb(par, reg):
        return bf(f"WK{par}{reg}")

    PS2A = [bf("WK1r1"), bf("WK1r2"), bf("WK1r3")]

    for n, s, dt in CONST_SPECS:
        t, src = sb[n], dI[n]
        sch.dma("ld", lambda e, t=t, src=src: e.dma_start(out=t[:, :], in_=src),
                writes=[bf(n)], tag=f"ld_{n}")
    for n in VEC_NAMES:
        t, src = sb[n], dI[n]
        sch.dma("ld", lambda e, t=t, src=src: e.dma_start(
            out=t[:, :], in_=src.rearrange("(c p) -> p c", p=128),
            allow_slow_non_contiguous=True), writes=[bf(n)], tag=f"ld_{n}")
    sch.dma("ld", lambda e: e.dma_start(
        out=sb["r_kv"][:, :],
        in_=dI["r_k"].rearrange("h d -> (h d)").rearrange("(c p) -> p c", p=128),
        allow_slow_non_contiguous=True), writes=[bf("r_kv")], tag="ld_rk")
    for n, src in [("kkh", "k_k"), ("kah", "k_a"), ("albh", "a_lbias")]:
        t, sr = sb[n], dI[src]
        sch.dma("ld", lambda e, t=t, sr=sr: e.dma_start(
            out=t[:, :], in_=sr.rearrange("(s j) -> j s", j=64),
            allow_slow_non_contiguous=True), writes=[bf(n)], tag=f"ld_{n}")
    sch.dma("ld", lambda e: e.dma_start(
        out=sb["negwlbh"][:, :], in_=dI["w_lbias"].rearrange("(s j) -> j s", j=64),
        allow_slow_non_contiguous=True), writes=[bf("negwlbh")], tag="ld_nwh")
    sch.dma("ld", lambda e: e.dma_start(
        out=sb["rkh"][:, :],
        in_=dI["r_k"].rearrange("h d -> (h d)").rearrange("(s j) -> j s", j=64),
        allow_slow_non_contiguous=True), writes=[bf("rkh")], tag="ld_rkh")
    sch.op("vector", lambda e: e.tensor_scalar_mul(
        sb["negwlbh"][:, :], sb["negwlbh"][:, :], -1.0),
        reads=[bf("negwlbh")], writes=[bf("negwlbh")], tag="nwh")
    sch.op("vector", lambda e: e.tensor_scalar(
        sb["onemkah"][:, :], sb["kah"][:, :], -1.0, 1.0, op0=ALU.mult, op1=ALU.add),
        reads=[bf("kah")], writes=[bf("onemkah")], tag="omkh")
    sch.dma("ld", lambda e: e.dma_start(out=sb["sm"][0:1, 1:2], in_=dI["clip_x"].unsqueeze(0)),
            writes=[bf("clipx")], tag="ld_clipx")
    sch.dma("ld", lambda e: e.dma_start(out=sb["sm"][0:1, 2:3],
                                        in_=dI["clip_down"].unsqueeze(0)),
            writes=[bf("clipd")], tag="ld_clipd")

    sch.op("vector", lambda e: e.tensor_scalar(sb["onemka"][:, :], sb["k_a"][:, :],
           -1.0, 1.0, op0=ALU.mult, op1=ALU.add), reads=[bf("k_a")], writes=[bf("onemka")])
    sch.op("vector", lambda e: e.tensor_scalar_mul(sb["negwlb"][:, :], sb["w_lbias"][:, :],
           -1.0), reads=[bf("w_lbias")], writes=[bf("negwlb")])
    sch.op("vector", lambda e: e.memset(sb["St"][:, :], 0.0),
           writes=[bf(f"St_{c}") for c in range(8)])
    sch.op("vector", lambda e: e.memset(sb["ymax"][:, :], 0.0), writes=[bf("ymax")])
    sch.op("vector", lambda e: e.memset(sb["hmax"][:, :], 0.0), writes=[bf("hmax")])
    sch.op("vector", lambda e: e.memset(sb["sm"][:, 0:1], 0.0), writes=[bf("zero1")])
    P3ALL = [bf("PS3q00"), bf("PS3q01"), bf("PS3q10"), bf("PS3q11")]
    psbufs = [[bf("PS0")], [bf("PS1")], [WKb(1, "r1")], P3ALL,
              [WKb(0, "r1")],
              [bf("PS5s"), bf("PS5o0"), bf("PS5o1"), bf("PS5o2"), bf("PS5o3")],
              [bf("QT0")], [bf("QT1")]]
    for i in range(8):
        eng = "vector"
        extra = []
        if i == 4:
            extra = [WKb(0, "r2"), WKb(0, "r3")]
        if i == 2:
            extra = [WKb(1, "r2"), WKb(1, "r3")]
        sch.op(eng, lambda e, i=i: e.memset(PS[i][:, :], 0.0),
               writes=psbufs[i] + extra, tag="psz")

    with ExitStack() as qctx:
        wtl = [qctx.enter_context(nc.sbuf_tensor(f"wq_in{i}", [128, 2816], F32))
               for i in range(2)]
        qtl = [qctx.enter_context(nc.sbuf_tensor(f"wq_q{i}", [128, 2816], BF16))
               for i in range(2)]
        ttl = qctx.enter_context(nc.sbuf_tensor("wq_t", [128, 1024], BF16))
        rmx = qctx.enter_context(nc.sbuf_tensor("wq_m", [128, 4], F32))

        def quant_weight(wsrc, n_chunks, row_len, scale_sb, qdst, tag):
            nblk = row_len // 128
            for i in range(n_chunks):
                p_ = i % 2
                wt, qt = wtl[p_], qtl[p_]
                bw, bq_ = bf(f"qw_w{p_}"), bf(f"qw_q{p_}")
                sch.dma("ldw", lambda e, wt=wt, i=i: e.dma_start(
                    out=wt[:, 0:row_len], in_=wsrc[i * 128:(i + 1) * 128, :]),
                    writes=[bw], tag=f"{tag}ld")
                mo = (i % 2) * 2
                sch.op("vector", lambda e, wt=wt, mo=mo: e.reduce_max(
                    rmx[:, mo:mo + 1], wt[:, 0:row_len], axis=AX.X,
                    apply_absolute_value=True),
                    reads=[bw], writes=[bf(f"qwm{p_}")], tag=f"{tag}mx")
                sch.op("vector", lambda e, i=i, scale_sb=scale_sb, mo=mo:
                    e.tensor_scalar_mul(
                        scale_sb[:, i:i + 1], rmx[:, mo:mo + 1], 1.0 / QMAX),
                    reads=[bf(f"qwm{p_}")], writes=[bf("qws")], tag=f"{tag}s")
                sch.op("vector", lambda e, mo=mo: e.reciprocal(
                    rmx[:, mo + 1:mo + 2], rmx[:, mo:mo + 1]),
                    reads=[bf(f"qwm{p_}")], writes=[bf(f"qwr{p_}")], tag=f"{tag}r")
                sch.op("vector", lambda e, mo=mo: e.tensor_scalar_mul(
                    rmx[:, mo + 1:mo + 2], rmx[:, mo + 1:mo + 2], QMAX),
                    reads=[bf(f"qwr{p_}")], writes=[bf(f"qwr{p_}")], tag=f"{tag}r2")
                sch.op("vector", lambda e, wt=wt, mo=mo: e.tensor_scalar(
                    wt[:, 0:row_len], wt[:, 0:row_len], rmx[:, mo + 1:mo + 2], MAGIC,
                    op0=ALU.mult, op1=ALU.add),
                    reads=[bw, bf(f"qwr{p_}")], writes=[bw], tag=f"{tag}q1")
                sch.op("vector", lambda e, wt=wt: e.tensor_scalar(
                    wt[:, 0:row_len], wt[:, 0:row_len], -MAGIC, -(QMAX + 1.0),
                    op0=ALU.add, op1=ALU.max), reads=[bw], writes=[bw], tag=f"{tag}q2")
                sch.op("vector", lambda e, wt=wt, qt=qt: e.tensor_single_scalar(
                    qt[:, 0:row_len], wt[:, 0:row_len], QMAX, op=ALU.min),
                    reads=[bw], writes=[bq_], tag=f"{tag}q3")
                for g0 in range(0, nblk, 4):
                    ng = min(4, nblk - g0)
                    tp_ = (g0 // 4) % 2
                    to = tp_ * 512
                    sch.op("tensor", lambda e, qt=qt, g0=g0, ng=ng: [
                        e.transpose(PSB[6][:, j * 128:(j + 1) * 128],
                                    qt[:, (g0 + j) * 128:(g0 + j + 1) * 128],
                                    sb["ident_bf"][:, :]) for j in range(ng)][-1],
                        reads=[bq_, bf("ident_bf"), bf("QT0")], writes=[bf("QT0")],
                        tag=f"{tag}tp")
                    sch.op("vector", lambda e, ng=ng, to=to: e.tensor_copy(
                        ttl[:, to:to + ng * 128], PSB[6][:, 0:ng * 128]),
                        reads=[bf("QT0")], writes=[bf(f"qwt{tp_}"), bf("QT0")],
                        tag=f"{tag}te")
                    sch.dma("st", lambda e, qdst=qdst, i=i, g0=g0, ng=ng, to=to:
                        e.dma_start(
                            out=qdst[g0:g0 + ng, :, i * 128:(i + 1) * 128]
                                .rearrange("k p c -> p k c"),
                            in_=ttl[:, to:to + ng * 128].rearrange(
                                "p (k c) -> p k c", k=ng)),
                        reads=[bf(f"qwt{tp_}")], tag=f"{tag}st")

        quant_weight(dI["wg"], NI, 1024, sb["sg"], wgT_dram, "qg")
        quant_weight(dI["wu"], NI, 1024, sb["su"], wuT_dram, "qu")
        quant_weight(dI["wd"], 8, 2816, sb["sd"], wdT_dram, "qd")
        sch.barrier()

    if cfg.stop_after == "wq":
        return _finish_early(nc, sch, ctx)

    actx = ExitStack()
    sba("Wb0", (128, 1024), BF16, actx)
    sba("Wb1", (128, 1024), BF16, actx)
    sba("wla", (128, 8 * 64), BF16, actx)
    sba("wlb", (64, 1024), BF16, actx)
    sba("ala", (128, 8 * 64), BF16, actx)
    sba("alb", (64, 1024), BF16, actx)
    sba("gla", (64, 16 * 128), BF16, actx)
    sba("glb", (128, 1024), BF16, actx)
    for n, src, ck in [("wla", "w_la", 8), ("ala", "a_la", 8)]:
        t, s = sb[n], dI[src]
        sch.dma("gld", lambda e, t=t, s=s, ck=ck: e.dma_start(
            out=t.ap().rearrange("p (k c) -> p k c", k=ck),
            in_=s.rearrange("(k p) c -> p k c", p=128)),
            writes=[bf(n)], tag=f"cast_{n}")
    for n, src in [("wlb", "w_lb"), ("alb", "a_lb"), ("glb", "g_lb")]:
        t, s = sb[n], dI[src]
        sch.dma("gld", lambda e, t=t, s=s: e.dma_start(out=t[:, :], in_=s),
                writes=[bf(n)], tag=f"cast_{n}")
    sch.dma("gld", lambda e: e.dma_start(
        out=sb["gla"].ap().rearrange("j (s m) -> j s m", s=16),
        in_=dI["g_la"].rearrange("(s j) m -> j s m", j=64)),
        writes=[bf("gla")], tag="cast_gla")

    sba("xtm", (128, 2048), F32, actx)
    sba("xfm", (128, NCH * TA), F32, actx)
    sba("delta", (128, NCH * TA), BF16, actx)
    sba("xprev", (128, 8), F32, actx)
    sba("wst", (64, 16 * TA), BF16, actx)
    sba("rst", (64, 16 * TA), BF16, actx)
    sba("kst", (64, 16 * TA), BF16, actx)
    sba("vst", (128, NCH * TA), BF16, actx)
    sba("ahst", (64, 16 * TA), BF16, actx)
    sba("kkst", (64, 16 * TA), BF16, actx)
    sba("gst", (128, NCH * TA), BF16, actx)
    sba("ost", (128, NCH * TA), BF16, actx)
    sba("mixA", (128, NCH * TA), BF16, actx)
    sba("mixB", (128, NCH * TA), BF16, actx)
    sba("lr", (64, 4 * NCH * TA), BF16, actx)
    sba("bk", (64, 4 * NCH * TA), BF16, actx)
    sba("z1", (64, TA), BF16, actx)
    sba("g1", (128, TA), BF16, actx)
    sba("ytile", (128, NCH * TA), YDT, actx)
    sba("prep", (64, 2 * 2 * TA), F32, actx)
    sba("prepE", (64, 2 * 4 * TA), BF16, actx)
    sba("prep2", (64, 2 * 2 * TA), F32, actx)
    sba("eb", (64, 16 * 3 * NCC + 8 * NCC), F32, actx)
    sba("kkr", (64, 2 * 2 * TA), BF16, actx)
    sba("sqh", (64, 2 * 2 * TA), BF16, actx)
    sba("sq", (128, 2 * TA), BF16, actx)
    sba("sq2", (128, 2 * TA), BF16, actx)
    sba("gnstat", (16, 3 * TA), F32, actx)
    sba("lnrow", (1, 3 * TA), F32, actx)
    for par in range(3):
        sba(f"qts{par}", (64, 512), BF16, actx)
        sba(f"m1{par}", (64, 128), BF16, actx)
        sba(f"tm{par}", (64, 512), BF16, actx)
        sba(f"x0{par}", (64, 256), BF16, actx)
        sba(f"x1{par}", (64, 256), BF16, actx)
        sba(f"xf{par}", (64, 256), F32, actx)
        sba(f"sqp{par}", (64, 5 * 256), BF16, actx)
        sba(f"wfm{par}", (64, 128), BF16, actx)
        sba(f"sgm{par}", (64, 128), F32, actx)
        sba(f"sgmb{par}", (64, 128), BF16, actx)
        sba(f"str{par}", (64, 128), BF16, actx)
        sba(f"stt{par}", (64, 128), F32, actx)
    sch.op("vector", lambda e: e.memset(sb["xprev"][:, :], 0.0), writes=[bf("xprev")])

    def mm_group(psum_ap_fn, lhs, lhs_cols, rhs, rhs_cols, nk):
        def fn(e):
            inst = None
            for k in range(nk):
                l0, l1 = lhs_cols(k)
                r0, r1 = rhs_cols(k)
                inst = e.matmul(psum_ap_fn(), lhs[:, l0:l1], rhs[:, r0:r1],
                                start=(k == 0), stop=(k == nk - 1))
            return inst
        return fn

    _sa = cfg.stop_after if isinstance(cfg.stop_after, str) else ""
    _alim = float(_sa[1:]) if _sa.startswith("A") and len(_sa) > 1 else 99
    for t in range(NT_A):
        t0 = t * TA
        xfm, delta = sb["xfm"], sb["delta"]

        for blk_i in range(TA // 128):
            bxt = bf(f"xtm{blk_i % 2}")
            xo = (blk_i % 2) * 1024
            sch.dma("ld", lambda e, blk_i=blk_i, t0=t0, xo=xo: e.dma_start(
                out=sb["xtm"][:, xo:xo + 1024],
                in_=dI["x"][t0 + blk_i * 128: t0 + (blk_i + 1) * 128, :]),
                writes=[bxt], tag="xld")
            for half in range(2):
                sch.op("tensor", lambda e, half=half, xo=xo: [
                    e.transpose(PS[0][:, i * 128:(i + 1) * 128],
                                sb["xtm"][:, xo + (half * 4 + i) * 128:
                                          xo + (half * 4 + i + 1) * 128],
                                sb["ident_f32"][:, :]) for i in range(4)][-1],
                    reads=[bxt, bf("ident_f32"), bf("PS0")], writes=[bf("PS0")], tag="xtp")
                sch.op("vector", lambda e, half=half, blk_i=blk_i: e.tensor_copy(
                    xfm.ap().rearrange("p (c u) -> p c u", c=NCH)[
                        :, half * 4:(half + 1) * 4, blk_i * 128:(blk_i + 1) * 128],
                    PS[0][:, 0:512].rearrange("p (c u) -> p c u", c=4)),
                    reads=[bf("PS0")],
                    writes=[bf(f"xfm{q}") for q in range(half * 4, half * 4 + 4)] +
                           [bf("PS0")], tag="xev")

        if _alim < 1:
            continue
        for c in range(NCH):
            co = c * TA
            sch.op("vector", lambda e, co=co, c=c: e.tensor_sub(
                delta[:, co + 1:co + TA], xfm[:, co:co + TA - 1], xfm[:, co + 1:co + TA]),
                reads=[bf(f"xfm{c}")], writes=[bf(f"delta{c}")], tag="dlt")
            sch.op("vector", lambda e, co=co, c=c: e.tensor_sub(
                delta[:, co:co + 1], sb["xprev"][:, c:c + 1], xfm[:, co:co + 1]),
                reads=[bf(f"xfm{c}"), bf("xprev")], writes=[bf(f"delta{c}")], tag="dlt0")
        sch.op("vector", lambda e: e.tensor_copy(
            sb["xprev"][:, :],
            xfm.ap().rearrange("p (c u) -> p c u", c=NCH)[:, :, TA - 1]),
            reads=[bf(f"xfm{c}") for c in range(NCH)], writes=[bf("xprev")], tag="xpv")

        def emit_mix(mixn, vecn, tag):
            mx = sb[mixn]
            for c in range(NCH):
                co = c * TA
                sch.op("vector", lambda e, co=co, c=c, mx=mx: e.scalar_tensor_tensor(
                    mx[:, co:co + TA], delta[:, co:co + TA], sb[vecn][:, c:c + 1],
                    xfm[:, co:co + TA], op0=ALU.mult, op1=ALU.add),
                    reads=[bf(f"delta{c}"), bf(f"xfm{c}"), bf(vecn), bf(f"{mixn}{c}")],
                    writes=[bf(f"{mixn}{c}")], tag=tag)

        def emit_proj(Wsrc, mixn, outn, tag):
            for f in range(NCH):
                bank, bb = (PS[0], bf("PS0")) if f % 2 == 0 else (PS[1], bf("PS1"))
                wn, bwo = ld_wblk(Wsrc, f, tag + "w")
                sch.op("tensor", mm_group(
                    lambda bank=bank: bank[:, 0:TA], sb[wn],
                    lambda k: (k * 128, (k + 1) * 128),
                    sb[mixn], lambda k: (k * TA, (k + 1) * TA), 8),
                    reads=[bwo, bb] + [bf(f"{mixn}{k}") for k in range(NCH)],
                    writes=[bb], tag=tag)
                sch.op("vector", lambda e, bank=bank, f=f, outn=outn: e.tensor_copy(
                    sb[outn][:, f * TA:(f + 1) * TA], bank[:, 0:TA]),
                    reads=[bb], writes=[bf(f"{outn}{f}"), bb], tag=tag + "e")

        def ld_wblk(Wsrc, f, tag):
            wn = f"Wb{f % 2}"
            bwo = bf(wn)
            sch.dma("gld", lambda e, f=f, wn=wn, Wsrc=Wsrc: e.dma_start(
                out=sb[wn].ap().rearrange("p (k c) -> p k c", k=8),
                in_=dI[Wsrc][:, f * 128:(f + 1) * 128].rearrange("(k p) c -> p k c",
                                                                 p=128)),
                writes=[bwo], tag=tag)
            return wn, bwo

        def emit_proj_hm(Wsrc, mixn, outn, tag):
            for c in range(NCH):
                bank, bb = (PS[0], bf("PS0")) if c % 2 == 0 else (PS[1], bf("PS1"))
                wn, bwo = ld_wblk(Wsrc, c, tag + "w")
                def fn(e, c=c, bank=bank, wn=wn, mixn=mixn):
                    inst = None
                    for h in range(2):
                        for k in range(8):
                            inst = e.matmul(
                                bank[0:64, h * TA:(h + 1) * TA],
                                sb[wn][:, k * 128 + h * 64:k * 128 + (h + 1) * 64],
                                sb[mixn][:, k * TA:(k + 1) * TA],
                                start=(k == 0), stop=(k == 7))
                    return inst
                sch.op("tensor", fn,
                       reads=[bwo, bb] + [bf(f"{mixn}{k}") for k in range(NCH)],
                       writes=[bb], tag=tag)
                sch.op("vector", lambda e, bank=bank, c=c, outn=outn: e.tensor_copy(
                    sb[outn][:, 2 * c * TA:(2 * c + 2) * TA], bank[0:64, 0:2 * TA]),
                    reads=[bb], writes=[bf(f"{outn}{c}"), bb], tag=tag + "e")

        emit_mix("mixA", "x_r", "mxr")
        emit_proj_hm("W_r", "mixA", "rst", "pjr")
        emit_mix("mixB", "x_k", "mxk")
        emit_proj_hm("W_k", "mixB", "kst", "pjk")
        emit_mix("mixA", "x_v", "mxv")
        emit_proj("W_v", "mixA", "vst", "pjv")

        if _alim < 2:
            continue
        emit_mix("mixB", "x_w", "mxw")
        sch.op("tensor", mm_group(lambda: PS[2][0:64, 0:TA], sb["wla"],
                                  lambda k: (k * 64, (k + 1) * 64),
                                  sb["mixB"], lambda k: (k * TA, (k + 1) * TA), 8),
               reads=[bf("wla")] + [bf(f"mixB{k}") for k in range(NCH)] + PS2A,
               writes=PS2A, tag="wl1")
        sch.op("scalar", lambda e: e.activation(sb["z1"][:, 0:TA], PS[2][0:64, 0:TA],
                                                AF.Tanh),
               reads=PS2A, writes=[bf("z1")] + PS2A, tag="wlt")
        for f in range(NCH):
            pb = 64 * (f % 2)
            P3Q = [bf(f"PS3q{f % 2}0"), bf(f"PS3q{f % 2}1")]
            po2 = (f % 2) * 2 * TA
            bp2 = bf(f"prep2{f % 2}")
            def wl2(e, f=f, pb=pb):
                e.matmul(PS[3][pb:pb + 64, 0:TA], sb["wlb"][:, f * 128:f * 128 + 64],
                         sb["z1"][:, 0:TA], start=True, stop=True,
                         tile_position=(0, pb))
                return e.matmul(PS[3][pb:pb + 64, TA:2 * TA],
                                sb["wlb"][:, f * 128 + 64:(f + 1) * 128],
                                sb["z1"][:, 0:TA], start=True, stop=True,
                                tile_position=(0, pb))
            sch.op("tensor", wl2, reads=[bf("wlb"), bf("z1")] + P3Q,
                   writes=P3Q, tag="wl2")
            for h in range(2):
                S = 2 * f + h
                sch.op("scalar", lambda e, h=h, S=S, pb=pb, po2=po2: e.activation(
                    sb["prep2"][:, po2 + h * TA:po2 + (h + 1) * TA],
                    PS[3][pb:pb + 64, h * TA:(h + 1) * TA],
                    AF.Exp, bias=sb["negwlbh"][:, S:S + 1], scale=-1.0),
                    reads=P3Q + [bf("negwlbh"), bp2],
                    writes=[bp2] + (P3Q if h == 1 else []), tag="wls")
            sch.op("vector", lambda e, po2=po2: e.tensor_scalar_add(
                sb["prep2"][:, po2:po2 + 2 * TA], sb["prep2"][:, po2:po2 + 2 * TA], 1.0),
                reads=[bp2], writes=[bp2], tag="wls1")
            sch.op("scalar", lambda e, po2=po2: e.activation(
                sb["prep2"][:, po2:po2 + 2 * TA], sb["prep2"][:, po2:po2 + 2 * TA],
                AF.Ln),
                reads=[bp2], writes=[bp2], tag="wls2")
            sch.op("vector", lambda e, f=f, po2=po2: e.tensor_scalar(
                sb["wst"][:, 2 * f * TA:(2 * f + 2) * TA],
                sb["prep2"][:, po2:po2 + 2 * TA],
                -1.0, -0.5, op0=ALU.mult, op1=ALU.add),
                reads=[bp2], writes=[bf(f"wst{f}")], tag="wlf")

        if _alim < 2.2:
            continue
        emit_mix("mixB", "x_a", "mxa")
        sch.op("tensor", mm_group(lambda: PS[2][0:64, 0:TA], sb["ala"],
                                  lambda k: (k * 64, (k + 1) * 64),
                                  sb["mixB"], lambda k: (k * TA, (k + 1) * TA), 8),
               reads=[bf("ala")] + [bf(f"mixB{k}") for k in range(NCH)] + PS2A,
               writes=PS2A, tag="al1")
        sch.op("vector", lambda e: e.tensor_copy(sb["z1"][:, 0:TA], PS[2][0:64, 0:TA]),
               reads=PS2A, writes=[bf("z1")] + PS2A, tag="alc")
        for f in range(NCH):
            pb = 64 * (f % 2)
            P3Q = [bf(f"PS3q{f % 2}0"), bf(f"PS3q{f % 2}1")]
            def al2(e, f=f, pb=pb):
                e.matmul(PS[3][pb:pb + 64, 0:TA], sb["alb"][:, f * 128:f * 128 + 64],
                         sb["z1"][:, 0:TA], start=True, stop=True,
                         tile_position=(0, pb))
                return e.matmul(PS[3][pb:pb + 64, TA:2 * TA],
                                sb["alb"][:, f * 128 + 64:(f + 1) * 128],
                                sb["z1"][:, 0:TA], start=True, stop=True,
                                tile_position=(0, pb))
            sch.op("tensor", al2, reads=[bf("alb"), bf("z1")] + P3Q,
                   writes=P3Q, tag="al2")
            for h in range(2):
                S = 2 * f + h
                sch.op("scalar", lambda e, h=h, S=S, pb=pb: e.activation(
                    sb["ahst"][:, S * TA:(S + 1) * TA],
                    PS[3][pb:pb + 64, h * TA:(h + 1) * TA],
                    AF.Sigmoid, bias=sb["albh"][:, S:S + 1]),
                    reads=P3Q + [bf("albh")],
                    writes=[bf(f"ahst{f}")] + (P3Q if h == 1 else []), tag="als")

        if _alim < 2.5:
            continue
        sch.op("tensor", mm_group(lambda: PS[2][:, 0:TA], sb["gla"],
                                  lambda k: (k * 128, (k + 1) * 128),
                                  sb["rst"], lambda k: (k * TA, (k + 1) * TA), 16),
               reads=[bf("gla")] + [bf(f"rst{k}") for k in range(NCH)] + PS2A,
               writes=PS2A, tag="gl1")
        sch.op("scalar", lambda e: e.activation(sb["g1"][:, 0:TA], PS[2][:, 0:TA],
                                                AF.Sigmoid),
               reads=PS2A, writes=[bf("g1")] + PS2A, tag="gls")
        if _alim < 2.7:
            continue
        for f in range(NCH):
            vo = 0 if FLATG else (f % 2) * TA
            P3Q = ([bf("PS3q00"), bf("PS3q10")] if FLATG else
                   [bf(f"PS3q0{f % 2}"), bf(f"PS3q1{f % 2}")])
            sch.op("tensor", lambda e, f=f, vo=vo: e.matmul(
                PS[3][:, vo:vo + TA], sb["glb"][:, f * 128:(f + 1) * 128],
                sb["g1"][:, 0:TA], start=True, stop=True),
                reads=[bf("glb"), bf("g1")] + P3Q, writes=P3Q, tag="gl2")
            sch.op("vector", lambda e, f=f, vo=vo: e.tensor_scalar(
                sb["gst"][:, f * TA:(f + 1) * TA], PS[3][:, vo:vo + TA],
                sb["g_lbias"][:, f:f + 1], None, op0=ALU.add),
                reads=P3Q + [bf("g_lbias")],
                writes=[bf(f"gst{f}")] + P3Q, tag="glf")

        if _alim < 3:
            continue
        for c in range(NCH):
            ho = 2 * c * TA
            pk = (c % 2) * 2 * TA
            bkr, bsh = bf(f"kkr{c % 2}"), bf(f"sqh{c % 2}")
            for h in range(2):
                S = 2 * c + h
                sch.op("vector", lambda e, ho=ho, h=h, S=S, pk=pk, c=c: e.tensor_scalar(
                    sb["kkr"][:, pk + h * TA:pk + (h + 1) * TA],
                    sb["kst"][:, ho + h * TA:ho + (h + 1) * TA],
                    sb["kkh"][:, S:S + 1], None, op0=ALU.mult),
                    reads=[bf(f"kst{c}"), bf("kkh"), bkr], writes=[bkr], tag="kk1")
            if _alim >= 2.81:
                sch.op("scalar", lambda e, pk=pk: e.activation(
                    sb["sqh"][:, pk:pk + 2 * TA], sb["kkr"][:, pk:pk + 2 * TA],
                    AF.Square),
                    reads=[bkr, bsh], writes=[bsh], tag="kk2")
            if _alim >= 2.83:
                def kk3(e, c=c, pk=pk):
                    inst = None
                    for h in range(2):
                        S = 2 * c + h
                        inst = e.matmul(PS[5][0:16, 0:TA],
                                        sb["bsel_hm"][:, 16 * S:16 * (S + 1)],
                                        sb["sqh"][:, pk + h * TA:pk + (h + 1) * TA],
                                        start=(S == 0), stop=(S == 15))
                    return inst
                sch.op("tensor", kk3, reads=[bsh, bf("bsel_hm"), bf("PS5s")],
                       writes=[bf("PS5s")], tag="kk3")
            if _alim >= 2.86:
                sch.op("scalar", lambda e, ho=ho, pk=pk: e.copy(
                    sb["kkst"][:, ho:ho + 2 * TA], sb["kkr"][:, pk:pk + 2 * TA]),
                    reads=[bkr], writes=[bf(f"kkst{c}"), bkr], tag="kk4")
        g_ = sb["gnstat"]
        if _alim < 2.8:
            continue
        sch.op("scalar", lambda e: e.activation(g_[0:16, 0:TA], PS[5][0:16, 0:TA], AF.Sqrt),
               reads=[bf("PS5s"), bf("gnstat")], writes=[bf("gnstat"), bf("PS5s")],
               tag="kk5")
        sch.op("vector", lambda e: e.tensor_single_scalar(
            g_[0:16, 0:TA], g_[0:16, 0:TA], 1e-12, op=ALU.max),
            reads=[bf("gnstat")], writes=[bf("gnstat")], tag="kk6")
        sch.op("vector", lambda e: e.reciprocal(g_[0:16, TA:2 * TA], g_[0:16, 0:TA]),
               reads=[bf("gnstat")], writes=[bf("gnstat")], tag="kk7")
        for c in range(NCH):
            ho = 2 * c * TA
            pb = 0 if FLATG else 64 * (c % 2)
            pk = (c % 2) * 2 * TA
            P3Q = ([bf("PS3q00"), bf("PS3q01")] if FLATG else
                   [bf(f"PS3q{c % 2}0"), bf(f"PS3q{c % 2}1")])
            bp2 = bf(f"prep2{c % 2}")
            def kk8(e, c=c, pb=pb):
                inst = None
                for h in range(2):
                    S = 2 * c + h
                    inst = e.matmul(PS[3][pb:pb + 64, h * TA:(h + 1) * TA],
                                    sb["sel_hm"][0:16, 64 * S:64 * (S + 1)],
                                    g_[0:16, TA:2 * TA], start=True, stop=True,
                                    tile_position=(0, pb))
                return inst
            sch.op("tensor", kk8, reads=[bf("gnstat"), bf("sel_hm")] + P3Q,
                   writes=P3Q, tag="kk8")
            sch.op("vector", lambda e, ho=ho, c=c, pb=pb: e.tensor_mul(
                sb["kkst"][:, ho:ho + 2 * TA], sb["kkst"][:, ho:ho + 2 * TA],
                PS[3][pb:pb + 64, 0:2 * TA]),
                reads=P3Q + [bf(f"kkst{c}")],
                writes=[bf(f"kkst{c}")] + P3Q, tag="kk9")
            if _alim < 2.9:
                continue
            for h in range(2):
                S = 2 * c + h
                sch.op("vector", lambda e, ho=ho, h=h, S=S, pk=pk, c=c: e.tensor_scalar(
                    sb["prep2"][:, pk + h * TA:pk + (h + 1) * TA],
                    sb["ahst"][:, ho + h * TA:ho + (h + 1) * TA],
                    sb["kah"][:, S:S + 1], sb["onemkah"][:, S:S + 1],
                    op0=ALU.mult, op1=ALU.add),
                    reads=[bf(f"ahst{c}"), bf("kah"), bf("onemkah"), bp2],
                    writes=[bp2], tag="kh1")
            sch.op("vector", lambda e, ho=ho, pk=pk, c=c: e.tensor_mul(
                sb["kst"][:, ho:ho + 2 * TA], sb["kst"][:, ho:ho + 2 * TA],
                sb["prep2"][:, pk:pk + 2 * TA]),
                reads=[bp2, bf(f"kst{c}")], writes=[bf(f"kst{c}")], tag="kh2")

        if _alim < 4:
            continue
        def prep_ops(PSCH, c):
            ho = 2 * c * TA
            P_, P2 = sb["prep"], sb["prep2"]
            W2 = 2 * TA
            po = (c % 2) * W2
            poE = (c % 2) * 2 * W2
            bP, bP2 = bf(f"prep{c % 2}"), bf(f"prep2{c % 2}")
            beb_, bebt = bf(f"eb{c}"), bf(f"ebt{c % 2}")
            for h in range(2):
                PSCH.op("vector", lambda e, ho=ho, h=h, po=po: e.tensor_tensor_scan(
                    P_[:, po + h * TA:po + (h + 1) * TA],
                    sb["wst"][:, ho + h * TA:ho + (h + 1) * TA],
                    sb["sm"][0:64, 0:1].broadcast_to([64, TA]), 0.0,
                    op0=ALU.add, op1=ALU.add),
                    reads=[bf(f"wst{c}"), bf("zero1"), bP], writes=[bP], tag="pg")
            ebc = 2 * c * 3 * NCC
            ebt = 16 * 3 * NCC + (c % 2) * 4 * NCC
            def gsel(col, po=po):
                return P_.ap()[:, po:po + W2].rearrange(
                    "p (s cc u) -> p s cc u", s=2, cc=NCC)[:, :, :, col]
            def ebsl(ofs, ebc=ebc):
                return sb["eb"].ap()[:, ebc:ebc + 6 * NCC].rearrange(
                    "p (s k cc) -> p s k cc", s=2, k=3)[:, :, ofs, :]
            def ebtmp(i, ebt=ebt):
                return sb["eb"].ap()[:, ebt + i * 2 * NCC:ebt + (i + 1) * 2 * NCC
                                     ].rearrange("p (s cc) -> p s cc", s=2)
            PSCH.op("vector", lambda e, ebtmp=ebtmp: e.memset(ebtmp(0)[:, :, 0:1], 0.0),
                   reads=[bebt], writes=[bebt], tag="pgp0")
            if NCC > 1:
                PSCH.op("vector", lambda e, gsel=gsel, ebtmp=ebtmp: e.tensor_copy(
                    ebtmp(0)[:, :, 1:NCC], gsel(63)[:, :, 0:NCC - 1]),
                    reads=[bP, bebt], writes=[bebt], tag="pgp1")
            PSCH.op("vector", lambda e, gsel=gsel, ebtmp=ebtmp: e.tensor_sub(
                ebtmp(1), gsel(31), ebtmp(0)),
                reads=[bP, bebt], writes=[bebt], tag="pgp2")
            PSCH.op("scalar", lambda e, ebsl=ebsl, ebtmp=ebtmp: e.activation(
                ebsl(0), ebtmp(1), AF.Exp),
                reads=[bebt], writes=[beb_], tag="per")
            PSCH.op("vector", lambda e, gsel=gsel, ebtmp=ebtmp: e.tensor_sub(
                ebtmp(1), gsel(63), ebtmp(0)),
                reads=[bP, bebt, beb_], writes=[bebt], tag="pgp3")
            PSCH.op("scalar", lambda e, ebsl=ebsl, ebtmp=ebtmp: e.activation(
                ebsl(2), ebtmp(1), AF.Exp),
                reads=[bebt], writes=[beb_], tag="peg")
            PSCH.op("vector", lambda e, gsel=gsel, ebtmp=ebtmp: e.tensor_sub(
                ebtmp(1), gsel(63), gsel(31)),
                reads=[bP, beb_, bebt], writes=[bebt], tag="ped0")
            PSCH.op("scalar", lambda e, ebsl=ebsl, ebtmp=ebtmp: e.activation(
                ebsl(1), ebtmp(1), AF.Exp),
                reads=[bebt], writes=[beb_], tag="ped")
            PSCH.op("vector", lambda e, po=po: e.tensor_sub(
                P_[:, po:po + W2], P_[:, po:po + W2],
                P_.ap()[:, po:po + W2].rearrange("p (s cc u) -> p s cc u", s=2, cc=NCC)
                [:, :, :, 31:32].broadcast_to([64, 2, NCC, LC])),
                reads=[bP], writes=[bP], tag="pgr")
            PSCH.op("vector", lambda e, ho=ho, P2=P2, W2=W2, po=po: e.tensor_sub(
                P2[:, po:po + W2], P_[:, po:po + W2], sb["wst"][:, ho:ho + W2]),
                reads=[bP, bf(f"wst{c}"), bP2], writes=[bP2], tag="pe2s")
            PSCH.op("scalar", lambda e, po=po: e.activation(
                P2[:, po:po + W2], P2[:, po:po + W2], AF.Exp),
                reads=[bP2], writes=[bP2], tag="pe2")
            PE_ = sb["prepE"]
            bPE = bf(f"prepE{c % 2}")
            PSCH.op("scalar", lambda e, PE_=PE_, po=po, poE=poE: e.activation(
                PE_[:, poE:poE + W2], P_[:, po:po + W2], AF.Exp),
                reads=[bP, bPE], writes=[bPE], tag="pe1")
            PSCH.op("scalar", lambda e, PE_=PE_, po=po, poE=poE: e.activation(
                PE_[:, poE + W2:poE + 2 * W2], P_[:, po:po + W2], AF.Exp, scale=-1.0),
                reads=[bP], writes=[bPE], tag="pen")
            lrc0 = 4 * TA * c

            def ilv(tsr, off, lrc0=lrc0):
                return tsr.ap()[:, lrc0:lrc0 + 2 * W2].rearrange(
                    "p (s cc two u) -> p s cc two u", s=2, cc=NCC, two=2)[:, :, :, off, :]

            def ch(tsr, c0=None, ho=ho):
                base = ho if c0 is None else c0
                return tsr.ap()[:, base:base + W2].rearrange(
                    "p (s cc u) -> p s cc u", s=2, cc=NCC)

            blr_, bbk_ = bf(f"lr{c}"), bf(f"bk{c}")
            PSCH.op("vector", lambda e, ilv=ilv, ch=ch, P2=P2, po=po: e.tensor_tensor(
                ilv(sb["lr"], 0), ch(sb["kkst"]), ch(P2, po), op=ALU.mult),
                reads=[bf(f"kkst{c}"), bP2, blr_], writes=[blr_], tag="psa")
            PSCH.op("vector", lambda e, ilv=ilv, ch=ch, poE=poE: e.tensor_tensor(
                ilv(sb["lr"], 1), ch(sb["rst"]), ch(sb["prepE"], poE), op=ALU.mult),
                reads=[bf(f"rst{c}"), bPE, blr_], writes=[blr_], tag="psr")
            PSCH.op("vector", lambda e, ho=ho, P2=P2, W2=W2, po=po: e.scalar_tensor_tensor(
                P2[:, po:po + W2], sb["kkst"][:, ho:ho + W2], -1.0,
                sb["ahst"][:, ho:ho + W2],
                op0=ALU.mult, op1=ALU.mult),
                reads=[bf(f"kkst{c}"), bf(f"ahst{c}"), bP2], writes=[bP2], tag="pbp")
            PSCH.op("vector", lambda e, ilv=ilv, ch=ch, P2=P2, W2=W2, po=po, poE=poE:
                e.tensor_tensor(
                    ilv(sb["bk"], 0), ch(P2, po), ch(sb["prepE"], poE + W2),
                    op=ALU.mult),
                reads=[bP2, bPE, bbk_], writes=[bbk_], tag="psb")
            PSCH.op("vector", lambda e, ilv=ilv, ch=ch, W2=W2, poE=poE: e.tensor_tensor(
                ilv(sb["bk"], 1), ch(sb["kst"]), ch(sb["prepE"], poE + W2),
                op=ALU.mult),
                reads=[bf(f"kst{c}"), bPE, bbk_], writes=[bbk_], tag="psk")

        for c0 in range(0, NCH, 2):
            pcols = []
            for c in (c0, c0 + 1):
                pcol = OpCollector(sch)
                prep_ops(pcol, c)
                pcols.append(pcol)
            zip_emit(sch, pcols)

        if _alim < 5:
            continue
        for cc in range(NCC):
            for c0 in range(0, NCH, 3):
                cols = []
                for c in range(c0, min(c0 + 3, NCH)):
                    col = OpCollector(sch)
                    unit_scan(col, sb, PS, PSB, WKbank, QTbank, bf, cfg, cc, c, c % 3)
                    cols.append(col)
                zip_emit(sch, cols)

        if _alim < 6:
            continue
        post_scan(sch, sb, PS, bf, dI, cfg, t, y_dram)

    if isinstance(cfg.stop_after, str) and cfg.stop_after.startswith("A"):
        actx.close()
        return _finish_early(nc, sch, ctx)
    finalize_scale(sch, sb, PS, bf, dI, cfg, "ymax", cc_io[0], cc_io[1], "clipx", 0)
    sch.barrier()
    actx.close()

    b1 = ExitStack()
    sba("WgT", (128, 8 * 2816), BF16, b1)
    sba("WuT", (128, 8 * 2816), BF16, b1)
    sba("yt", (128, NCH * TB), YDT, b1)
    sba("xq", (128, NCH * TB), BF16, b1)
    sba("qf", (128, 2 * TB), F32, b1)
    sba("hb", (128, 2 * TB), HDT, b1)
    sba("silu", (128, 2 * TB), BF16, b1)
    for k in range(8):
        sch.dma("ldw", lambda e, k=k: e.dma_start(
            out=sb["WgT"][:, k * 2816:(k + 1) * 2816], in_=wgT_dram[k]),
            writes=[bf("WgT")], tag="lWg")
        sch.dma("ldw", lambda e, k=k: e.dma_start(
            out=sb["WuT"][:, k * 2816:(k + 1) * 2816], in_=wuT_dram[k]),
            writes=[bf("WuT")], tag="lWu")
    sch.op("vector", lambda e: e.tensor_scalar(
        sb["sgx"][:, :], sb["sg"][:, :], sb["scal"][:, 0:1], None, op0=ALU.mult),
        reads=[bf("sg"), bf("scal")], writes=[bf("sgx")], tag="sgx")
    sch.op("vector", lambda e: e.tensor_scalar(
        sb["sux"][:, :], sb["su"][:, :], sb["scal"][:, 0:1], None, op0=ALU.mult),
        reads=[bf("su"), bf("scal")], writes=[bf("sux")], tag="sux")

    for t in range(NT_B):
        t0 = t * TB
        for c in range(NCH):
            sch.dma("ld", lambda e, c=c, t0=t0: e.dma_start(
                out=sb["yt"][:, c * TB:(c + 1) * TB], in_=y_dram[c, :, t0:t0 + TB]),
                writes=[bf(f"yt{c}")], tag="lys")
        for c in range(NCH):
            co = c * TB
            po = (c % 2) * TB
            bq_ = bf(f"qf{c % 2}")
            sch.op("vector", lambda e, co=co, po=po, c=c: e.tensor_scalar(
                sb["qf"][:, po:po + TB], sb["yt"][:, co:co + TB], sb["scal"][:, 1:2],
                MAGIC, op0=ALU.mult, op1=ALU.add),
                reads=[bf(f"yt{c}"), bf("scal"), bq_], writes=[bq_], tag="xq1")
            sch.op("vector", lambda e, po=po: e.tensor_scalar(
                sb["qf"][:, po:po + TB], sb["qf"][:, po:po + TB], -MAGIC, -(QMAX + 1.0),
                op0=ALU.add, op1=ALU.max), reads=[bq_], writes=[bq_], tag="xq2")
            sch.op("vector", lambda e, co=co, po=po: e.tensor_single_scalar(
                sb["xq"][:, co:co + TB], sb["qf"][:, po:po + TB], QMAX, op=ALU.min),
                reads=[bq_, bf(f"xq{c}")], writes=[bf(f"xq{c}")], tag="xq3")
        BXQ = [bf(f"xq{k}") for k in range(NCH)]
        for i in range(NI):
            pi = i % 2
            bhb, bsil = bf(f"hb{pi}"), bf(f"silu{pi}")
            sch.op("tensor", mm_group(
                lambda: PS[0][:, 0:TB], sb["WgT"],
                lambda k, i=i: (k * 2816 + i * 128, k * 2816 + (i + 1) * 128),
                sb["xq"], lambda k: (k * TB, (k + 1) * TB), 8),
                reads=[bf("WgT"), bf("PS0")] + BXQ, writes=[bf("PS0")], tag="gmm")
            sch.op("tensor", mm_group(
                lambda: PS[1][:, 0:TB], sb["WuT"],
                lambda k, i=i: (k * 2816 + i * 128, k * 2816 + (i + 1) * 128),
                sb["xq"], lambda k: (k * TB, (k + 1) * TB), 8),
                reads=[bf("WuT"), bf("PS1")] + BXQ, writes=[bf("PS1")], tag="umm")
            sch.op("scalar", lambda e, i=i, pi=pi: e.activation(
                sb["silu"][:, pi * TB:pi * TB + TB], PS[0][:, 0:TB], AF.Silu,
                scale=sb["sgx"][:, i:i + 1]),
                reads=[bf("PS0"), bf("sgx"), bsil], writes=[bsil, bf("PS0")],
                tag="sil")
            sch.op("vector", lambda e, i=i, pi=pi: e.scalar_tensor_tensor(
                sb["hb"][:, pi * TB:pi * TB + TB], PS[1][:, 0:TB], sb["sux"][:, i:i + 1],
                sb["silu"][:, pi * TB:pi * TB + TB], op0=ALU.mult, op1=ALU.mult),
                reads=[bf("PS1"), bf("sux"), bsil, bhb],
                writes=[bhb, bf("PS1")], tag="hm")
            sch.op("vector", lambda e, pi=pi: e.reduce_max(
                sb["sm"][:, 5:6], sb["hb"][:, pi * TB:pi * TB + TB], axis=AX.X,
                apply_absolute_value=True),
                reads=[bhb], writes=[bf("hmx")], tag="hx1")
            sch.op("vector", lambda e: e.tensor_max(
                sb["hmax"][:, 0:1], sb["hmax"][:, 0:1], sb["sm"][:, 5:6]),
                reads=[bf("hmx"), bf("hmax")], writes=[bf("hmax")], tag="hx2")
            sch.dma("st", lambda e, i=i, t0=t0, pi=pi: e.dma_start(
                out=h_dram[i, :, t0:t0 + TB], in_=sb["hb"][:, pi * TB:pi * TB + TB]),
                reads=[bhb], tag="sth")

    if cfg.stop_after == "B1":
        b1.close()
        return _finish_early(nc, sch, ctx)
    finalize_scale(sch, sb, PS, bf, dI, cfg, "hmax", cc_io[2], cc_io[3], "clipd", 2)
    sch.barrier()
    b1.close()

    b2 = ExitStack()
    sba("WdT", (128, 22 * 1024), BF16, b2)
    sba("yt2", (128, NCH * TB), YDT, b2)
    sba("ht", (128, NI * TB), HDT, b2)
    sba("hq", (128, NI * TB), BF16, b2)
    sba("qf2", (128, 2 * TB), F32, b2)
    sba("zpre", (128, NCH * TB), F32, b2)
    sba("sqb", (128, 2 * TB), BF16, b2)
    sba("zfin", (128, NCH * TB), F32, b2)
    sba("ztm", (128, 2048), F32, b2)
    sba("lnrow", (1, 3 * TB), F32, b2)
    for k in range(22):
        sch.dma("ldw", lambda e, k=k: e.dma_start(
            out=sb["WdT"][:, k * 1024:(k + 1) * 1024], in_=wdT_dram[k]),
            writes=[bf("WdT")], tag="lWd")
    sch.op("vector", lambda e: e.tensor_scalar(
        sb["sdh"][:, :], sb["sd"][:, :], sb["scal"][:, 2:3], None, op0=ALU.mult),
        reads=[bf("sd"), bf("scal")], writes=[bf("sdh")], tag="sdh")

    for t in range(NT_B):
        t0 = t * TB
        for c in range(NCH):
            sch.dma("ld", lambda e, c=c, t0=t0: e.dma_start(
                out=sb["yt2"][:, c * TB:(c + 1) * TB], in_=y_dram[c, :, t0:t0 + TB]),
                writes=[bf(f"yt2{c}")], tag="ly2")
        for i in range(NI):
            sch.dma("ld", lambda e, i=i, t0=t0: e.dma_start(
                out=sb["ht"][:, i * TB:(i + 1) * TB], in_=h_dram[i, :, t0:t0 + TB]),
                writes=[bf(f"ht{i}")], tag="lh")
        for i in range(NI):
            co = i * TB
            po = (i % 2) * TB
            bq2 = bf(f"qf2{i % 2}")
            sch.op("vector", lambda e, co=co, po=po, i=i: e.tensor_scalar(
                sb["qf2"][:, po:po + TB], sb["ht"][:, co:co + TB], sb["scal"][:, 3:4],
                MAGIC, op0=ALU.mult, op1=ALU.add),
                reads=[bf(f"ht{i}"), bf("scal"), bq2], writes=[bq2], tag="hq1")
            sch.op("vector", lambda e, po=po: e.tensor_scalar(
                sb["qf2"][:, po:po + TB], sb["qf2"][:, po:po + TB], -MAGIC,
                -(QMAX + 1.0),
                op0=ALU.add, op1=ALU.max), reads=[bq2], writes=[bq2], tag="hq2")
            sch.op("vector", lambda e, co=co, po=po, i=i: e.tensor_single_scalar(
                sb["hq"][:, co:co + TB], sb["qf2"][:, po:po + TB], QMAX, op=ALU.min),
                reads=[bq2, bf(f"hq{i}")], writes=[bf(f"hq{i}")], tag="hq3")
        BHQ = [bf(f"hq{k}") for k in range(NI)]
        for f in range(NCH):
            bank, bb = (PS[0], bf("PS0")) if f % 2 == 0 else (PS[1], bf("PS1"))
            sch.op("tensor", mm_group(
                lambda bank=bank: bank[:, 0:TB], sb["WdT"],
                lambda k, f=f: (k * 1024 + f * 128, k * 1024 + (f + 1) * 128),
                sb["hq"], lambda k: (k * TB, (k + 1) * TB), NI),
                reads=[bf("WdT"), bb] + BHQ, writes=[bb], tag="dmm")
            sch.op("vector", lambda e, f=f, bank=bank: e.scalar_tensor_tensor(
                sb["zpre"][:, f * TB:(f + 1) * TB], bank[:, 0:TB], sb["sdh"][:, f:f + 1],
                sb["yt2"][:, f * TB:(f + 1) * TB], op0=ALU.mult, op1=ALU.add),
                reads=[bb, bf("sdh"), bf(f"yt2{f}"), bf(f"zpre{f}")],
                writes=[bf(f"zpre{f}"), bb], tag="dre")
        emit_ln(sch, sb, PS, bf, cfg, "zpre", "zfin", TB, "sqb")
        for blk_i in range(TB // 128):
            zo = (blk_i % 2) * 1024
            bzt = bf(f"ztm{blk_i % 2}")
            for half in range(2):
                sch.op("tensor", lambda e, blk_i=blk_i, half=half: [
                    e.transpose(PS[1][:, i * 128:(i + 1) * 128],
                                sb["zfin"][:, (half * 4 + i) * TB + blk_i * 128:
                                           (half * 4 + i) * TB + (blk_i + 1) * 128],
                                sb["ident_f32"][:, :]) for i in range(4)][-1],
                    reads=[bf(f"zfin{q}") for q in range(half * 4, half * 4 + 4)] +
                          [bf("ident_f32"), bf("PS1")], writes=[bf("PS1")],
                    tag="ztp")
                sch.op("vector", lambda e, half=half, zo=zo: e.tensor_copy(
                    sb["ztm"][:, zo + half * 512:zo + (half + 1) * 512], PS[1][:, 0:512]),
                    reads=[bf("PS1"), bzt], writes=[bzt, bf("PS1")], tag="zev")
            sch.dma("st", lambda e, t0=t0, blk_i=blk_i, zo=zo: e.dma_start(
                out=z_out[t0 + blk_i * 128:t0 + (blk_i + 1) * 128, :],
                in_=sb["ztm"][:, zo:zo + 1024]),
                reads=[bzt], tag="stz")
    b2.close()

    for i in range(sch.DMA_ROT):
        if sch.dmaq["st"][2][i] > 0:
            sch.items["sync"].append(("wait", (sch.dmaq["st"][1][i],
                                               sch.dmaq["st"][2][i] * 16)))
    global _LAST_SCHED
    _LAST_SCHED = sch
    sch.finalize()
    ctx.close()
    return nc



def unit_scan(sch, sb, PS, PSB, WKbank, QTbank, bf, cfg, cc, c, par):
    ulim = getattr(cfg, "ulim", 99)
    _ucnt = [0]
    _sch_op = sch.op

    def _op(*a, **k):
        if _ucnt[0] >= ulim:
            return None
        _ucnt[0] += 1
        return _sch_op(*a, **k)

    class _S:
        op = staticmethod(_op)
        buf = sch.buf
    sch = _S()

    TA, LC, NCC, NCH = cfg.TA, cfg.LC, cfg.NCC, cfg.NCH

    def lrc(h):
        return (2 * c + h) * 2 * TA + cc * 128
    vco = c * TA + cc * LC
    ebb = [(2 * c + h) * 3 * NCC for h in range(2)]
    qts, m1, tm = sb[f"qts{par}"], sb[f"m1{par}"], sb[f"tm{par}"]
    x0, x1 = sb[f"x0{par}"], sb[f"x1{par}"]
    sqp, wfm = sb[f"sqp{par}"], sb[f"wfm{par}"]
    sgm, stR, stt = sb[f"sgm{par}"], sb[f"str{par}"], sb[f"stt{par}"]
    bq, bm1, btm = bf(f"qts{par}"), bf(f"m1{par}"), bf(f"tm{par}")
    bx = [bf(f"x0{par}"), bf(f"x1{par}")]
    bsq, bwfm = bf(f"sqp{par}"), bf(f"wfm{par}")
    bsgm, bstR, bstt = bf(f"sgm{par}"), bf(f"str{par}"), bf(f"stt{par}")
    blr, bbk, bSt = bf(f"lr{c}"), bf(f"bk{c}"), bf(f"St_{c}")
    beb = bf(f"eb{c}")
    St = sb["St"]
    stc = 2 * c * 64
    QT = QTbank[par]
    QTB = PSB[{0: 6, 1: 7, 2: 0}[par]]
    WK = WKbank[par]
    WKB = PSB[{0: 4, 1: 2, 2: 1}[par]]
    bQT = bf(f"QT{par}")
    bR1, bR2, bR3 = bf(f"WK{par}r1"), bf(f"WK{par}r2"), bf(f"WK{par}r3")
    O_ = PS[5]
    oq = c % 3
    og0 = (c // 3) * 3
    ong = min(3, NCH - og0)
    oc0 = 256 + oq * 64
    bO = bf(f"PS5o{oq}")

    def quad(e):
        inst = None
        for h in range(2):
            lo = lrc(h)
            for part in range(2):
                colo = h * 256 + part * 128
                inst = e.matmul(QT[0:64, colo:colo + 128],
                                sb["bk"][:, lo + part * 64:lo + (part + 1) * 64],
                                sb["lr"][:, lo:lo + 128], start=True, stop=True)
        return inst
    sch.op("tensor", quad, reads=[blr, bbk, bQT], writes=[bQT], tag="uq")
    sch.op("vector", lambda e: e.tensor_tensor(
        qts.ap().rearrange("p (a b) -> p a b", a=4),
        QT[0:64, 0:512].rearrange("p (a b) -> p a b", a=4),
        sb["mask_quadT"].ap().unsqueeze(1).broadcast_to([64, 4, 128]),
        op=ALU.mult),
        reads=[bQT, bf("mask_quadT")], writes=[bq, bQT], tag="uqe")

    def fwd(e):
        inst = None
        for h in range(2):
            lo = lrc(h)
            inst = e.matmul(WK[0:64, h * 64:(h + 1) * 64],
                            sb["lr"][:, lo:lo + 64], sb["bk"][:, lo:lo + 64],
                            start=True, stop=True)
        return inst
    sch.op("tensor", fwd, reads=[blr, bbk, bR1], writes=[bR1], tag="uf")
    sch.op("vector", lambda e: e.tensor_tensor(
        m1.ap().rearrange("p (a b) -> p a b", a=2),
        WK[0:64, 0:128].rearrange("p (a b) -> p a b", a=2),
        sb["mask_sl"].ap().unsqueeze(1).broadcast_to([64, 2, 64]), op=ALU.mult),
        reads=[bR1, bf("mask_sl")], writes=[bm1, bR1], tag="ufe")

    def tps(e):
        e.transpose(QTB[0:64, 0:64], sb["lr"][:, lrc(0):lrc(0) + 64],
                    sb["ident_bf"][0:64, 0:64])
        e.transpose(QTB[0:64, 64:128], sb["lr"][:, lrc(1):lrc(1) + 64],
                    sb["ident_bf"][0:64, 0:64])
        e.transpose(QTB[0:64, 128:256], sb["vst"][:, vco:vco + 64],
                    sb["ident_bf"][:, :])
        e.transpose(QTB[0:64, 256:320], sb["bk"][:, lrc(0):lrc(0) + 64],
                    sb["ident_bf"][0:64, 0:64])
        e.transpose(QTB[0:64, 320:384], sb["bk"][:, lrc(1):lrc(1) + 64],
                    sb["ident_bf"][0:64, 0:64])
        e.transpose(QTB[0:64, 384:448], sb["bk"][:, lrc(0) + 64:lrc(0) + 128],
                    sb["ident_bf"][0:64, 0:64])
        return e.transpose(QTB[0:64, 448:512], sb["bk"][:, lrc(1) + 64:lrc(1) + 128],
                           sb["ident_bf"][0:64, 0:64])
    sch.op("tensor", tps, reads=[blr, bbk, bf(f"vst{c}"), bf("ident_bf"), bQT],
           writes=[bQT], tag="utp")
    sch.op("scalar", lambda e: e.copy(tm[:, :], QTB[0:64, 0:512]),
           reads=[bQT], writes=[btm, bQT], tag="utme")

    def nv(e):
        e.matmul(WK[0:64, 0:64], qts[:, 128:192], tm[:, 128:192], start=True, stop=True)
        return e.matmul(WK[0:64, 64:128], qts[:, 384:448], tm[:, 192:256],
                        start=True, stop=True)
    sch.op("tensor", nv, reads=[bq, btm, bR1], writes=[bR1], tag="unv")

    def x0a(e):
        e.tensor_copy(x0.ap().rearrange("p (h q) -> p h q", h=2)[:, :, 0:64],
                      tm.ap()[:, 0:128].rearrange("p (h q) -> p h q", h=2))
        return e.tensor_copy(x0.ap().rearrange("p (h q) -> p h q", h=2)[:, :, 64:128],
                             WK[0:64, 0:128].rearrange("p (h q) -> p h q", h=2))
    sch.op("vector", x0a, reads=[btm, bR1, bx[0]], writes=[bx[0], bR1], tag="ux0")

    def p1(h):
        return qts[:, h * 256:h * 256 + 64]
    for lvl in range(5):
        def sq_(e, lvl=lvl):
            inst = None
            for h in range(2):
                if lvl == 0:
                    mp, pp = m1[:, h * 64:(h + 1) * 64], p1(h)
                else:
                    b0 = (lvl - 1) * 256
                    mp = sqp[:, b0 + h * 64:b0 + (h + 1) * 64]
                    pp = sqp[:, b0 + 128 + h * 64:b0 + 128 + (h + 1) * 64]
                e.matmul(WK[0:64, h * 64:(h + 1) * 64], pp, mp, start=True, stop=True)
                inst = e.matmul(WK[0:64, 128 + h * 64:128 + (h + 1) * 64], mp, pp,
                                start=True, stop=True)
            return inst
        sch.op("tensor", sq_, reads=[bm1, bq, bsq, bR1], writes=[bR1], tag=f"us{lvl}")
        sch.op("scalar", lambda e, lvl=lvl: e.copy(
            sqp[:, lvl * 256:(lvl + 1) * 256], WK[0:64, 0:256]),
            reads=[bR1, bsq], writes=[bsq, bR1], tag=f"use{lvl}")

    xf = sb[f"xf{par}"]
    bxf = bf(f"xf{par}")
    a_, b_ = 0, 1
    for lvl in range(6):
        def ap_(e, lvl=lvl, a_=a_):
            inst = None
            xa = sb[f"x{a_}{par}"]
            for h in range(2):
                if lvl == 0:
                    pp = p1(h)
                else:
                    b0 = (lvl - 1) * 256
                    pp = sqp[:, b0 + 128 + h * 64:b0 + 128 + (h + 1) * 64]
                inst = e.matmul(WK[0:64, h * 128:(h + 1) * 128], pp,
                                xa[:, h * 128:(h + 1) * 128], start=True, stop=True)
            return inst
        sch.op("tensor", ap_, reads=[bq, bsq, bx[a_], bR1], writes=[bR1], tag=f"ua{lvl}")
        if lvl == 0:
            sch.op("vector", lambda e: e.tensor_add(
                xf[:, :], WK[0:64, 0:256], sb[f"x0{par}"][:, :]),
                reads=[bR1, bx[0], bxf], writes=[bxf, bR1], tag=f"uae{lvl}")
        else:
            sch.op("vector", lambda e: e.tensor_add(
                xf[:, :], WK[0:64, 0:256], xf[:, :]),
                reads=[bR1, bxf], writes=[bxf, bR1], tag=f"uae{lvl}")
        if lvl < 5:
            sch.op("scalar", lambda e, b_=b_: e.copy(sb[f"x{b_}{par}"][:, :], xf[:, :]),
                   reads=[bxf, bx[b_]], writes=[bx[b_]], tag=f"uac{lvl}")
        a_, b_ = b_, a_
    sch.op("scalar", lambda e: e.copy(sb[f"x{a_}{par}"][:, :], xf[:, :]),
           reads=[bxf, bx[a_]], writes=[bx[a_]], tag="uacf")
    xa = sb[f"x{a_}{par}"]
    bxa = bx[a_]

    def strop(e):
        e.tensor_scalar(stR[:, 0:64], St[:, stc:stc + 64],
                        sb["eb"][:, ebb[0] + cc:ebb[0] + cc + 1], None, op0=ALU.mult)
        return e.tensor_scalar(stR[:, 64:128], St[:, stc + 64:stc + 128],
                               sb["eb"][:, ebb[1] + cc:ebb[1] + cc + 1], None,
                               op0=ALU.mult)
    sch.op("vector", strop, reads=[bSt, beb, bstR], writes=[bstR], tag="ustr")

    def wtp(e):
        e.transpose(WKB[0:64, 768:832], xa[:, 0:64], sb["ident_bf"][0:64, 0:64])
        return e.transpose(WKB[0:64, 832:896], xa[:, 128:192], sb["ident_bf"][0:64, 0:64])
    sch.op("tensor", wtp, reads=[bxa, bf("ident_bf"), bR3], writes=[bR3], tag="uwt")
    sch.op("scalar", lambda e: e.copy(wfm[:, :], WKB[0:64, 768:896]),
           reads=[bR3, bwfm], writes=[bwfm, bR3], tag="uwe")

    def smm(e):
        e.matmul(WK[0:64, 256:320], wfm[:, 0:64], stR[:, 0:64], start=True, stop=True)
        return e.matmul(WK[0:64, 320:384], wfm[:, 64:128], stR[:, 64:128],
                        start=True, stop=True)
    sch.op("tensor", smm, reads=[bwfm, bstR, bR2], writes=[bR2], tag="usm")
    sch.op("vector", lambda e: e.tensor_add(
        sgm.ap().rearrange("p (h q) -> p h q", h=2),
        WK[0:64, 256:384].rearrange("p (h q) -> p h q", h=2),
        xf.ap().rearrange("p (h q) -> p h q", h=2)[:, :, 64:128]),
        reads=[bR2, bxf, bsgm], writes=[bsgm, bR2], tag="usge")
    sgmb = sb[f"sgmb{par}"]
    bsgmb = bf(f"sgmb{par}")
    sch.op("scalar", lambda e: e.copy(sgmb[:, :], sgm[:, :]),
           reads=[bsgm, bsgmb], writes=[bsgmb], tag="usgc")

    def stm(e):
        btm_, vtm_, ktm_ = tm[:, 256:384], tm[:, 128:256], tm[:, 384:512]
        e.matmul(WK[0:64, 384:448], btm_[:, 0:64], sgmb[:, 0:64], start=True, stop=False)
        e.matmul(WK[0:64, 384:448], ktm_[:, 0:64], vtm_[:, 0:64], start=False, stop=True)
        e.matmul(WK[0:64, 448:512], btm_[:, 64:128], sgmb[:, 64:128],
                 start=True, stop=False)
        return e.matmul(WK[0:64, 448:512], ktm_[:, 64:128], vtm_[:, 64:128],
                        start=False, stop=True)
    sch.op("tensor", stm, reads=[btm, bsgmb, bR3], writes=[bR3], tag="ust")

    def stup(e):
        e.tensor_scalar(stt[:, 0:64], WK[0:64, 384:448],
                        sb["eb"][:, ebb[0] + NCC + cc:ebb[0] + NCC + cc + 1], None,
                        op0=ALU.mult)
        return e.tensor_scalar(stt[:, 64:128], WK[0:64, 448:512],
                               sb["eb"][:, ebb[1] + NCC + cc:ebb[1] + NCC + cc + 1],
                               None, op0=ALU.mult)
    sch.op("vector", stup, reads=[bR3, beb, bstt], writes=[bstt, bR3], tag="usu1")

    def stup2(e):
        e.scalar_tensor_tensor(
            St[:, stc:stc + 64], St[:, stc:stc + 64],
            sb["eb"][:, ebb[0] + 2 * NCC + cc:ebb[0] + 2 * NCC + cc + 1],
            stt[:, 0:64], op0=ALU.mult, op1=ALU.add)
        return e.scalar_tensor_tensor(
            St[:, stc + 64:stc + 128], St[:, stc + 64:stc + 128],
            sb["eb"][:, ebb[1] + 2 * NCC + cc:ebb[1] + 2 * NCC + cc + 1],
            stt[:, 64:128], op0=ALU.mult, op1=ALU.add)
    sch.op("vector", stup2, reads=[bstt, beb, bSt], writes=[bSt], tag="usu2")

    def omm(e):
        e.matmul(O_[0:64, oc0:oc0 + 64], stR[:, 0:64],
                 sb["lr"][:, lrc(0) + 64:lrc(0) + 128],
                 start=True, stop=False)
        e.matmul(O_[0:64, oc0:oc0 + 64], sgmb[:, 0:64], qts[:, 64:128],
                 start=False, stop=False)
        e.matmul(O_[0:64, oc0:oc0 + 64], tm[:, 128:192], qts[:, 192:256],
                 start=False, stop=True)
        e.matmul(O_[64:128, oc0:oc0 + 64], stR[:, 64:128],
                 sb["lr"][:, lrc(1) + 64:lrc(1) + 128],
                 start=True, stop=False, tile_position=(0, 64))
        e.matmul(O_[64:128, oc0:oc0 + 64], sgmb[:, 64:128], qts[:, 320:384],
                 start=False, stop=False, tile_position=(0, 64))
        return e.matmul(O_[64:128, oc0:oc0 + 64], tm[:, 192:256], qts[:, 448:512],
                        start=False, stop=True, tile_position=(0, 64))
    sch.op("tensor", omm, reads=[bstR, bsgmb, btm, bq, blr, bO], writes=[bO], tag="uo")
    if c == og0 + ong - 1:
        OB = [bf(f"PS5o{j}") for j in range(ong)]
        sch.op("scalar", lambda e: e.copy(
            sb["ost"].ap().rearrange("p (k u) -> p k u", k=NCH)
            [:, og0:og0 + ong, cc * LC:(cc + 1) * LC],
            O_[:, 256:256 + ong * 64].rearrange("p (k u) -> p k u", k=ong)),
            reads=OB, writes=[bf(f"ost{cj}") for cj in range(og0, og0 + ong)] + OB,
            tag="uoe")



def post_scan(sch, sb, PS, bf, dI, cfg, t, y_dram):
    TA, NCH = cfg.TA, cfg.NCH
    t0 = t * TA
    g_ = sb["gnstat"]
    bg = bf("gnstat")
    for c in range(NCH):
        co = c * TA
        pq = (c % 2) * TA
        bsq_, bkr = bf(f"sq{c % 2}"), bf(f"kkr{c % 2}")
        pk = (c % 2) * 2 * TA
        sch.op("scalar", lambda e, co=co, pq=pq: e.activation(
            sb["sq"][:, pq:pq + TA], sb["ost"][:, co:co + TA], AF.Square),
            reads=[bf(f"ost{c}"), bsq_], writes=[bsq_], tag="g0")
        sch.op("tensor", lambda e, c=c, co=co, pq=pq: [
            e.matmul(PS[5][0:16, 0:TA], sb["sel_gn"][:, 16 * c:16 * (c + 1)],
                     sb["ost"][:, co:co + TA], start=(c == 0), stop=(c == NCH - 1)),
            e.matmul(PS[6][0:16, 0:TA], sb["sel_gn"][:, 16 * c:16 * (c + 1)],
                     sb["sq"][:, pq:pq + TA], start=(c == 0), stop=(c == NCH - 1))][-1],
            reads=[bf(f"ost{c}"), bsq_, bf("sel_gn"), bf("PS5s"), bf("QT0")],
            writes=[bf("PS5s"), bf("QT0")], tag="g1")
        ho = 2 * c * TA
        for h in range(2):
            S = 2 * c + h
            sch.op("vector", lambda e, ho=ho, h=h, S=S, pk=pk, c=c:
                e.scalar_tensor_tensor(
                    sb["kkr"][:, pk + h * TA:pk + (h + 1) * TA],
                    sb["rst"][:, ho + h * TA:ho + (h + 1) * TA], sb["rkh"][:, S:S + 1],
                    sb["kst"][:, ho + h * TA:ho + (h + 1) * TA],
                    op0=ALU.mult, op1=ALU.mult),
                reads=[bf(f"rst{c}"), bf(f"kst{c}"), bf("rkh"), bkr], writes=[bkr],
                tag="g2")
        def g3(e, c=c, pk=pk):
            inst = None
            for h in range(2):
                S = 2 * c + h
                inst = e.matmul(PS[3][0:16, 0:TA], sb["bsel_hm"][:, 16 * S:16 * (S + 1)],
                                sb["kkr"][:, pk + h * TA:pk + (h + 1) * TA],
                                start=(S == 0), stop=(S == 15))
            return inst
        sch.op("tensor", g3, reads=[bkr, bf("bsel_hm"), bf("PS3q00")],
               writes=[bf("PS3q00")], tag="g3")
    sch.op("vector", lambda e: e.tensor_scalar_mul(g_[0:16, 0:TA], PS[5][0:16, 0:TA],
                                                   1.0 / 64.0),
           reads=[bf("PS5s"), bg], writes=[bg], tag="g4")
    sch.op("vector", lambda e: e.tensor_scalar_mul(g_[0:16, TA:2 * TA],
                                                   PS[6][0:16, 0:TA], 1.0 / 64.0),
           reads=[bf("QT0")], writes=[bg, bf("QT0")], tag="g5")
    sch.op("vector", lambda e: e.scalar_tensor_tensor(
        g_[0:16, 2 * TA:3 * TA], g_[0:16, 0:TA], -1.0, g_[0:16, 0:TA],
        op0=ALU.mult, op1=ALU.mult), reads=[bg], writes=[bg], tag="g6")
    sch.op("vector", lambda e: e.tensor_tensor(
        g_[0:16, TA:2 * TA], g_[0:16, TA:2 * TA], g_[0:16, 2 * TA:3 * TA], op=ALU.add),
        reads=[bg], writes=[bg], tag="g7")
    sch.op("vector", lambda e: e.tensor_scalar_add(g_[0:16, TA:2 * TA],
                                                   g_[0:16, TA:2 * TA], EPS),
           reads=[bg], writes=[bg], tag="g8")
    sch.op("vector", lambda e: e.reciprocal(g_[0:16, 2 * TA:3 * TA], g_[0:16, TA:2 * TA]),
           reads=[bg], writes=[bg], tag="g9")
    sch.op("scalar", lambda e: e.activation(g_[0:16, TA:2 * TA], g_[0:16, 2 * TA:3 * TA],
                                            AF.Sqrt),
           reads=[bg], writes=[bg], tag="gA")
    sch.op("vector", lambda e: e.scalar_tensor_tensor(
        g_[0:16, 2 * TA:3 * TA], g_[0:16, 0:TA], -1.0, g_[0:16, TA:2 * TA],
        op0=ALU.mult, op1=ALU.mult), reads=[bg], writes=[bg], tag="gB")
    sch.op("vector", lambda e: e.tensor_copy(g_[0:16, 0:TA], PS[3][0:16, 0:TA]),
           reads=[bf("PS3q00")], writes=[bg, bf("PS3q00")], tag="gC")
    P3ALL = [bf("PS3q00"), bf("PS3q01"), bf("PS3q10"), bf("PS3q11")]
    for c in range(NCH):
        co = c * TA
        pq = (c % 2) * TA
        bsq_, bsq2 = bf(f"sq{c % 2}"), bf(f"sq2{c % 2}")
        sch.op("tensor", lambda e, c=c: [
            e.matmul(PS[3][:, 0:TA], sb["sel_gnb"][0:16, 128 * c:128 * (c + 1)],
                     g_[0:16, TA:2 * TA], start=True, stop=True),
            e.matmul(PS[3][:, 256:256 + TA], sb["sel_gnb"][0:16, 128 * c:128 * (c + 1)],
                     g_[0:16, 2 * TA:3 * TA], start=True, stop=True)][-1],
            reads=[bg, bf("sel_gnb")] + P3ALL, writes=P3ALL, tag="gD")
        sch.op("tensor", lambda e, c=c: e.matmul(
            PS[5][:, 0:TA], sb["sel_gnb"][0:16, 128 * c:128 * (c + 1)], g_[0:16, 0:TA],
            start=True, stop=True),
            reads=[bg, bf("sel_gnb"), bf("PS5s")], writes=[bf("PS5s")], tag="gE")
        sch.op("vector", lambda e, co=co, c=c, pq=pq: e.scalar_tensor_tensor(
            sb["sq2"][:, pq:pq + TA], sb["ost"][:, co:co + TA], sb["gn_w"][:, c:c + 1],
            PS[3][:, 0:TA], op0=ALU.mult, op1=ALU.mult),
            reads=[bf(f"ost{c}"), bf("gn_w"), bf("PS3q00"), bf("PS3q10"), bsq2],
            writes=[bsq2], tag="gF")
        sch.op("vector", lambda e, c=c, pq=pq: e.tensor_scalar(
            sb["sq"][:, pq:pq + TA], PS[3][:, 256:256 + TA], sb["gn_w"][:, c:c + 1],
            sb["gn_b"][:, c:c + 1], op0=ALU.mult, op1=ALU.add),
            reads=[bf("PS3q01"), bf("PS3q11"), bf("gn_w"), bf("gn_b"), bsq_],
            writes=[bsq_], tag="gG")
        sch.op("vector", lambda e, pq=pq: e.tensor_add(
            sb["sq2"][:, pq:pq + TA], sb["sq2"][:, pq:pq + TA], sb["sq"][:, pq:pq + TA]),
            reads=[bsq_, bsq2], writes=[bsq2, bsq_], tag="gH")
        sch.op("vector", lambda e, co=co, pq=pq: e.tensor_mul(
            sb["sq"][:, pq:pq + TA], sb["vst"][:, co:co + TA], PS[5][:, 0:TA]),
            reads=[bf(f"vst{c}"), bf("PS5s"), bsq_], writes=[bsq_, bf("PS5s")], tag="gI")
        sch.op("vector", lambda e, pq=pq: e.tensor_add(
            sb["sq2"][:, pq:pq + TA], sb["sq2"][:, pq:pq + TA], sb["sq"][:, pq:pq + TA]),
            reads=[bsq_, bsq2], writes=[bsq2, bsq_], tag="gJ")
        sch.op("vector", lambda e, co=co, c=c, pq=pq: e.tensor_mul(
            sb["mixA"][:, co:co + TA], sb["sq2"][:, pq:pq + TA],
            sb["gst"][:, co:co + TA]),
            reads=[bsq2, bf(f"gst{c}"), bf(f"mixA{c}")],
            writes=[bf(f"mixA{c}"), bsq2], tag="gK")
    for f in range(NCH):
        bank, bb = (PS[0], bf("PS0")) if f % 2 == 0 else (PS[1], bf("PS1"))
        won = f"Wb{f % 2}"
        bwo = bf(won)
        sch.dma("gld", lambda e, f=f, won=won: e.dma_start(
            out=sb[won].ap().rearrange("p (k c) -> p k c", k=8),
            in_=dI["W_o"][:, f * 128:(f + 1) * 128].rearrange("(k p) c -> p k c", p=128)),
            writes=[bwo], tag="ldwo")
        sch.op("tensor", lambda e, f=f, bank=bank, won=won: [
            e.matmul(bank[:, 0:TA], sb[won][:, k * 128:(k + 1) * 128],
                     sb["mixA"][:, k * TA:(k + 1) * TA], start=(k == 0), stop=(k == 7))
            for k in range(8)][-1],
            reads=[bwo, bb] + [bf(f"mixA{k}") for k in range(NCH)],
            writes=[bb], tag="wo")
        sch.op("vector", lambda e, f=f, bank=bank: e.tensor_add(
            sb["ytile"][:, f * TA:(f + 1) * TA], bank[:, 0:TA],
            sb["xfm"][:, f * TA:(f + 1) * TA]),
            reads=[bb, bf(f"xfm{f}"), bf(f"ytile{f}")],
            writes=[bf(f"ytile{f}"), bb], tag="yp")
    emit_ln(sch, sb, PS, bf, cfg, "ytile", "ytile", TA, "sq")
    sch.op("vector", lambda e: e.reduce_max(
        sb["sm"][:, 6:7], sb["ytile"][:, 0:NCH * TA], axis=AX.X,
        apply_absolute_value=True),
        reads=[bf(f"ytile{k}") for k in range(NCH)], writes=[bf("ymx")], tag="ym1")
    sch.op("vector", lambda e: e.tensor_max(
        sb["ymax"][:, 0:1], sb["ymax"][:, 0:1], sb["sm"][:, 6:7]),
        reads=[bf("ymx"), bf("ymax")], writes=[bf("ymax")], tag="ym2")
    for c in range(NCH):
        sch.dma("st", lambda e, c=c, t0=t0: e.dma_start(
            out=y_dram[c, :, t0:t0 + TA], in_=sb["ytile"][:, c * TA:(c + 1) * TA]),
            reads=[bf(f"ytile{c}")], tag="sty")



def emit_ln(sch, sb, PS, bf, cfg, src_n, dst_n, TT, scratch_n):
    NCH = cfg.NCH
    src, dst, scr = sb[src_n], sb[dst_n], sb[scratch_n]
    ln = sb["lnrow"]
    bln = bf("lnrow")
    src_f32 = src.dtype == F32
    ones1 = sb["ones_colf"] if src_f32 else sb["ones_col"]
    bones = bf("ones_colf") if src_f32 else bf("ones_col")
    for c in range(NCH):
        co = c * TT
        po = (c % 2) * TT
        bs, bscr = bf(f"{src_n}{c}"), bf(f"{scratch_n}{c % 2}")
        sch.op("scalar", lambda e, co=co, po=po: e.activation(
            scr[:, po:po + TT], src[:, co:co + TT], AF.Square),
            reads=[bs, bscr], writes=[bscr], tag="l0")
        sch.op("tensor", lambda e, c=c, co=co, po=po, ones1=ones1: [
            e.matmul(PS[6][0:1, 0:TT], ones1[:, :], src[:, co:co + TT],
                     start=(c == 0), stop=(c == NCH - 1)),
            e.matmul(PS[7][0:1, 0:TT], sb["ones_col"][:, :], scr[:, po:po + TT],
                     start=(c == 0), stop=(c == NCH - 1))][-1],
            reads=[bs, bscr, bones, bf("ones_col"), bf("QT0"), bf("QT1")],
            writes=[bf("QT0"), bf("QT1")], tag="l1")
    sch.op("vector", lambda e: e.tensor_scalar_mul(ln[0:1, 0:TT], PS[6][0:1, 0:TT],
                                                   1.0 / 1024.0),
           reads=[bf("QT0"), bln], writes=[bln, bf("QT0")], tag="l2")
    sch.op("vector", lambda e: e.tensor_scalar_mul(ln[0:1, TT:2 * TT], PS[7][0:1, 0:TT],
                                                   1.0 / 1024.0),
           reads=[bf("QT1")], writes=[bln, bf("QT1")], tag="l3")
    sch.op("vector", lambda e: e.scalar_tensor_tensor(
        ln[0:1, 2 * TT:3 * TT], ln[0:1, 0:TT], -1.0, ln[0:1, 0:TT],
        op0=ALU.mult, op1=ALU.mult), reads=[bln], writes=[bln], tag="l4")
    sch.op("vector", lambda e: e.tensor_tensor(
        ln[0:1, TT:2 * TT], ln[0:1, TT:2 * TT], ln[0:1, 2 * TT:3 * TT], op=ALU.add),
        reads=[bln], writes=[bln], tag="l5")
    sch.op("vector", lambda e: e.tensor_scalar_add(ln[0:1, TT:2 * TT], ln[0:1, TT:2 * TT],
                                                   EPS),
           reads=[bln], writes=[bln], tag="l6")
    sch.op("vector", lambda e: e.reciprocal(ln[0:1, 2 * TT:3 * TT], ln[0:1, TT:2 * TT]),
           reads=[bln], writes=[bln], tag="l7")
    sch.op("scalar", lambda e: e.activation(ln[0:1, TT:2 * TT], ln[0:1, 2 * TT:3 * TT],
                                            AF.Sqrt),
           reads=[bln], writes=[bln], tag="l8")
    sch.op("vector", lambda e: e.scalar_tensor_tensor(
        ln[0:1, 2 * TT:3 * TT], ln[0:1, 0:TT], -1.0, ln[0:1, TT:2 * TT],
        op0=ALU.mult, op1=ALU.mult), reads=[bln], writes=[bln], tag="l9")
    sch.op("tensor", lambda e: e.matmul(
        PS[6][:, 0:TT], sb["ones_row"][0:1, :], ln[0:1, TT:2 * TT],
        start=True, stop=True),
        reads=[bln, bf("ones_row"), bf("QT0")], writes=[bf("QT0")], tag="lA")
    sch.op("tensor", lambda e: e.matmul(
        PS[7][:, 0:TT], sb["ones_row"][0:1, :], ln[0:1, 2 * TT:3 * TT],
        start=True, stop=True),
        reads=[bln, bf("ones_row"), bf("QT1")], writes=[bf("QT1")], tag="lB")
    for c in range(NCH):
        co = c * TT
        po = (c % 2) * TT
        bs, bscr = bf(f"{src_n}{c}"), bf(f"{scratch_n}{c % 2}")
        bd = bf(f"{dst_n}{c}")
        sch.op("vector", lambda e, co=co, po=po: e.tensor_mul(
            scr[:, po:po + TT], src[:, co:co + TT], PS[6][:, 0:TT]),
            reads=[bs, bf("QT0"), bscr], writes=[bscr], tag="lC")
        sch.op("vector", lambda e, co=co, po=po: e.tensor_add(
            dst[:, co:co + TT], scr[:, po:po + TT], PS[7][:, 0:TT]),
            reads=[bscr, bf("QT1"), bd], writes=[bd], tag="lD")
    sch.op("vector", lambda e: e.nop(), reads=[bf("QT0"), bf("QT1")],
           writes=[bf("QT0"), bf("QT1")], tag="lE")



def finalize_scale(sch, sb, PS, bf, dI, cfg, maxn, ccin, ccout, clipb, col):
    P3ALL = [bf("PS3q00"), bf("PS3q01"), bf("PS3q10"), bf("PS3q11")]
    sch.op("tensor", lambda e: e.transpose(PS[3][0:1, 0:128], sb[maxn][:, 0:1],
                                           sb["ident_f32"][:, :]),
           reads=[bf(maxn), bf("ident_f32")] + P3ALL, writes=P3ALL, tag="f0")
    sch.op("vector", lambda e: e.reduce_max(sb["sm"][0:1, 6:7], PS[3][0:1, 0:128],
                                            axis=AX.X),
           reads=P3ALL, writes=[bf("gm1")] + P3ALL, tag="f1")
    sch.op("vector", lambda e: e.tensor_copy(
        sb["sm"][0:1, 8:16], sb["sm"][0:1, 6:7].broadcast_to([1, 8])),
        reads=[bf("gm1")], writes=[bf("cc8")], tag="f2")
    sch.dma("st", lambda e: e.dma_start(out=ccin[0:1, 0:8], in_=sb["sm"][0:1, 8:16]),
            reads=[bf("cc8")], writes=[bf("ccd_" + maxn)], tag="f3")
    sch.op("gpsimd", lambda e: e.collective_compute(
        "AllReduce", ALU.max, replica_groups=[list(range(cfg.n_cores))],
        ins=[ccin.opt()], outs=[ccout.opt()]),
        reads=[bf("ccd_" + maxn)], writes=[bf("cco_" + maxn)], tag="f4",
        async_done=True)
    sch.dma("gld", lambda e: e.dma_start(out=sb["sm"][0:1, 8:16], in_=ccout[0:1, 0:8]),
            reads=[bf("cco_" + maxn)], writes=[bf("cc8")], tag="f5")
    sch.op("scalar", lambda e: e.activation(sb["sm"][0:1, 4:5], sb["sm"][0:1,
           (1 if clipb == "clipx" else 2):(2 if clipb == "clipx" else 3)], AF.Sigmoid),
           reads=[bf(clipb)], writes=[bf("sig_" + maxn)], tag="f6")
    sch.op("vector", lambda e: e.tensor_mul(sb["sm"][0:1, 3:4], sb["sm"][0:1, 8:9],
                                            sb["sm"][0:1, 4:5]),
           reads=[bf("cc8"), bf("sig_" + maxn), bf("gmx")], writes=[bf("gmx")], tag="f7")
    sch.op("vector", lambda e: e.tensor_scalar_mul(sb["sm"][0:1, 3:4], sb["sm"][0:1, 3:4],
                                                   1.0 / QMAX),
           reads=[bf("gmx")], writes=[bf("gmx")], tag="f8")
    sch.op("vector", lambda e: e.reciprocal(sb["sm"][0:1, 4:5], sb["sm"][0:1, 3:4]),
           reads=[bf("gmx"), bf("sig_" + maxn)], writes=[bf("sig_" + maxn)], tag="f9")
    sch.op("tensor", lambda e: e.matmul(
        PS[3][:, 0:2], sb["ones_row"][0:1, :], sb["sm"][0:1, 3:5],
        start=True, stop=True),
        reads=[bf("gmx"), bf("sig_" + maxn), bf("ones_row")] + P3ALL,
        writes=P3ALL, tag="fA")
    sch.op("vector", lambda e, col=col: e.tensor_copy(
        sb["scal"][:, col:col + 2], PS[3][:, 0:2]),
        reads=P3ALL + [bf("scal")], writes=[bf("scal")] + P3ALL, tag="fB")



_PROGRAM_CACHE = {}
_LAST_SCHED = None


def get_program(cfg):
    key = (cfg.T, cfg.TA, cfg.TB, cfg.n_cores)
    if key not in _PROGRAM_CACHE:
        _PROGRAM_CACHE[key] = build_program(cfg)
    return _PROGRAM_CACHE[key]


def make_in_maps(cfg, inputs, consts):
    x = np.ascontiguousarray(inputs["L0_attn_hidden_states"])
    n = cfg.n_cores
    maps = []
    common = {
        "W_r": inputs["L0_attn_W_r"], "W_k": inputs["L0_attn_W_k"],
        "W_v": inputs["L0_attn_W_v"], "W_o": inputs["L0_attn_W_o"],
        "w_la": inputs["L0_attn_w_lora_a"], "w_lb": inputs["L0_attn_w_lora_b"],
        "a_la": inputs["L0_attn_a_lora_a"], "a_lb": inputs["L0_attn_a_lora_b"],
        "g_la": inputs["L0_attn_g_lora_a"], "g_lb": inputs["L0_attn_g_lora_b"],
        "wg": inputs["L0_ffn_w_gate"], "wu": inputs["L0_ffn_w_up"],
        "wd": inputs["L0_ffn_w_down"], "r_k": inputs["L0_attn_r_k"],
        "x_r": inputs["L0_attn_x_r"], "x_w": inputs["L0_attn_x_w"],
        "x_k": inputs["L0_attn_x_k"], "x_v": inputs["L0_attn_x_v"],
        "x_a": inputs["L0_attn_x_a"], "k_k": inputs["L0_attn_k_k"],
        "k_a": inputs["L0_attn_k_a"], "w_lbias": inputs["L0_attn_w_lora_bias"],
        "a_lbias": inputs["L0_attn_a_lora_bias"], "g_lbias": inputs["L0_attn_g_lora_bias"],
        "gn_w": inputs["L0_attn_gn_weight"], "gn_b": inputs["L0_attn_gn_bias"],
        "clip_x": inputs["L0_ffn_clip_x"], "clip_down": inputs["L0_ffn_clip_down"],
    }
    common = {k: np.ascontiguousarray(np.asarray(v, dtype=np.float32))
              for k, v in common.items()}
    for i in range(n):
        m = dict(common)
        m["x"] = np.ascontiguousarray(x[i].astype(np.float32))
        for name, _, _ in CONST_SPECS:
            m[name] = consts[name]
        maps.append(m)
    return maps


def kernel(**inputs):
    cfg = Cfg()
    consts = host_constants()
    nc = get_program(cfg)
    in_maps = make_in_maps(cfg, inputs, consts)
    res = run_bass_kernel_spmd(nc, in_maps, core_ids=list(range(cfg.n_cores)))
    out = np.stack([r["z"] for r in res.results], axis=0)
    return out.astype(np.float32)
